# version 12
# baseline (speedup 1.0000x reference)
import numpy as np
import ml_dtypes
import concourse.bass as bass
import concourse.mybir as mybir
from concourse.bass_utils import run_bass_kernel_spmd

F32 = mybir.dt.float32
BF16 = mybir.dt.bfloat16
AF = mybir.ActivationFunctionType
ALU = mybir.AluOpType
AX = mybir.AxisListType
BF = ml_dtypes.bfloat16

NCORE = 8
B, S, D = 4, 4096, 1024
POS = S // NCORE
TOK = B * POS
NT = TOK // 128
ND = D // 128
NH, HD, HARM = 16, 64, 32
TWO_PI = float(2.0 * np.pi)
MAGIC = float(1.5 * 2 ** 23)


class Plan:
    AUTO = {"dve": "DVEa", "act": "ACTa"}

    def __init__(self):
        self.steps = []
        self.cnt = {}

    def c(self, s):
        return self.cnt.get(s, 0)

    def add(self, eng, emit, waits=None, incs=None):
        incs = list(incs or [])
        auto = self.AUTO.get(eng)
        if auto is not None:
            incs.append((auto, 1))
        for s, d in incs:
            self.cnt[s] = self.c(s) + d
        self.steps.append((eng, emit, list(waits or []), incs))
        return {s: self.c(s) for s, _ in incs}

    def now(self, *sems):
        return [(s, self.c(s)) for s in sems]

    NQ = 8

    def dma(self, eng, emit, waits=None):
        i = self.cnt.get("_dmaidx_" + eng, 0)
        self.cnt["_dmaidx_" + eng] = i + 1
        sem = f"D{eng}{i % self.NQ}"
        w = list(waits or []) + [(sem, self.c(sem))]
        return self.add(eng, emit, waits=w, incs=[(sem, 16)])

    def dma_done(self, eng):
        return [(f"D{eng}{j}", self.c(f"D{eng}{j}")) for j in range(self.NQ)
                if self.c(f"D{eng}{j}") > 0]


ENGMAP = [("sy", "sync"), ("te", "tensor"), ("dve", "vector"),
          ("act", "scalar"), ("gp", "gpsimd")]


def realize(nc, plan):
    names = set(plan.cnt.keys())
    for (_, _, waits, _) in plan.steps:
        for (s, _) in waits:
            names.add(s)
    sems = {n: nc.alloc_semaphore(n) for n in sorted(names)}
    with nc.Block() as blk:
        for nm, attr in ENGMAP:
            def mk(nm=nm):
                def body(eng):
                    seen = {}
                    auto = Plan.AUTO.get(nm)
                    nauto = 0
                    for (e, emit, waits, incs) in plan.steps:
                        if e != nm:
                            continue
                        if auto is not None and nauto > 0:
                            waits = waits + [(auto, nauto)]
                        for (s, v) in waits:
                            if v > 0 and seen.get(s, -1) < v:
                                eng.wait_ge(sems[s], v)
                                seen[s] = v
                        inst = emit(eng)
                        for (s, d) in incs:
                            inst = inst.then_inc(sems[s], d)
                        if auto is not None:
                            nauto += 1
                return body
            getattr(blk, attr)(mk())
    return sems


class Arena:

    def __init__(self, nc, name, dtype, words):
        self.t = nc.alloc_sbuf_tensor(name, [128, words], dtype)
        self.off = 0
        self.marks = []

    def alloc(self, *shape):
        n = int(np.prod(shape))
        v = self.t[:, self.off:self.off + n]
        self.off += n
        if len(shape) > 1:
            pat = " ".join(f"a{i}" for i in range(len(shape)))
            v = v.rearrange(f"p ({pat}) -> p {pat}",
                            **{f"a{i}": shape[i] for i in range(len(shape))})
        return v

    def push(self):
        self.marks.append(self.off)

    def pop(self):
        self.off = self.marks.pop()


def ln_phase(p, T, n_tiles, src_tm, dst_tm, eps=1e-5):
    xc, sq, st = T["xc"], T["sq"], T["st"]
    for t in range(n_tiles):
        x_t = src_tm[:, t, :]
        p.add("dve", lambda e, x_t=x_t, st=st: e.reduce_sum(
            st[:, 0:1], x_t, axis=AX.X), waits=p.now("ACTa"))
        p.add("dve", lambda e, st=st: e.tensor_scalar(
            st[:, 1:2], st[:, 0:1], 1.0 / 1024.0, None, ALU.mult))
        p.add("dve", lambda e, x_t=x_t, xc=xc, st=st: e.tensor_scalar(
            xc, x_t, st[:, 1:2], None, ALU.subtract))
        p.add("dve", lambda e, xc=xc, sq=sq, st=st: e.scalar_tensor_tensor(
            sq, xc, 0.0, xc, ALU.add, ALU.mult, accum_out=st[:, 2:3]))
        p.add("act", lambda e, st=st, eps_ap=T["eps"]: e.activation(
            st[:, 3:4], st[:, 2:3], AF.Ln, bias=eps_ap, scale=1.0 / 1024.0),
            waits=p.now("DVEa"))
        p.add("act", lambda e, st=st: e.activation(
            st[:, 4:5], st[:, 3:4], AF.Exp, bias=0.0, scale=-0.5))
        p.add("dve", lambda e, xc=xc, st=st, d=dst_tm[:, t, :]: e.tensor_scalar(
            d, xc, st[:, 4:5], None, ALU.mult), waits=p.now("ACTa"))


def ln_t_phase(p, T, n_tiles, src_tm, dst_fm, pt_bf, ident, sem, eps=1e-5,
               first_waits=()):
    xc, sq, st, xnt = T["xc"], T["sq"], T["st"], T["xnt"]
    te_tok = 0
    for t in range(n_tiles):
        x_t = src_tm[:, t, :]
        p.add("dve", lambda e, x_t=x_t, st=st: e.reduce_sum(
            st[:, 0:1], x_t, axis=AX.X),
            waits=p.now("ACTa") + (list(first_waits) if t == 0 else []))
        p.add("dve", lambda e, st=st: e.tensor_scalar(
            st[:, 1:2], st[:, 0:1], 1.0 / 1024.0, None, ALU.mult))
        p.add("dve", lambda e, x_t=x_t, xc=xc, st=st: e.tensor_scalar(
            xc, x_t, st[:, 1:2], None, ALU.subtract))
        p.add("dve", lambda e, xc=xc, sq=sq, st=st: e.scalar_tensor_tensor(
            sq, xc, 0.0, xc, ALU.add, ALU.mult, accum_out=st[:, 2:3]))
        p.add("act", lambda e, st=st, eps_ap=T["eps"]: e.activation(
            st[:, 3:4], st[:, 2:3], AF.Ln, bias=eps_ap, scale=1.0 / 1024.0),
            waits=p.now("DVEa"))
        p.add("act", lambda e, st=st: e.activation(
            st[:, 4:5], st[:, 3:4], AF.Exp, bias=0.0, scale=-0.5))
        w = p.now("ACTa")
        if t > 0:
            w += [(sem, te_tok)]
        p.add("dve", lambda e, xc=xc, st=st, xnt=xnt: e.tensor_scalar(
            xnt, xc, st[:, 4:5], None, ALU.mult), waits=w)
        mult_tok = p.now("DVEa")
        for dt in range(ND):
            w = list(mult_tok) if dt == 0 else []
            if dt == 0:
                w += p.now("DVEa")
            p.add("te", lambda e, dt=dt, xnt=xnt: e.transpose(
                pt_bf[:, bass.ts(dt, 128)], xnt[:, bass.ts(dt, 128)], ident),
                waits=w, incs=([(sem, 1)] if dt == ND - 1 else None))
        te_tok = p.c(sem)
        p.add("dve", lambda e, t=t: e.tensor_copy(
            dst_fm[:, :, bass.ts(t, 128)],
            pt_bf[:].rearrange("p (a b) -> p a b", a=ND)),
            waits=[(sem, te_tok)])


def transpose_tm_to_fm(p, T, src_tm, dst_fm, n_tiles, pt_bf, ident, sem="TT",
                       src_ready=()):
    copy_tok = None
    for t in range(n_tiles):
        for dt in range(ND):
            last = dt == ND - 1
            w = list(src_ready) if (t == 0 and dt == 0) else []
            if dt == 0 and copy_tok is not None:
                w += list(copy_tok.items())
            p.add("te", lambda e, s=src_tm[:, t, bass.ts(dt, 128)],
                  d=pt_bf[:, bass.ts(dt, 128)], ident=ident:
                  e.transpose(d, s, ident),
                  waits=w,
                  incs=([(sem, 1)] if last else None))
        copy_tok = p.add(
            "dve", lambda e, t=t, pt_bf=pt_bf:
            e.tensor_copy(dst_fm[:, :, bass.ts(t, 128)],
                          pt_bf[:].rearrange("p (a b) -> p a b", a=ND)),
            waits=p.now(sem))


def host_prep(inputs):
    H = {}
    x = np.asarray(inputs["L0_attn_x"])
    H["x_shards"] = [np.ascontiguousarray(
        x[:, c * POS:(c + 1) * POS, :]).reshape(TOK, D) for c in range(NCORE)]
    H["x_full"] = np.ascontiguousarray(x.reshape(B * S, D))
    for nm in ("q", "k", "v", "o"):
        H[f"basis_{nm}T"] = np.ascontiguousarray(
            np.asarray(inputs[f"L0_attn_basis_{nm}"]).T)
        H[f"phase_{nm}T"] = np.ascontiguousarray(
            np.asarray(inputs[f"L0_attn_phase_{nm}"]).T)
        H[f"amp_{nm}T"] = np.ascontiguousarray(
            np.asarray(inputs[f"L0_attn_amp_{nm}"]).T)
    for nm in ("wf", "uf", "wi", "ui", "wc", "uc", "wo", "uo"):
        H[nm + "T"] = np.ascontiguousarray(np.asarray(inputs["L0_ffn_" + nm]).T)
    for nm in ("bf", "bi", "bc", "bo", "in_c"):
        H[nm + "_r"] = np.ascontiguousarray(
            np.asarray(inputs["L0_ffn_" + nm]).reshape(ND, 128).T)
    H["bo_row"] = np.ascontiguousarray(
        np.asarray(inputs["L0_ffn_bo"]).reshape(1, D))
    for nm in ("Wq", "Wk", "Wv", "Wo"):
        H[nm + "T"] = np.ascontiguousarray(np.asarray(inputs["L1_attn_" + nm]).T)
    H["rmT"] = np.ascontiguousarray(
        np.asarray(inputs["L1_ffn_rail_maker"]).transpose(0, 1, 3, 2))
    H["rupT"] = np.ascontiguousarray(
        np.asarray(inputs["L1_ffn_rail_up_proj"]).transpose(0, 1, 3, 2))
    wcT = np.asarray(inputs["L1_ffn_w_cproj"]).T
    wcT = wcT.reshape(4, 8, 8, 64, 1024).transpose(1, 2, 0, 3, 4)
    H["wcprojT"] = np.ascontiguousarray(wcT.reshape(16384, 1024))
    H["bkvT"] = np.ascontiguousarray(np.concatenate(
        [np.asarray(inputs["L0_attn_basis_k"]).T,
         np.asarray(inputs["L0_attn_basis_v"]).T], axis=1))
    H["rmJ"] = np.ascontiguousarray(
        np.asarray(inputs["L1_ffn_rail_maker"]).transpose(3, 0, 1, 2))
    H["rupJ"] = np.ascontiguousarray(
        np.asarray(inputs["L1_ffn_rail_up_proj"]).transpose(3, 0, 1, 2))
    H["ident_bf"] = np.eye(128, dtype=BF)
    H["ident_f32"] = np.eye(128, dtype=np.float32)
    H["triu"] = np.triu(np.ones((128, 128), np.float32)).astype(BF)
    return H


def build_l1():
    nc = bass.Bass("TRN2", target_bir_lowering=False)
    x_d = nc.dram_tensor("x", [TOK, D], F32, kind="ExternalInput")
    ident_d = nc.dram_tensor("ident_bf", [128, 128], BF16, kind="ExternalInput")
    basis_d = {nm: nc.dram_tensor(f"basis_{nm}T", [D, HARM], F32,
                                  kind="ExternalInput") for nm in ("q", "k", "v")}
    r_out = {nm: nc.dram_tensor(f"r{nm}", [HARM, TOK], BF16,
                                kind="ExternalOutput") for nm in ("q", "k", "v")}

    x_tm = nc.alloc_sbuf_tensor("x_tm", [128, NT, D], F32)
    xn_tm = nc.alloc_sbuf_tensor("xn_tm", [128, NT, D], BF16)
    xn_fm = nc.alloc_sbuf_tensor("xn_fm", [128, ND, TOK], BF16)
    ident = nc.alloc_sbuf_tensor("ident", [128, 128], BF16)
    bsb = {nm: nc.alloc_sbuf_tensor(f"b_{nm}", [128, ND, HARM], BF16)
           for nm in ("q", "k", "v")}
    rsb = {nm: nc.alloc_sbuf_tensor(f"r_{nm}", [HARM, TOK], BF16)
           for nm in ("q", "k", "v")}
    xc = nc.alloc_sbuf_tensor("xc", [128, D], F32)
    sq = nc.alloc_sbuf_tensor("sq", [128, D], F32)
    st = nc.alloc_sbuf_tensor("st", [128, 16], F32)
    pt_bf = nc.alloc_psum_tensor("pt_bf", [128, 1024], BF16)
    pj = nc.alloc_psum_tensor("pj", [128, 1024], F32)

    T = {"xc": xc[:], "sq": sq[:], "st": st, "eps": st[:, 7:8]}
    p = Plan()
    p.add("dve", lambda e: e.memset(st[:, 7:8], 1e-5))
    p.dma("sy", lambda e: e.dma_start(out=x_tm[:],
                                      in_=x_d[:].rearrange("(t p) d -> p t d", p=128)))
    p.dma("sy", lambda e: e.dma_start(out=ident[:], in_=ident_d[:]))
    for i, nm in enumerate(("q", "k", "v")):
        p.dma("gp", lambda e, nm=nm: e.dma_start(
            out=bsb[nm][:],
            in_=basis_d[nm][:].rearrange("(a p) b -> p a b", p=128)))
    p.add("dve", lambda e: e.memset(st[:, 6:7], 0.0),
          waits=p.dma_done("sy"))
    ln_phase(p, T, NT, x_tm, xn_tm)
    transpose_tm_to_fm(p, T, xn_tm, xn_fm, NT, pt_bf, ident[:],
                       src_ready=p.now("DVEa") + p.dma_done("sy"))
    p.add("te", lambda e: e.nop(), waits=p.dma_done("gp"))
    for i, nm in enumerate(("q", "k", "v")):
        for n in range(4):
            buf = pj[:, bass.ts(n % 2, 512)]
            for dt in range(ND):
                p.add("te", lambda e, nm=nm, n=n, dt=dt, buf=buf: e.matmul(
                    buf[0:HARM, :], bsb[nm][:, dt, :],
                    xn_fm[:, dt, bass.ts(n, 512)],
                    start=(dt == 0), stop=(dt == ND - 1)),
                    waits=(p.now("DVEa", "ACTa") if dt == 0 else None),
                    incs=([("TEP", 1)] if dt == ND - 1 else None))
            p.add("act", lambda e, nm=nm, n=n, buf=buf: e.copy(
                rsb[nm][:, bass.ts(n, 512)], buf[0:HARM, :]),
                waits=p.now("TEP"))
    for nm in ("q", "k", "v"):
        p.dma("sy", lambda e, nm=nm: e.dma_start(out=r_out[nm][:], in_=rsb[nm][:]),
              waits=p.now("ACTa"))
    p.add("sy", lambda e: e.nop(), waits=p.dma_done("sy"))
    realize(nc, p)
    return nc


def build_l2():
    nc = bass.Bass("TRN2", target_bir_lowering=False)
    x_d = nc.dram_tensor("x", [TOK, D], F32, kind="ExternalInput")
    rkvf_d = nc.dram_tensor("rkvf", [64, B * S], BF16, kind="ExternalInput")
    rqin_d = nc.dram_tensor("rqin", [HARM, TOK], BF16, kind="ExternalInput")
    ident_d = nc.dram_tensor("ident_bf", [128, 128], BF16, kind="ExternalInput")
    identf_d = nc.dram_tensor("ident_f32", [128, 128], F32, kind="ExternalInput")
    pa_d = {}
    for nm in ("q", "k", "v", "o"):
        pa_d[nm] = (nc.dram_tensor(f"phase_{nm}T", [HARM, D], F32, kind="ExternalInput"),
                    nc.dram_tensor(f"amp_{nm}T", [HARM, D], F32, kind="ExternalInput"))
    basiso_d = nc.dram_tensor("basis_oT", [D, HARM], F32, kind="ExternalInput")
    w_d = {nm: nc.dram_tensor(nm + "T", [D, D], F32, kind="ExternalInput")
           for nm in ("wf", "uf", "wi", "ui", "wc", "uc", "wo", "uo")}
    b_d = {nm: nc.dram_tensor(nm + "_r", [128, ND], F32, kind="ExternalInput")
           for nm in ("bf", "bi", "bc", "bo", "in_c")}
    borow_d = nc.dram_tensor("bo_row", [1, D], F32, kind="ExternalInput")
    wl1_d = {nm: nc.dram_tensor(nm + "T", [D, 64], F32, kind="ExternalInput")
             for nm in ("Wq", "Wk", "Wv")}
    scr_d = nc.dram_tensor("w_scr", [8, ND, 128, ND, 128], BF16)
    x1_d = nc.dram_tensor("x1", [TOK, D], F32, kind="ExternalOutput")
    xn1_d = nc.dram_tensor("xn1", [D, TOK], BF16, kind="ExternalOutput")
    q1_d = nc.dram_tensor("Q1", [64, TOK], BF16, kind="ExternalOutput")
    k1_d = nc.dram_tensor("K1", [64, TOK], BF16, kind="ExternalOutput")
    v1_d = nc.dram_tensor("V1", [64, TOK], BF16, kind="ExternalOutput")
    st_d = nc.dram_tensor("states", [B, 64, 65], F32, kind="ExternalOutput")

    x_tm = nc.alloc_sbuf_tensor("x_tm", [128, NT, D], F32)
    xn_fm = nc.alloc_sbuf_tensor("xn_fm", [128, ND, TOK], BF16)
    UD = nc.alloc_sbuf_tensor("UD", [128, 8192], F32)
    UE = nc.alloc_sbuf_tensor("UE", [128, 10240], F32)
    UF = nc.alloc_sbuf_tensor("UF", [128, 8192], F32)
    ident = nc.alloc_sbuf_tensor("ident", [128, 128], BF16)
    identf = nc.alloc_sbuf_tensor("identf", [128, 128], F32)
    wmatA = nc.alloc_sbuf_tensor("wmatA", [128, D], BF16)
    wmatB = nc.alloc_sbuf_tensor("wmatB", [32, D], BF16)
    basiso = nc.alloc_sbuf_tensor("basiso", [128, ND, HARM], BF16)
    st = nc.alloc_sbuf_tensor("st", [128, 16], F32)
    rec = nc.alloc_sbuf_tensor("rec", [128, 8], F32)
    bia = nc.alloc_sbuf_tensor("bia", [128, 5 * ND], F32)

    def carve(arena, w0, w1, dtype=None, shape=None):
        v = arena[:, w0:w1]
        if dtype is not None:
            v = v.bitcast(dtype)
        if shape is not None:
            pat = " ".join(f"a{i}" for i in range(len(shape)))
            v = v.rearrange(f"p ({pat}) -> p {pat}",
                            **{f"a{i}": shape[i] for i in range(len(shape))})
        return v

    rkv = carve(UD, 0, 8192, BF16)
    c_fm = carve(UD, 0, 4096, None, (ND, POS))
    c_bf = carve(UD, 4096, 6144, BF16, (ND, POS))
    gf = carve(UD, 6144, 6656)
    gi = carve(UD, 6656, 7168)
    gg = carve(UD, 7168, 7680)
    htmp = carve(UD, 7680, 8192)
    htmp2 = carve(UD, 0, 2048, None, (2, D))
    st64 = carve(UD, 0, 260)
    k2 = carve(UE, 0, 2048, BF16)
    v2p = carve(UE, 2048, 4352, BF16, (32, 2, 72))
    q2 = carve(UE, 4352, 4608, BF16)
    e4 = carve(UE, 4608, 5632, BF16, (4, POS))
    o_tm = carve(UE, 5632, 7680, BF16, (4, D))
    o_fm = carve(UE, 7680, 9728, BF16, (ND, POS))
    ro_sb = carve(UE, 9728, 9984, BF16)
    q1sb = carve(UE, 0, 1024, BF16)
    kfsb = carve(UE, 1024, 2048, BF16)
    v1sb = carve(UE, 2048, 3072, BF16)
    kf_tmv = carve(UE, 3072, 3584, BF16, (NT, 64))
    vp_tmv = carve(UE, 3584, 4160, BF16, (NT, 72))
    el_t = carve(UE, 4160, 4672)
    el_e = carve(UE, 4672, 5184)
    wuo = carve(UE, 0, 8192, BF16, (2, ND, D))
    borow = carve(UE, 8192, 8704, BF16)
    ones1 = carve(UE, 8704, 8768, BF16)
    xf_tm = carve(UE, 0, 4096, None, (4, D))
    xnf_t = carve(UE, 4096, 5120, BF16, (2, ND, 128))
    bq_sb = carve(UE, 5120, 5248, BF16, (ND, HARM))
    bkv_sb = carve(UE, 5248, 5504, BF16, (ND, 64))
    stA = carve(UE, 0, 4096, BF16, (ND, D))
    stB = carve(UE, 5632, 9728, BF16, (ND, ND, 128))
    xc = carve(UF, 0, 1024)
    sq = carve(UF, 1024, 2048)
    pst = carve(UF, 2048, 3072)
    wtmp = carve(UF, 3072, 4096)
    xnt = carve(UF, 4096, 4608, BF16)
    os_bf = carve(UF, 4608, 4864, BF16)
    xnt2 = carve(UF, 4864, 5376, BF16)
    wstr = carve(UF, 0, 6144, BF16, (2, 6, ND, 128))
    wl1 = carve(UF, 6144, 6912, BF16, (3, ND, 64))

    PZ = nc.alloc_psum_tensor("PZ", [128, 2048], F32)
    PY = nc.alloc_psum_tensor("PY", [128, 1536], F32)
    PTB = nc.alloc_psum_tensor("PTB", [128, 1024], BF16)
    LOG = [PZ[:, 0:512], PZ[:, 512:1024]]
    PO = PZ[:, 1024:1536]
    PG = PZ[:, 1536:2048]
    PW = PY[:, 0:1024]
    PW0 = PY[:, 0:512]
    PW1 = PY[:, 512:1024]
    PT32 = PY[:, 1024:1536]

    T = {"xc": xc, "sq": sq, "st": st, "eps": st[:, 7:8], "xnt": xnt}
    p = Plan()

    p.dma("sy", lambda e: e.dma_start(out=x_tm[:],
          in_=x_d[:].rearrange("(t p) d -> p t d", p=128)))
    p.dma("sy", lambda e: e.dma_start(out=ident[:], in_=ident_d[:]))
    p.dma("sy", lambda e: e.dma_start(out=identf[:], in_=identf_d[:]))
    ds_loads = p.dma_done("sy")
    p.dma("gp", lambda e: e.dma_start(out=basiso[:],
          in_=basiso_d[:].rearrange("(a p) b -> p a b", p=128)))
    for i, nm in enumerate(("bf", "bi", "bc", "bo", "in_c")):
        p.dma("gp", lambda e, nm=nm, i=i: e.dma_start(
            out=bia[:, bass.ts(i, ND)], in_=b_d[nm][:]))
    for i, nm in enumerate(("Wq", "Wk", "Wv")):
        p.dma("gp", lambda e, nm=nm, i=i: e.dma_start(
            out=wl1[:, i, :, :], in_=wl1_d[nm][:].rearrange(
                "(a p) b -> p a b", p=128)))
    dl_loads = p.dma_done("gp")

    wdst = {"q": wmatA[64:96, :], "k": wmatA[0:32, :],
            "v": wmatA[32:64, :], "o": wmatB[0:32, :]}
    pa_tok = None
    for nm in ("q", "k", "v", "o"):
        w = list(pa_tok.items()) if pa_tok else []
        p.dma("gp", lambda e, nm=nm: e.dma_start(out=pst[0:32, :], in_=pa_d[nm][0][:]),
              waits=w)
        p.dma("gp", lambda e, nm=nm: e.dma_start(out=wtmp[0:32, :], in_=pa_d[nm][1][:]))
        dl_now = p.dma_done("gp")
        ph, am = pst[0:32, :], wtmp[0:32, :]
        t1, t2 = xc[0:32, :], sq[0:32, :]
        p.add("dve", lambda e, ph=ph, t1=t1: e.tensor_scalar(
            t1, ph, 1.0 / TWO_PI, MAGIC, ALU.mult, ALU.add),
            waits=dl_now)
        p.add("dve", lambda e, t1=t1: e.tensor_scalar(
            t1, t1, MAGIC, None, ALU.subtract))
        p.add("dve", lambda e, ph=ph, t1=t1: e.scalar_tensor_tensor(
            t1, t1, -TWO_PI, ph, ALU.mult, ALU.add))
        p.add("act", lambda e, t1=t1, t2=t2: e.activation(
            t2, t1, AF.Sin, bias=0.0, scale=0.5), waits=p.now("DVEa"))
        p.add("dve", lambda e, t1=t1, t2=t2: e.scalar_tensor_tensor(
            t1, t2, -2.0, t2, ALU.mult, ALU.mult),
            waits=p.now("ACTa"))
        p.add("dve", lambda e, t1=t1: e.tensor_scalar(
            t1, t1, 1.0, None, ALU.add))
        pa_tok = p.add("dve", lambda e, am=am, t1=t1, dd=wdst[nm]: e.tensor_tensor(
            dd, am, t1, ALU.mult))

    WMATS = ("wf", "uf", "wi", "ui", "wc", "uc", "uo", "wo")
    stg_cp = None
    stg_out = None
    for mi, nm in enumerate(WMATS):
        w = []
        if stg_out is not None:
            w = list(stg_out)
        p.dma("gp", lambda e, nm=nm: e.dma_start(
            out=stA[:], in_=w_d[nm][:].rearrange("(a p) b -> p a b", p=128)),
            waits=(list(stg_cp.items()) if stg_cp else []))
        stg_dma = p.dma_done("gp")
        stg_cp = p.add("dve", lambda e: e.tensor_copy(
            stB[:], stA[:].rearrange("p a (o f) -> p o a f", o=ND)),
            waits=stg_dma + (stg_out or []))
        p.dma("sy", lambda e, mi=mi: e.dma_start(
            out=scr_d[mi, :, :, :, :].rearrange("o q a f -> q o a f"),
            in_=stB[:]),
            waits=list(stg_cp.items()))
        stg_out = p.dma_done("sy")
    scr_ready = stg_out

    p.add("dve", lambda e: e.memset(st[:, 7:8], 1e-5))
    rkv_ld = p.dma("sy", lambda e: e.dma_start(out=rkv[0:64, :], in_=rkvf_d[:]))
    rq_ld = p.dma("sy", lambda e: e.dma_start(
        out=rkv[64:96, 0:TOK], in_=rqin_d[:]))
    rkv_kv_done = list(rkv_ld.items()) + list(rq_ld.items())

    p.add("dve", lambda e: e.memset(st[:, 6:7], 0.0), waits=ds_loads)
    ln_t_phase(p, T, NT, x_tm, xn_fm, PTB, ident[:], sem="TT",
               first_waits=p.now("TTf"))

    def tok_w(tok):
        return list(tok.items()) if tok else []

    rq_done = []

    LOG4 = [PZ[:, bass.ts(i, 512)] for i in range(4)]
    POa = PY[:, 0:512]
    PEX = [PY[:, 512:1024], PY[:, 1024:1536]]
    PEXw = PY[:, 512:1536]
    p.add("dve", lambda e: e.memset(
        v2p[:].rearrange("p a b c -> p (a b) c")[:, :, 64:65], 1.0))

    exp_super = {"A": None, "B": None}
    eslot = [None] * 4
    kq_free = None
    v_free = None
    pw_tok = None
    poA_tok = None
    poB_tok = None
    otm_tok = None
    wm_tok = pa_tok
    kexp_cp = None
    PEX1 = PY[:, 512:1024]
    POa = PY[:, 0:512]
    POb = PY[:, 1024:1536]
    pair_i = 0
    for b in range(B):
        for j in range(NH // 2):
            hA, hB = 2 * j, 2 * j + 1
            EXR = [PEX1, POa, POb]
            exg = [tok_w(pw_tok), tok_w(poA_tok), tok_w(poB_tok)]
            ecp = [None, None, None]
            ei = 0
            for n in range(ND):
                w = list(exg[ei % 3]) + tok_w(ecp[ei % 3])
                if pair_i == 0 and n == 0:
                    w += (tok_w(wm_tok) + ds_loads + rq_done + rkv_kv_done
                          + scr_ready)
                if n == 0:
                    w += tok_w(kq_free)
                p.add("te", lambda e, hA=hA, b=b, n=n, ei=ei: e.matmul(
                    EXR[ei % 3][:, :], wmatA[0:32, hA * HD:hA * HD + 128],
                    rkv[0:32, b * 4096 + n * 512: b * 4096 + (n + 1) * 512],
                    start=True, stop=True), waits=w, incs=[("TEk", 1)])
                tek = p.c("TEk")
                ecp[ei % 3] = p.add("dve", lambda e, n=n, ei=ei: e.tensor_copy(
                    k2[:, bass.ts(n, 512)], EXR[ei % 3][:, :]),
                    waits=[("TEk", tek)])
                ei += 1
            w = list(exg[ei % 3]) + tok_w(ecp[ei % 3])
            p.add("te", lambda e, hA=hA, b=b, ei=ei: e.matmul(
                EXR[ei % 3][:, :], wmatA[64:96, hA * HD:hA * HD + 128],
                rkv[64:96, b * 512:(b + 1) * 512], start=True, stop=True),
                waits=w, incs=[("TEk", 1)])
            tek = p.c("TEk")
            q_cp = p.add("dve", lambda e, ei=ei: e.tensor_copy(
                q2[:, :], EXR[ei % 3][:, :]), waits=[("TEk", tek)])
            ecp[ei % 3] = q_cp
            ei += 1
            kexp_cp = q_cp
            for g in range(8):
                w0 = list(exg[ei % 3]) + tok_w(ecp[ei % 3])
                for j4 in range(4):
                    kt = g * 4 + j4
                    w = w0 if j4 == 0 else []
                    if g == 0 and j4 == 0:
                        w = w + tok_w(v_free)
                    p.add("te", lambda e, b=b, hA=hA, kt=kt, j4=j4, ei=ei:
                          e.matmul(
                              EXR[ei % 3][:, bass.ts(j4, 128)],
                              rkv[32:64, b * 4096 + kt * 128:b * 4096 + (kt + 1) * 128],
                              wmatA[32:64, hA * HD:hA * HD + 128],
                              start=True, stop=True), waits=w, incs=[("TEk", 1)])
                tek = p.c("TEk")
                ecp[ei % 3] = p.add("dve", lambda e, g=g, ei=ei: e.tensor_copy(
                    v2p[:, bass.ts(g, 4), :, 0:64],
                    EXR[ei % 3][:].rearrange("p (a c d) -> p a c d", a=4, c=2)),
                    waits=[("TEk", tek)])
                ei += 1
            v_cp = ecp[(ei - 1) % 3]
            stg_guard = tok_w(ecp[0]) + tok_w(ecp[1]) + tok_w(ecp[2])
            pw_tok = v_cp
            for pk in range(16):
                kt0 = 2 * pk
                for kq in range(2):
                    kt = kt0 + kq
                    for st_h, base in (("A", 0), ("B", 2)):
                        hp = 0 if st_h == "A" else 1
                        w = []
                        if exp_super[st_h]:
                            w += [("ACTa", exp_super[st_h]["ACTa"])]
                        if pk == 0 and kq == 0 and st_h == "A":
                            w += tok_w(kexp_cp) + tok_w(q_cp)
                        p.add("te", lambda e, hp=hp, kt=kt, base=base, kq=kq:
                              e.matmul(LOG4[base + kq][:, :],
                                       k2[bass.ts(hp, 64), bass.ts(kt, 128)],
                                       q2[bass.ts(hp, 64), :],
                                       start=True, stop=True),
                              waits=w, incs=[("TEl", 1)])
                for st_h, base in (("A", 0), ("B", 2)):
                    w = [("TEl", p.c("TEl"))]
                    for sl in (base, base + 1):
                        if eslot[sl]:
                            w += [("TEo", eslot[sl]["TEo"])]
                    ex = p.add("act", lambda e, base=base: e.activation(
                        e4[:, base:base + 2, :].rearrange("p a c -> p (a c)"),
                        PZ[:, base * 512:base * 512 + 1024],
                        AF.Exp, bias=0.0, scale=0.125), waits=w)
                    exp_super[st_h] = ex
                for kq in range(2):
                    kt = kt0 + kq
                    for st_h, base, po in (("A", 0, POa), ("B", 2, POb)):
                        hp = 0 if st_h == "A" else 1
                        w = [("ACTa", exp_super[st_h]["ACTa"])]
                        if kt == 0 and st_h == "A":
                            w += stg_guard
                        p.add("te", lambda e, hp=hp, kt=kt, po=po, base=base,
                              kq=kq: e.matmul(
                                  po[0:65, :], v2p[:, kt, hp, 0:65],
                                  e4[:, base + kq, :],
                                  start=(kt == 0), stop=(kt == 31)),
                              waits=w, incs=[("TEo", 1)])
                        eslot[base + kq] = {"TEo": p.c("TEo")}
            v_free = {"TEo": p.c("TEo")}
            kq_free = {"TEl": p.c("TEl")}
            pw_tok = v_cp
            for st_h, po in (("A", POa), ("B", POb)):
                h = hA if st_h == "A" else hB
                pt = p.add("dve", lambda e, po=po: e.tensor_copy(
                    os_bf[0:65, 0:512], po[0:65, :]),
                    waits=[("TEo", p.c("TEo"))] + tok_w(otm_tok) + p.now("TEt"))
                if st_h == "A":
                    poA_tok = pt
                else:
                    poB_tok = pt
                for i in range(4):
                    w = tok_w(pt) + p.now("DVEa") if i == 0 else []
                    p.add("te", lambda e, i=i: e.transpose(
                        PTB[:, i * 256:i * 256 + 65],
                        os_bf[0:65, bass.ts(i, 128)],
                        ident[0:65, 0:65]), waits=w, incs=[("TEt", 1)])
                tet = p.c("TEt")
                for i in range(4):
                    p.add("dve", lambda e, i=i: e.reciprocal(
                        rec[:, i:i + 1], PTB[:, i * 256 + 64:i * 256 + 65]),
                        waits=([("TEt", tet)] if i == 0 else []))
                    p.add("dve", lambda e, i=i, h=h: e.tensor_scalar(
                        o_tm[:, i, bass.ts(h, HD)], PTB[:, i * 256:i * 256 + 64],
                        rec[:, i:i + 1], None, ALU.mult))
            pair_i += 1
        pg_tok = v_cp
        transpose_tm_to_fm(p, T, o_tm, o_fm, 4, PTB, ident[:], sem="TTo",
                           src_ready=p.now("DVEa"))
        otm_tok = {"TTo": p.c("TTo")}
        ofm_cp = p.now("DVEa")
        for dt in range(ND):
            w = list(ofm_cp) if dt == 0 else []
            if dt == 0:
                w += tok_w(pg_tok) + tok_w(pw_tok)
            p.add("te", lambda e, dt=dt: e.matmul(
                PEX[0][0:32, :], basiso[:, dt, :], o_fm[:, dt, :],
                start=(dt == 0), stop=(dt == ND - 1)), waits=w,
                incs=([("TEr", 1)] if dt == ND - 1 else None))
        ro_cp = p.add("act", lambda e: e.copy(ro_sb[0:32, :], PEX[0][0:32, :]),
                      waits=[("TEr", p.c("TEr"))])
        pg_tok = ro_cp
        pw_tok = ro_cp
        for qt in range(4):
            w = (tok_w(ro_cp) + tok_w(poA_tok) + tok_w(poB_tok)) if qt == 0 else []
            w += tok_w(pw_tok)
            for half in range(2):
                p.add("te", lambda e, qt=qt, half=half: e.matmul(
                    PEX[half][:, :], ro_sb[0:32, bass.ts(qt, 128)],
                    wmatB[0:32, bass.ts(half, 512)],
                    start=True, stop=True),
                    waits=(w if half == 0 else []), incs=[("TEk", 1)])
            tek = p.c("TEk")
            pw_tok = p.add("dve", lambda e, b=b, qt=qt: e.tensor_add(
                x_tm[:, b * 4 + qt, :], x_tm[:, b * 4 + qt, :], PEXw[:, :]),
                waits=[("TEk", tek)])

    exp_bank = [exp_super["A"], exp_super["B"]]
    pt32_tok = {"DVEa": p.c("DVEa"), "ACTa": p.c("ACTa")}
    po_tok = poB_tok

    for dt in range(ND):
        p.add("dve", lambda e, dt=dt: e.memset(c_fm[:, dt, :], 1.0),
              waits=(dl_loads + p.now("TEk", "TEl", "TEo")
                     if dt == 0 else []))
        p.add("dve", lambda e, dt=dt: e.tensor_scalar(
            c_fm[:, dt, :], c_fm[:, dt, :], bia[:, 4 * ND + dt:4 * ND + dt + 1],
            None, ALU.mult))
    cbf_tok = p.add("dve", lambda e: e.tensor_copy(c_bf[:], c_fm[:]))
    ws_tok = [None, None]
    gact = [None, None, None]
    gsb_tok = None
    GACT_F = [AF.Sigmoid, AF.Sigmoid, AF.Tanh]
    gate_w = ("wf", "wi", "wc")
    for t in range(4):
        for ot in range(ND):
            s = (t * ND + ot) % 2
            for g in range(3):
                w = []
                if g == 0:
                    w = p.now("TEf")
                    if t == 0 and ot == 0:
                        w += p.now("TEt", "ACTa", "DVEa") + scr_ready
                p.dma("gp", lambda e, g=g, ot=ot, s=s: e.dma_start(
                    out=wstr[:, s, g, :, :],
                    in_=scr_d[2 * g, ot, :, :, :]),
                    waits=w)
            for g in range(3):
                p.dma("gp", lambda e, g=g, ot=ot, s=s: e.dma_start(
                    out=wstr[:, s, 3 + g, :, :],
                    in_=scr_d[2 * g + 1, ot, :, :, :]))
            dl_now = p.dma_done("gp")
            for g in range(3):
                pgm = PZ[:, bass.ts(g, 512)]
                for dt in range(ND):
                    w = []
                    if dt == 0:
                        w = list(dl_now) + tok_w(gact[g])
                        if g == 0:
                            w += tok_w(cbf_tok)
                        if t == 0 and ot == 0:
                            w += tok_w(exp_bank[0]) + tok_w(exp_bank[1])
                    p.add("te", lambda e, g=g, dt=dt, ot=ot, t=t, s=s, pgm=pgm: e.matmul(
                        pgm, wstr[:, s, g, dt, :], xn_fm[:, dt, bass.ts(t, 512)],
                        start=(dt == 0), stop=False), waits=w)
                for dt in range(ND):
                    p.add("te", lambda e, g=g, dt=dt, ot=ot, s=s, pgm=pgm: e.matmul(
                        pgm, wstr[:, s, 3 + g, dt, :], c_bf[:, dt, :],
                        start=False, stop=(dt == ND - 1)),
                        incs=([("TEf", 1)] if dt == ND - 1 else None))
                tef = p.c("TEf")
                gact[g] = p.add("act", lambda e, g=g, ot=ot, pgm=pgm,
                                d=(gf, gi, gg)[g]: e.activation(
                    d[:], pgm, GACT_F[g], bias=bia[:, g * ND + ot:g * ND + ot + 1],
                    scale=1.0), waits=[("TEf", tef)] + (tok_w(gsb_tok) if g == 0 else []))
            ws_tok[s] = {"TEf": p.c("TEf")}
            w = tok_w(gact[0]) + tok_w(gact[1]) + tok_w(gact[2])
            p.add("dve", lambda e, ot=ot: e.tensor_mul(
                gf[:], gf[:], c_fm[:, ot, :]), waits=w)
            p.add("dve", lambda e: e.tensor_mul(gi[:], gi[:], gg[:]))
            gsb_tok = p.add("dve", lambda e, ot=ot: e.tensor_add(
                c_fm[:, ot, :], gf[:], gi[:]))
        cbf_tok = p.add("dve", lambda e: e.tensor_copy(c_bf[:], c_fm[:]))
    p.add("dve", lambda e: e.memset(ones1[0:1, 0:128], 1.0))
    p.dma("gp", lambda e: e.dma_start(out=borow[0:1, 0:1024], in_=borow_d[:]),
          waits=p.now("TEo"))
    for wi_, nm in ((0, "wo"), (1, "uo")):
        for dt in range(ND):
            p.dma("gp", lambda e, wi_=wi_, nm=nm, dt=dt: e.dma_start(
                out=wuo[:, wi_, dt, :], in_=w_d[nm][bass.ts(dt, 128), :]))
    wuo_done = p.dma_done("gp")
    hfree = [None, None]
    htanh = [None, None]
    for ti in range(NT):
        t, pt = ti // 4, ti % 4
        hb = ti % 2
        HP = PZ[:, hb * 1024:(hb + 1) * 1024]
        tsl = slice(t * 512 + pt * 128, t * 512 + (pt + 1) * 128)
        w0 = tok_w(cbf_tok) + wuo_done + tok_w(hfree[hb]) \
            + (p.now("DVEa", "ACTa") if ti < 2 else [])
        first = True
        for half in range(2):
            osl = slice(half * 512, (half + 1) * 512)
            for dt in range(ND):
                w = list(w0) if first else []
                first = False
                p.add("te", lambda e, dt=dt, tsl=tsl, osl=osl, HP=HP, half=half:
                      e.matmul(HP[:, osl], xn_fm[:, dt, tsl],
                               wuo[:, 0, dt, osl],
                               start=(dt == 0), stop=False),
                      waits=w)
            psl = slice(pt * 128, (pt + 1) * 128)
            for dt in range(ND):
                p.add("te", lambda e, dt=dt, psl=psl, osl=osl, HP=HP, half=half:
                      e.matmul(HP[:, osl], c_bf[:, dt, psl],
                               wuo[:, 1, dt, osl],
                               start=False, stop=False))
        p.add("te", lambda e, HP=HP: e.matmul(
            HP[:, 0:512], ones1[0:1, 0:128], borow[0:1, 0:512],
            start=False, stop=True))
        p.add("te", lambda e, HP=HP: e.matmul(
            HP[:, 512:1024], ones1[0:1, 0:128], borow[0:1, 512:1024],
            start=False, stop=True), incs=[("TEf", 1)])
        tef = p.c("TEf")
        ht = p.add("act", lambda e, HP=HP, hb=hb: e.activation(
            htmp2[:, hb, :], HP[:, :], AF.Tanh, bias=0.0, scale=1.0),
            waits=[("TEf", tef)] + tok_w(htanh[hb]))
        htanh[hb] = ht
        hfree[hb] = p.add("dve", lambda e, ti=ti, hb=hb: e.tensor_add(
            x_tm[:, ti, :], x_tm[:, ti, :], htmp2[:, hb, :]),
            waits=list(ht.items()))
    ln_t_phase(p, T, NT, x_tm, xn_fm, PTB, ident[:], sem="TT1",
               first_waits=p.now("TEf"))
    xn1_done = p.now("DVEa")
    p.dma("sy", lambda e: e.dma_start(
        out=x1_d[:].rearrange("(t p) d -> p t d", p=128), in_=x_tm[:]),
        waits=p.now("DVEa"))
    p.dma("sy", lambda e: e.dma_start(
        out=xn1_d[:].rearrange("(a p) n -> p a n", p=128), in_=xn_fm[:]),
        waits=xn1_done)

    l1dst = [q1sb[0:64, 0:TOK], kfsb[0:64, 0:TOK], v1sb[0:64, 0:TOK]]
    prev = None
    for i in range(3):
        for n in range(4):
            for dt in range(ND):
                w = []
                if dt == 0:
                    w = list(xn1_done) + dl_loads
                    w += tok_w(prev)
                    if i == 0 and n == 0:
                        w += tok_w(pg_tok) + tok_w(exp_bank[1])
                p.add("te", lambda e, i=i, n=n, dt=dt: e.matmul(
                    PG[0:64, :], wl1[:, i, dt, :], xn_fm[:, dt, bass.ts(n, 512)],
                    start=(dt == 0), stop=(dt == ND - 1)),
                    waits=w, incs=([("TEr", 1)] if dt == ND - 1 else None))
            ter = p.c("TEr")
            if i < 2:
                p.add("dve", lambda e: e.tensor_scalar(
                    el_t[0:64, :], PG[0:64, :], 0.0, None, ALU.min),
                    waits=[("TEr", ter)] + (p.now("TEt", "TEf", "TEo", "TEl",
                                                  "ACTa")
                                            if (i == 0 and n == 0) else []))
                p.add("act", lambda e: e.activation(
                    el_e[0:64, :], el_t[0:64, :], AF.Exp, bias=0.0, scale=1.0),
                    waits=p.now("DVEa"))
                prev = p.add("dve", lambda e, i=i, n=n: e.scalar_tensor_tensor(
                    l1dst[i][:, bass.ts(n, 512)], PG[0:64, :], 0.0, el_e[0:64, :],
                    ALU.max, ALU.add), waits=p.now("ACTa"))
            else:
                prev = p.add("act", lambda e, i=i, n=n: e.copy(
                    l1dst[i][:, bass.ts(n, 512)], PG[0:64, :]),
                    waits=[("TEr", ter)])
    qkv_done = p.now("DVEa", "ACTa")
    for nm, dd in (("Q1", q1sb[0:64, 0:TOK]), ("K1", kfsb[0:64, 0:TOK]),
                   ("V1", v1sb[0:64, 0:TOK])):
        p.dma("sy", lambda e, nm=nm, dd=dd: e.dma_start(out={
            "Q1": q1_d, "K1": k1_d, "V1": v1_d}[nm][:], in_=dd),
            waits=qkv_done)
    p.add("dve", lambda e: e.memset(vp_tmv[:, :, 64:65], 1.0))
    for src, dst, tag in ((kfsb[0:64, 0:TOK], kf_tmv, "K"),
                          (v1sb[0:64, 0:TOK], vp_tmv, "V")):
        for rnd in range(2):
            for j in range(8):
                kt = rnd * 8 + j
                w = list(qkv_done) if (rnd == 0 and j == 0) else []
                if j == 0:
                    w += p.now("DVEa")
                p.add("te", lambda e, src=src, kt=kt, j=j: e.transpose(
                    PTB[:, j * 64:(j + 1) * 64],
                    src[:, bass.ts(kt, 128)], ident[0:64, 0:64]),
                    waits=w, incs=([("TEt", 1)] if j == 7 else None))
            tet = p.c("TEt")
            p.add("dve", lambda e, dst=dst, rnd=rnd: e.tensor_copy(
                dst[:, bass.ts(rnd, 8), 0:64],
                PTB[:, 0:512].rearrange("p (a b) -> p a b", a=8)),
                waits=[("TEt", tet)])
    tr_done = p.now("DVEa")
    for b in range(B):
        for ct in range(4):
            w = tr_done + tok_w(prev) if ct == 0 else []
            p.add("te", lambda e, b=b, ct=ct: e.matmul(
                PG[0:64, 0:65], kf_tmv[:, b * 4 + ct, 0:64],
                vp_tmv[:, b * 4 + ct, 0:65],
                start=(ct == 0), stop=(ct == 3)),
                waits=w, incs=([("TEr", 1)] if ct == 3 else None))
        prev = p.add("dve", lambda e, b=b: e.tensor_copy(
            st64[0:64, b * 65:(b + 1) * 65], PG[0:64, 0:65]),
            waits=[("TEr", p.c("TEr"))])
    p.dma("sy", lambda e: e.dma_start(
        out=st_d[:].rearrange("b k c -> k b c"),
        in_=st64[0:64, :].rearrange("p (b c) -> p b c", b=B)),
        waits=p.now("DVEa"))
    p.add("sy", lambda e: e.nop(), waits=p.dma_done("sy"))
    realize(nc, p)
    return nc


def build_fused():
    nc = bass.Bass("TRN2", target_bir_lowering=False)
    x_d = nc.dram_tensor("x", [TOK, D], F32, kind="ExternalInput")
    basis_d = {nm: nc.dram_tensor(f"basis_{nm}T", [D, HARM], F32,
                                  kind="ExternalInput") for nm in ("q", "k", "v")}
    ident_d = nc.dram_tensor("ident_bf", [128, 128], BF16, kind="ExternalInput")
    identf_d = nc.dram_tensor("ident_f32", [128, 128], F32, kind="ExternalInput")
    pa_d = {}
    for nm in ("q", "k", "v", "o"):
        pa_d[nm] = (nc.dram_tensor(f"phase_{nm}T", [HARM, D], F32, kind="ExternalInput"),
                    nc.dram_tensor(f"amp_{nm}T", [HARM, D], F32, kind="ExternalInput"))
    basiso_d = nc.dram_tensor("basis_oT", [D, HARM], F32, kind="ExternalInput")
    pm_d = nc.dram_tensor("prefix_mask", [64, NCORE], F32, kind="ExternalInput")
    w_d = {nm: nc.dram_tensor(nm + "T", [D, D], F32, kind="ExternalInput")
           for nm in ("wf", "uf", "wi", "ui", "wc", "uc", "wo", "uo")}
    b_d = {nm: nc.dram_tensor(nm + "_r", [128, ND], F32, kind="ExternalInput")
           for nm in ("bf", "bi", "bc", "bo", "in_c")}
    borow_d = nc.dram_tensor("bo_row", [1, D], F32, kind="ExternalInput")
    wl1_d = {nm: nc.dram_tensor(nm + "T", [D, 64], F32, kind="ExternalInput")
             for nm in ("Wq", "Wk", "Wv")}
    wo1_d = nc.dram_tensor("Wo1T", [64, D], F32, kind="ExternalInput")
    rm_d = nc.dram_tensor("rmJ", [16, 4, 64, 16], F32, kind="ExternalInput")
    rup_d = nc.dram_tensor("rupJ", [16, 4, 64, 64], F32, kind="ExternalInput")
    wc_d = nc.dram_tensor("wcprojT", [16384, D], F32, kind="ExternalInput")
    triu_d = nc.dram_tensor("triu", [128, 128], BF16, kind="ExternalInput")
    out_d = nc.dram_tensor("out", [TOK, D], F32, kind="ExternalOutput")
    c_scr = nc.dram_tensor("c_scr", [16, 2, 2, 8, 8, 64], BF16)
    rkv_loc = nc.dram_tensor("rkv_loc", [64, TOK], BF16)
    rkv_gath = nc.dram_tensor("rkv_gath", [NCORE * 64, TOK], BF16)
    stt_loc = nc.dram_tensor("stt_loc", [64, B * 65], F32)
    stt_gath = nc.dram_tensor("stt_gath", [NCORE * 64, B * 65], F32)

    x_tm = nc.alloc_sbuf_tensor("x_tm", [128, NT, D], F32)
    xn_fm = nc.alloc_sbuf_tensor("xn_fm", [128, ND, TOK], BF16)
    UD = nc.alloc_sbuf_tensor("UD", [128, 8192], F32)
    UE = nc.alloc_sbuf_tensor("UE", [128, 10240], F32)
    UF = nc.alloc_sbuf_tensor("UF", [128, 8192], F32)
    ident = nc.alloc_sbuf_tensor("ident", [128, 128], BF16)
    identf = nc.alloc_sbuf_tensor("identf", [128, 128], F32)
    wmatA = nc.alloc_sbuf_tensor("wmatA", [128, D], BF16)
    wmatB = nc.alloc_sbuf_tensor("wmatB", [32, D], BF16)
    basiso = nc.alloc_sbuf_tensor("basiso", [128, ND, HARM], BF16)
    st = nc.alloc_sbuf_tensor("st", [128, 16], F32)
    rec = nc.alloc_sbuf_tensor("rec", [128, 8], F32)
    bia = nc.alloc_sbuf_tensor("bia", [128, 5 * ND], F32)

    def carve(arena, w0, w1, dtype=None, shape=None):
        v = arena[:, w0:w1]
        if dtype is not None:
            v = v.bitcast(dtype)
        if shape is not None:
            pat = " ".join(f"a{i}" for i in range(len(shape)))
            v = v.rearrange(f"p ({pat}) -> p {pat}",
                            **{f"a{i}": shape[i] for i in range(len(shape))})
        return v

    rkv = carve(UD, 0, 8192, BF16)
    c_fm = carve(UD, 0, 4096, None, (ND, POS))
    c_bf = carve(UD, 4096, 6144, BF16, (ND, POS))
    gf = carve(UD, 6144, 6656)
    gi = carve(UD, 6656, 7168)
    gg = carve(UD, 7168, 7680)
    htmp = carve(UD, 7680, 8192)
    htmp2 = carve(UD, 0, 2048, None, (2, D))
    st64 = carve(UD, 0, 260)
    k2 = carve(UE, 0, 2048, BF16)
    v2p = carve(UE, 2048, 4352, BF16, (32, 2, 72))
    q2 = carve(UE, 4352, 4608, BF16)
    e4 = carve(UE, 4608, 5632, BF16, (4, POS))
    o_tm = carve(UE, 5632, 7680, BF16, (4, D))
    o_fm = carve(UE, 7680, 9728, BF16, (ND, POS))
    ro_sb = carve(UE, 9728, 9984, BF16)
    q1sb = carve(UE, 0, 1024, BF16)
    kfsb = carve(UE, 1024, 2048, BF16)
    v1sb = carve(UE, 2048, 3072, BF16)
    kf_tmv = carve(UE, 3072, 3584, BF16, (NT, 64))
    vp_tmv = carve(UE, 3584, 4160, BF16, (NT, 72))
    el_t = carve(UE, 4160, 4672)
    el_e = carve(UE, 4672, 5184)
    wuo = carve(UE, 0, 8192, BF16, (2, ND, D))
    borow = carve(UE, 8192, 8704, BF16)
    ones1 = carve(UE, 8704, 8768, BF16)
    xf_tm = carve(UE, 0, 4096, None, (4, D))
    xnf_t = carve(UE, 4096, 5120, BF16, (2, ND, 128))
    bq_sb = carve(UE, 5120, 5248, BF16, (ND, HARM))
    bkv_sb = carve(UE, 5248, 5504, BF16, (ND, 64))
    stA = carve(UE, 0, 4096, BF16, (ND, D))
    stB = carve(UE, 5632, 9728, BF16, (ND, ND, 128))
    xc = carve(UF, 0, 1024)
    sq = carve(UF, 1024, 2048)
    pst = carve(UF, 2048, 3072)
    wtmp = carve(UF, 3072, 4096)
    xnt = carve(UF, 4096, 4608, BF16)
    os_bf = carve(UF, 4608, 4864, BF16)
    xnt2 = carve(UF, 4864, 5376, BF16)
    wstr = carve(UF, 0, 6144, BF16, (2, 6, ND, 128))
    wl1 = carve(UF, 6144, 6912, BF16, (3, ND, 64))
    rloc = carve(UE, 0, 1024, BF16)
    bsb = {nm: carve(UE, 1024 + 128 * i, 1152 + 128 * i, BF16, (ND, HARM))
           for i, nm in enumerate(("q", "k", "v"))}
    rm_sb = carve(UE, 7680, 9728, BF16)
    rup_sb = carve(UF, 0, 8192, BF16)
    cflat = carve(UD, 0, 8192, BF16)
    wcs = carve(UF, 0, 8192, BF16, (2, ND, D))
    hhg = carve(UD, 0, 8192, BF16, (ND, TOK))
    wzg = carve(UE, 5632, 7680, BF16, (2, 16, 128))
    sa_sb = carve(UE, 4160, 6240)
    sa4 = sa_sb.rearrange("p (r b c) -> p r b c", r=NCORE, b=B)
    wo1 = carve(UF, 0, 512, BF16)
    triu_sb = carve(UF, 512, 576, BF16)
    et_sb = carve(UF, 576, 640, BF16)
    obf = carve(UF, 640, 672, BF16)
    ofs = carve(UF, 672, 736, BF16)
    ssb = carve(UF, 736, 769, BF16)
    sacc = carve(UF, 769, 834)
    s0_sb = carve(UF, 834, 1094, None, (B, 65))
    pm_sb = carve(UF, 1094, 1102)

    PZ = nc.alloc_psum_tensor("PZ", [128, 2048], F32)
    PY = nc.alloc_psum_tensor("PY", [128, 1536], F32)
    PTB = nc.alloc_psum_tensor("PTB", [128, 1024], BF16)
    LOG = [PZ[:, 0:512], PZ[:, 512:1024]]
    PO = PZ[:, 1024:1536]
    PG = PZ[:, 1536:2048]
    PW = PY[:, 0:1024]
    PW0 = PY[:, 0:512]
    PW1 = PY[:, 512:1024]
    PT32 = PY[:, 1024:1536]

    T = {"xc": xc, "sq": sq, "st": st, "eps": st[:, 7:8], "xnt": xnt}

    def tok_wf(tok):
        return list(tok.items()) if tok else []

    p = Plan()

    p.dma("sy", lambda e: e.dma_start(out=x_tm[:],
          in_=x_d[:].rearrange("(t p) d -> p t d", p=128)))
    p.dma("sy", lambda e: e.dma_start(out=ident[:], in_=ident_d[:]))
    p.dma("sy", lambda e: e.dma_start(out=identf[:], in_=identf_d[:]))
    ds_loads = p.dma_done("sy")
    p.dma("gp", lambda e: e.dma_start(out=basiso[:],
          in_=basiso_d[:].rearrange("(a p) b -> p a b", p=128)))
    for i, nm in enumerate(("bf", "bi", "bc", "bo", "in_c")):
        p.dma("gp", lambda e, nm=nm, i=i: e.dma_start(
            out=bia[:, bass.ts(i, ND)], in_=b_d[nm][:]))
    for i, nm in enumerate(("Wq", "Wk", "Wv")):
        p.dma("gp", lambda e, nm=nm, i=i: e.dma_start(
            out=wl1[:, i, :, :], in_=wl1_d[nm][:].rearrange(
                "(a p) b -> p a b", p=128)))
    for nm in ("q", "k", "v"):
        p.dma("gp", lambda e, nm=nm: e.dma_start(
            out=bsb[nm][:],
            in_=basis_d[nm][:].rearrange("(a p) b -> p a b", p=128)))
    dl_loads = p.dma_done("gp")

    wdst = {"q": wmatA[64:96, :], "k": wmatA[0:32, :],
            "v": wmatA[32:64, :], "o": wmatB[0:32, :]}
    pa_tok = None
    for nm in ("q", "k", "v", "o"):
        w = list(pa_tok.items()) if pa_tok else []
        p.dma("gp", lambda e, nm=nm: e.dma_start(out=pst[0:32, :], in_=pa_d[nm][0][:]),
              waits=w)
        p.dma("gp", lambda e, nm=nm: e.dma_start(out=wtmp[0:32, :], in_=pa_d[nm][1][:]))
        dl_now = p.dma_done("gp")
        ph, am = pst[0:32, :], wtmp[0:32, :]
        t1, t2 = xc[0:32, :], sq[0:32, :]
        p.add("dve", lambda e, ph=ph, t1=t1: e.tensor_scalar(
            t1, ph, 1.0 / TWO_PI, MAGIC, ALU.mult, ALU.add),
            waits=dl_now)
        p.add("dve", lambda e, t1=t1: e.tensor_scalar(
            t1, t1, MAGIC, None, ALU.subtract))
        p.add("dve", lambda e, ph=ph, t1=t1: e.scalar_tensor_tensor(
            t1, t1, -TWO_PI, ph, ALU.mult, ALU.add))
        p.add("act", lambda e, t1=t1, t2=t2: e.activation(
            t2, t1, AF.Sin, bias=0.0, scale=0.5), waits=p.now("DVEa"))
        p.add("dve", lambda e, t1=t1, t2=t2: e.scalar_tensor_tensor(
            t1, t2, -2.0, t2, ALU.mult, ALU.mult),
            waits=p.now("ACTa"))
        p.add("dve", lambda e, t1=t1: e.tensor_scalar(
            t1, t1, 1.0, None, ALU.add))
        pa_tok = p.add("dve", lambda e, am=am, t1=t1, dd=wdst[nm]: e.tensor_tensor(
            dd, am, t1, ALU.mult))

    WMATS = ("wf", "uf", "wi", "ui", "wc", "uc", "uo", "wo")
    stg_cp = None
    stg_out = None
    for mi, nm in enumerate(WMATS):
        w = []
        if stg_out is not None:
            w = list(stg_out)
        p.dma("gp", lambda e, nm=nm: e.dma_start(
            out=stA[:], in_=w_d[nm][:].rearrange("(a p) b -> p a b", p=128)),
            waits=(list(stg_cp.items()) if stg_cp else []))
        stg_dma = p.dma_done("gp")
        stg_cp = p.add("dve", lambda e: e.tensor_copy(
            stB[:], stA[:].rearrange("p a (o f) -> p o a f", o=ND)),
            waits=stg_dma + (stg_out or []))
        p.dma("sy", lambda e, mi=mi: e.dma_start(
            out=scr_d[mi, :, :, :, :].rearrange("o q a f -> q o a f"),
            in_=stB[:]),
            waits=list(stg_cp.items()))
        stg_out = p.dma_done("sy")
    scr_ready = stg_out

    p.add("dve", lambda e: e.memset(st[:, 7:8], 1e-5))
    evac_tok = [None, None]
    mm_tok = [None, None]
    xn_cp = None
    act_tok = [None, None]
    for ft in range(B * S // 128):
        rb = ft % 4
        pw2 = ft % 2
        sb_ = 8 * pw2
        xnt_p = (xnt, xnt2)[pw2]
        sqr_p = (xc, sq)[pw2]
        p.dma("sy", lambda e, ft=ft, rb=rb: e.dma_start(
            out=xf_tm[:, rb, :],
            in_=xfull_d[:].rearrange("(t p) d -> p t d", p=128)[:, ft, :]),
            waits=((mm_tok[rb % 2] or []) if ft >= 4 else
                   (p.now("DVEa") if ft == 0 else [])))
        xf_dma = p.dma_done("sy")
        x_t = xf_tm[:, rb, :]
        p.add("dve", lambda e, x_t=x_t, sb_=sb_: e.reduce_sum(
            st[:, sb_ + 0:sb_ + 1], x_t, axis=AX.X),
            waits=xf_dma + (act_tok[pw2] or [])
            + (scr_ready if ft == 0 else []))
        p.add("dve", lambda e, sb_=sb_: e.tensor_scalar(
            st[:, sb_ + 1:sb_ + 2], st[:, sb_ + 0:sb_ + 1],
            1.0 / 1024.0, None, ALU.mult))
        p.add("dve", lambda e, sb_=sb_: e.tensor_scalar(
            st[:, sb_ + 5:sb_ + 6], st[:, sb_ + 0:sb_ + 1],
            -1.0 / 1024.0, None, ALU.mult))
        p.add("act", lambda e, x_t=x_t, sb_=sb_, sqr_p=sqr_p: e.activation(
            sqr_p, x_t, AF.Square, bias=st[:, sb_ + 5:sb_ + 6], scale=1.0,
            accum_out=st[:, sb_ + 2:sb_ + 3]), waits=p.now("DVEa"))
        p.add("act", lambda e, sb_=sb_: e.activation(
            st[:, sb_ + 3:sb_ + 4], st[:, sb_ + 2:sb_ + 3], AF.Ln,
            bias=T["eps"], scale=1.0 / 1024.0))
        act_tok[pw2] = list(p.add("act", lambda e, sb_=sb_: e.activation(
            st[:, sb_ + 4:sb_ + 5], st[:, sb_ + 3:sb_ + 4], AF.Exp,
            bias=0.0, scale=-0.5)).items())
        w = [("ACTa", p.c("ACTa"))]
        if ft > 1:
            w += [("TTf", p.c("TTf") - 1)]
        p.add("dve", lambda e, x_t=x_t, sb_=sb_, xnt_p=xnt_p: e.tensor_scalar(
            xnt_p, x_t, st[:, sb_ + 1:sb_ + 2], st[:, sb_ + 4:sb_ + 5],
            ALU.subtract, ALU.mult), waits=w)
        mult_tok = p.now("DVEa")
        for dt in range(ND):
            w2 = list(mult_tok) if dt == 0 else []
            if dt == 0:
                w2 += p.now("DVEa")
            p.add("te", lambda e, dt=dt, xnt_p=xnt_p: e.transpose(
                PTB[:, bass.ts(dt, 128)], xnt_p[:, bass.ts(dt, 128)], ident[:]),
                waits=w2, incs=([("TTf", 1)] if dt == ND - 1 else None))
        xb = pw2
        xn_cp = p.add("act", lambda e, xb=xb: e.copy(
            xnf_t[:, xb, :, :], PTB[:].rearrange("p (a b) -> p a b", a=ND)),
            waits=[("TTf", p.c("TTf"))] + (mm_tok[xb] or []))
        for dt in range(ND):
            w2 = []
            if dt == 0:
                w2 = list(xn_cp.items()) + (evac_tok[xb] or [])
                if ft == 0:
                    w2 += dl_loads
            p.add("te", lambda e, dt=dt, xb=xb: e.matmul(
                PW[0:64, bass.ts(xb, 512)][:, 0:128], bkv_sb[:, dt, :],
                xnf_t[:, xb, dt, :], start=(dt == 0), stop=(dt == ND - 1)),
                waits=w2, incs=([("TEx", 1)] if dt == ND - 1 else None))
        mm_tok[xb] = [("TEx", p.c("TEx"))]
        evac_tok[xb] = list(p.add("act", lambda e, ft=ft, xb=xb: e.copy(
            rkv[0:64, ft * 128:(ft + 1) * 128],
            PW[0:64, bass.ts(xb, 512)][:, 0:128]),
            waits=[("TEx", p.c("TEx"))]).items())
        rkv_kv_done = p.now("ACTa")

    p.add("dve", lambda e: e.memset(st[:, 6:7], 0.0), waits=ds_loads)
    ln_t_phase(p, T, NT, x_tm, xn_fm, PTB, ident[:], sem="TT",
               first_waits=p.now("TTf"))

    prj_prev = None
    prj_dst = {"q": rkv[64:96, 0:TOK], "k": rloc[0:32, 0:TOK],
               "v": rloc[32:64, 0:TOK]}
    for pi, pn in enumerate(("q", "k", "v")):
        for n in range(4):
            pjb = PG if (n % 2 == 0) else PO
            for dt in range(ND):
                w = []
                if dt == 0:
                    w = p.now("DVEa") + dl_loads + tok_wf(prj_prev)
                p.add("te", lambda e, pn=pn, n=n, dt=dt, pjb=pjb: e.matmul(
                    pjb[0:HARM, :], bsb[pn][:, dt, :],
                    xn_fm[:, dt, bass.ts(n, 512)],
                    start=(dt == 0), stop=(dt == ND - 1)),
                    waits=w, incs=([("TEp", 1)] if dt == ND - 1 else None))
            prj_prev = p.add("act", lambda e, pn=pn, n=n, pjb=pjb: e.copy(
                prj_dst[pn][:, bass.ts(n, 512)], pjb[0:HARM, :]),
                waits=[("TEp", p.c("TEp"))])
    p.dma("sy", lambda e: e.dma_start(out=rkv_loc[:], in_=rloc[0:64, 0:TOK]),
          waits=p.now("ACTa"))
    cc_in_done = p.dma_done("sy")
    p.add("gp", lambda e: e.collective_compute(
        "AllGather", ALU.bypass, replica_groups=[list(range(NCORE))],
        ins=[rkv_loc[:]], outs=[rkv_gath[:]]),
        waits=cc_in_done, incs=[("CC", 16)])
    p.dma("sy", lambda e: e.dma_start(
        out=rkv[0:32, :],
        in_=rkv_gath[:].rearrange("(r hh) (b pp) -> hh b r pp",
                                  hh=64, b=B)[0:32, :, :, :]),
        waits=[("CC", 16)])
    p.dma("sy", lambda e: e.dma_start(
        out=rkv[32:64, :],
        in_=rkv_gath[:].rearrange("(r hh) (b pp) -> hh b r pp",
                                  hh=64, b=B)[32:64, :, :, :]),
        waits=[("CC", 16)])
    rkv_ready = p.dma_done("sy")

    def tok_w(tok):
        return list(tok.items()) if tok else []

    rq_prev = None
    for n in range(4):
        for dt in range(ND):
            w = []
            if dt == 0:
                w = p.now("DVEa") + tok_w(rq_prev)
            p.add("te", lambda e, n=n, dt=dt: e.matmul(
                PG[0:HARM, :], bq_sb[:, dt, :], xn_fm[:, dt, bass.ts(n, 512)],
                start=(dt == 0), stop=(dt == ND - 1)),
                waits=w, incs=([("TEx", 1)] if dt == ND - 1 else None))
        rq_prev = p.add("act", lambda e, n=n: e.copy(
            rkv[64:96, bass.ts(n, 512)], PG[0:HARM, :]),
            waits=[("TEx", p.c("TEx"))])
    rq_done = list(rq_prev.items())

    p.add("dve", lambda e: e.memset(
        v2p[:].rearrange("p a b c -> p (a b) c")[:, :, 64:65], 1.0))

    exp_bank = [None, None]
    eslot = [None] * 4
    k_free = [None, None]
    q_free = [None, None]
    v_free = [None, None]
    pw_tok = None
    pg_tok = None
    po_tok = None
    pt32_tok = None
    otm_tok = None
    ofm_cp = None
    wm_tok = pa_tok
    ln_done = p.now("DVEa")
    hi = 0
    kexp_cp = None
    for b in range(B):
        for h in range(NH):
            kb = hi % 2
            for n in range(ND):
                w = []
                if hi == 0 and n == 0:
                    w += tok_w(wm_tok) + rkv_ready
                if n == 0:
                    w += tok_w(k_free[kb])
                if n >= 2 and n % 2 == 0:
                    w += tok_w(kexp_cp)
                if n == 0 and pw_tok:
                    w += tok_w(pw_tok)
                p.add("te", lambda e, h=h, b=b, n=n: e.matmul(
                    PW[0:64, bass.ts(n % 2, 512)], wmatA[0:32, bass.ts(h, HD)],
                    rkv[0:32, b * 4096 + n * 512: b * 4096 + (n + 1) * 512],
                    start=True, stop=True), waits=w, incs=[("TEk", 1)])
                if n % 2 == 1:
                    tek = p.c("TEk")
                    kexp_cp = p.add(
                        "dve", lambda e, kb=kb, n=n:
                        e.tensor_copy(k2[bass.ts(kb, 64),
                                         (n - 1) * 512:(n + 1) * 512],
                                      PW[0:64, :]),
                        waits=[("TEk", tek)])
            pw_tok = kexp_cp
            p.add("te", lambda e, h=h, b=b: e.matmul(
                PW[0:64, 0:512], wmatA[64:96, bass.ts(h, HD)],
                rkv[64:96, b * 512:(b + 1) * 512], start=True, stop=True),
                waits=tok_w(pw_tok) + tok_w(q_free[kb]), incs=[("TEk", 1)])
            tek = p.c("TEk")
            q_cp = p.add("dve", lambda e, kb=kb: e.tensor_copy(
                q2[bass.ts(kb, 64), :], PW[0:64, 0:512]), waits=[("TEk", tek)])
            pw_tok = q_cp
            for g in range(4):
                for j in range(8):
                    kt = g * 8 + j
                    w = []
                    if g == 0 and j == 0:
                        w += (tok_w(pg_tok) if pg_tok else rq_done) + tok_w(v_free[kb])
                    if j == 0 and g > 0:
                        w += tok_w(v_cp)
                    p.add("te", lambda e, b=b, h=h, kt=kt, j=j: e.matmul(
                        PG[:, bass.ts(j, 64)],
                        rkv[32:64, b * 4096 + kt * 128:b * 4096 + (kt + 1) * 128],
                        wmatA[32:64, bass.ts(h, HD)],
                        start=True, stop=True), waits=w, incs=[("TEk", 1)])
                tek = p.c("TEk")
                v_cp = p.add("dve", lambda e, kb=kb, g=g: e.tensor_copy(
                    v2[:, kb, bass.ts(g, 8), 0:64],
                    PG[:].rearrange("p (a b) -> p a b", a=8)),
                    waits=[("TEk", tek)])
            pg_tok = v_cp
            for kt in range(32):
                w = [("ACTa", exp_bank[kt % 2]["ACTa"])] if exp_bank[kt % 2] else []
                if kt == 0:
                    w += tok_w(kexp_cp) + tok_w(q_cp)
                qk = p.add("te", lambda e, kb=kb, kt=kt: e.matmul(
                    LOG[kt % 2][:, :], k2[bass.ts(kb, 64), bass.ts(kt, 128)],
                    q2[bass.ts(kb, 64), :], start=True, stop=True),
                    waits=w, incs=[("TEl", 1)])
                k_free[kb] = {"TEl": p.c("TEl")}
                q_free[kb] = {"TEl": p.c("TEl")}
                w = [("TEl", p.c("TEl"))]
                if eslot[kt % 4]:
                    w += [("TEo", eslot[kt % 4]["TEo"])]
                ex = p.add("act", lambda e, kt=kt: e.activation(
                    e4[:, kt % 4, :], LOG[kt % 2][:, :], AF.Exp,
                    bias=0.0, scale=0.125), waits=w)
                exp_bank[kt % 2] = ex
                w = [("ACTa", ex["ACTa"])]
                if kt == 0:
                    w += tok_w(po_tok) + tok_w(v_cp)
                p.add("te", lambda e, kb=kb, kt=kt: e.matmul(
                    PO[0:65, :], v2[:, kb, kt, 0:65], e4[:, kt % 4, :],
                    start=(kt == 0), stop=(kt == 31)),
                    waits=w, incs=[("TEo", 1)])
                eslot[kt % 4] = {"TEo": p.c("TEo")}
                v_free[kb] = {"TEo": p.c("TEo")}
            po_tok = p.add("dve", lambda e: e.tensor_copy(os_sb[0:65, 0:512], PO[0:65, :]),
                           waits=[("TEo", p.c("TEo"))] + tok_w(otm_tok)
                           + p.now("TEt"))
            trs = []
            for i in range(4):
                w = tok_w(po_tok) if i == 0 else []
                if i == 0 and pt32_tok:
                    w += tok_w(pt32_tok)
                p.add("te", lambda e, i=i: e.transpose(
                    PT32[:, i * 128:i * 128 + 65], os_sb[0:65, bass.ts(i, 128)],
                    identf[0:65, 0:65]), waits=w, incs=[("TEt", 1)])
            tet = p.c("TEt")
            for i in range(4):
                p.add("dve", lambda e, i=i: e.reciprocal(
                    rec[:, i:i + 1], PT32[:, i * 128 + 64:i * 128 + 65]),
                    waits=([("TEt", tet)] if i == 0 else []))
                pt32_tok = p.add("dve", lambda e, i=i, h=h: e.tensor_scalar(
                    o_tm[:, i, bass.ts(h, HD)], PT32[:, i * 128:i * 128 + 64],
                    rec[:, i:i + 1], None, ALU.mult))
            hi += 1
        transpose_tm_to_fm(p, T, o_tm, o_fm, 4, PTB, ident[:], sem="TTo",
                           src_ready=p.now("DVEa"))
        otm_tok = p.now("TTo") and {"TTo": p.c("TTo")}
        ofm_cp = p.now("DVEa")
        for dt in range(ND):
            w = list(ofm_cp) if dt == 0 else []
            if dt == 0:
                w += tok_w(pg_tok)
            p.add("te", lambda e, dt=dt: e.matmul(
                PG[0:32, :], basiso[:, dt, :], o_fm[:, dt, :],
                start=(dt == 0), stop=(dt == ND - 1)), waits=w,
                incs=([("TEr", 1)] if dt == ND - 1 else None))
        ro_cp = p.add("act", lambda e: e.copy(ro_sb[0:32, :], PG[0:32, :]),
                      waits=[("TEr", p.c("TEr"))])
        pg_tok = ro_cp
        for qt in range(4):
            w = (tok_w(ro_cp) + tok_w(poA_tok) + tok_w(poB_tok)) if qt == 0 else []
            w += tok_w(pw_tok)
            for half in range(2):
                p.add("te", lambda e, qt=qt, half=half: e.matmul(
                    PW[:, bass.ts(half, 512)], ro_sb[0:32, bass.ts(qt, 128)],
                    wmatB[0:32, bass.ts(half, 512)],
                    start=True, stop=True),
                    waits=(w if half == 0 else []), incs=[("TEk", 1)])
            tek = p.c("TEk")
            pw_tok = p.add("dve", lambda e, b=b, qt=qt: e.tensor_add(
                x_tm[:, b * 4 + qt, :], x_tm[:, b * 4 + qt, :], PW[:, :]),
                waits=[("TEk", tek)])

    for dt in range(ND):
        p.add("dve", lambda e, dt=dt: e.memset(c_fm[:, dt, :], 1.0),
              waits=(dl_loads + p.now("TEk", "TEl", "TEo")
                     if dt == 0 else []))
        p.add("dve", lambda e, dt=dt: e.tensor_scalar(
            c_fm[:, dt, :], c_fm[:, dt, :], bia[:, 4 * ND + dt:4 * ND + dt + 1],
            None, ALU.mult))
    cbf_tok = p.add("dve", lambda e: e.tensor_copy(c_bf[:], c_fm[:]))
    ws_tok = [None, None]
    gact = [None, None, None]
    gsb_tok = None
    GACT_F = [AF.Sigmoid, AF.Sigmoid, AF.Tanh]
    gate_w = ("wf", "wi", "wc")
    for t in range(4):
        for ot in range(ND):
            s = (t * ND + ot) % 2
            for g in range(3):
                w = []
                if g == 0:
                    w = p.now("TEf")
                    if t == 0 and ot == 0:
                        w += p.now("TEt", "ACTa", "DVEa") + scr_ready
                p.dma("gp", lambda e, g=g, ot=ot, s=s: e.dma_start(
                    out=wstr[:, s, g, :, :],
                    in_=scr_d[2 * g, ot, :, :, :]),
                    waits=w)
            for g in range(3):
                p.dma("gp", lambda e, g=g, ot=ot, s=s: e.dma_start(
                    out=wstr[:, s, 3 + g, :, :],
                    in_=scr_d[2 * g + 1, ot, :, :, :]))
            dl_now = p.dma_done("gp")
            for g in range(3):
                pgm = PZ[:, bass.ts(g, 512)]
                for dt in range(ND):
                    w = []
                    if dt == 0:
                        w = list(dl_now) + tok_w(gact[g])
                        if g == 0:
                            w += tok_w(cbf_tok)
                        if t == 0 and ot == 0:
                            w += tok_w(exp_bank[0]) + tok_w(exp_bank[1])
                    p.add("te", lambda e, g=g, dt=dt, ot=ot, t=t, s=s, pgm=pgm: e.matmul(
                        pgm, wstr[:, s, g, dt, :], xn_fm[:, dt, bass.ts(t, 512)],
                        start=(dt == 0), stop=False), waits=w)
                for dt in range(ND):
                    p.add("te", lambda e, g=g, dt=dt, ot=ot, s=s, pgm=pgm: e.matmul(
                        pgm, wstr[:, s, 3 + g, dt, :], c_bf[:, dt, :],
                        start=False, stop=(dt == ND - 1)),
                        incs=([("TEf", 1)] if dt == ND - 1 else None))
                tef = p.c("TEf")
                gact[g] = p.add("act", lambda e, g=g, ot=ot, pgm=pgm,
                                d=(gf, gi, gg)[g]: e.activation(
                    d[:], pgm, GACT_F[g], bias=bia[:, g * ND + ot:g * ND + ot + 1],
                    scale=1.0), waits=[("TEf", tef)] + (tok_w(gsb_tok) if g == 0 else []))
            ws_tok[s] = {"TEf": p.c("TEf")}
            w = tok_w(gact[0]) + tok_w(gact[1]) + tok_w(gact[2])
            p.add("dve", lambda e, ot=ot: e.tensor_mul(
                gf[:], gf[:], c_fm[:, ot, :]), waits=w)
            p.add("dve", lambda e: e.tensor_mul(gi[:], gi[:], gg[:]))
            gsb_tok = p.add("dve", lambda e, ot=ot: e.tensor_add(
                c_fm[:, ot, :], gf[:], gi[:]))
        cbf_tok = p.add("dve", lambda e: e.tensor_copy(c_bf[:], c_fm[:]))
    cuo_tok = [None] * ND
    for ot in range(ND):
        s = ot % 2
        p.dma("gp", lambda e, ot=ot, s=s: e.dma_start(
            out=wstr[:, s, 0, :, :], in_=scr_d[6, ot, :, :, :]),
            waits=p.now("TEf"))
        dl_now = p.dma_done("gp")
        for dt in range(ND):
            w = list(dl_now) if dt == 0 else []
            if dt == 0:
                w += tok_w(cbf_tok)
            if dt == 0 and ot >= 2:
                w += tok_w(cuo_tok[ot - 2])
            if dt == 0 and ot < 2:
                w += tok_w(gact[0]) + tok_w(gact[1]) + tok_w(gact[2])
            p.add("te", lambda e, ot=ot, dt=dt, s=s: e.matmul(
                PZ[:, bass.ts(ot % 2, 512)], wstr[:, s, 0, dt, :], c_bf[:, dt, :],
                start=(dt == 0), stop=(dt == ND - 1)),
                waits=w, incs=([("TEf", 1)] if dt == ND - 1 else None))
        ws_tok[s] = {"TEf": p.c("TEf")}
        cuo_tok[ot] = p.add("dve", lambda e, ot=ot: e.tensor_copy(
            cuo[:, ot, :], PZ[:, bass.ts(ot % 2, 512)]),
            waits=[("TEf", p.c("TEf"))])
    ht_tok = [None] * ND
    haddB = [tok_w(cuo_tok[ND - 2]), tok_w(cuo_tok[ND - 1])]
    tet_after = None
    for t in range(4):
        for ot in range(ND):
            s = ot % 2
            p.dma("gp", lambda e, ot=ot, s=s: e.dma_start(
                out=wstr[:, s, 1, :, :], in_=scr_d[7, ot, :, :, :]),
                waits=p.now("TEf"))
            dl_now = p.dma_done("gp")
            for dt in range(ND):
                w = list(dl_now) if dt == 0 else []
                if dt == 0:
                    w += list(haddB[ot % 2])
                p.add("te", lambda e, t=t, ot=ot, dt=dt, s=s: e.matmul(
                    PZ[:, bass.ts(ot % 2, 512)], wstr[:, s, 1, dt, :],
                    xn_fm[:, dt, bass.ts(t, 512)],
                    start=(dt == 0), stop=(dt == ND - 1)),
                    waits=w, incs=([("TEf", 1)] if dt == ND - 1 else None))
            ws_tok[s] = {"TEf": p.c("TEf")}
            hadd = p.add("dve", lambda e, ot=ot: e.tensor_add(
                htmp[:], cuo[:, ot, :], PZ[:, bass.ts(ot % 2, 512)]),
                waits=[("TEf", p.c("TEf"))] + tok_w(cuo_tok[ot]) + p.now("ACTa"))
            haddB[ot % 2] = tok_w(hadd)
            aw = p.now("DVEa")
            if tet_after is not None:
                aw = aw + [("TEt", tet_after)]
            ht_tok[ot] = p.add("act", lambda e, ot=ot: e.activation(
                h_fm[:, ot, :], htmp[:], AF.Tanh,
                bias=bia[:, 3 * ND + ot:3 * ND + ot + 1], scale=1.0),
                waits=aw)
        hw = []
        for g in range(ND):
            hw += tok_w(ht_tok[g])
        for pt in range(4):
            for rnd in range(2):
                for j in range(4):
                    dt = rnd * 4 + j
                    w = []
                    if pt == 0 and rnd == 0 and j == 0:
                        w += hw
                    if j == 0 and pt32_tok:
                        w += tok_w(pt32_tok)
                    p.add("te", lambda e, t=t, pt=pt, dt=dt, j=j: e.transpose(
                        PT32[:, bass.ts(j, 128)], h_fm[:, dt, bass.ts(pt, 128)],
                        identf[:]), waits=w,
                        incs=([("TEt", 1)] if j == 3 else None))
                tet = p.c("TEt")
                pt32_tok = p.add("dve", lambda e, t=t, pt=pt, rnd=rnd: e.tensor_add(
                    x_tm[:, t * 4 + pt, bass.ts(rnd, 512)],
                    x_tm[:, t * 4 + pt, bass.ts(rnd, 512)], PT32[:, 0:512]),
                    waits=[("TEt", tet)])
        tet_after = p.c("TEt")

    ln_t_phase(p, T, NT, x_tm, xn_fm, PTB, ident[:], sem="TT1",
               first_waits=p.now("TEf"))
    xn1_done = p.now("DVEa")

    l1dst = [q1sb[0:64, 0:TOK], kfsb[0:64, 0:TOK], v1sb[0:64, 0:TOK]]
    prev = None
    for i in range(3):
        for n in range(4):
            for dt in range(ND):
                w = []
                if dt == 0:
                    w = list(xn1_done) + dl_loads
                    w += tok_w(prev)
                    if i == 0 and n == 0:
                        w += tok_w(pg_tok) + tok_w(exp_bank[1])
                p.add("te", lambda e, i=i, n=n, dt=dt: e.matmul(
                    PG[0:64, :], wl1[:, i, dt, :], xn_fm[:, dt, bass.ts(n, 512)],
                    start=(dt == 0), stop=(dt == ND - 1)),
                    waits=w, incs=([("TEr", 1)] if dt == ND - 1 else None))
            ter = p.c("TEr")
            if i < 2:
                p.add("dve", lambda e: e.tensor_scalar(
                    el_t[0:64, :], PG[0:64, :], 0.0, None, ALU.min),
                    waits=[("TEr", ter)] + (p.now("TEt", "TEf", "TEo", "TEl",
                                                  "ACTa")
                                            if (i == 0 and n == 0) else []))
                p.add("act", lambda e: e.activation(
                    el_e[0:64, :], el_t[0:64, :], AF.Exp, bias=0.0, scale=1.0),
                    waits=p.now("DVEa"))
                prev = p.add("dve", lambda e, i=i, n=n: e.scalar_tensor_tensor(
                    l1dst[i][:, bass.ts(n, 512)], PG[0:64, :], 0.0, el_e[0:64, :],
                    ALU.max, ALU.add), waits=p.now("ACTa"))
            else:
                prev = p.add("act", lambda e, i=i, n=n: e.copy(
                    l1dst[i][:, bass.ts(n, 512)], PG[0:64, :]),
                    waits=[("TEr", ter)])
    qkv_done = p.now("DVEa", "ACTa")
    p.add("dve", lambda e: e.memset(vp_tmv[:, :, 64:65], 1.0))
    for src, dst, tag in ((kfsb[0:64, 0:TOK], kf_tmv, "K"),
                          (v1sb[0:64, 0:TOK], vp_tmv, "V")):
        for rnd in range(2):
            for j in range(8):
                kt = rnd * 8 + j
                w = list(qkv_done) if (rnd == 0 and j == 0) else []
                if j == 0:
                    w += p.now("DVEa")
                p.add("te", lambda e, src=src, kt=kt, j=j: e.transpose(
                    PTB[:, j * 64:(j + 1) * 64],
                    src[:, bass.ts(kt, 128)], ident[0:64, 0:64]),
                    waits=w, incs=([("TEt", 1)] if j == 7 else None))
            tet = p.c("TEt")
            p.add("dve", lambda e, dst=dst, rnd=rnd: e.tensor_copy(
                dst[:, bass.ts(rnd, 8), 0:64],
                PTB[:, 0:512].rearrange("p (a b) -> p a b", a=8)),
                waits=[("TEt", tet)])
    tr_done = p.now("DVEa")
    for b in range(B):
        for ct in range(4):
            w = tr_done + tok_w(prev) if ct == 0 else []
            p.add("te", lambda e, b=b, ct=ct: e.matmul(
                PG[0:64, 0:65], kf_tmv[:, b * 4 + ct, 0:64],
                vp_tmv[:, b * 4 + ct, 0:65],
                start=(ct == 0), stop=(ct == 3)),
                waits=w, incs=([("TEr", 1)] if ct == 3 else None))
        prev = p.add("dve", lambda e, b=b: e.tensor_copy(
            st64[0:64, b * 65:(b + 1) * 65], PG[0:64, 0:65]),
            waits=[("TEr", p.c("TEr"))])
    p.dma("sy", lambda e: e.dma_start(out=stt_loc[:], in_=st64[0:64, :]),
          waits=p.now("DVEa"))
    st_in_done = p.dma_done("sy")
    p.add("gp", lambda e: e.collective_compute(
        "AllGather", ALU.bypass, replica_groups=[list(range(NCORE))],
        ins=[stt_loc[:]], outs=[stt_gath[:]]),
        waits=st_in_done, incs=[("CC", 16)])
    p.dma("sy", lambda e: e.dma_start(
        out=sa_sb[0:64, :],
        in_=stt_gath[:].rearrange("(r kk) bc -> kk r bc", kk=64)),
        waits=[("CC", 32)] + p.now("DVEa"))
    sa_ready = p.dma_done("sy")

    PGc = PZ[:, 1536:2048]
    PECc = PZ[:, 0:512]
    POc = PZ[:, 512:1024]
    PSc = PZ[:, 1024:1536]
    p.dma("gp", lambda e: e.dma_start(
        out=rm_sb[0:16, 0:4096], in_=rm_d[:].rearrange("j h r i -> j (h r i)")),
        waits=p.now("DVEa", "ACTa", "TEr", "TEf", "TEt"))
    p.dma("gp", lambda e: e.dma_start(
        out=rup_sb[0:16, 0:16384], in_=rup_d[:].rearrange("j h r k -> j (h r k)")))
    rails_done = p.dma_done("gp")
    ccp = None
    for blk in range(32):
        for j in range(8):
            hr = blk * 8 + j
            w = []
            if blk == 0 and j == 0:
                w = list(rails_done) + p.now('DVEa', 'TEf')
            if j == 0 and ccp is not None:
                w = w + tok_w(ccp)
            p.add("te", lambda e, hr=hr, j=j: e.matmul(
                PGc[0:16, bass.ts(j, 64)], rm_sb[0:16, hr * 16:(hr + 1) * 16],
                rup_sb[0:16, hr * 64:(hr + 1) * 64], start=True, stop=True),
                waits=w, incs=([("TEc", 1)] if j == 7 else None))
        ccp = p.add("dve", lambda e, blk=blk: e.tensor_copy(
            cflat[0:16, bass.ts(blk, 512)], PGc[0:16, :]),
            waits=[("TEc", p.c("TEc"))]
            + (p.now("TEf") + st_in_done if blk == 0 else []))
    p.dma("sy", lambda e: e.dma_start(
        out=c_scr[:].rearrange("i a b g rl k -> i (a b g rl k)"),
        in_=cflat[0:16, :]),
        waits=p.now("DVEa"))
    cscr_done = p.dma_done("sy")
    p.dma("gp", lambda e: e.dma_start(out=wo1[0:64, 0:1024], in_=wo1_d[:]),
          waits=p.now("TEc"))
    p.dma("sy", lambda e: e.dma_start(out=triu_sb[0:128, 0:128], in_=triu_d[:]),
          waits=p.now("TEc"))
    p.dma("sy", lambda e: e.dma_start(out=pm_sb[0:64, 0:NCORE], in_=pm_d[:]))
    late_done = p.dma_done("sy") + p.dma_done("gp")

    for b in range(B):
        p.add("dve", lambda e, b=b: e.tensor_scalar(
            s0_sb[0:64, b, :], sa4[0:64, :, :, :][:, 0, b, :], pm_sb[0:64, 0:1],
            None, ALU.mult),
            waits=(sa_ready + late_done + p.now("TEc") if b == 0 else []))
        for r in range(1, NCORE):
            p.add("dve", lambda e, b=b, r=r: e.scalar_tensor_tensor(
                s0_sb[0:64, b, :], sa4[0:64, :, :, :][:, r, b, :],
                pm_sb[0:64, r:r + 1], s0_sb[0:64, b, :], ALU.mult, ALU.add))
    l1pfx_done = p.now("DVEa")
    sup_tok = None
    for b in range(B):
        for ct in range(4):
            tile = b * 4 + ct
            if ct == 0:
                sw = list(l1pfx_done) + list(tr_done) if b == 0 else p.now("TEe")
                scp = p.add("dve", lambda e, b=b: e.tensor_copy(
                    ssb[0:64, 0:65], s0_sb[0:64, b, :]), waits=sw)
            else:
                scp = p.add("dve", lambda e, b=b: e.tensor_add(
                    ssb[0:64, 0:65], s0_sb[0:64, b, :], sacc[0:64, 0:65]),
                    waits=p.now("TEe"))
            w = p.now("DVEa")
            p.add("te", lambda e, tile=tile: e.matmul(
                PECc[0:128, 0:128], kfsb[0:64, 0:TOK][:, bass.ts(tile, 128)],
                q1sb[0:64, 0:TOK][:, bass.ts(tile, 128)], start=True, stop=True),
                waits=w, incs=[("TEe", 1)])
            mcp = p.add("dve", lambda e: e.tensor_tensor(
                et_sb[0:128, 0:128], PECc[0:128, 0:128], triu_sb[0:128, 0:128],
                ALU.mult),
                waits=[("TEe", p.c("TEe"))])
            p.add("te", lambda e, tile=tile: e.matmul(
                POc[0:128, 0:65], et_sb[0:128, 0:128], vp_tmv[:, tile, 0:65],
                start=True, stop=False), waits=tok_w(mcp) + tok_w(sup_tok))
            p.add("te", lambda e, tile=tile: e.matmul(
                POc[0:128, 0:65], q1sb[0:64, 0:TOK][:, bass.ts(tile, 128)],
                ssb[0:64, 0:65],
                start=False, stop=True), incs=[("TEe", 1)])
            p.add("dve", lambda e: e.tensor_scalar(
                rec[:, 0:1], POc[:, 64:65], 1e-6, None, ALU.add),
                waits=[("TEe", p.c("TEe"))])
            p.add("dve", lambda e: e.reciprocal(rec[:, 1:2], rec[:, 0:1]))
            ob = p.add("dve", lambda e: e.tensor_scalar(
                obf[0:128, 0:64], POc[:, 0:64], rec[:, 1:2], None, ALU.mult))
            p.add("te", lambda e: e.transpose(
                PTB[0:64, 0:128], obf[0:128, 0:64], ident[:]),
                waits=tok_w(ob) + p.now("DVEa"), incs=[("TEt", 1)])
            ocp = p.add("act", lambda e: e.copy(ofs[0:64, 0:128], PTB[0:64, 0:128]),
                waits=[("TEt", p.c("TEt"))])
            for half in range(2):
                w = tok_w(ocp) + tok_w(pw_tok) if half == 0 else []
                p.add("te", lambda e, half=half: e.matmul(
                    PW[:, bass.ts(half, 512)], ofs[0:64, 0:128],
                    wo1[0:64, bass.ts(half, 512)], start=True, stop=True),
                    waits=w + (late_done if (b == 0 and ct == 0 and half == 0) else []),
                    incs=([("TEe", 1)] if half == 1 else None))
            pw_tok = p.add("dve", lambda e, tile=tile: e.tensor_add(
                x_tm[:, tile, :], x_tm[:, tile, :], PW[:, :]),
                waits=[("TEe", p.c("TEe"))])
            if ct < 3:
                p.add("te", lambda e, tile=tile: e.matmul(
                    PSc[0:64, 0:65], kf_tmv[:, tile, 0:64], vp_tmv[:, tile, 0:65],
                    start=True, stop=True),
                    waits=p.now("DVEa"), incs=[("TEs", 1)])
                if ct == 0:
                    p.add("dve", lambda e: e.tensor_copy(
                        sacc[0:64, 0:65], PSc[0:64, 0:65]),
                        waits=[("TEs", p.c("TEs"))])
                else:
                    p.add("dve", lambda e: e.tensor_add(
                        sacc[0:64, 0:65], sacc[0:64, 0:65], PSc[0:64, 0:65]),
                        waits=[("TEs", p.c("TEs"))])
            sup_tok = {"TEs": p.c("TEs")}

    wz_tok = [None, None]
    wc_tok = [None, None]
    gelu_toks = [None] * ND
    zfree = None
    hfree = None
    for g in range(ND):
        sg = g % 2
        mz = p.add("dve", lambda e, sg=sg: e.memset(wzg[:, sg, :, :], 0.0),
                   waits=tok_w(wz_tok[sg]) + (cscr_done if g == 0 else []))
        for rl in range(8):
            p.dma("gp", lambda e, g=g, sg=sg, rl=rl: e.dma_start(
                out=wzg[rl * 16:(rl + 1) * 16, sg, 2 * rl:2 * rl + 2, :]
                .rearrange("p a f -> p a f") if False else
                wzg[rl * 16:(rl + 1) * 16, sg, :, :].rearrange(
                    "p m f -> p (m f)")[:, (2 * rl) * 128:(2 * rl + 2) * 128]
                .rearrange("p (a b k) -> p a b k", a=2, b=2),
                in_=c_scr[:, :, :, g, rl, :]),
                waits=tok_w(mz) if rl == 0 else [])
        wzdma = p.dma_done("gp")
        for half in range(2):
            hg = g * 2 + half
            sc = hg % 2
            p.dma("gp", lambda e, g=g, half=half, sc=sc: e.dma_start(
                out=wcs[:, sc, :, :],
                in_=wc_d[:].rearrange("(gg m p) d -> p gg m d", m=8, p=128)[
                    :, 2 * g + half, :, :]),
                waits=tok_w(wc_tok[sc]) + (cscr_done + p.now("TEe", "DVEa")
                                           if (g == 0 and half == 0) else []))
            wcdma = p.dma_done("gp")
            for m in range(8):
                mi = half * 8 + m
                for n in range(4):
                    w = []
                    if n == 0:
                        w = tok_w(zfree)
                    if m == 0 and n == 0:
                        w = w + wzdma + tok_w(hfree)
                        if hg == 0:
                            w += p.now("ACTa", "DVEa") + st_in_done
                    p.add("te", lambda e, g=g, mi=mi, n=n: e.matmul(
                        PZ[:, bass.ts(n, 512)], wzg[:, g % 2, mi, :],
                        xn_fm[:, g, bass.ts(n, 512)], start=True, stop=True),
                        waits=w, incs=([("TEz", 1)] if n == 3 else None))
                zfree = p.add("act", lambda e, m=m: e.activation(
                    hhg[:, m, :], PZ[:, 0:2048], AF.Gelu, bias=0.0, scale=1.0),
                    waits=[("TEz", p.c("TEz"))])
            for tt in range(NT):
                for m in range(8):
                    w = []
                    if tt == 0 and m == 0:
                        w = wcdma + p.now("ACTa")
                    if m == 0:
                        w = w + tok_w(pw_tok)
                    for half2 in range(2):
                        p.add("te", lambda e, tt=tt, m=m, sc=sc, half2=half2:
                              e.matmul(PW[:, bass.ts(half2, 512)],
                                       hhg[:, m, bass.ts(tt, 128)],
                                       wcs[:, sc, m, bass.ts(half2, 512)],
                                       start=(m == 0), stop=(m == 7)),
                              waits=(w if half2 == 0 else []),
                              incs=([("TEe", 1)] if (m == 7 and half2 == 1)
                                    else None))
                pw_tok = p.add("dve", lambda e, tt=tt: e.tensor_add(
                    x_tm[:, tt, :], x_tm[:, tt, :], PW[:, :]),
                    waits=[("TEe", p.c("TEe"))])
            hfree = {"TEe": p.c("TEe")}
            wc_tok[sc] = {"TEe": p.c("TEe")}
        wz_tok[sg] = {"TEe": p.c("TEe")}
    p.dma("sy", lambda e: e.dma_start(
        out=out_d[:].rearrange("(t p) d -> p t d", p=128), in_=x_tm[:]),
        waits=p.now("DVEa"))
    p.add("sy", lambda e: e.nop(), waits=p.dma_done("sy"))

    realize(nc, p)
    return nc




def assemble_rkv(r1):
    def asm(nm):
        return np.stack([np.asarray(r1[c][nm]).reshape(HARM, B, POS)
                         for c in range(NCORE)], axis=2).reshape(HARM, B * S)
    return np.ascontiguousarray(np.concatenate([asm("rk"), asm("rv")], axis=0))


def run_l2(H, r1):
    nc = build_l2()
    rkvf = assemble_rkv(r1)
    in_maps = []
    for c in range(NCORE):
        m = {"x": H["x_shards"][c], "rkvf": rkvf,
             "rqin": np.asarray(r1[c]["rq"]),
             "ident_bf": H["ident_bf"], "ident_f32": H["ident_f32"],
             "basis_oT": H["basis_oT"]}
        for nm in ("q", "k", "v", "o"):
            m[f"phase_{nm}T"] = H[f"phase_{nm}T"]
            m[f"amp_{nm}T"] = H[f"amp_{nm}T"]
        for nm in ("wf", "uf", "wi", "ui", "wc", "uc", "wo", "uo"):
            m[nm + "T"] = H[nm + "T"]
        for nm in ("bf", "bi", "bc", "bo", "in_c"):
            m[nm + "_r"] = H[nm + "_r"]
        m["bo_row"] = H["bo_row"]
        for nm in ("Wq", "Wk", "Wv"):
            m[nm + "T"] = H[nm + "T"]
        in_maps.append(m)
    res = run_bass_kernel_spmd(nc, in_maps, core_ids=list(range(NCORE)))
    return res.results


def build_l3():
    nc = bass.Bass("TRN2", target_bir_lowering=False)
    x1_d = nc.dram_tensor("x1", [TOK, D], F32, kind="ExternalInput")
    xn1_d = nc.dram_tensor("xn1", [D, TOK], BF16, kind="ExternalInput")
    q1_d = nc.dram_tensor("Q1", [64, TOK], BF16, kind="ExternalInput")
    k1_d = nc.dram_tensor("K1", [64, TOK], BF16, kind="ExternalInput")
    v1_d = nc.dram_tensor("V1", [64, TOK], BF16, kind="ExternalInput")
    sa_d = nc.dram_tensor("states_all", [NCORE, B, 64, 65], F32, kind="ExternalInput")
    pm_d = nc.dram_tensor("prefix_mask", [64, NCORE], F32, kind="ExternalInput")
    wo1_d = nc.dram_tensor("Wo1T", [64, D], F32, kind="ExternalInput")
    rm_d = nc.dram_tensor("rmJ", [16, 4, 64, 16], F32, kind="ExternalInput")
    rup_d = nc.dram_tensor("rupJ", [16, 4, 64, 64], F32, kind="ExternalInput")
    wc_d = nc.dram_tensor("wcprojT", [16384, D], F32, kind="ExternalInput")
    triu_d = nc.dram_tensor("triu", [128, 128], BF16, kind="ExternalInput")
    ident_d = nc.dram_tensor("ident_bf", [128, 128], BF16, kind="ExternalInput")
    out_d = nc.dram_tensor("out", [TOK, D], F32, kind="ExternalOutput")
    c_scr = nc.dram_tensor("c_scr", [16, 2, 2, 8, 8, 64], BF16)

    x_tm = nc.alloc_sbuf_tensor("x_tm", [128, NT, D], F32)
    xn_fm = nc.alloc_sbuf_tensor("xn_fm", [128, ND, TOK], BF16)
    UC = nc.alloc_sbuf_tensor("UC", [128, 18432], F32)

    def carve(arena, w0, w1, dtype=None, shape=None):
        v = arena[:, w0:w1]
        if dtype is not None:
            v = v.bitcast(dtype)
        if shape is not None:
            pat = " ".join(f"a{i}" for i in range(len(shape)))
            v = v.rearrange(f"p ({pat}) -> p {pat}",
                            **{f"a{i}": shape[i] for i in range(len(shape))})
        return v

    wcs = carve(UC, 0, 8192, BF16, (2, ND, D))
    hhg = carve(UC, 8192, 16384, BF16, (ND, TOK))
    wzg = carve(UC, 16384, 18432, BF16, (2, 16, 128))
    qf = nc.alloc_sbuf_tensor("qf", [64, TOK], BF16)
    kff = nc.alloc_sbuf_tensor("kff", [64, TOK], BF16)
    vf = nc.alloc_sbuf_tensor("vf", [64, TOK], BF16)
    kf_tmv = nc.alloc_sbuf_tensor("kf_tmv", [128, NT, 64], BF16)
    vp_tmv = nc.alloc_sbuf_tensor("vp_tmv", [128, NT, 72], BF16)
    sa_sb = nc.alloc_sbuf_tensor("sa_sb", [64, NCORE, B, 65], F32)
    s0_sb = nc.alloc_sbuf_tensor("s0_sb", [64, B, 65], F32)
    pm_sb = nc.alloc_sbuf_tensor("pm_sb", [64, NCORE], F32)
    ssb = nc.alloc_sbuf_tensor("ssb", [64, 65], BF16)
    sacc = nc.alloc_sbuf_tensor("sacc", [64, 65], F32)
    et_sb = nc.alloc_sbuf_tensor("et_sb", [128, 128], BF16)
    obf = nc.alloc_sbuf_tensor("obf", [128, 64], BF16)
    ofs = nc.alloc_sbuf_tensor("ofs", [64, 128], BF16)
    wo1 = nc.alloc_sbuf_tensor("wo1", [64, D], BF16)
    rm_sb = carve(UC, 0, 2048, BF16)
    rup_sb = carve(UC, 2048, 10240, BF16)
    cflat = carve(UC, 10240, 18432, BF16)
    triu_sb = nc.alloc_sbuf_tensor("triu_sb", [128, 128], BF16)
    ident = nc.alloc_sbuf_tensor("ident", [128, 128], BF16)
    rec = nc.alloc_sbuf_tensor("rec", [128, 8], F32)

    PZ = nc.alloc_psum_tensor("PZ", [128, 2048], F32)
    PY = nc.alloc_psum_tensor("PY", [128, 1536], F32)
    PTB = nc.alloc_psum_tensor("PTB", [128, 1024], BF16)
    PW = PY[:, 0:1024]
    PEC = PZ[:, 0:512]
    PO = PZ[:, 512:1024]
    PS = PZ[:, 1024:1536]
    PC = PZ[:, 1536:2048]

    def tok_w(tok):
        return list(tok.items()) if tok else []

    p = Plan()
    p.dma("sy", lambda e: e.dma_start(out=x_tm[:],
          in_=x1_d[:].rearrange("(t p) d -> p t d", p=128)))
    p.dma("sy", lambda e: e.dma_start(out=xn_fm[:],
          in_=xn1_d[:].rearrange("(a p) n -> p a n", p=128)))
    p.dma("sy", lambda e: e.dma_start(out=qf[:], in_=q1_d[:]))
    p.dma("sy", lambda e: e.dma_start(out=kff[:], in_=k1_d[:]))
    p.dma("sy", lambda e: e.dma_start(out=vf[:], in_=v1_d[:]))
    p.dma("sy", lambda e: e.dma_start(out=triu_sb[:], in_=triu_d[:]))
    p.dma("sy", lambda e: e.dma_start(out=ident[:], in_=ident_d[:]))
    p.dma("sy", lambda e: e.dma_start(out=sa_sb[:],
          in_=sa_d[:].rearrange("r b k c -> k r b c")))
    p.dma("sy", lambda e: e.dma_start(out=pm_sb[:], in_=pm_d[:]))
    ds0 = p.dma_done("sy")
    p.dma("gp", lambda e: e.dma_start(out=wo1[:], in_=wo1_d[:]))
    p.dma("gp", lambda e: e.dma_start(
        out=rm_sb[0:16, 0:4096], in_=rm_d[:].rearrange("j h r i -> j (h r i)")))
    p.dma("gp", lambda e: e.dma_start(
        out=rup_sb[0:16, :], in_=rup_d[:].rearrange("j h r k -> j (h r k)")))
    dl0 = p.dma_done("gp")

    ccp = None
    for blk in range(32):
        for j in range(8):
            hr = blk * 8 + j
            w = []
            if blk == 0 and j == 0:
                w = dl0
            if j == 0 and ccp is not None:
                w = w + tok_w(ccp)
            p.add("te", lambda e, hr=hr, j=j: e.matmul(
                PC[0:16, bass.ts(j, 64)], rm_sb[0:16, hr * 16:(hr + 1) * 16],
                rup_sb[0:16, hr * 64:(hr + 1) * 64], start=True, stop=True),
                waits=w, incs=([("TEc", 1)] if j == 7 else None))
        ccp = p.add("dve", lambda e, blk=blk: e.tensor_copy(
            cflat[0:16, bass.ts(blk, 512)], PC[0:16, :]),
            waits=[("TEc", p.c("TEc"))])
    p.dma("sy", lambda e: e.dma_start(
        out=c_scr[:].rearrange("i a b g rl k -> i (a b g rl k)"),
        in_=cflat[0:16, :]),
        waits=p.now("DVEa"))
    cscr_done = p.dma_done("sy")

    p.add("dve", lambda e: e.memset(vp_tmv[:, :, 64:65], 1.0), waits=ds0)
    for srcv, dst in ((kff, kf_tmv), (vf, vp_tmv)):
        for rnd in range(2):
            for j in range(8):
                kt = rnd * 8 + j
                w = p.now("DVEa") if j == 0 else []
                p.add("te", lambda e, srcv=srcv, kt=kt, j=j: e.transpose(
                    PTB[:, j * 64:(j + 1) * 64], srcv[:, bass.ts(kt, 128)],
                    ident[0:64, 0:64]),
                    waits=w, incs=([("TEt", 1)] if j == 7 else None))
            p.add("dve", lambda e, dst=dst, rnd=rnd: e.tensor_copy(
                dst[:, bass.ts(rnd, 8), 0:64],
                PTB[:, 0:512].rearrange("p (a b) -> p a b", a=8)),
                waits=[("TEt", p.c("TEt"))])
    for b in range(B):
        p.add("dve", lambda e, b=b: e.tensor_scalar(
            s0_sb[:, b, :], sa_sb[:, 0, b, :], pm_sb[:, 0:1], None, ALU.mult))
        for r in range(1, NCORE):
            p.add("dve", lambda e, b=b, r=r: e.scalar_tensor_tensor(
                s0_sb[:, b, :], sa_sb[:, r, b, :], pm_sb[:, r:r + 1],
                s0_sb[:, b, :], ALU.mult, ALU.add))
    tr_done = p.now("DVEa")

    pw_tok = None
    sup_tok = None
    for b in range(B):
        for ct in range(4):
            tile = b * 4 + ct
            if ct == 0:
                sw = list(tr_done) if b == 0 else p.now("TEe")
                scp = p.add("dve", lambda e, b=b: e.tensor_copy(
                    ssb[0:64, 0:65], s0_sb[0:64, b, :]), waits=sw)
            else:
                scp = p.add("dve", lambda e, b=b: e.tensor_add(
                    ssb[0:64, 0:65], s0_sb[0:64, b, :], sacc[0:64, 0:65]),
                    waits=p.now("TEe"))
            w = p.now("DVEa")
            p.add("te", lambda e, tile=tile: e.matmul(
                PEC[0:128, 0:128], kff[:, bass.ts(tile, 128)],
                qf[:, bass.ts(tile, 128)], start=True, stop=True),
                waits=w, incs=[("TEe", 1)])
            mcp = p.add("dve", lambda e: e.tensor_tensor(
                et_sb[:], PEC[0:128, 0:128], triu_sb[:], ALU.mult),
                waits=[("TEe", p.c("TEe"))])
            p.add("te", lambda e, tile=tile: e.matmul(
                PO[0:128, 0:65], et_sb[:], vp_tmv[:, tile, 0:65],
                start=True, stop=False), waits=tok_w(mcp) + tok_w(sup_tok))
            p.add("te", lambda e, tile=tile: e.matmul(
                PO[0:128, 0:65], qf[:, bass.ts(tile, 128)], ssb[0:64, 0:65],
                start=False, stop=True), incs=[("TEe", 1)])
            p.add("dve", lambda e: e.tensor_scalar(
                rec[:, 0:1], PO[:, 64:65], 1e-6, None, ALU.add),
                waits=[("TEe", p.c("TEe"))])
            p.add("dve", lambda e: e.reciprocal(rec[:, 1:2], rec[:, 0:1]))
            ob = p.add("dve", lambda e: e.tensor_scalar(
                obf[:], PO[:, 0:64], rec[:, 1:2], None, ALU.mult))
            p.add("te", lambda e: e.transpose(
                PTB[0:64, 0:128], obf[0:128, 0:64], ident[:]),
                waits=tok_w(ob) + p.now("DVEa"), incs=[("TEt", 1)])
            ocp = p.add("act", lambda e: e.copy(ofs[0:64, 0:128], PTB[0:64, 0:128]),
                waits=[("TEt", p.c("TEt"))])
            for half in range(2):
                w = tok_w(ocp) + tok_w(pw_tok) if half == 0 else []
                p.add("te", lambda e, half=half: e.matmul(
                    PW[:, bass.ts(half, 512)], ofs[0:64, 0:128],
                    wo1[0:64, bass.ts(half, 512)], start=True, stop=True),
                    waits=w + (dl0 if (b == 0 and ct == 0 and half == 0) else []),
                    incs=([("TEe", 1)] if half == 1 else None))
            pw_tok = p.add("dve", lambda e, tile=tile: e.tensor_add(
                x_tm[:, tile, :], x_tm[:, tile, :], PW[:, :]),
                waits=[("TEe", p.c("TEe"))])
            if ct < 3:
                p.add("te", lambda e, tile=tile: e.matmul(
                    PS[0:64, 0:65], kf_tmv[:, tile, 0:64], vp_tmv[:, tile, 0:65],
                    start=True, stop=True),
                    waits=p.now("DVEa"), incs=[("TEs", 1)])
                if ct == 0:
                    p.add("dve", lambda e: e.tensor_copy(sacc[:], PS[0:64, 0:65]),
                          waits=[("TEs", p.c("TEs"))])
                else:
                    p.add("dve", lambda e: e.tensor_add(
                        sacc[:], sacc[:], PS[0:64, 0:65]),
                        waits=[("TEs", p.c("TEs"))])
            sup_tok = {"TEs": p.c("TEs")}

    wz_tok = [None, None]
    wc_tok = [None, None]
    gelu_toks = [None] * ND
    zfree = None
    hfree = None
    for g in range(ND):
        sg = g % 2
        mz = p.add("dve", lambda e, sg=sg: e.memset(wzg[:, sg, :, :], 0.0),
                   waits=tok_w(wz_tok[sg]) + (cscr_done if g == 0 else []))
        for rl in range(8):
            p.dma("gp", lambda e, g=g, sg=sg, rl=rl: e.dma_start(
                out=wzg[rl * 16:(rl + 1) * 16, sg, 2 * rl:2 * rl + 2, :]
                .rearrange("p a f -> p a f") if False else
                wzg[rl * 16:(rl + 1) * 16, sg, :, :].rearrange(
                    "p m f -> p (m f)")[:, (2 * rl) * 128:(2 * rl + 2) * 128]
                .rearrange("p (a b k) -> p a b k", a=2, b=2),
                in_=c_scr[:, :, :, g, rl, :]),
                waits=tok_w(mz) if rl == 0 else [])
        wzdma = p.dma_done("gp")
        for half in range(2):
            hg = g * 2 + half
            sc = hg % 2
            p.dma("gp", lambda e, g=g, half=half, sc=sc: e.dma_start(
                out=wcs[:, sc, :, :],
                in_=wc_d[:].rearrange("(gg m p) d -> p gg m d", m=8, p=128)[
                    :, 2 * g + half, :, :]),
                waits=tok_w(wc_tok[sc]) + (cscr_done + p.now("TEe", "DVEa")
                                           if (g == 0 and half == 0) else []))
            wcdma = p.dma_done("gp")
            for m in range(8):
                mi = half * 8 + m
                for n in range(4):
                    w = []
                    if n == 0:
                        w = tok_w(zfree)
                    if m == 0 and n == 0:
                        w = w + wzdma + tok_w(hfree)
                        if hg == 0:
                            w += p.now("ACTa", "DVEa")
                    p.add("te", lambda e, g=g, mi=mi, n=n: e.matmul(
                        PZ[:, bass.ts(n, 512)], wzg[:, g % 2, mi, :],
                        xn_fm[:, g, bass.ts(n, 512)], start=True, stop=True),
                        waits=w, incs=([("TEz", 1)] if n == 3 else None))
                zfree = p.add("act", lambda e, m=m: e.activation(
                    hhg[:, m, :], PZ[:, 0:2048], AF.Gelu, bias=0.0, scale=1.0),
                    waits=[("TEz", p.c("TEz"))])
            for tt in range(NT):
                for m in range(8):
                    w = []
                    if tt == 0 and m == 0:
                        w = wcdma + p.now("ACTa")
                    if m == 0:
                        w = w + tok_w(pw_tok)
                    for half2 in range(2):
                        p.add("te", lambda e, tt=tt, m=m, sc=sc, half2=half2:
                              e.matmul(PW[:, bass.ts(half2, 512)],
                                       hhg[:, m, bass.ts(tt, 128)],
                                       wcs[:, sc, m, bass.ts(half2, 512)],
                                       start=(m == 0), stop=(m == 7)),
                              waits=(w if half2 == 0 else []),
                              incs=([("TEe", 1)] if (m == 7 and half2 == 1)
                                    else None))
                pw_tok = p.add("dve", lambda e, tt=tt: e.tensor_add(
                    x_tm[:, tt, :], x_tm[:, tt, :], PW[:, :]),
                    waits=[("TEe", p.c("TEe"))])
            hfree = {"TEe": p.c("TEe")}
            wc_tok[sc] = {"TEe": p.c("TEe")}
        wz_tok[sg] = {"TEe": p.c("TEe")}
    p.dma("sy", lambda e: e.dma_start(
        out=out_d[:].rearrange("(t p) d -> p t d", p=128), in_=x_tm[:]),
        waits=p.now("DVEa"))
    p.add("sy", lambda e: e.nop(), waits=p.dma_done("sy"))
    realize(nc, p)
    return nc


def run_l3(H, r2):
    nc = build_l3()
    states_all = np.stack([np.asarray(r["states"]) for r in r2])
    in_maps = []
    for c in range(NCORE):
        pm = np.tile((np.arange(NCORE) < c).astype(np.float32), (64, 1))
        in_maps.append({
            "x1": np.asarray(r2[c]["x1"]), "xn1": np.asarray(r2[c]["xn1"]),
            "Q1": np.asarray(r2[c]["Q1"]), "K1": np.asarray(r2[c]["K1"]),
            "V1": np.asarray(r2[c]["V1"]),
            "states_all": states_all, "prefix_mask": np.ascontiguousarray(pm),
            "Wo1T": H["WoT"], "rmJ": H["rmJ"], "rupJ": H["rupJ"],
            "wcprojT": H["wcprojT"], "triu": H["triu"],
            "ident_bf": H["ident_bf"],
        })
    res = run_bass_kernel_spmd(nc, in_maps, core_ids=list(range(NCORE)))
    return res.results


def run_l1(H):
    nc = build_l1()
    in_maps = []
    for c in range(NCORE):
        in_maps.append({
            "x": H["x_shards"][c],
            "ident_bf": H["ident_bf"],
            "basis_qT": H["basis_qT"], "basis_kT": H["basis_kT"],
            "basis_vT": H["basis_vT"],
        })
    res = run_bass_kernel_spmd(nc, in_maps, core_ids=list(range(NCORE)))
    return res.results


def fused_in_map(H, c):
    m = {"x": H["x_shards"][c],
         "ident_bf": H["ident_bf"], "ident_f32": H["ident_f32"],
         "basis_oT": H["basis_oT"],
         "basis_qT": H["basis_qT"], "basis_kT": H["basis_kT"],
         "basis_vT": H["basis_vT"],
         "Wo1T": H["WoT"], "rmJ": H["rmJ"], "rupJ": H["rupJ"],
         "wcprojT": H["wcprojT"], "triu": H["triu"],
         "prefix_mask": np.ascontiguousarray(
             np.tile((np.arange(NCORE) < c).astype(np.float32), (64, 1)))}
    for nm in ("q", "k", "v", "o"):
        m[f"phase_{nm}T"] = H[f"phase_{nm}T"]
        m[f"amp_{nm}T"] = H[f"amp_{nm}T"]
    for nm in ("wf", "uf", "wi", "ui", "wc", "uc", "wo", "uo"):
        m[nm + "T"] = H[nm + "T"]
    for nm in ("bf", "bi", "bc", "bo", "in_c"):
        m[nm + "_r"] = H[nm + "_r"]
    for nm in ("Wq", "Wk", "Wv"):
        m[nm + "T"] = H[nm + "T"]
    return m


def run_fused(H):
    nc = build_fused()
    in_maps = [fused_in_map(H, c) for c in range(NCORE)]
    res = run_bass_kernel_spmd(nc, in_maps, core_ids=list(range(NCORE)))
    return res.results


def kernel(**inputs):
    H = host_prep(inputs)
    r1 = run_l1(H)
    r2 = run_l2(H, r1)
    r3 = run_l3(H, r2)
    out = np.empty((B, S, D), np.float32)
    for c in range(NCORE):
        out[:, c * POS:(c + 1) * POS, :] = np.asarray(
            r3[c]["out"]).reshape(B, POS, D)
    return out


if __name__ == "__main__":
    pass


# version 13
# speedup vs baseline: 1.0937x; 1.0937x over previous
import numpy as np
import ml_dtypes
import concourse.bass as bass
import concourse.mybir as mybir
from concourse.bass_utils import run_bass_kernel_spmd

F32 = mybir.dt.float32
BF16 = mybir.dt.bfloat16
AF = mybir.ActivationFunctionType
ALU = mybir.AluOpType
AX = mybir.AxisListType
BF = ml_dtypes.bfloat16

NCORE = 8
B, S, D = 4, 4096, 1024
POS = S // NCORE
TOK = B * POS
NT = TOK // 128
ND = D // 128
NH, HD, HARM = 16, 64, 32
TWO_PI = float(2.0 * np.pi)
MAGIC = float(1.5 * 2 ** 23)


class Plan:
    AUTO = {"dve": "DVEa", "act": "ACTa"}

    def __init__(self):
        self.steps = []
        self.cnt = {}

    def c(self, s):
        return self.cnt.get(s, 0)

    def add(self, eng, emit, waits=None, incs=None):
        incs = list(incs or [])
        auto = self.AUTO.get(eng)
        if auto is not None:
            incs.append((auto, 1))
        for s, d in incs:
            self.cnt[s] = self.c(s) + d
        self.steps.append((eng, emit, list(waits or []), incs))
        return {s: self.c(s) for s, _ in incs}

    def now(self, *sems):
        return [(s, self.c(s)) for s in sems]

    NQ = 8

    def dma(self, eng, emit, waits=None):
        i = self.cnt.get("_dmaidx_" + eng, 0)
        self.cnt["_dmaidx_" + eng] = i + 1
        sem = f"D{eng}{i % self.NQ}"
        w = list(waits or []) + [(sem, self.c(sem))]
        return self.add(eng, emit, waits=w, incs=[(sem, 16)])

    def dma_done(self, eng):
        return [(f"D{eng}{j}", self.c(f"D{eng}{j}")) for j in range(self.NQ)
                if self.c(f"D{eng}{j}") > 0]


ENGMAP = [("sy", "sync"), ("te", "tensor"), ("dve", "vector"),
          ("act", "scalar"), ("gp", "gpsimd")]


def realize(nc, plan):
    names = set(plan.cnt.keys())
    for (_, _, waits, _) in plan.steps:
        for (s, _) in waits:
            names.add(s)
    sems = {n: nc.alloc_semaphore(n) for n in sorted(names)}
    with nc.Block() as blk:
        for nm, attr in ENGMAP:
            def mk(nm=nm):
                def body(eng):
                    seen = {}
                    auto = Plan.AUTO.get(nm)
                    nauto = 0
                    for (e, emit, waits, incs) in plan.steps:
                        if e != nm:
                            continue
                        if auto is not None and nauto > 0:
                            waits = waits + [(auto, nauto)]
                        for (s, v) in waits:
                            if v > 0 and seen.get(s, -1) < v:
                                eng.wait_ge(sems[s], v)
                                seen[s] = v
                        inst = emit(eng)
                        for (s, d) in incs:
                            inst = inst.then_inc(sems[s], d)
                        if auto is not None:
                            nauto += 1
                return body
            getattr(blk, attr)(mk())
    return sems


class Arena:

    def __init__(self, nc, name, dtype, words):
        self.t = nc.alloc_sbuf_tensor(name, [128, words], dtype)
        self.off = 0
        self.marks = []

    def alloc(self, *shape):
        n = int(np.prod(shape))
        v = self.t[:, self.off:self.off + n]
        self.off += n
        if len(shape) > 1:
            pat = " ".join(f"a{i}" for i in range(len(shape)))
            v = v.rearrange(f"p ({pat}) -> p {pat}",
                            **{f"a{i}": shape[i] for i in range(len(shape))})
        return v

    def push(self):
        self.marks.append(self.off)

    def pop(self):
        self.off = self.marks.pop()


def ln_phase(p, T, n_tiles, src_tm, dst_tm, eps=1e-5):
    xc, sq, st = T["xc"], T["sq"], T["st"]
    for t in range(n_tiles):
        x_t = src_tm[:, t, :]
        p.add("dve", lambda e, x_t=x_t, st=st: e.reduce_sum(
            st[:, 0:1], x_t, axis=AX.X), waits=p.now("ACTa"))
        p.add("dve", lambda e, st=st: e.tensor_scalar(
            st[:, 1:2], st[:, 0:1], 1.0 / 1024.0, None, ALU.mult))
        p.add("dve", lambda e, x_t=x_t, xc=xc, st=st: e.tensor_scalar(
            xc, x_t, st[:, 1:2], None, ALU.subtract))
        p.add("dve", lambda e, xc=xc, sq=sq, st=st: e.scalar_tensor_tensor(
            sq, xc, 0.0, xc, ALU.add, ALU.mult, accum_out=st[:, 2:3]))
        p.add("act", lambda e, st=st, eps_ap=T["eps"]: e.activation(
            st[:, 3:4], st[:, 2:3], AF.Ln, bias=eps_ap, scale=1.0 / 1024.0),
            waits=p.now("DVEa"))
        p.add("act", lambda e, st=st: e.activation(
            st[:, 4:5], st[:, 3:4], AF.Exp, bias=0.0, scale=-0.5))
        p.add("dve", lambda e, xc=xc, st=st, d=dst_tm[:, t, :]: e.tensor_scalar(
            d, xc, st[:, 4:5], None, ALU.mult), waits=p.now("ACTa"))


def ln_t_phase(p, T, n_tiles, src_tm, dst_fm, pt_bf, ident, sem, eps=1e-5,
               first_waits=()):
    xc, sq, st, xnt = T["xc"], T["sq"], T["st"], T["xnt"]
    te_tok = 0
    for t in range(n_tiles):
        x_t = src_tm[:, t, :]
        p.add("dve", lambda e, x_t=x_t, st=st: e.reduce_sum(
            st[:, 0:1], x_t, axis=AX.X),
            waits=p.now("ACTa") + (list(first_waits) if t == 0 else []))
        p.add("dve", lambda e, st=st: e.tensor_scalar(
            st[:, 1:2], st[:, 0:1], 1.0 / 1024.0, None, ALU.mult))
        p.add("dve", lambda e, x_t=x_t, xc=xc, st=st: e.tensor_scalar(
            xc, x_t, st[:, 1:2], None, ALU.subtract))
        p.add("dve", lambda e, xc=xc, sq=sq, st=st: e.scalar_tensor_tensor(
            sq, xc, 0.0, xc, ALU.add, ALU.mult, accum_out=st[:, 2:3]))
        p.add("act", lambda e, st=st, eps_ap=T["eps"]: e.activation(
            st[:, 3:4], st[:, 2:3], AF.Ln, bias=eps_ap, scale=1.0 / 1024.0),
            waits=p.now("DVEa"))
        p.add("act", lambda e, st=st: e.activation(
            st[:, 4:5], st[:, 3:4], AF.Exp, bias=0.0, scale=-0.5))
        w = p.now("ACTa")
        if t > 0:
            w += [(sem, te_tok)]
        p.add("dve", lambda e, xc=xc, st=st, xnt=xnt: e.tensor_scalar(
            xnt, xc, st[:, 4:5], None, ALU.mult), waits=w)
        mult_tok = p.now("DVEa")
        for dt in range(ND):
            w = list(mult_tok) if dt == 0 else []
            if dt == 0:
                w += p.now("DVEa")
            p.add("te", lambda e, dt=dt, xnt=xnt: e.transpose(
                pt_bf[:, bass.ts(dt, 128)], xnt[:, bass.ts(dt, 128)], ident),
                waits=w, incs=([(sem, 1)] if dt == ND - 1 else None))
        te_tok = p.c(sem)
        p.add("dve", lambda e, t=t: e.tensor_copy(
            dst_fm[:, :, bass.ts(t, 128)],
            pt_bf[:].rearrange("p (a b) -> p a b", a=ND)),
            waits=[(sem, te_tok)])


def transpose_tm_to_fm(p, T, src_tm, dst_fm, n_tiles, pt_bf, ident, sem="TT",
                       src_ready=()):
    copy_tok = None
    for t in range(n_tiles):
        for dt in range(ND):
            last = dt == ND - 1
            w = list(src_ready) if (t == 0 and dt == 0) else []
            if dt == 0 and copy_tok is not None:
                w += list(copy_tok.items())
            p.add("te", lambda e, s=src_tm[:, t, bass.ts(dt, 128)],
                  d=pt_bf[:, bass.ts(dt, 128)], ident=ident:
                  e.transpose(d, s, ident),
                  waits=w,
                  incs=([(sem, 1)] if last else None))
        copy_tok = p.add(
            "dve", lambda e, t=t, pt_bf=pt_bf:
            e.tensor_copy(dst_fm[:, :, bass.ts(t, 128)],
                          pt_bf[:].rearrange("p (a b) -> p a b", a=ND)),
            waits=p.now(sem))


def host_prep(inputs):
    H = {}
    x = np.asarray(inputs["L0_attn_x"])
    H["x_shards"] = [np.ascontiguousarray(
        x[:, c * POS:(c + 1) * POS, :]).reshape(TOK, D) for c in range(NCORE)]
    H["x_full"] = np.ascontiguousarray(x.reshape(B * S, D))
    for nm in ("q", "k", "v", "o"):
        H[f"basis_{nm}T"] = np.ascontiguousarray(
            np.asarray(inputs[f"L0_attn_basis_{nm}"]).T)
        H[f"phase_{nm}T"] = np.ascontiguousarray(
            np.asarray(inputs[f"L0_attn_phase_{nm}"]).T)
        H[f"amp_{nm}T"] = np.ascontiguousarray(
            np.asarray(inputs[f"L0_attn_amp_{nm}"]).T)
    for nm in ("wf", "uf", "wi", "ui", "wc", "uc", "wo", "uo"):
        H[nm + "T"] = np.ascontiguousarray(np.asarray(inputs["L0_ffn_" + nm]).T)
    for nm in ("bf", "bi", "bc", "bo", "in_c"):
        H[nm + "_r"] = np.ascontiguousarray(
            np.asarray(inputs["L0_ffn_" + nm]).reshape(ND, 128).T)
    H["bo_row"] = np.ascontiguousarray(
        np.asarray(inputs["L0_ffn_bo"]).reshape(1, D))
    for nm in ("Wq", "Wk", "Wv", "Wo"):
        H[nm + "T"] = np.ascontiguousarray(np.asarray(inputs["L1_attn_" + nm]).T)
    H["rmT"] = np.ascontiguousarray(
        np.asarray(inputs["L1_ffn_rail_maker"]).transpose(0, 1, 3, 2))
    H["rupT"] = np.ascontiguousarray(
        np.asarray(inputs["L1_ffn_rail_up_proj"]).transpose(0, 1, 3, 2))
    wcT = np.asarray(inputs["L1_ffn_w_cproj"]).T
    wcT = wcT.reshape(4, 8, 8, 64, 1024).transpose(1, 2, 0, 3, 4)
    H["wcprojT"] = np.ascontiguousarray(wcT.reshape(16384, 1024))
    H["bkvT"] = np.ascontiguousarray(np.concatenate(
        [np.asarray(inputs["L0_attn_basis_k"]).T,
         np.asarray(inputs["L0_attn_basis_v"]).T], axis=1))
    H["rmJ"] = np.ascontiguousarray(
        np.asarray(inputs["L1_ffn_rail_maker"]).transpose(3, 0, 1, 2))
    H["rupJ"] = np.ascontiguousarray(
        np.asarray(inputs["L1_ffn_rail_up_proj"]).transpose(3, 0, 1, 2))
    H["ident_bf"] = np.eye(128, dtype=BF)
    H["ident_f32"] = np.eye(128, dtype=np.float32)
    H["triu"] = np.triu(np.ones((128, 128), np.float32)).astype(BF)
    return H


def build_l1():
    nc = bass.Bass("TRN2", target_bir_lowering=False)
    x_d = nc.dram_tensor("x", [TOK, D], F32, kind="ExternalInput")
    ident_d = nc.dram_tensor("ident_bf", [128, 128], BF16, kind="ExternalInput")
    basis_d = {nm: nc.dram_tensor(f"basis_{nm}T", [D, HARM], F32,
                                  kind="ExternalInput") for nm in ("q", "k", "v")}
    r_out = {nm: nc.dram_tensor(f"r{nm}", [HARM, TOK], BF16,
                                kind="ExternalOutput") for nm in ("q", "k", "v")}

    x_tm = nc.alloc_sbuf_tensor("x_tm", [128, NT, D], F32)
    xn_tm = nc.alloc_sbuf_tensor("xn_tm", [128, NT, D], BF16)
    xn_fm = nc.alloc_sbuf_tensor("xn_fm", [128, ND, TOK], BF16)
    ident = nc.alloc_sbuf_tensor("ident", [128, 128], BF16)
    bsb = {nm: nc.alloc_sbuf_tensor(f"b_{nm}", [128, ND, HARM], BF16)
           for nm in ("q", "k", "v")}
    rsb = {nm: nc.alloc_sbuf_tensor(f"r_{nm}", [HARM, TOK], BF16)
           for nm in ("q", "k", "v")}
    xc = nc.alloc_sbuf_tensor("xc", [128, D], F32)
    sq = nc.alloc_sbuf_tensor("sq", [128, D], F32)
    st = nc.alloc_sbuf_tensor("st", [128, 16], F32)
    pt_bf = nc.alloc_psum_tensor("pt_bf", [128, 1024], BF16)
    pj = nc.alloc_psum_tensor("pj", [128, 1024], F32)

    T = {"xc": xc[:], "sq": sq[:], "st": st, "eps": st[:, 7:8]}
    p = Plan()
    p.add("dve", lambda e: e.memset(st[:, 7:8], 1e-5))
    p.dma("sy", lambda e: e.dma_start(out=x_tm[:],
                                      in_=x_d[:].rearrange("(t p) d -> p t d", p=128)))
    p.dma("sy", lambda e: e.dma_start(out=ident[:], in_=ident_d[:]))
    for i, nm in enumerate(("q", "k", "v")):
        p.dma("gp", lambda e, nm=nm: e.dma_start(
            out=bsb[nm][:],
            in_=basis_d[nm][:].rearrange("(a p) b -> p a b", p=128)))
    p.add("dve", lambda e: e.memset(st[:, 6:7], 0.0),
          waits=p.dma_done("sy"))
    ln_phase(p, T, NT, x_tm, xn_tm)
    transpose_tm_to_fm(p, T, xn_tm, xn_fm, NT, pt_bf, ident[:],
                       src_ready=p.now("DVEa") + p.dma_done("sy"))
    p.add("te", lambda e: e.nop(), waits=p.dma_done("gp"))
    for i, nm in enumerate(("q", "k", "v")):
        for n in range(4):
            buf = pj[:, bass.ts(n % 2, 512)]
            for dt in range(ND):
                p.add("te", lambda e, nm=nm, n=n, dt=dt, buf=buf: e.matmul(
                    buf[0:HARM, :], bsb[nm][:, dt, :],
                    xn_fm[:, dt, bass.ts(n, 512)],
                    start=(dt == 0), stop=(dt == ND - 1)),
                    waits=(p.now("DVEa", "ACTa") if dt == 0 else None),
                    incs=([("TEP", 1)] if dt == ND - 1 else None))
            p.add("act", lambda e, nm=nm, n=n, buf=buf: e.copy(
                rsb[nm][:, bass.ts(n, 512)], buf[0:HARM, :]),
                waits=p.now("TEP"))
    for nm in ("q", "k", "v"):
        p.dma("sy", lambda e, nm=nm: e.dma_start(out=r_out[nm][:], in_=rsb[nm][:]),
              waits=p.now("ACTa"))
    p.add("sy", lambda e: e.nop(), waits=p.dma_done("sy"))
    realize(nc, p)
    return nc


def build_l2():
    nc = bass.Bass("TRN2", target_bir_lowering=False)
    x_d = nc.dram_tensor("x", [TOK, D], F32, kind="ExternalInput")
    rkvf_d = nc.dram_tensor("rkvf", [64, B * S], BF16, kind="ExternalInput")
    rqin_d = nc.dram_tensor("rqin", [HARM, TOK], BF16, kind="ExternalInput")
    ident_d = nc.dram_tensor("ident_bf", [128, 128], BF16, kind="ExternalInput")
    identf_d = nc.dram_tensor("ident_f32", [128, 128], F32, kind="ExternalInput")
    pa_d = {}
    for nm in ("q", "k", "v", "o"):
        pa_d[nm] = (nc.dram_tensor(f"phase_{nm}T", [HARM, D], F32, kind="ExternalInput"),
                    nc.dram_tensor(f"amp_{nm}T", [HARM, D], F32, kind="ExternalInput"))
    basiso_d = nc.dram_tensor("basis_oT", [D, HARM], F32, kind="ExternalInput")
    w_d = {nm: nc.dram_tensor(nm + "T", [D, D], F32, kind="ExternalInput")
           for nm in ("wf", "uf", "wi", "ui", "wc", "uc", "wo", "uo")}
    b_d = {nm: nc.dram_tensor(nm + "_r", [128, ND], F32, kind="ExternalInput")
           for nm in ("bf", "bi", "bc", "bo", "in_c")}
    borow_d = nc.dram_tensor("bo_row", [1, D], F32, kind="ExternalInput")
    wl1_d = {nm: nc.dram_tensor(nm + "T", [D, 64], F32, kind="ExternalInput")
             for nm in ("Wq", "Wk", "Wv")}
    scr_d = nc.dram_tensor("w_scr", [8, ND, 128, ND, 128], BF16)
    x1_d = nc.dram_tensor("x1", [TOK, D], F32, kind="ExternalOutput")
    xn1_d = nc.dram_tensor("xn1", [D, TOK], BF16, kind="ExternalOutput")
    q1_d = nc.dram_tensor("Q1", [64, TOK], BF16, kind="ExternalOutput")
    k1_d = nc.dram_tensor("K1", [64, TOK], BF16, kind="ExternalOutput")
    v1_d = nc.dram_tensor("V1", [64, TOK], BF16, kind="ExternalOutput")
    st_d = nc.dram_tensor("states", [B, 64, 65], F32, kind="ExternalOutput")

    x_tm = nc.alloc_sbuf_tensor("x_tm", [128, NT, D], F32)
    xn_fm = nc.alloc_sbuf_tensor("xn_fm", [128, ND, TOK], BF16)
    UD = nc.alloc_sbuf_tensor("UD", [128, 8192], F32)
    UE = nc.alloc_sbuf_tensor("UE", [128, 10240], F32)
    UF = nc.alloc_sbuf_tensor("UF", [128, 8192], F32)
    ident = nc.alloc_sbuf_tensor("ident", [128, 128], BF16)
    identf = nc.alloc_sbuf_tensor("identf", [128, 128], F32)
    wmatA = nc.alloc_sbuf_tensor("wmatA", [128, D], BF16)
    wmatB = nc.alloc_sbuf_tensor("wmatB", [32, D], BF16)
    basiso = nc.alloc_sbuf_tensor("basiso", [128, ND, HARM], BF16)
    st = nc.alloc_sbuf_tensor("st", [128, 16], F32)
    rec = nc.alloc_sbuf_tensor("rec", [128, 8], F32)
    bia = nc.alloc_sbuf_tensor("bia", [128, 5 * ND], F32)

    def carve(arena, w0, w1, dtype=None, shape=None):
        v = arena[:, w0:w1]
        if dtype is not None:
            v = v.bitcast(dtype)
        if shape is not None:
            pat = " ".join(f"a{i}" for i in range(len(shape)))
            v = v.rearrange(f"p ({pat}) -> p {pat}",
                            **{f"a{i}": shape[i] for i in range(len(shape))})
        return v

    rkv = carve(UD, 0, 8192, BF16)
    c_fm = carve(UD, 0, 4096, None, (ND, POS))
    c_bf = carve(UD, 4096, 6144, BF16, (ND, POS))
    gf = carve(UD, 6144, 6656)
    gi = carve(UD, 6656, 7168)
    gg = carve(UD, 7168, 7680)
    htmp = carve(UD, 7680, 8192)
    htmp2 = carve(UD, 0, 2048, None, (2, D))
    st64 = carve(UD, 0, 260)
    k2 = carve(UE, 0, 2048, BF16)
    v2p = carve(UE, 2048, 4352, BF16, (32, 2, 72))
    q2 = carve(UE, 4352, 4608, BF16)
    e4 = carve(UE, 4608, 5632, BF16, (4, POS))
    o_tm = carve(UE, 5632, 7680, BF16, (4, D))
    o_fm = carve(UE, 7680, 9728, BF16, (ND, POS))
    ro_sb = carve(UE, 9728, 9984, BF16)
    q1sb = carve(UE, 0, 1024, BF16)
    kfsb = carve(UE, 1024, 2048, BF16)
    v1sb = carve(UE, 2048, 3072, BF16)
    kf_tmv = carve(UE, 3072, 3584, BF16, (NT, 64))
    vp_tmv = carve(UE, 3584, 4160, BF16, (NT, 72))
    el_t = carve(UE, 4160, 4672)
    el_e = carve(UE, 4672, 5184)
    wuo = carve(UE, 0, 8192, BF16, (2, ND, D))
    borow = carve(UE, 8192, 8704, BF16)
    ones1 = carve(UE, 8704, 8768, BF16)
    xf_tm = carve(UE, 0, 4096, None, (4, D))
    xnf_t = carve(UE, 4096, 5120, BF16, (2, ND, 128))
    bq_sb = carve(UE, 5120, 5248, BF16, (ND, HARM))
    bkv_sb = carve(UE, 5248, 5504, BF16, (ND, 64))
    stA = carve(UE, 0, 4096, BF16, (ND, D))
    stB = carve(UE, 5632, 9728, BF16, (ND, ND, 128))
    xc = carve(UF, 0, 1024)
    sq = carve(UF, 1024, 2048)
    pst = carve(UF, 2048, 3072)
    wtmp = carve(UF, 3072, 4096)
    xnt = carve(UF, 4096, 4608, BF16)
    os_bf = carve(UF, 4608, 4864, BF16)
    xnt2 = carve(UF, 4864, 5376, BF16)
    wstr = carve(UF, 0, 6144, BF16, (2, 6, ND, 128))
    wl1 = carve(UF, 6144, 6912, BF16, (3, ND, 64))

    PZ = nc.alloc_psum_tensor("PZ", [128, 2048], F32)
    PY = nc.alloc_psum_tensor("PY", [128, 1536], F32)
    PTB = nc.alloc_psum_tensor("PTB", [128, 1024], BF16)
    LOG = [PZ[:, 0:512], PZ[:, 512:1024]]
    PO = PZ[:, 1024:1536]
    PG = PZ[:, 1536:2048]
    PW = PY[:, 0:1024]
    PW0 = PY[:, 0:512]
    PW1 = PY[:, 512:1024]
    PT32 = PY[:, 1024:1536]

    T = {"xc": xc, "sq": sq, "st": st, "eps": st[:, 7:8], "xnt": xnt}
    p = Plan()

    p.dma("sy", lambda e: e.dma_start(out=x_tm[:],
          in_=x_d[:].rearrange("(t p) d -> p t d", p=128)))
    p.dma("sy", lambda e: e.dma_start(out=ident[:], in_=ident_d[:]))
    p.dma("sy", lambda e: e.dma_start(out=identf[:], in_=identf_d[:]))
    ds_loads = p.dma_done("sy")
    p.dma("gp", lambda e: e.dma_start(out=basiso[:],
          in_=basiso_d[:].rearrange("(a p) b -> p a b", p=128)))
    for i, nm in enumerate(("bf", "bi", "bc", "bo", "in_c")):
        p.dma("gp", lambda e, nm=nm, i=i: e.dma_start(
            out=bia[:, bass.ts(i, ND)], in_=b_d[nm][:]))
    for i, nm in enumerate(("Wq", "Wk", "Wv")):
        p.dma("gp", lambda e, nm=nm, i=i: e.dma_start(
            out=wl1[:, i, :, :], in_=wl1_d[nm][:].rearrange(
                "(a p) b -> p a b", p=128)))
    dl_loads = p.dma_done("gp")

    wdst = {"q": wmatA[64:96, :], "k": wmatA[0:32, :],
            "v": wmatA[32:64, :], "o": wmatB[0:32, :]}
    pa_tok = None
    for nm in ("q", "k", "v", "o"):
        w = list(pa_tok.items()) if pa_tok else []
        p.dma("gp", lambda e, nm=nm: e.dma_start(out=pst[0:32, :], in_=pa_d[nm][0][:]),
              waits=w)
        p.dma("gp", lambda e, nm=nm: e.dma_start(out=wtmp[0:32, :], in_=pa_d[nm][1][:]))
        dl_now = p.dma_done("gp")
        ph, am = pst[0:32, :], wtmp[0:32, :]
        t1, t2 = xc[0:32, :], sq[0:32, :]
        p.add("dve", lambda e, ph=ph, t1=t1: e.tensor_scalar(
            t1, ph, 1.0 / TWO_PI, MAGIC, ALU.mult, ALU.add),
            waits=dl_now)
        p.add("dve", lambda e, t1=t1: e.tensor_scalar(
            t1, t1, MAGIC, None, ALU.subtract))
        p.add("dve", lambda e, ph=ph, t1=t1: e.scalar_tensor_tensor(
            t1, t1, -TWO_PI, ph, ALU.mult, ALU.add))
        p.add("act", lambda e, t1=t1, t2=t2: e.activation(
            t2, t1, AF.Sin, bias=0.0, scale=0.5), waits=p.now("DVEa"))
        p.add("dve", lambda e, t1=t1, t2=t2: e.scalar_tensor_tensor(
            t1, t2, -2.0, t2, ALU.mult, ALU.mult),
            waits=p.now("ACTa"))
        p.add("dve", lambda e, t1=t1: e.tensor_scalar(
            t1, t1, 1.0, None, ALU.add))
        pa_tok = p.add("dve", lambda e, am=am, t1=t1, dd=wdst[nm]: e.tensor_tensor(
            dd, am, t1, ALU.mult))

    WMATS = ("wf", "uf", "wi", "ui", "wc", "uc", "uo", "wo")
    stg_cp = None
    stg_out = None
    for mi, nm in enumerate(WMATS):
        w = []
        if stg_out is not None:
            w = list(stg_out)
        p.dma("gp", lambda e, nm=nm: e.dma_start(
            out=stA[:], in_=w_d[nm][:].rearrange("(a p) b -> p a b", p=128)),
            waits=(list(stg_cp.items()) if stg_cp else []))
        stg_dma = p.dma_done("gp")
        stg_cp = p.add("dve", lambda e: e.tensor_copy(
            stB[:], stA[:].rearrange("p a (o f) -> p o a f", o=ND)),
            waits=stg_dma + (stg_out or []))
        p.dma("sy", lambda e, mi=mi: e.dma_start(
            out=scr_d[mi, :, :, :, :].rearrange("o q a f -> q o a f"),
            in_=stB[:]),
            waits=list(stg_cp.items()))
        stg_out = p.dma_done("sy")
    scr_ready = stg_out

    p.add("dve", lambda e: e.memset(st[:, 7:8], 1e-5))
    rkv_ld = p.dma("sy", lambda e: e.dma_start(out=rkv[0:64, :], in_=rkvf_d[:]))
    rq_ld = p.dma("sy", lambda e: e.dma_start(
        out=rkv[64:96, 0:TOK], in_=rqin_d[:]))
    rkv_kv_done = list(rkv_ld.items()) + list(rq_ld.items())

    p.add("dve", lambda e: e.memset(st[:, 6:7], 0.0), waits=ds_loads)
    ln_t_phase(p, T, NT, x_tm, xn_fm, PTB, ident[:], sem="TT",
               first_waits=p.now("TTf"))

    def tok_w(tok):
        return list(tok.items()) if tok else []

    rq_done = []

    LOG4 = [PZ[:, bass.ts(i, 512)] for i in range(4)]
    POa = PY[:, 0:512]
    PEX = [PY[:, 512:1024], PY[:, 1024:1536]]
    PEXw = PY[:, 512:1536]
    p.add("dve", lambda e: e.memset(
        v2p[:].rearrange("p a b c -> p (a b) c")[:, :, 64:65], 1.0))

    exp_super = {"A": None, "B": None}
    eslot = [None] * 4
    kq_free = None
    v_free = None
    pw_tok = None
    poA_tok = None
    poB_tok = None
    otm_tok = None
    wm_tok = pa_tok
    kexp_cp = None
    PEX1 = PY[:, 512:1024]
    POa = PY[:, 0:512]
    POb = PY[:, 1024:1536]
    pair_i = 0
    for b in range(B):
        for j in range(NH // 2):
            hA, hB = 2 * j, 2 * j + 1
            EXR = [PEX1, POa, POb]
            exg = [tok_w(pw_tok), tok_w(poA_tok), tok_w(poB_tok)]
            ecp = [None, None, None]
            ei = 0
            for n in range(ND):
                w = list(exg[ei % 3]) + tok_w(ecp[ei % 3])
                if pair_i == 0 and n == 0:
                    w += (tok_w(wm_tok) + ds_loads + rq_done + rkv_kv_done
                          + scr_ready)
                if n == 0:
                    w += tok_w(kq_free)
                p.add("te", lambda e, hA=hA, b=b, n=n, ei=ei: e.matmul(
                    EXR[ei % 3][:, :], wmatA[0:32, hA * HD:hA * HD + 128],
                    rkv[0:32, b * 4096 + n * 512: b * 4096 + (n + 1) * 512],
                    start=True, stop=True), waits=w, incs=[("TEk", 1)])
                tek = p.c("TEk")
                ecp[ei % 3] = p.add("dve", lambda e, n=n, ei=ei: e.tensor_copy(
                    k2[:, bass.ts(n, 512)], EXR[ei % 3][:, :]),
                    waits=[("TEk", tek)])
                ei += 1
            w = list(exg[ei % 3]) + tok_w(ecp[ei % 3])
            p.add("te", lambda e, hA=hA, b=b, ei=ei: e.matmul(
                EXR[ei % 3][:, :], wmatA[64:96, hA * HD:hA * HD + 128],
                rkv[64:96, b * 512:(b + 1) * 512], start=True, stop=True),
                waits=w, incs=[("TEk", 1)])
            tek = p.c("TEk")
            q_cp = p.add("dve", lambda e, ei=ei: e.tensor_copy(
                q2[:, :], EXR[ei % 3][:, :]), waits=[("TEk", tek)])
            ecp[ei % 3] = q_cp
            ei += 1
            kexp_cp = q_cp
            for g in range(8):
                w0 = list(exg[ei % 3]) + tok_w(ecp[ei % 3])
                for j4 in range(4):
                    kt = g * 4 + j4
                    w = w0 if j4 == 0 else []
                    if g == 0 and j4 == 0:
                        w = w + tok_w(v_free)
                    p.add("te", lambda e, b=b, hA=hA, kt=kt, j4=j4, ei=ei:
                          e.matmul(
                              EXR[ei % 3][:, bass.ts(j4, 128)],
                              rkv[32:64, b * 4096 + kt * 128:b * 4096 + (kt + 1) * 128],
                              wmatA[32:64, hA * HD:hA * HD + 128],
                              start=True, stop=True), waits=w, incs=[("TEk", 1)])
                tek = p.c("TEk")
                ecp[ei % 3] = p.add("dve", lambda e, g=g, ei=ei: e.tensor_copy(
                    v2p[:, bass.ts(g, 4), :, 0:64],
                    EXR[ei % 3][:].rearrange("p (a c d) -> p a c d", a=4, c=2)),
                    waits=[("TEk", tek)])
                ei += 1
            v_cp = ecp[(ei - 1) % 3]
            stg_guard = tok_w(ecp[0]) + tok_w(ecp[1]) + tok_w(ecp[2])
            pw_tok = v_cp
            for pk in range(16):
                kt0 = 2 * pk
                for st_h, base in (("A", 0), ("B", 2)):
                    hp = 0 if st_h == "A" else 1
                    for kq in range(2):
                        kt = kt0 + kq
                        w = []
                        if exp_super[st_h]:
                            w += [("ACTa", exp_super[st_h]["ACTa"])]
                        if pk == 0 and kq == 0:
                            w += tok_w(kexp_cp) + tok_w(q_cp)
                        p.add("te", lambda e, hp=hp, kt=kt, base=base, kq=kq:
                              e.matmul(LOG4[base + kq][:, :],
                                       k2[bass.ts(hp, 64), bass.ts(kt, 128)],
                                       q2[bass.ts(hp, 64), :],
                                       start=True, stop=True),
                              waits=w, incs=[("TEl", 1)])
                    w = [("TEl", p.c("TEl"))]
                    for sl in (base, base + 1):
                        if eslot[sl]:
                            w += [("TEo", eslot[sl]["TEo"])]
                    ex = p.add("act", lambda e, base=base: e.activation(
                        e4[:, base:base + 2, :].rearrange("p a c -> p (a c)"),
                        PZ[:, base * 512:base * 512 + 1024],
                        AF.Exp, bias=0.0, scale=0.125), waits=w)
                    exp_super[st_h] = ex
                for st_h, base, po in (("A", 0, POa), ("B", 2, POb)):
                    hp = 0 if st_h == "A" else 1
                    for kq in range(2):
                        kt = kt0 + kq
                        w = [("ACTa", exp_super[st_h]["ACTa"])]
                        if kt == 0:
                            w += stg_guard
                        p.add("te", lambda e, hp=hp, kt=kt, po=po, base=base,
                              kq=kq: e.matmul(
                                  po[0:65, :], v2p[:, kt, hp, 0:65],
                                  e4[:, base + kq, :],
                                  start=(kt == 0), stop=(kt == 31)),
                              waits=w, incs=[("TEo", 1)])
                        eslot[base + kq] = {"TEo": p.c("TEo")}
            v_free = {"TEo": p.c("TEo")}
            kq_free = {"TEl": p.c("TEl")}
            pw_tok = v_cp
            for st_h, po in (("A", POa), ("B", POb)):
                h = hA if st_h == "A" else hB
                pt = p.add("dve", lambda e, po=po: e.tensor_copy(
                    os_bf[0:65, 0:512], po[0:65, :]),
                    waits=[("TEo", p.c("TEo"))] + tok_w(otm_tok) + p.now("TEt"))
                if st_h == "A":
                    poA_tok = pt
                else:
                    poB_tok = pt
                for i in range(4):
                    w = tok_w(pt) + p.now("DVEa") if i == 0 else []
                    p.add("te", lambda e, i=i: e.transpose(
                        PTB[:, i * 256:i * 256 + 65],
                        os_bf[0:65, bass.ts(i, 128)],
                        ident[0:65, 0:65]), waits=w, incs=[("TEt", 1)])
                tet = p.c("TEt")
                for i in range(4):
                    p.add("dve", lambda e, i=i: e.reciprocal(
                        rec[:, i:i + 1], PTB[:, i * 256 + 64:i * 256 + 65]),
                        waits=([("TEt", tet)] if i == 0 else []))
                    p.add("dve", lambda e, i=i, h=h: e.tensor_scalar(
                        o_tm[:, i, bass.ts(h, HD)], PTB[:, i * 256:i * 256 + 64],
                        rec[:, i:i + 1], None, ALU.mult))
            pair_i += 1
        pg_tok = v_cp
        transpose_tm_to_fm(p, T, o_tm, o_fm, 4, PTB, ident[:], sem="TTo",
                           src_ready=p.now("DVEa"))
        otm_tok = {"TTo": p.c("TTo")}
        ofm_cp = p.now("DVEa")
        for dt in range(ND):
            w = list(ofm_cp) if dt == 0 else []
            if dt == 0:
                w += tok_w(pg_tok) + tok_w(pw_tok)
            p.add("te", lambda e, dt=dt: e.matmul(
                PEX[0][0:32, :], basiso[:, dt, :], o_fm[:, dt, :],
                start=(dt == 0), stop=(dt == ND - 1)), waits=w,
                incs=([("TEr", 1)] if dt == ND - 1 else None))
        ro_cp = p.add("act", lambda e: e.copy(ro_sb[0:32, :], PEX[0][0:32, :]),
                      waits=[("TEr", p.c("TEr"))])
        pg_tok = ro_cp
        pw_tok = ro_cp
        for qt in range(4):
            w = (tok_w(ro_cp) + tok_w(poA_tok) + tok_w(poB_tok)) if qt == 0 else []
            w += tok_w(pw_tok)
            for half in range(2):
                p.add("te", lambda e, qt=qt, half=half: e.matmul(
                    PEX[half][:, :], ro_sb[0:32, bass.ts(qt, 128)],
                    wmatB[0:32, bass.ts(half, 512)],
                    start=True, stop=True),
                    waits=(w if half == 0 else []), incs=[("TEk", 1)])
            tek = p.c("TEk")
            pw_tok = p.add("dve", lambda e, b=b, qt=qt: e.tensor_add(
                x_tm[:, b * 4 + qt, :], x_tm[:, b * 4 + qt, :], PEXw[:, :]),
                waits=[("TEk", tek)])

    exp_bank = [exp_super["A"], exp_super["B"]]
    pt32_tok = {"DVEa": p.c("DVEa"), "ACTa": p.c("ACTa")}
    po_tok = poB_tok

    for dt in range(ND):
        p.add("dve", lambda e, dt=dt: e.memset(c_fm[:, dt, :], 1.0),
              waits=(dl_loads + p.now("TEk", "TEl", "TEo")
                     if dt == 0 else []))
        p.add("dve", lambda e, dt=dt: e.tensor_scalar(
            c_fm[:, dt, :], c_fm[:, dt, :], bia[:, 4 * ND + dt:4 * ND + dt + 1],
            None, ALU.mult))
    cbf_tok = p.add("dve", lambda e: e.tensor_copy(c_bf[:], c_fm[:]))
    ws_tok = [None, None]
    gact = [None, None, None]
    gsb_tok = None
    GACT_F = [AF.Sigmoid, AF.Sigmoid, AF.Tanh]
    gate_w = ("wf", "wi", "wc")
    for t in range(4):
        for ot in range(ND):
            s = (t * ND + ot) % 2
            for g in range(3):
                w = []
                if g == 0:
                    w = p.now("TEf")
                    if t == 0 and ot == 0:
                        w += p.now("TEt", "ACTa", "DVEa") + scr_ready
                p.dma("gp", lambda e, g=g, ot=ot, s=s: e.dma_start(
                    out=wstr[:, s, g, :, :],
                    in_=scr_d[2 * g, ot, :, :, :]),
                    waits=w)
            for g in range(3):
                p.dma("gp", lambda e, g=g, ot=ot, s=s: e.dma_start(
                    out=wstr[:, s, 3 + g, :, :],
                    in_=scr_d[2 * g + 1, ot, :, :, :]))
            dl_now = p.dma_done("gp")
            for g in range(3):
                pgm = PZ[:, bass.ts(g, 512)]
                for dt in range(ND):
                    w = []
                    if dt == 0:
                        w = list(dl_now) + tok_w(gact[g])
                        if g == 0:
                            w += tok_w(cbf_tok)
                        if t == 0 and ot == 0:
                            w += tok_w(exp_bank[0]) + tok_w(exp_bank[1])
                    p.add("te", lambda e, g=g, dt=dt, ot=ot, t=t, s=s, pgm=pgm: e.matmul(
                        pgm, wstr[:, s, g, dt, :], xn_fm[:, dt, bass.ts(t, 512)],
                        start=(dt == 0), stop=False), waits=w)
                for dt in range(ND):
                    p.add("te", lambda e, g=g, dt=dt, ot=ot, s=s, pgm=pgm: e.matmul(
                        pgm, wstr[:, s, 3 + g, dt, :], c_bf[:, dt, :],
                        start=False, stop=(dt == ND - 1)),
                        incs=([("TEf", 1)] if dt == ND - 1 else None))
                tef = p.c("TEf")
                gact[g] = p.add("act", lambda e, g=g, ot=ot, pgm=pgm,
                                d=(gf, gi, gg)[g]: e.activation(
                    d[:], pgm, GACT_F[g], bias=bia[:, g * ND + ot:g * ND + ot + 1],
                    scale=1.0), waits=[("TEf", tef)] + (tok_w(gsb_tok) if g == 0 else []))
            ws_tok[s] = {"TEf": p.c("TEf")}
            w = tok_w(gact[0]) + tok_w(gact[1]) + tok_w(gact[2])
            p.add("dve", lambda e, ot=ot: e.tensor_mul(
                gf[:], gf[:], c_fm[:, ot, :]), waits=w)
            p.add("dve", lambda e: e.tensor_mul(gi[:], gi[:], gg[:]))
            gsb_tok = p.add("dve", lambda e, ot=ot: e.tensor_add(
                c_fm[:, ot, :], gf[:], gi[:]))
        cbf_tok = p.add("dve", lambda e: e.tensor_copy(c_bf[:], c_fm[:]))
    p.add("dve", lambda e: e.memset(ones1[0:1, 0:128], 1.0))
    p.dma("gp", lambda e: e.dma_start(out=borow[0:1, 0:1024], in_=borow_d[:]),
          waits=p.now("TEo"))
    for wi_, nm in ((0, "wo"), (1, "uo")):
        for dt in range(ND):
            p.dma("gp", lambda e, wi_=wi_, nm=nm, dt=dt: e.dma_start(
                out=wuo[:, wi_, dt, :], in_=w_d[nm][bass.ts(dt, 128), :]))
    wuo_done = p.dma_done("gp")
    hfree = [None, None]
    htanh = [None, None]
    for ti in range(NT):
        t, pt = ti // 4, ti % 4
        hb = ti % 2
        HP = PZ[:, hb * 1024:(hb + 1) * 1024]
        tsl = slice(t * 512 + pt * 128, t * 512 + (pt + 1) * 128)
        w0 = tok_w(cbf_tok) + wuo_done + tok_w(hfree[hb]) \
            + (p.now("DVEa", "ACTa") if ti < 2 else [])
        first = True
        for half in range(2):
            osl = slice(half * 512, (half + 1) * 512)
            for dt in range(ND):
                w = list(w0) if first else []
                first = False
                p.add("te", lambda e, dt=dt, tsl=tsl, osl=osl, HP=HP, half=half:
                      e.matmul(HP[:, osl], xn_fm[:, dt, tsl],
                               wuo[:, 0, dt, osl],
                               start=(dt == 0), stop=False),
                      waits=w)
            psl = slice(pt * 128, (pt + 1) * 128)
            for dt in range(ND):
                p.add("te", lambda e, dt=dt, psl=psl, osl=osl, HP=HP, half=half:
                      e.matmul(HP[:, osl], c_bf[:, dt, psl],
                               wuo[:, 1, dt, osl],
                               start=False, stop=False))
        p.add("te", lambda e, HP=HP: e.matmul(
            HP[:, 0:512], ones1[0:1, 0:128], borow[0:1, 0:512],
            start=False, stop=True))
        p.add("te", lambda e, HP=HP: e.matmul(
            HP[:, 512:1024], ones1[0:1, 0:128], borow[0:1, 512:1024],
            start=False, stop=True), incs=[("TEf", 1)])
        tef = p.c("TEf")
        ht = p.add("act", lambda e, HP=HP, hb=hb: e.activation(
            htmp2[:, hb, :], HP[:, :], AF.Tanh, bias=0.0, scale=1.0),
            waits=[("TEf", tef)] + tok_w(htanh[hb]))
        htanh[hb] = ht
        hfree[hb] = p.add("dve", lambda e, ti=ti, hb=hb: e.tensor_add(
            x_tm[:, ti, :], x_tm[:, ti, :], htmp2[:, hb, :]),
            waits=list(ht.items()))
    ln_t_phase(p, T, NT, x_tm, xn_fm, PTB, ident[:], sem="TT1",
               first_waits=p.now("TEf"))
    xn1_done = p.now("DVEa")
    p.dma("sy", lambda e: e.dma_start(
        out=x1_d[:].rearrange("(t p) d -> p t d", p=128), in_=x_tm[:]),
        waits=p.now("DVEa"))
    p.dma("sy", lambda e: e.dma_start(
        out=xn1_d[:].rearrange("(a p) n -> p a n", p=128), in_=xn_fm[:]),
        waits=xn1_done)

    l1dst = [q1sb[0:64, 0:TOK], kfsb[0:64, 0:TOK], v1sb[0:64, 0:TOK]]
    prev = None
    for i in range(3):
        for n in range(4):
            for dt in range(ND):
                w = []
                if dt == 0:
                    w = list(xn1_done) + dl_loads
                    w += tok_w(prev)
                    if i == 0 and n == 0:
                        w += tok_w(pg_tok) + tok_w(exp_bank[1])
                p.add("te", lambda e, i=i, n=n, dt=dt: e.matmul(
                    PG[0:64, :], wl1[:, i, dt, :], xn_fm[:, dt, bass.ts(n, 512)],
                    start=(dt == 0), stop=(dt == ND - 1)),
                    waits=w, incs=([("TEr", 1)] if dt == ND - 1 else None))
            ter = p.c("TEr")
            if i < 2:
                p.add("dve", lambda e: e.tensor_scalar(
                    el_t[0:64, :], PG[0:64, :], 0.0, None, ALU.min),
                    waits=[("TEr", ter)] + (p.now("TEt", "TEf", "TEo", "TEl",
                                                  "ACTa")
                                            if (i == 0 and n == 0) else []))
                p.add("act", lambda e: e.activation(
                    el_e[0:64, :], el_t[0:64, :], AF.Exp, bias=0.0, scale=1.0),
                    waits=p.now("DVEa"))
                prev = p.add("dve", lambda e, i=i, n=n: e.scalar_tensor_tensor(
                    l1dst[i][:, bass.ts(n, 512)], PG[0:64, :], 0.0, el_e[0:64, :],
                    ALU.max, ALU.add), waits=p.now("ACTa"))
            else:
                prev = p.add("act", lambda e, i=i, n=n: e.copy(
                    l1dst[i][:, bass.ts(n, 512)], PG[0:64, :]),
                    waits=[("TEr", ter)])
    qkv_done = p.now("DVEa", "ACTa")
    for nm, dd in (("Q1", q1sb[0:64, 0:TOK]), ("K1", kfsb[0:64, 0:TOK]),
                   ("V1", v1sb[0:64, 0:TOK])):
        p.dma("sy", lambda e, nm=nm, dd=dd: e.dma_start(out={
            "Q1": q1_d, "K1": k1_d, "V1": v1_d}[nm][:], in_=dd),
            waits=qkv_done)
    p.add("dve", lambda e: e.memset(vp_tmv[:, :, 64:65], 1.0))
    for src, dst, tag in ((kfsb[0:64, 0:TOK], kf_tmv, "K"),
                          (v1sb[0:64, 0:TOK], vp_tmv, "V")):
        for rnd in range(2):
            for j in range(8):
                kt = rnd * 8 + j
                w = list(qkv_done) if (rnd == 0 and j == 0) else []
                if j == 0:
                    w += p.now("DVEa")
                p.add("te", lambda e, src=src, kt=kt, j=j: e.transpose(
                    PTB[:, j * 64:(j + 1) * 64],
                    src[:, bass.ts(kt, 128)], ident[0:64, 0:64]),
                    waits=w, incs=([("TEt", 1)] if j == 7 else None))
            tet = p.c("TEt")
            p.add("dve", lambda e, dst=dst, rnd=rnd: e.tensor_copy(
                dst[:, bass.ts(rnd, 8), 0:64],
                PTB[:, 0:512].rearrange("p (a b) -> p a b", a=8)),
                waits=[("TEt", tet)])
    tr_done = p.now("DVEa")
    for b in range(B):
        for ct in range(4):
            w = tr_done + tok_w(prev) if ct == 0 else []
            p.add("te", lambda e, b=b, ct=ct: e.matmul(
                PG[0:64, 0:65], kf_tmv[:, b * 4 + ct, 0:64],
                vp_tmv[:, b * 4 + ct, 0:65],
                start=(ct == 0), stop=(ct == 3)),
                waits=w, incs=([("TEr", 1)] if ct == 3 else None))
        prev = p.add("dve", lambda e, b=b: e.tensor_copy(
            st64[0:64, b * 65:(b + 1) * 65], PG[0:64, 0:65]),
            waits=[("TEr", p.c("TEr"))])
    p.dma("sy", lambda e: e.dma_start(
        out=st_d[:].rearrange("b k c -> k b c"),
        in_=st64[0:64, :].rearrange("p (b c) -> p b c", b=B)),
        waits=p.now("DVEa"))
    p.add("sy", lambda e: e.nop(), waits=p.dma_done("sy"))
    realize(nc, p)
    return nc


def build_fused():
    nc = bass.Bass("TRN2", target_bir_lowering=False)
    x_d = nc.dram_tensor("x", [TOK, D], F32, kind="ExternalInput")
    basis_d = {nm: nc.dram_tensor(f"basis_{nm}T", [D, HARM], F32,
                                  kind="ExternalInput") for nm in ("q", "k", "v")}
    ident_d = nc.dram_tensor("ident_bf", [128, 128], BF16, kind="ExternalInput")
    identf_d = nc.dram_tensor("ident_f32", [128, 128], F32, kind="ExternalInput")
    pa_d = {}
    for nm in ("q", "k", "v", "o"):
        pa_d[nm] = (nc.dram_tensor(f"phase_{nm}T", [HARM, D], F32, kind="ExternalInput"),
                    nc.dram_tensor(f"amp_{nm}T", [HARM, D], F32, kind="ExternalInput"))
    basiso_d = nc.dram_tensor("basis_oT", [D, HARM], F32, kind="ExternalInput")
    pm_d = nc.dram_tensor("prefix_mask", [64, NCORE], F32, kind="ExternalInput")
    w_d = {nm: nc.dram_tensor(nm + "T", [D, D], F32, kind="ExternalInput")
           for nm in ("wf", "uf", "wi", "ui", "wc", "uc", "wo", "uo")}
    b_d = {nm: nc.dram_tensor(nm + "_r", [128, ND], F32, kind="ExternalInput")
           for nm in ("bf", "bi", "bc", "bo", "in_c")}
    borow_d = nc.dram_tensor("bo_row", [1, D], F32, kind="ExternalInput")
    wl1_d = {nm: nc.dram_tensor(nm + "T", [D, 64], F32, kind="ExternalInput")
             for nm in ("Wq", "Wk", "Wv")}
    wo1_d = nc.dram_tensor("Wo1T", [64, D], F32, kind="ExternalInput")
    rm_d = nc.dram_tensor("rmJ", [16, 4, 64, 16], F32, kind="ExternalInput")
    rup_d = nc.dram_tensor("rupJ", [16, 4, 64, 64], F32, kind="ExternalInput")
    wc_d = nc.dram_tensor("wcprojT", [16384, D], F32, kind="ExternalInput")
    triu_d = nc.dram_tensor("triu", [128, 128], BF16, kind="ExternalInput")
    out_d = nc.dram_tensor("out", [TOK, D], F32, kind="ExternalOutput")
    c_scr = nc.dram_tensor("c_scr", [16, 2, 2, 8, 8, 64], BF16)
    rkv_loc = nc.dram_tensor("rkv_loc", [64, TOK], BF16)
    rkv_gath = nc.dram_tensor("rkv_gath", [NCORE * 64, TOK], BF16)
    stt_loc = nc.dram_tensor("stt_loc", [64, B * 65], F32)
    stt_gath = nc.dram_tensor("stt_gath", [NCORE * 64, B * 65], F32)

    x_tm = nc.alloc_sbuf_tensor("x_tm", [128, NT, D], F32)
    xn_fm = nc.alloc_sbuf_tensor("xn_fm", [128, ND, TOK], BF16)
    UD = nc.alloc_sbuf_tensor("UD", [128, 8192], F32)
    UE = nc.alloc_sbuf_tensor("UE", [128, 10240], F32)
    UF = nc.alloc_sbuf_tensor("UF", [128, 8192], F32)
    ident = nc.alloc_sbuf_tensor("ident", [128, 128], BF16)
    identf = nc.alloc_sbuf_tensor("identf", [128, 128], F32)
    wmatA = nc.alloc_sbuf_tensor("wmatA", [128, D], BF16)
    wmatB = nc.alloc_sbuf_tensor("wmatB", [32, D], BF16)
    basiso = nc.alloc_sbuf_tensor("basiso", [128, ND, HARM], BF16)
    st = nc.alloc_sbuf_tensor("st", [128, 16], F32)
    rec = nc.alloc_sbuf_tensor("rec", [128, 8], F32)
    bia = nc.alloc_sbuf_tensor("bia", [128, 5 * ND], F32)

    def carve(arena, w0, w1, dtype=None, shape=None):
        v = arena[:, w0:w1]
        if dtype is not None:
            v = v.bitcast(dtype)
        if shape is not None:
            pat = " ".join(f"a{i}" for i in range(len(shape)))
            v = v.rearrange(f"p ({pat}) -> p {pat}",
                            **{f"a{i}": shape[i] for i in range(len(shape))})
        return v

    rkv = carve(UD, 0, 8192, BF16)
    c_fm = carve(UD, 0, 4096, None, (ND, POS))
    c_bf = carve(UD, 4096, 6144, BF16, (ND, POS))
    gf = carve(UD, 6144, 6656)
    gi = carve(UD, 6656, 7168)
    gg = carve(UD, 7168, 7680)
    htmp = carve(UD, 7680, 8192)
    htmp2 = carve(UD, 0, 2048, None, (2, D))
    st64 = carve(UD, 0, 260)
    k2 = carve(UE, 0, 2048, BF16)
    v2p = carve(UE, 2048, 4352, BF16, (32, 2, 72))
    q2 = carve(UE, 4352, 4608, BF16)
    e4 = carve(UE, 4608, 5632, BF16, (4, POS))
    o_tm = carve(UE, 5632, 7680, BF16, (4, D))
    o_fm = carve(UE, 7680, 9728, BF16, (ND, POS))
    ro_sb = carve(UE, 9728, 9984, BF16)
    q1sb = carve(UE, 0, 1024, BF16)
    kfsb = carve(UE, 1024, 2048, BF16)
    v1sb = carve(UE, 2048, 3072, BF16)
    kf_tmv = carve(UE, 3072, 3584, BF16, (NT, 64))
    vp_tmv = carve(UE, 3584, 4160, BF16, (NT, 72))
    el_t = carve(UE, 4160, 4672)
    el_e = carve(UE, 4672, 5184)
    wuo = carve(UE, 0, 8192, BF16, (2, ND, D))
    borow = carve(UE, 8192, 8704, BF16)
    ones1 = carve(UE, 8704, 8768, BF16)
    xf_tm = carve(UE, 0, 4096, None, (4, D))
    xnf_t = carve(UE, 4096, 5120, BF16, (2, ND, 128))
    bq_sb = carve(UE, 5120, 5248, BF16, (ND, HARM))
    bkv_sb = carve(UE, 5248, 5504, BF16, (ND, 64))
    stA = carve(UE, 0, 4096, BF16, (ND, D))
    stB = carve(UE, 5632, 9728, BF16, (ND, ND, 128))
    xc = carve(UF, 0, 1024)
    sq = carve(UF, 1024, 2048)
    pst = carve(UF, 2048, 3072)
    wtmp = carve(UF, 3072, 4096)
    xnt = carve(UF, 4096, 4608, BF16)
    os_bf = carve(UF, 4608, 4864, BF16)
    xnt2 = carve(UF, 4864, 5376, BF16)
    wstr = carve(UF, 0, 6144, BF16, (2, 6, ND, 128))
    wl1 = carve(UF, 6144, 6912, BF16, (3, ND, 64))
    rloc = carve(UE, 0, 1024, BF16)
    bsb = {nm: carve(UE, 1024 + 128 * i, 1152 + 128 * i, BF16, (ND, HARM))
           for i, nm in enumerate(("q", "k", "v"))}
    rm_sb = carve(UE, 7680, 9728, BF16)
    rup_sb = carve(UF, 0, 8192, BF16)
    cflat = carve(UD, 0, 8192, BF16)
    wcs = carve(UF, 0, 8192, BF16, (2, ND, D))
    hhg = carve(UD, 0, 8192, BF16, (ND, TOK))
    wzg = carve(UE, 5632, 7680, BF16, (2, 16, 128))
    sa_sb = carve(UE, 4160, 6240)
    sa4 = sa_sb.rearrange("p (r b c) -> p r b c", r=NCORE, b=B)
    wo1 = carve(UF, 0, 512, BF16)
    triu_sb = carve(UF, 512, 576, BF16)
    et_sb = carve(UF, 576, 640, BF16)
    obf = carve(UF, 640, 672, BF16)
    ofs = carve(UF, 672, 736, BF16)
    ssb = carve(UF, 736, 769, BF16)
    sacc = carve(UF, 769, 834)
    s0_sb = carve(UF, 834, 1094, None, (B, 65))
    pm_sb = carve(UF, 1094, 1102)

    PZ = nc.alloc_psum_tensor("PZ", [128, 2048], F32)
    PY = nc.alloc_psum_tensor("PY", [128, 1536], F32)
    PTB = nc.alloc_psum_tensor("PTB", [128, 1024], BF16)
    LOG = [PZ[:, 0:512], PZ[:, 512:1024]]
    PO = PZ[:, 1024:1536]
    PG = PZ[:, 1536:2048]
    PW = PY[:, 0:1024]
    PW0 = PY[:, 0:512]
    PW1 = PY[:, 512:1024]
    PT32 = PY[:, 1024:1536]

    T = {"xc": xc, "sq": sq, "st": st, "eps": st[:, 7:8], "xnt": xnt}

    def tok_wf(tok):
        return list(tok.items()) if tok else []

    p = Plan()

    p.dma("sy", lambda e: e.dma_start(out=x_tm[:],
          in_=x_d[:].rearrange("(t p) d -> p t d", p=128)))
    p.dma("sy", lambda e: e.dma_start(out=ident[:], in_=ident_d[:]))
    p.dma("sy", lambda e: e.dma_start(out=identf[:], in_=identf_d[:]))
    ds_loads = p.dma_done("sy")
    p.dma("gp", lambda e: e.dma_start(out=basiso[:],
          in_=basiso_d[:].rearrange("(a p) b -> p a b", p=128)))
    for i, nm in enumerate(("bf", "bi", "bc", "bo", "in_c")):
        p.dma("gp", lambda e, nm=nm, i=i: e.dma_start(
            out=bia[:, bass.ts(i, ND)], in_=b_d[nm][:]))
    for i, nm in enumerate(("Wq", "Wk", "Wv")):
        p.dma("gp", lambda e, nm=nm, i=i: e.dma_start(
            out=wl1[:, i, :, :], in_=wl1_d[nm][:].rearrange(
                "(a p) b -> p a b", p=128)))
    for nm in ("q", "k", "v"):
        p.dma("gp", lambda e, nm=nm: e.dma_start(
            out=bsb[nm][:],
            in_=basis_d[nm][:].rearrange("(a p) b -> p a b", p=128)))
    dl_loads = p.dma_done("gp")

    wdst = {"q": wmatA[64:96, :], "k": wmatA[0:32, :],
            "v": wmatA[32:64, :], "o": wmatB[0:32, :]}
    pa_tok = None
    for nm in ("q", "k", "v", "o"):
        w = list(pa_tok.items()) if pa_tok else []
        p.dma("gp", lambda e, nm=nm: e.dma_start(out=pst[0:32, :], in_=pa_d[nm][0][:]),
              waits=w)
        p.dma("gp", lambda e, nm=nm: e.dma_start(out=wtmp[0:32, :], in_=pa_d[nm][1][:]))
        dl_now = p.dma_done("gp")
        ph, am = pst[0:32, :], wtmp[0:32, :]
        t1, t2 = xc[0:32, :], sq[0:32, :]
        p.add("dve", lambda e, ph=ph, t1=t1: e.tensor_scalar(
            t1, ph, 1.0 / TWO_PI, MAGIC, ALU.mult, ALU.add),
            waits=dl_now)
        p.add("dve", lambda e, t1=t1: e.tensor_scalar(
            t1, t1, MAGIC, None, ALU.subtract))
        p.add("dve", lambda e, ph=ph, t1=t1: e.scalar_tensor_tensor(
            t1, t1, -TWO_PI, ph, ALU.mult, ALU.add))
        p.add("act", lambda e, t1=t1, t2=t2: e.activation(
            t2, t1, AF.Sin, bias=0.0, scale=0.5), waits=p.now("DVEa"))
        p.add("dve", lambda e, t1=t1, t2=t2: e.scalar_tensor_tensor(
            t1, t2, -2.0, t2, ALU.mult, ALU.mult),
            waits=p.now("ACTa"))
        p.add("dve", lambda e, t1=t1: e.tensor_scalar(
            t1, t1, 1.0, None, ALU.add))
        pa_tok = p.add("dve", lambda e, am=am, t1=t1, dd=wdst[nm]: e.tensor_tensor(
            dd, am, t1, ALU.mult))

    WMATS = ("wf", "uf", "wi", "ui", "wc", "uc", "uo", "wo")
    stg_cp = None
    stg_out = None
    for mi, nm in enumerate(WMATS):
        w = []
        if stg_out is not None:
            w = list(stg_out)
        p.dma("gp", lambda e, nm=nm: e.dma_start(
            out=stA[:], in_=w_d[nm][:].rearrange("(a p) b -> p a b", p=128)),
            waits=(list(stg_cp.items()) if stg_cp else []))
        stg_dma = p.dma_done("gp")
        stg_cp = p.add("dve", lambda e: e.tensor_copy(
            stB[:], stA[:].rearrange("p a (o f) -> p o a f", o=ND)),
            waits=stg_dma + (stg_out or []))
        p.dma("sy", lambda e, mi=mi: e.dma_start(
            out=scr_d[mi, :, :, :, :].rearrange("o q a f -> q o a f"),
            in_=stB[:]),
            waits=list(stg_cp.items()))
        stg_out = p.dma_done("sy")
    scr_ready = stg_out

    p.add("dve", lambda e: e.memset(st[:, 7:8], 1e-5))
    evac_tok = [None, None]
    mm_tok = [None, None]
    xn_cp = None
    act_tok = [None, None]
    for ft in range(B * S // 128):
        rb = ft % 4
        pw2 = ft % 2
        sb_ = 8 * pw2
        xnt_p = (xnt, xnt2)[pw2]
        sqr_p = (xc, sq)[pw2]
        p.dma("sy", lambda e, ft=ft, rb=rb: e.dma_start(
            out=xf_tm[:, rb, :],
            in_=xfull_d[:].rearrange("(t p) d -> p t d", p=128)[:, ft, :]),
            waits=((mm_tok[rb % 2] or []) if ft >= 4 else
                   (p.now("DVEa") if ft == 0 else [])))
        xf_dma = p.dma_done("sy")
        x_t = xf_tm[:, rb, :]
        p.add("dve", lambda e, x_t=x_t, sb_=sb_: e.reduce_sum(
            st[:, sb_ + 0:sb_ + 1], x_t, axis=AX.X),
            waits=xf_dma + (act_tok[pw2] or [])
            + (scr_ready if ft == 0 else []))
        p.add("dve", lambda e, sb_=sb_: e.tensor_scalar(
            st[:, sb_ + 1:sb_ + 2], st[:, sb_ + 0:sb_ + 1],
            1.0 / 1024.0, None, ALU.mult))
        p.add("dve", lambda e, sb_=sb_: e.tensor_scalar(
            st[:, sb_ + 5:sb_ + 6], st[:, sb_ + 0:sb_ + 1],
            -1.0 / 1024.0, None, ALU.mult))
        p.add("act", lambda e, x_t=x_t, sb_=sb_, sqr_p=sqr_p: e.activation(
            sqr_p, x_t, AF.Square, bias=st[:, sb_ + 5:sb_ + 6], scale=1.0,
            accum_out=st[:, sb_ + 2:sb_ + 3]), waits=p.now("DVEa"))
        p.add("act", lambda e, sb_=sb_: e.activation(
            st[:, sb_ + 3:sb_ + 4], st[:, sb_ + 2:sb_ + 3], AF.Ln,
            bias=T["eps"], scale=1.0 / 1024.0))
        act_tok[pw2] = list(p.add("act", lambda e, sb_=sb_: e.activation(
            st[:, sb_ + 4:sb_ + 5], st[:, sb_ + 3:sb_ + 4], AF.Exp,
            bias=0.0, scale=-0.5)).items())
        w = [("ACTa", p.c("ACTa"))]
        if ft > 1:
            w += [("TTf", p.c("TTf") - 1)]
        p.add("dve", lambda e, x_t=x_t, sb_=sb_, xnt_p=xnt_p: e.tensor_scalar(
            xnt_p, x_t, st[:, sb_ + 1:sb_ + 2], st[:, sb_ + 4:sb_ + 5],
            ALU.subtract, ALU.mult), waits=w)
        mult_tok = p.now("DVEa")
        for dt in range(ND):
            w2 = list(mult_tok) if dt == 0 else []
            if dt == 0:
                w2 += p.now("DVEa")
            p.add("te", lambda e, dt=dt, xnt_p=xnt_p: e.transpose(
                PTB[:, bass.ts(dt, 128)], xnt_p[:, bass.ts(dt, 128)], ident[:]),
                waits=w2, incs=([("TTf", 1)] if dt == ND - 1 else None))
        xb = pw2
        xn_cp = p.add("act", lambda e, xb=xb: e.copy(
            xnf_t[:, xb, :, :], PTB[:].rearrange("p (a b) -> p a b", a=ND)),
            waits=[("TTf", p.c("TTf"))] + (mm_tok[xb] or []))
        for dt in range(ND):
            w2 = []
            if dt == 0:
                w2 = list(xn_cp.items()) + (evac_tok[xb] or [])
                if ft == 0:
                    w2 += dl_loads
            p.add("te", lambda e, dt=dt, xb=xb: e.matmul(
                PW[0:64, bass.ts(xb, 512)][:, 0:128], bkv_sb[:, dt, :],
                xnf_t[:, xb, dt, :], start=(dt == 0), stop=(dt == ND - 1)),
                waits=w2, incs=([("TEx", 1)] if dt == ND - 1 else None))
        mm_tok[xb] = [("TEx", p.c("TEx"))]
        evac_tok[xb] = list(p.add("act", lambda e, ft=ft, xb=xb: e.copy(
            rkv[0:64, ft * 128:(ft + 1) * 128],
            PW[0:64, bass.ts(xb, 512)][:, 0:128]),
            waits=[("TEx", p.c("TEx"))]).items())
        rkv_kv_done = p.now("ACTa")

    p.add("dve", lambda e: e.memset(st[:, 6:7], 0.0), waits=ds_loads)
    ln_t_phase(p, T, NT, x_tm, xn_fm, PTB, ident[:], sem="TT",
               first_waits=p.now("TTf"))

    prj_prev = None
    prj_dst = {"q": rkv[64:96, 0:TOK], "k": rloc[0:32, 0:TOK],
               "v": rloc[32:64, 0:TOK]}
    for pi, pn in enumerate(("q", "k", "v")):
        for n in range(4):
            pjb = PG if (n % 2 == 0) else PO
            for dt in range(ND):
                w = []
                if dt == 0:
                    w = p.now("DVEa") + dl_loads + tok_wf(prj_prev)
                p.add("te", lambda e, pn=pn, n=n, dt=dt, pjb=pjb: e.matmul(
                    pjb[0:HARM, :], bsb[pn][:, dt, :],
                    xn_fm[:, dt, bass.ts(n, 512)],
                    start=(dt == 0), stop=(dt == ND - 1)),
                    waits=w, incs=([("TEp", 1)] if dt == ND - 1 else None))
            prj_prev = p.add("act", lambda e, pn=pn, n=n, pjb=pjb: e.copy(
                prj_dst[pn][:, bass.ts(n, 512)], pjb[0:HARM, :]),
                waits=[("TEp", p.c("TEp"))])
    p.dma("sy", lambda e: e.dma_start(out=rkv_loc[:], in_=rloc[0:64, 0:TOK]),
          waits=p.now("ACTa"))
    cc_in_done = p.dma_done("sy")
    p.add("gp", lambda e: e.collective_compute(
        "AllGather", ALU.bypass, replica_groups=[list(range(NCORE))],
        ins=[rkv_loc[:]], outs=[rkv_gath[:]]),
        waits=cc_in_done, incs=[("CC", 16)])
    p.dma("sy", lambda e: e.dma_start(
        out=rkv[0:32, :],
        in_=rkv_gath[:].rearrange("(r hh) (b pp) -> hh b r pp",
                                  hh=64, b=B)[0:32, :, :, :]),
        waits=[("CC", 16)])
    p.dma("sy", lambda e: e.dma_start(
        out=rkv[32:64, :],
        in_=rkv_gath[:].rearrange("(r hh) (b pp) -> hh b r pp",
                                  hh=64, b=B)[32:64, :, :, :]),
        waits=[("CC", 16)])
    rkv_ready = p.dma_done("sy")

    def tok_w(tok):
        return list(tok.items()) if tok else []

    rq_prev = None
    for n in range(4):
        for dt in range(ND):
            w = []
            if dt == 0:
                w = p.now("DVEa") + tok_w(rq_prev)
            p.add("te", lambda e, n=n, dt=dt: e.matmul(
                PG[0:HARM, :], bq_sb[:, dt, :], xn_fm[:, dt, bass.ts(n, 512)],
                start=(dt == 0), stop=(dt == ND - 1)),
                waits=w, incs=([("TEx", 1)] if dt == ND - 1 else None))
        rq_prev = p.add("act", lambda e, n=n: e.copy(
            rkv[64:96, bass.ts(n, 512)], PG[0:HARM, :]),
            waits=[("TEx", p.c("TEx"))])
    rq_done = list(rq_prev.items())

    p.add("dve", lambda e: e.memset(
        v2p[:].rearrange("p a b c -> p (a b) c")[:, :, 64:65], 1.0))

    exp_bank = [None, None]
    eslot = [None] * 4
    k_free = [None, None]
    q_free = [None, None]
    v_free = [None, None]
    pw_tok = None
    pg_tok = None
    po_tok = None
    pt32_tok = None
    otm_tok = None
    ofm_cp = None
    wm_tok = pa_tok
    ln_done = p.now("DVEa")
    hi = 0
    kexp_cp = None
    for b in range(B):
        for h in range(NH):
            kb = hi % 2
            for n in range(ND):
                w = []
                if hi == 0 and n == 0:
                    w += tok_w(wm_tok) + rkv_ready
                if n == 0:
                    w += tok_w(k_free[kb])
                if n >= 2 and n % 2 == 0:
                    w += tok_w(kexp_cp)
                if n == 0 and pw_tok:
                    w += tok_w(pw_tok)
                p.add("te", lambda e, h=h, b=b, n=n: e.matmul(
                    PW[0:64, bass.ts(n % 2, 512)], wmatA[0:32, bass.ts(h, HD)],
                    rkv[0:32, b * 4096 + n * 512: b * 4096 + (n + 1) * 512],
                    start=True, stop=True), waits=w, incs=[("TEk", 1)])
                if n % 2 == 1:
                    tek = p.c("TEk")
                    kexp_cp = p.add(
                        "dve", lambda e, kb=kb, n=n:
                        e.tensor_copy(k2[bass.ts(kb, 64),
                                         (n - 1) * 512:(n + 1) * 512],
                                      PW[0:64, :]),
                        waits=[("TEk", tek)])
            pw_tok = kexp_cp
            p.add("te", lambda e, h=h, b=b: e.matmul(
                PW[0:64, 0:512], wmatA[64:96, bass.ts(h, HD)],
                rkv[64:96, b * 512:(b + 1) * 512], start=True, stop=True),
                waits=tok_w(pw_tok) + tok_w(q_free[kb]), incs=[("TEk", 1)])
            tek = p.c("TEk")
            q_cp = p.add("dve", lambda e, kb=kb: e.tensor_copy(
                q2[bass.ts(kb, 64), :], PW[0:64, 0:512]), waits=[("TEk", tek)])
            pw_tok = q_cp
            for g in range(4):
                for j in range(8):
                    kt = g * 8 + j
                    w = []
                    if g == 0 and j == 0:
                        w += (tok_w(pg_tok) if pg_tok else rq_done) + tok_w(v_free[kb])
                    if j == 0 and g > 0:
                        w += tok_w(v_cp)
                    p.add("te", lambda e, b=b, h=h, kt=kt, j=j: e.matmul(
                        PG[:, bass.ts(j, 64)],
                        rkv[32:64, b * 4096 + kt * 128:b * 4096 + (kt + 1) * 128],
                        wmatA[32:64, bass.ts(h, HD)],
                        start=True, stop=True), waits=w, incs=[("TEk", 1)])
                tek = p.c("TEk")
                v_cp = p.add("dve", lambda e, kb=kb, g=g: e.tensor_copy(
                    v2[:, kb, bass.ts(g, 8), 0:64],
                    PG[:].rearrange("p (a b) -> p a b", a=8)),
                    waits=[("TEk", tek)])
            pg_tok = v_cp
            for kt in range(32):
                w = [("ACTa", exp_bank[kt % 2]["ACTa"])] if exp_bank[kt % 2] else []
                if kt == 0:
                    w += tok_w(kexp_cp) + tok_w(q_cp)
                qk = p.add("te", lambda e, kb=kb, kt=kt: e.matmul(
                    LOG[kt % 2][:, :], k2[bass.ts(kb, 64), bass.ts(kt, 128)],
                    q2[bass.ts(kb, 64), :], start=True, stop=True),
                    waits=w, incs=[("TEl", 1)])
                k_free[kb] = {"TEl": p.c("TEl")}
                q_free[kb] = {"TEl": p.c("TEl")}
                w = [("TEl", p.c("TEl"))]
                if eslot[kt % 4]:
                    w += [("TEo", eslot[kt % 4]["TEo"])]
                ex = p.add("act", lambda e, kt=kt: e.activation(
                    e4[:, kt % 4, :], LOG[kt % 2][:, :], AF.Exp,
                    bias=0.0, scale=0.125), waits=w)
                exp_bank[kt % 2] = ex
                w = [("ACTa", ex["ACTa"])]
                if kt == 0:
                    w += tok_w(po_tok) + tok_w(v_cp)
                p.add("te", lambda e, kb=kb, kt=kt: e.matmul(
                    PO[0:65, :], v2[:, kb, kt, 0:65], e4[:, kt % 4, :],
                    start=(kt == 0), stop=(kt == 31)),
                    waits=w, incs=[("TEo", 1)])
                eslot[kt % 4] = {"TEo": p.c("TEo")}
                v_free[kb] = {"TEo": p.c("TEo")}
            po_tok = p.add("dve", lambda e: e.tensor_copy(os_sb[0:65, 0:512], PO[0:65, :]),
                           waits=[("TEo", p.c("TEo"))] + tok_w(otm_tok)
                           + p.now("TEt"))
            trs = []
            for i in range(4):
                w = tok_w(po_tok) if i == 0 else []
                if i == 0 and pt32_tok:
                    w += tok_w(pt32_tok)
                p.add("te", lambda e, i=i: e.transpose(
                    PT32[:, i * 128:i * 128 + 65], os_sb[0:65, bass.ts(i, 128)],
                    identf[0:65, 0:65]), waits=w, incs=[("TEt", 1)])
            tet = p.c("TEt")
            for i in range(4):
                p.add("dve", lambda e, i=i: e.reciprocal(
                    rec[:, i:i + 1], PT32[:, i * 128 + 64:i * 128 + 65]),
                    waits=([("TEt", tet)] if i == 0 else []))
                pt32_tok = p.add("dve", lambda e, i=i, h=h: e.tensor_scalar(
                    o_tm[:, i, bass.ts(h, HD)], PT32[:, i * 128:i * 128 + 64],
                    rec[:, i:i + 1], None, ALU.mult))
            hi += 1
        transpose_tm_to_fm(p, T, o_tm, o_fm, 4, PTB, ident[:], sem="TTo",
                           src_ready=p.now("DVEa"))
        otm_tok = p.now("TTo") and {"TTo": p.c("TTo")}
        ofm_cp = p.now("DVEa")
        for dt in range(ND):
            w = list(ofm_cp) if dt == 0 else []
            if dt == 0:
                w += tok_w(pg_tok)
            p.add("te", lambda e, dt=dt: e.matmul(
                PG[0:32, :], basiso[:, dt, :], o_fm[:, dt, :],
                start=(dt == 0), stop=(dt == ND - 1)), waits=w,
                incs=([("TEr", 1)] if dt == ND - 1 else None))
        ro_cp = p.add("act", lambda e: e.copy(ro_sb[0:32, :], PG[0:32, :]),
                      waits=[("TEr", p.c("TEr"))])
        pg_tok = ro_cp
        for qt in range(4):
            w = (tok_w(ro_cp) + tok_w(poA_tok) + tok_w(poB_tok)) if qt == 0 else []
            w += tok_w(pw_tok)
            for half in range(2):
                p.add("te", lambda e, qt=qt, half=half: e.matmul(
                    PW[:, bass.ts(half, 512)], ro_sb[0:32, bass.ts(qt, 128)],
                    wmatB[0:32, bass.ts(half, 512)],
                    start=True, stop=True),
                    waits=(w if half == 0 else []), incs=[("TEk", 1)])
            tek = p.c("TEk")
            pw_tok = p.add("dve", lambda e, b=b, qt=qt: e.tensor_add(
                x_tm[:, b * 4 + qt, :], x_tm[:, b * 4 + qt, :], PW[:, :]),
                waits=[("TEk", tek)])

    for dt in range(ND):
        p.add("dve", lambda e, dt=dt: e.memset(c_fm[:, dt, :], 1.0),
              waits=(dl_loads + p.now("TEk", "TEl", "TEo")
                     if dt == 0 else []))
        p.add("dve", lambda e, dt=dt: e.tensor_scalar(
            c_fm[:, dt, :], c_fm[:, dt, :], bia[:, 4 * ND + dt:4 * ND + dt + 1],
            None, ALU.mult))
    cbf_tok = p.add("dve", lambda e: e.tensor_copy(c_bf[:], c_fm[:]))
    ws_tok = [None, None]
    gact = [None, None, None]
    gsb_tok = None
    GACT_F = [AF.Sigmoid, AF.Sigmoid, AF.Tanh]
    gate_w = ("wf", "wi", "wc")
    for t in range(4):
        for ot in range(ND):
            s = (t * ND + ot) % 2
            for g in range(3):
                w = []
                if g == 0:
                    w = p.now("TEf")
                    if t == 0 and ot == 0:
                        w += p.now("TEt", "ACTa", "DVEa") + scr_ready
                p.dma("gp", lambda e, g=g, ot=ot, s=s: e.dma_start(
                    out=wstr[:, s, g, :, :],
                    in_=scr_d[2 * g, ot, :, :, :]),
                    waits=w)
            for g in range(3):
                p.dma("gp", lambda e, g=g, ot=ot, s=s: e.dma_start(
                    out=wstr[:, s, 3 + g, :, :],
                    in_=scr_d[2 * g + 1, ot, :, :, :]))
            dl_now = p.dma_done("gp")
            for g in range(3):
                pgm = PZ[:, bass.ts(g, 512)]
                for dt in range(ND):
                    w = []
                    if dt == 0:
                        w = list(dl_now) + tok_w(gact[g])
                        if g == 0:
                            w += tok_w(cbf_tok)
                        if t == 0 and ot == 0:
                            w += tok_w(exp_bank[0]) + tok_w(exp_bank[1])
                    p.add("te", lambda e, g=g, dt=dt, ot=ot, t=t, s=s, pgm=pgm: e.matmul(
                        pgm, wstr[:, s, g, dt, :], xn_fm[:, dt, bass.ts(t, 512)],
                        start=(dt == 0), stop=False), waits=w)
                for dt in range(ND):
                    p.add("te", lambda e, g=g, dt=dt, ot=ot, s=s, pgm=pgm: e.matmul(
                        pgm, wstr[:, s, 3 + g, dt, :], c_bf[:, dt, :],
                        start=False, stop=(dt == ND - 1)),
                        incs=([("TEf", 1)] if dt == ND - 1 else None))
                tef = p.c("TEf")
                gact[g] = p.add("act", lambda e, g=g, ot=ot, pgm=pgm,
                                d=(gf, gi, gg)[g]: e.activation(
                    d[:], pgm, GACT_F[g], bias=bia[:, g * ND + ot:g * ND + ot + 1],
                    scale=1.0), waits=[("TEf", tef)] + (tok_w(gsb_tok) if g == 0 else []))
            ws_tok[s] = {"TEf": p.c("TEf")}
            w = tok_w(gact[0]) + tok_w(gact[1]) + tok_w(gact[2])
            p.add("dve", lambda e, ot=ot: e.tensor_mul(
                gf[:], gf[:], c_fm[:, ot, :]), waits=w)
            p.add("dve", lambda e: e.tensor_mul(gi[:], gi[:], gg[:]))
            gsb_tok = p.add("dve", lambda e, ot=ot: e.tensor_add(
                c_fm[:, ot, :], gf[:], gi[:]))
        cbf_tok = p.add("dve", lambda e: e.tensor_copy(c_bf[:], c_fm[:]))
    cuo_tok = [None] * ND
    for ot in range(ND):
        s = ot % 2
        p.dma("gp", lambda e, ot=ot, s=s: e.dma_start(
            out=wstr[:, s, 0, :, :], in_=scr_d[6, ot, :, :, :]),
            waits=p.now("TEf"))
        dl_now = p.dma_done("gp")
        for dt in range(ND):
            w = list(dl_now) if dt == 0 else []
            if dt == 0:
                w += tok_w(cbf_tok)
            if dt == 0 and ot >= 2:
                w += tok_w(cuo_tok[ot - 2])
            if dt == 0 and ot < 2:
                w += tok_w(gact[0]) + tok_w(gact[1]) + tok_w(gact[2])
            p.add("te", lambda e, ot=ot, dt=dt, s=s: e.matmul(
                PZ[:, bass.ts(ot % 2, 512)], wstr[:, s, 0, dt, :], c_bf[:, dt, :],
                start=(dt == 0), stop=(dt == ND - 1)),
                waits=w, incs=([("TEf", 1)] if dt == ND - 1 else None))
        ws_tok[s] = {"TEf": p.c("TEf")}
        cuo_tok[ot] = p.add("dve", lambda e, ot=ot: e.tensor_copy(
            cuo[:, ot, :], PZ[:, bass.ts(ot % 2, 512)]),
            waits=[("TEf", p.c("TEf"))])
    ht_tok = [None] * ND
    haddB = [tok_w(cuo_tok[ND - 2]), tok_w(cuo_tok[ND - 1])]
    tet_after = None
    for t in range(4):
        for ot in range(ND):
            s = ot % 2
            p.dma("gp", lambda e, ot=ot, s=s: e.dma_start(
                out=wstr[:, s, 1, :, :], in_=scr_d[7, ot, :, :, :]),
                waits=p.now("TEf"))
            dl_now = p.dma_done("gp")
            for dt in range(ND):
                w = list(dl_now) if dt == 0 else []
                if dt == 0:
                    w += list(haddB[ot % 2])
                p.add("te", lambda e, t=t, ot=ot, dt=dt, s=s: e.matmul(
                    PZ[:, bass.ts(ot % 2, 512)], wstr[:, s, 1, dt, :],
                    xn_fm[:, dt, bass.ts(t, 512)],
                    start=(dt == 0), stop=(dt == ND - 1)),
                    waits=w, incs=([("TEf", 1)] if dt == ND - 1 else None))
            ws_tok[s] = {"TEf": p.c("TEf")}
            hadd = p.add("dve", lambda e, ot=ot: e.tensor_add(
                htmp[:], cuo[:, ot, :], PZ[:, bass.ts(ot % 2, 512)]),
                waits=[("TEf", p.c("TEf"))] + tok_w(cuo_tok[ot]) + p.now("ACTa"))
            haddB[ot % 2] = tok_w(hadd)
            aw = p.now("DVEa")
            if tet_after is not None:
                aw = aw + [("TEt", tet_after)]
            ht_tok[ot] = p.add("act", lambda e, ot=ot: e.activation(
                h_fm[:, ot, :], htmp[:], AF.Tanh,
                bias=bia[:, 3 * ND + ot:3 * ND + ot + 1], scale=1.0),
                waits=aw)
        hw = []
        for g in range(ND):
            hw += tok_w(ht_tok[g])
        for pt in range(4):
            for rnd in range(2):
                for j in range(4):
                    dt = rnd * 4 + j
                    w = []
                    if pt == 0 and rnd == 0 and j == 0:
                        w += hw
                    if j == 0 and pt32_tok:
                        w += tok_w(pt32_tok)
                    p.add("te", lambda e, t=t, pt=pt, dt=dt, j=j: e.transpose(
                        PT32[:, bass.ts(j, 128)], h_fm[:, dt, bass.ts(pt, 128)],
                        identf[:]), waits=w,
                        incs=([("TEt", 1)] if j == 3 else None))
                tet = p.c("TEt")
                pt32_tok = p.add("dve", lambda e, t=t, pt=pt, rnd=rnd: e.tensor_add(
                    x_tm[:, t * 4 + pt, bass.ts(rnd, 512)],
                    x_tm[:, t * 4 + pt, bass.ts(rnd, 512)], PT32[:, 0:512]),
                    waits=[("TEt", tet)])
        tet_after = p.c("TEt")

    ln_t_phase(p, T, NT, x_tm, xn_fm, PTB, ident[:], sem="TT1",
               first_waits=p.now("TEf"))
    xn1_done = p.now("DVEa")

    l1dst = [q1sb[0:64, 0:TOK], kfsb[0:64, 0:TOK], v1sb[0:64, 0:TOK]]
    prev = None
    for i in range(3):
        for n in range(4):
            for dt in range(ND):
                w = []
                if dt == 0:
                    w = list(xn1_done) + dl_loads
                    w += tok_w(prev)
                    if i == 0 and n == 0:
                        w += tok_w(pg_tok) + tok_w(exp_bank[1])
                p.add("te", lambda e, i=i, n=n, dt=dt: e.matmul(
                    PG[0:64, :], wl1[:, i, dt, :], xn_fm[:, dt, bass.ts(n, 512)],
                    start=(dt == 0), stop=(dt == ND - 1)),
                    waits=w, incs=([("TEr", 1)] if dt == ND - 1 else None))
            ter = p.c("TEr")
            if i < 2:
                p.add("dve", lambda e: e.tensor_scalar(
                    el_t[0:64, :], PG[0:64, :], 0.0, None, ALU.min),
                    waits=[("TEr", ter)] + (p.now("TEt", "TEf", "TEo", "TEl",
                                                  "ACTa")
                                            if (i == 0 and n == 0) else []))
                p.add("act", lambda e: e.activation(
                    el_e[0:64, :], el_t[0:64, :], AF.Exp, bias=0.0, scale=1.0),
                    waits=p.now("DVEa"))
                prev = p.add("dve", lambda e, i=i, n=n: e.scalar_tensor_tensor(
                    l1dst[i][:, bass.ts(n, 512)], PG[0:64, :], 0.0, el_e[0:64, :],
                    ALU.max, ALU.add), waits=p.now("ACTa"))
            else:
                prev = p.add("act", lambda e, i=i, n=n: e.copy(
                    l1dst[i][:, bass.ts(n, 512)], PG[0:64, :]),
                    waits=[("TEr", ter)])
    qkv_done = p.now("DVEa", "ACTa")
    p.add("dve", lambda e: e.memset(vp_tmv[:, :, 64:65], 1.0))
    for src, dst, tag in ((kfsb[0:64, 0:TOK], kf_tmv, "K"),
                          (v1sb[0:64, 0:TOK], vp_tmv, "V")):
        for rnd in range(2):
            for j in range(8):
                kt = rnd * 8 + j
                w = list(qkv_done) if (rnd == 0 and j == 0) else []
                if j == 0:
                    w += p.now("DVEa")
                p.add("te", lambda e, src=src, kt=kt, j=j: e.transpose(
                    PTB[:, j * 64:(j + 1) * 64],
                    src[:, bass.ts(kt, 128)], ident[0:64, 0:64]),
                    waits=w, incs=([("TEt", 1)] if j == 7 else None))
            tet = p.c("TEt")
            p.add("dve", lambda e, dst=dst, rnd=rnd: e.tensor_copy(
                dst[:, bass.ts(rnd, 8), 0:64],
                PTB[:, 0:512].rearrange("p (a b) -> p a b", a=8)),
                waits=[("TEt", tet)])
    tr_done = p.now("DVEa")
    for b in range(B):
        for ct in range(4):
            w = tr_done + tok_w(prev) if ct == 0 else []
            p.add("te", lambda e, b=b, ct=ct: e.matmul(
                PG[0:64, 0:65], kf_tmv[:, b * 4 + ct, 0:64],
                vp_tmv[:, b * 4 + ct, 0:65],
                start=(ct == 0), stop=(ct == 3)),
                waits=w, incs=([("TEr", 1)] if ct == 3 else None))
        prev = p.add("dve", lambda e, b=b: e.tensor_copy(
            st64[0:64, b * 65:(b + 1) * 65], PG[0:64, 0:65]),
            waits=[("TEr", p.c("TEr"))])
    p.dma("sy", lambda e: e.dma_start(out=stt_loc[:], in_=st64[0:64, :]),
          waits=p.now("DVEa"))
    st_in_done = p.dma_done("sy")
    p.add("gp", lambda e: e.collective_compute(
        "AllGather", ALU.bypass, replica_groups=[list(range(NCORE))],
        ins=[stt_loc[:]], outs=[stt_gath[:]]),
        waits=st_in_done, incs=[("CC", 16)])
    p.dma("sy", lambda e: e.dma_start(
        out=sa_sb[0:64, :],
        in_=stt_gath[:].rearrange("(r kk) bc -> kk r bc", kk=64)),
        waits=[("CC", 32)] + p.now("DVEa"))
    sa_ready = p.dma_done("sy")

    PGc = PZ[:, 1536:2048]
    PECc = PZ[:, 0:512]
    POc = PZ[:, 512:1024]
    PSc = PZ[:, 1024:1536]
    p.dma("gp", lambda e: e.dma_start(
        out=rm_sb[0:16, 0:4096], in_=rm_d[:].rearrange("j h r i -> j (h r i)")),
        waits=p.now("DVEa", "ACTa", "TEr", "TEf", "TEt"))
    p.dma("gp", lambda e: e.dma_start(
        out=rup_sb[0:16, 0:16384], in_=rup_d[:].rearrange("j h r k -> j (h r k)")))
    rails_done = p.dma_done("gp")
    ccp = None
    for blk in range(32):
        for j in range(8):
            hr = blk * 8 + j
            w = []
            if blk == 0 and j == 0:
                w = list(rails_done) + p.now('DVEa', 'TEf')
            if j == 0 and ccp is not None:
                w = w + tok_w(ccp)
            p.add("te", lambda e, hr=hr, j=j: e.matmul(
                PGc[0:16, bass.ts(j, 64)], rm_sb[0:16, hr * 16:(hr + 1) * 16],
                rup_sb[0:16, hr * 64:(hr + 1) * 64], start=True, stop=True),
                waits=w, incs=([("TEc", 1)] if j == 7 else None))
        ccp = p.add("dve", lambda e, blk=blk: e.tensor_copy(
            cflat[0:16, bass.ts(blk, 512)], PGc[0:16, :]),
            waits=[("TEc", p.c("TEc"))]
            + (p.now("TEf") + st_in_done if blk == 0 else []))
    p.dma("sy", lambda e: e.dma_start(
        out=c_scr[:].rearrange("i a b g rl k -> i (a b g rl k)"),
        in_=cflat[0:16, :]),
        waits=p.now("DVEa"))
    cscr_done = p.dma_done("sy")
    p.dma("gp", lambda e: e.dma_start(out=wo1[0:64, 0:1024], in_=wo1_d[:]),
          waits=p.now("TEc"))
    p.dma("sy", lambda e: e.dma_start(out=triu_sb[0:128, 0:128], in_=triu_d[:]),
          waits=p.now("TEc"))
    p.dma("sy", lambda e: e.dma_start(out=pm_sb[0:64, 0:NCORE], in_=pm_d[:]))
    late_done = p.dma_done("sy") + p.dma_done("gp")

    for b in range(B):
        p.add("dve", lambda e, b=b: e.tensor_scalar(
            s0_sb[0:64, b, :], sa4[0:64, :, :, :][:, 0, b, :], pm_sb[0:64, 0:1],
            None, ALU.mult),
            waits=(sa_ready + late_done + p.now("TEc") if b == 0 else []))
        for r in range(1, NCORE):
            p.add("dve", lambda e, b=b, r=r: e.scalar_tensor_tensor(
                s0_sb[0:64, b, :], sa4[0:64, :, :, :][:, r, b, :],
                pm_sb[0:64, r:r + 1], s0_sb[0:64, b, :], ALU.mult, ALU.add))
    l1pfx_done = p.now("DVEa")
    sup_tok = None
    for b in range(B):
        for ct in range(4):
            tile = b * 4 + ct
            if ct == 0:
                sw = list(l1pfx_done) + list(tr_done) if b == 0 else p.now("TEe")
                scp = p.add("dve", lambda e, b=b: e.tensor_copy(
                    ssb[0:64, 0:65], s0_sb[0:64, b, :]), waits=sw)
            else:
                scp = p.add("dve", lambda e, b=b: e.tensor_add(
                    ssb[0:64, 0:65], s0_sb[0:64, b, :], sacc[0:64, 0:65]),
                    waits=p.now("TEe"))
            w = p.now("DVEa")
            p.add("te", lambda e, tile=tile: e.matmul(
                PECc[0:128, 0:128], kfsb[0:64, 0:TOK][:, bass.ts(tile, 128)],
                q1sb[0:64, 0:TOK][:, bass.ts(tile, 128)], start=True, stop=True),
                waits=w, incs=[("TEe", 1)])
            mcp = p.add("dve", lambda e: e.tensor_tensor(
                et_sb[0:128, 0:128], PECc[0:128, 0:128], triu_sb[0:128, 0:128],
                ALU.mult),
                waits=[("TEe", p.c("TEe"))])
            p.add("te", lambda e, tile=tile: e.matmul(
                POc[0:128, 0:65], et_sb[0:128, 0:128], vp_tmv[:, tile, 0:65],
                start=True, stop=False), waits=tok_w(mcp) + tok_w(sup_tok))
            p.add("te", lambda e, tile=tile: e.matmul(
                POc[0:128, 0:65], q1sb[0:64, 0:TOK][:, bass.ts(tile, 128)],
                ssb[0:64, 0:65],
                start=False, stop=True), incs=[("TEe", 1)])
            p.add("dve", lambda e: e.tensor_scalar(
                rec[:, 0:1], POc[:, 64:65], 1e-6, None, ALU.add),
                waits=[("TEe", p.c("TEe"))])
            p.add("dve", lambda e: e.reciprocal(rec[:, 1:2], rec[:, 0:1]))
            ob = p.add("dve", lambda e: e.tensor_scalar(
                obf[0:128, 0:64], POc[:, 0:64], rec[:, 1:2], None, ALU.mult))
            p.add("te", lambda e: e.transpose(
                PTB[0:64, 0:128], obf[0:128, 0:64], ident[:]),
                waits=tok_w(ob) + p.now("DVEa"), incs=[("TEt", 1)])
            ocp = p.add("act", lambda e: e.copy(ofs[0:64, 0:128], PTB[0:64, 0:128]),
                waits=[("TEt", p.c("TEt"))])
            for half in range(2):
                w = tok_w(ocp) + tok_w(pw_tok) if half == 0 else []
                p.add("te", lambda e, half=half: e.matmul(
                    PW[:, bass.ts(half, 512)], ofs[0:64, 0:128],
                    wo1[0:64, bass.ts(half, 512)], start=True, stop=True),
                    waits=w + (late_done if (b == 0 and ct == 0 and half == 0) else []),
                    incs=([("TEe", 1)] if half == 1 else None))
            pw_tok = p.add("dve", lambda e, tile=tile: e.tensor_add(
                x_tm[:, tile, :], x_tm[:, tile, :], PW[:, :]),
                waits=[("TEe", p.c("TEe"))])
            if ct < 3:
                p.add("te", lambda e, tile=tile: e.matmul(
                    PSc[0:64, 0:65], kf_tmv[:, tile, 0:64], vp_tmv[:, tile, 0:65],
                    start=True, stop=True),
                    waits=p.now("DVEa"), incs=[("TEs", 1)])
                if ct == 0:
                    p.add("dve", lambda e: e.tensor_copy(
                        sacc[0:64, 0:65], PSc[0:64, 0:65]),
                        waits=[("TEs", p.c("TEs"))])
                else:
                    p.add("dve", lambda e: e.tensor_add(
                        sacc[0:64, 0:65], sacc[0:64, 0:65], PSc[0:64, 0:65]),
                        waits=[("TEs", p.c("TEs"))])
            sup_tok = {"TEs": p.c("TEs")}

    wz_tok = [None, None]
    wc_tok = [None, None]
    gelu_toks = [None] * ND
    zfree = None
    hfree = None
    for g in range(ND):
        sg = g % 2
        mz = p.add("dve", lambda e, sg=sg: e.memset(wzg[:, sg, :, :], 0.0),
                   waits=tok_w(wz_tok[sg]) + (cscr_done if g == 0 else []))
        for rl in range(8):
            p.dma("gp", lambda e, g=g, sg=sg, rl=rl: e.dma_start(
                out=wzg[rl * 16:(rl + 1) * 16, sg, 2 * rl:2 * rl + 2, :]
                .rearrange("p a f -> p a f") if False else
                wzg[rl * 16:(rl + 1) * 16, sg, :, :].rearrange(
                    "p m f -> p (m f)")[:, (2 * rl) * 128:(2 * rl + 2) * 128]
                .rearrange("p (a b k) -> p a b k", a=2, b=2),
                in_=c_scr[:, :, :, g, rl, :]),
                waits=tok_w(mz) if rl == 0 else [])
        wzdma = p.dma_done("gp")
        for half in range(2):
            hg = g * 2 + half
            sc = hg % 2
            p.dma("gp", lambda e, g=g, half=half, sc=sc: e.dma_start(
                out=wcs[:, sc, :, :],
                in_=wc_d[:].rearrange("(gg m p) d -> p gg m d", m=8, p=128)[
                    :, 2 * g + half, :, :]),
                waits=tok_w(wc_tok[sc]) + (cscr_done + p.now("TEe", "DVEa")
                                           if (g == 0 and half == 0) else []))
            wcdma = p.dma_done("gp")
            for m in range(8):
                mi = half * 8 + m
                for n in range(4):
                    w = []
                    if n == 0:
                        w = tok_w(zfree)
                    if m == 0 and n == 0:
                        w = w + wzdma + tok_w(hfree)
                        if hg == 0:
                            w += p.now("ACTa", "DVEa") + st_in_done
                    p.add("te", lambda e, g=g, mi=mi, n=n: e.matmul(
                        PZ[:, bass.ts(n, 512)], wzg[:, g % 2, mi, :],
                        xn_fm[:, g, bass.ts(n, 512)], start=True, stop=True),
                        waits=w, incs=([("TEz", 1)] if n == 3 else None))
                zfree = p.add("act", lambda e, m=m: e.activation(
                    hhg[:, m, :], PZ[:, 0:2048], AF.Gelu, bias=0.0, scale=1.0),
                    waits=[("TEz", p.c("TEz"))])
            for tt in range(NT):
                for m in range(8):
                    w = []
                    if tt == 0 and m == 0:
                        w = wcdma + p.now("ACTa")
                    if m == 0:
                        w = w + tok_w(pw_tok)
                    for half2 in range(2):
                        p.add("te", lambda e, tt=tt, m=m, sc=sc, half2=half2:
                              e.matmul(PW[:, bass.ts(half2, 512)],
                                       hhg[:, m, bass.ts(tt, 128)],
                                       wcs[:, sc, m, bass.ts(half2, 512)],
                                       start=(m == 0), stop=(m == 7)),
                              waits=(w if half2 == 0 else []),
                              incs=([("TEe", 1)] if (m == 7 and half2 == 1)
                                    else None))
                pw_tok = p.add("dve", lambda e, tt=tt: e.tensor_add(
                    x_tm[:, tt, :], x_tm[:, tt, :], PW[:, :]),
                    waits=[("TEe", p.c("TEe"))])
            hfree = {"TEe": p.c("TEe")}
            wc_tok[sc] = {"TEe": p.c("TEe")}
        wz_tok[sg] = {"TEe": p.c("TEe")}
    p.dma("sy", lambda e: e.dma_start(
        out=out_d[:].rearrange("(t p) d -> p t d", p=128), in_=x_tm[:]),
        waits=p.now("DVEa"))
    p.add("sy", lambda e: e.nop(), waits=p.dma_done("sy"))

    realize(nc, p)
    return nc




def assemble_rkv(r1):
    def asm(nm):
        return np.stack([np.asarray(r1[c][nm]).reshape(HARM, B, POS)
                         for c in range(NCORE)], axis=2).reshape(HARM, B * S)
    return np.ascontiguousarray(np.concatenate([asm("rk"), asm("rv")], axis=0))


def run_l2(H, r1):
    nc = build_l2()
    rkvf = assemble_rkv(r1)
    in_maps = []
    for c in range(NCORE):
        m = {"x": H["x_shards"][c], "rkvf": rkvf,
             "rqin": np.asarray(r1[c]["rq"]),
             "ident_bf": H["ident_bf"], "ident_f32": H["ident_f32"],
             "basis_oT": H["basis_oT"]}
        for nm in ("q", "k", "v", "o"):
            m[f"phase_{nm}T"] = H[f"phase_{nm}T"]
            m[f"amp_{nm}T"] = H[f"amp_{nm}T"]
        for nm in ("wf", "uf", "wi", "ui", "wc", "uc", "wo", "uo"):
            m[nm + "T"] = H[nm + "T"]
        for nm in ("bf", "bi", "bc", "bo", "in_c"):
            m[nm + "_r"] = H[nm + "_r"]
        m["bo_row"] = H["bo_row"]
        for nm in ("Wq", "Wk", "Wv"):
            m[nm + "T"] = H[nm + "T"]
        in_maps.append(m)
    res = run_bass_kernel_spmd(nc, in_maps, core_ids=list(range(NCORE)))
    return res.results


def build_l3():
    nc = bass.Bass("TRN2", target_bir_lowering=False)
    x1_d = nc.dram_tensor("x1", [TOK, D], F32, kind="ExternalInput")
    xn1_d = nc.dram_tensor("xn1", [D, TOK], BF16, kind="ExternalInput")
    q1_d = nc.dram_tensor("Q1", [64, TOK], BF16, kind="ExternalInput")
    k1_d = nc.dram_tensor("K1", [64, TOK], BF16, kind="ExternalInput")
    v1_d = nc.dram_tensor("V1", [64, TOK], BF16, kind="ExternalInput")
    sa_d = nc.dram_tensor("states_all", [NCORE, B, 64, 65], F32, kind="ExternalInput")
    pm_d = nc.dram_tensor("prefix_mask", [64, NCORE], F32, kind="ExternalInput")
    wo1_d = nc.dram_tensor("Wo1T", [64, D], F32, kind="ExternalInput")
    rm_d = nc.dram_tensor("rmJ", [16, 4, 64, 16], F32, kind="ExternalInput")
    rup_d = nc.dram_tensor("rupJ", [16, 4, 64, 64], F32, kind="ExternalInput")
    wc_d = nc.dram_tensor("wcprojT", [16384, D], F32, kind="ExternalInput")
    triu_d = nc.dram_tensor("triu", [128, 128], BF16, kind="ExternalInput")
    ident_d = nc.dram_tensor("ident_bf", [128, 128], BF16, kind="ExternalInput")
    out_d = nc.dram_tensor("out", [TOK, D], F32, kind="ExternalOutput")
    c_scr = nc.dram_tensor("c_scr", [16, 2, 2, 8, 8, 64], BF16)

    x_tm = nc.alloc_sbuf_tensor("x_tm", [128, NT, D], F32)
    xn_fm = nc.alloc_sbuf_tensor("xn_fm", [128, ND, TOK], BF16)
    UC = nc.alloc_sbuf_tensor("UC", [128, 18432], F32)

    def carve(arena, w0, w1, dtype=None, shape=None):
        v = arena[:, w0:w1]
        if dtype is not None:
            v = v.bitcast(dtype)
        if shape is not None:
            pat = " ".join(f"a{i}" for i in range(len(shape)))
            v = v.rearrange(f"p ({pat}) -> p {pat}",
                            **{f"a{i}": shape[i] for i in range(len(shape))})
        return v

    wcs = carve(UC, 0, 8192, BF16, (2, ND, D))
    hhg = carve(UC, 8192, 16384, BF16, (ND, TOK))
    wzg = carve(UC, 16384, 18432, BF16, (2, 16, 128))
    qf = nc.alloc_sbuf_tensor("qf", [64, TOK], BF16)
    kff = nc.alloc_sbuf_tensor("kff", [64, TOK], BF16)
    vf = nc.alloc_sbuf_tensor("vf", [64, TOK], BF16)
    kf_tmv = nc.alloc_sbuf_tensor("kf_tmv", [128, NT, 64], BF16)
    vp_tmv = nc.alloc_sbuf_tensor("vp_tmv", [128, NT, 72], BF16)
    sa_sb = nc.alloc_sbuf_tensor("sa_sb", [64, NCORE, B, 65], F32)
    s0_sb = nc.alloc_sbuf_tensor("s0_sb", [64, B, 65], F32)
    pm_sb = nc.alloc_sbuf_tensor("pm_sb", [64, NCORE], F32)
    ssb = nc.alloc_sbuf_tensor("ssb", [64, 65], BF16)
    sacc = nc.alloc_sbuf_tensor("sacc", [64, 65], F32)
    et_sb = nc.alloc_sbuf_tensor("et_sb", [128, 128], BF16)
    obf = nc.alloc_sbuf_tensor("obf", [128, 64], BF16)
    ofs = nc.alloc_sbuf_tensor("ofs", [64, 128], BF16)
    wo1 = nc.alloc_sbuf_tensor("wo1", [64, D], BF16)
    rm_sb = carve(UC, 0, 2048, BF16)
    rup_sb = carve(UC, 2048, 10240, BF16)
    cflat = carve(UC, 10240, 18432, BF16)
    triu_sb = nc.alloc_sbuf_tensor("triu_sb", [128, 128], BF16)
    ident = nc.alloc_sbuf_tensor("ident", [128, 128], BF16)
    rec = nc.alloc_sbuf_tensor("rec", [128, 8], F32)

    PZ = nc.alloc_psum_tensor("PZ", [128, 2048], F32)
    PY = nc.alloc_psum_tensor("PY", [128, 1536], F32)
    PTB = nc.alloc_psum_tensor("PTB", [128, 1024], BF16)
    PW = PY[:, 0:1024]
    PEC = PZ[:, 0:512]
    PO = PZ[:, 512:1024]
    PS = PZ[:, 1024:1536]
    PC = PZ[:, 1536:2048]

    def tok_w(tok):
        return list(tok.items()) if tok else []

    p = Plan()
    p.dma("sy", lambda e: e.dma_start(out=x_tm[:],
          in_=x1_d[:].rearrange("(t p) d -> p t d", p=128)))
    p.dma("sy", lambda e: e.dma_start(out=xn_fm[:],
          in_=xn1_d[:].rearrange("(a p) n -> p a n", p=128)))
    p.dma("sy", lambda e: e.dma_start(out=qf[:], in_=q1_d[:]))
    p.dma("sy", lambda e: e.dma_start(out=kff[:], in_=k1_d[:]))
    p.dma("sy", lambda e: e.dma_start(out=vf[:], in_=v1_d[:]))
    p.dma("sy", lambda e: e.dma_start(out=triu_sb[:], in_=triu_d[:]))
    p.dma("sy", lambda e: e.dma_start(out=ident[:], in_=ident_d[:]))
    p.dma("sy", lambda e: e.dma_start(out=sa_sb[:],
          in_=sa_d[:].rearrange("r b k c -> k r b c")))
    p.dma("sy", lambda e: e.dma_start(out=pm_sb[:], in_=pm_d[:]))
    ds0 = p.dma_done("sy")
    p.dma("gp", lambda e: e.dma_start(out=wo1[:], in_=wo1_d[:]))
    p.dma("gp", lambda e: e.dma_start(
        out=rm_sb[0:16, 0:4096], in_=rm_d[:].rearrange("j h r i -> j (h r i)")))
    p.dma("gp", lambda e: e.dma_start(
        out=rup_sb[0:16, :], in_=rup_d[:].rearrange("j h r k -> j (h r k)")))
    dl0 = p.dma_done("gp")

    ccp = None
    for blk in range(32):
        for j in range(8):
            hr = blk * 8 + j
            w = []
            if blk == 0 and j == 0:
                w = dl0
            if j == 0 and ccp is not None:
                w = w + tok_w(ccp)
            p.add("te", lambda e, hr=hr, j=j: e.matmul(
                PC[0:16, bass.ts(j, 64)], rm_sb[0:16, hr * 16:(hr + 1) * 16],
                rup_sb[0:16, hr * 64:(hr + 1) * 64], start=True, stop=True),
                waits=w, incs=([("TEc", 1)] if j == 7 else None))
        ccp = p.add("dve", lambda e, blk=blk: e.tensor_copy(
            cflat[0:16, bass.ts(blk, 512)], PC[0:16, :]),
            waits=[("TEc", p.c("TEc"))])
    p.dma("sy", lambda e: e.dma_start(
        out=c_scr[:].rearrange("i a b g rl k -> i (a b g rl k)"),
        in_=cflat[0:16, :]),
        waits=p.now("DVEa"))
    cscr_done = p.dma_done("sy")

    p.add("dve", lambda e: e.memset(vp_tmv[:, :, 64:65], 1.0), waits=ds0)
    for srcv, dst in ((kff, kf_tmv), (vf, vp_tmv)):
        for rnd in range(2):
            for j in range(8):
                kt = rnd * 8 + j
                w = p.now("DVEa") if j == 0 else []
                p.add("te", lambda e, srcv=srcv, kt=kt, j=j: e.transpose(
                    PTB[:, j * 64:(j + 1) * 64], srcv[:, bass.ts(kt, 128)],
                    ident[0:64, 0:64]),
                    waits=w, incs=([("TEt", 1)] if j == 7 else None))
            p.add("dve", lambda e, dst=dst, rnd=rnd: e.tensor_copy(
                dst[:, bass.ts(rnd, 8), 0:64],
                PTB[:, 0:512].rearrange("p (a b) -> p a b", a=8)),
                waits=[("TEt", p.c("TEt"))])
    for b in range(B):
        p.add("dve", lambda e, b=b: e.tensor_scalar(
            s0_sb[:, b, :], sa_sb[:, 0, b, :], pm_sb[:, 0:1], None, ALU.mult))
        for r in range(1, NCORE):
            p.add("dve", lambda e, b=b, r=r: e.scalar_tensor_tensor(
                s0_sb[:, b, :], sa_sb[:, r, b, :], pm_sb[:, r:r + 1],
                s0_sb[:, b, :], ALU.mult, ALU.add))
    tr_done = p.now("DVEa")

    pw_tok = None
    sup_tok = None
    for b in range(B):
        for ct in range(4):
            tile = b * 4 + ct
            if ct == 0:
                sw = list(tr_done) if b == 0 else p.now("TEe")
                scp = p.add("dve", lambda e, b=b: e.tensor_copy(
                    ssb[0:64, 0:65], s0_sb[0:64, b, :]), waits=sw)
            else:
                scp = p.add("dve", lambda e, b=b: e.tensor_add(
                    ssb[0:64, 0:65], s0_sb[0:64, b, :], sacc[0:64, 0:65]),
                    waits=p.now("TEe"))
            w = p.now("DVEa")
            p.add("te", lambda e, tile=tile: e.matmul(
                PEC[0:128, 0:128], kff[:, bass.ts(tile, 128)],
                qf[:, bass.ts(tile, 128)], start=True, stop=True),
                waits=w, incs=[("TEe", 1)])
            mcp = p.add("dve", lambda e: e.tensor_tensor(
                et_sb[:], PEC[0:128, 0:128], triu_sb[:], ALU.mult),
                waits=[("TEe", p.c("TEe"))])
            p.add("te", lambda e, tile=tile: e.matmul(
                PO[0:128, 0:65], et_sb[:], vp_tmv[:, tile, 0:65],
                start=True, stop=False), waits=tok_w(mcp) + tok_w(sup_tok))
            p.add("te", lambda e, tile=tile: e.matmul(
                PO[0:128, 0:65], qf[:, bass.ts(tile, 128)], ssb[0:64, 0:65],
                start=False, stop=True), incs=[("TEe", 1)])
            p.add("dve", lambda e: e.tensor_scalar(
                rec[:, 0:1], PO[:, 64:65], 1e-6, None, ALU.add),
                waits=[("TEe", p.c("TEe"))])
            p.add("dve", lambda e: e.reciprocal(rec[:, 1:2], rec[:, 0:1]))
            ob = p.add("dve", lambda e: e.tensor_scalar(
                obf[:], PO[:, 0:64], rec[:, 1:2], None, ALU.mult))
            p.add("te", lambda e: e.transpose(
                PTB[0:64, 0:128], obf[0:128, 0:64], ident[:]),
                waits=tok_w(ob) + p.now("DVEa"), incs=[("TEt", 1)])
            ocp = p.add("act", lambda e: e.copy(ofs[0:64, 0:128], PTB[0:64, 0:128]),
                waits=[("TEt", p.c("TEt"))])
            for half in range(2):
                w = tok_w(ocp) + tok_w(pw_tok) if half == 0 else []
                p.add("te", lambda e, half=half: e.matmul(
                    PW[:, bass.ts(half, 512)], ofs[0:64, 0:128],
                    wo1[0:64, bass.ts(half, 512)], start=True, stop=True),
                    waits=w + (dl0 if (b == 0 and ct == 0 and half == 0) else []),
                    incs=([("TEe", 1)] if half == 1 else None))
            pw_tok = p.add("dve", lambda e, tile=tile: e.tensor_add(
                x_tm[:, tile, :], x_tm[:, tile, :], PW[:, :]),
                waits=[("TEe", p.c("TEe"))])
            if ct < 3:
                p.add("te", lambda e, tile=tile: e.matmul(
                    PS[0:64, 0:65], kf_tmv[:, tile, 0:64], vp_tmv[:, tile, 0:65],
                    start=True, stop=True),
                    waits=p.now("DVEa"), incs=[("TEs", 1)])
                if ct == 0:
                    p.add("dve", lambda e: e.tensor_copy(sacc[:], PS[0:64, 0:65]),
                          waits=[("TEs", p.c("TEs"))])
                else:
                    p.add("dve", lambda e: e.tensor_add(
                        sacc[:], sacc[:], PS[0:64, 0:65]),
                        waits=[("TEs", p.c("TEs"))])
            sup_tok = {"TEs": p.c("TEs")}

    PWp = [PY[:, 0:1024], PZ[:, 1024:2048]]
    wz_tok = [None, None]
    wc_tok = [None, None]
    zfree = None
    hfree = None
    pw_evac = [None, None]
    for g in range(ND):
        sg = g % 2
        mz = p.add("dve", lambda e, sg=sg: e.memset(wzg[:, sg, :, :], 0.0),
                   waits=tok_w(wz_tok[sg]) + (cscr_done if g == 0 else []))
        for rl in range(8):
            p.dma("gp", lambda e, g=g, sg=sg, rl=rl: e.dma_start(
                out=wzg[rl * 16:(rl + 1) * 16, sg, :, :].rearrange(
                    "p m f -> p (m f)")[:, (2 * rl) * 128:(2 * rl + 2) * 128]
                .rearrange("p (a b k) -> p a b k", a=2, b=2),
                in_=c_scr[:, :, :, g, rl, :]),
                waits=tok_w(mz) if rl == 0 else [])
        wzdma = p.dma_done("gp")
        for half in range(2):
            hg = g * 2 + half
            sc = hg % 2
            p.dma("gp", lambda e, g=g, half=half, sc=sc: e.dma_start(
                out=wcs[:, sc, :, :],
                in_=wc_d[:].rearrange("(gg m p) d -> p gg m d", m=8, p=128)[
                    :, 2 * g + half, :, :]),
                waits=tok_w(wc_tok[sc]) + (cscr_done + p.now("TEe", "DVEa")
                                           if (g == 0 and half == 0) else []))
            wcdma = p.dma_done("gp")
            for m in range(8):
                mi = half * 8 + m
                for n in range(4):
                    w = []
                    if n == 0:
                        w = tok_w(zfree)
                    if n == 2:
                        w = tok_w(zhalf)
                    if m == 0 and n == 0:
                        w = w + wzdma + tok_w(hfree)
                        if hg == 0:
                            w += p.now("ACTa", "DVEa")
                    p.add("te", lambda e, g=g, mi=mi, n=n: e.matmul(
                        PZ[:, bass.ts(n % 2, 512)], wzg[:, g % 2, mi, :],
                        xn_fm[:, g, bass.ts(n, 512)], start=True, stop=True),
                        waits=w, incs=[("TEz", 1)] if n in (1, 3) else None)
                    if n == 1:
                        zhalf = p.add("act", lambda e, m=m: e.activation(
                            hhg[:, m, 0:1024], PZ[:, 0:1024], AF.Gelu,
                            bias=0.0, scale=1.0), waits=[("TEz", p.c("TEz"))])
                zfree = p.add("act", lambda e, m=m: e.activation(
                    hhg[:, m, 1024:2048], PZ[:, 0:1024], AF.Gelu,
                    bias=0.0, scale=1.0), waits=[("TEz", p.c("TEz"))])
            for tt in range(NT):
                par = tt % 2
                w = []
                if tt >= 2 or g > 0 or half > 0:
                    w = w + tok_w(pw_evac[par])
                if tt == 1 and g == 0 and half == 0:
                    w = w + p.now("TEs", "TEc")
                if tt == 0:
                    w = w + wcdma + p.now("ACTa")
                    if g == 0 and half == 0:
                        w = w + p.now("DVEa")
                for m in range(8):
                    for half2 in range(2):
                        p.add("te", lambda e, tt=tt, m=m, sc=sc, par=par,
                              half2=half2:
                              e.matmul(PWp[par][:, bass.ts(half2, 512)],
                                       hhg[:, m, bass.ts(tt, 128)],
                                       wcs[:, sc, m, bass.ts(half2, 512)],
                                       start=(m == 0), stop=(m == 7)),
                              waits=(w if (m == 0 and half2 == 0) else []),
                              incs=([("TEe", 1)] if (m == 7 and half2 == 1)
                                    else None))
                pw_evac[par] = p.add("dve", lambda e, tt=tt, par=par:
                                     e.tensor_add(
                                         x_tm[:, tt, :], x_tm[:, tt, :],
                                         PWp[par][:, :]),
                                     waits=[("TEe", p.c("TEe"))])
            hfree = {"TEe": p.c("TEe")}
            wc_tok[sc] = {"TEe": p.c("TEe")}
        wz_tok[sg] = {"TEe": p.c("TEe")}
    p.dma("sy", lambda e: e.dma_start(
        out=out_d[:].rearrange("(t p) d -> p t d", p=128), in_=x_tm[:]),
        waits=p.now("DVEa"))
    p.add("sy", lambda e: e.nop(), waits=p.dma_done("sy"))
    realize(nc, p)
    return nc


def run_l3(H, r2):
    nc = build_l3()
    states_all = np.stack([np.asarray(r["states"]) for r in r2])
    in_maps = []
    for c in range(NCORE):
        pm = np.tile((np.arange(NCORE) < c).astype(np.float32), (64, 1))
        in_maps.append({
            "x1": np.asarray(r2[c]["x1"]), "xn1": np.asarray(r2[c]["xn1"]),
            "Q1": np.asarray(r2[c]["Q1"]), "K1": np.asarray(r2[c]["K1"]),
            "V1": np.asarray(r2[c]["V1"]),
            "states_all": states_all, "prefix_mask": np.ascontiguousarray(pm),
            "Wo1T": H["WoT"], "rmJ": H["rmJ"], "rupJ": H["rupJ"],
            "wcprojT": H["wcprojT"], "triu": H["triu"],
            "ident_bf": H["ident_bf"],
        })
    res = run_bass_kernel_spmd(nc, in_maps, core_ids=list(range(NCORE)))
    return res.results


def run_l1(H):
    nc = build_l1()
    in_maps = []
    for c in range(NCORE):
        in_maps.append({
            "x": H["x_shards"][c],
            "ident_bf": H["ident_bf"],
            "basis_qT": H["basis_qT"], "basis_kT": H["basis_kT"],
            "basis_vT": H["basis_vT"],
        })
    res = run_bass_kernel_spmd(nc, in_maps, core_ids=list(range(NCORE)))
    return res.results


def fused_in_map(H, c):
    m = {"x": H["x_shards"][c],
         "ident_bf": H["ident_bf"], "ident_f32": H["ident_f32"],
         "basis_oT": H["basis_oT"],
         "basis_qT": H["basis_qT"], "basis_kT": H["basis_kT"],
         "basis_vT": H["basis_vT"],
         "Wo1T": H["WoT"], "rmJ": H["rmJ"], "rupJ": H["rupJ"],
         "wcprojT": H["wcprojT"], "triu": H["triu"],
         "prefix_mask": np.ascontiguousarray(
             np.tile((np.arange(NCORE) < c).astype(np.float32), (64, 1)))}
    for nm in ("q", "k", "v", "o"):
        m[f"phase_{nm}T"] = H[f"phase_{nm}T"]
        m[f"amp_{nm}T"] = H[f"amp_{nm}T"]
    for nm in ("wf", "uf", "wi", "ui", "wc", "uc", "wo", "uo"):
        m[nm + "T"] = H[nm + "T"]
    for nm in ("bf", "bi", "bc", "bo", "in_c"):
        m[nm + "_r"] = H[nm + "_r"]
    for nm in ("Wq", "Wk", "Wv"):
        m[nm + "T"] = H[nm + "T"]
    return m


def run_fused(H):
    nc = build_fused()
    in_maps = [fused_in_map(H, c) for c in range(NCORE)]
    res = run_bass_kernel_spmd(nc, in_maps, core_ids=list(range(NCORE)))
    return res.results


def kernel(**inputs):
    H = host_prep(inputs)
    r1 = run_l1(H)
    r2 = run_l2(H, r1)
    r3 = run_l3(H, r2)
    out = np.empty((B, S, D), np.float32)
    for c in range(NCORE):
        out[:, c * POS:(c + 1) * POS, :] = np.asarray(
            r3[c]["out"]).reshape(B, POS, D)
    return out


if __name__ == "__main__":
    pass


# version 14
# speedup vs baseline: 1.2425x; 1.1361x over previous
import numpy as np
import ml_dtypes
import concourse.bass as bass
import concourse.mybir as mybir
from concourse.bass_utils import run_bass_kernel_spmd

F32 = mybir.dt.float32
BF16 = mybir.dt.bfloat16
AF = mybir.ActivationFunctionType
ALU = mybir.AluOpType
AX = mybir.AxisListType
BF = ml_dtypes.bfloat16

NCORE = 8
B, S, D = 4, 4096, 1024
POS = S // NCORE
TOK = B * POS
NT = TOK // 128
ND = D // 128
NH, HD, HARM = 16, 64, 32
TWO_PI = float(2.0 * np.pi)
MAGIC = float(1.5 * 2 ** 23)


class Plan:
    AUTO = {"dve": "DVEa", "act": "ACTa"}

    def __init__(self):
        self.steps = []
        self.cnt = {}

    def c(self, s):
        return self.cnt.get(s, 0)

    def add(self, eng, emit, waits=None, incs=None):
        incs = list(incs or [])
        auto = self.AUTO.get(eng)
        if auto is not None:
            incs.append((auto, 1))
        for s, d in incs:
            self.cnt[s] = self.c(s) + d
        self.steps.append((eng, emit, list(waits or []), incs))
        return {s: self.c(s) for s, _ in incs}

    def now(self, *sems):
        return [(s, self.c(s)) for s in sems]

    NQ = 8

    def dma(self, eng, emit, waits=None):
        i = self.cnt.get("_dmaidx_" + eng, 0)
        self.cnt["_dmaidx_" + eng] = i + 1
        sem = f"D{eng}{i % self.NQ}"
        w = list(waits or []) + [(sem, self.c(sem))]
        return self.add(eng, emit, waits=w, incs=[(sem, 16)])

    def dma_done(self, eng):
        return [(f"D{eng}{j}", self.c(f"D{eng}{j}")) for j in range(self.NQ)
                if self.c(f"D{eng}{j}") > 0]


ENGMAP = [("sy", "sync"), ("te", "tensor"), ("dve", "vector"),
          ("act", "scalar"), ("gp", "gpsimd")]


def realize(nc, plan):
    names = set(plan.cnt.keys())
    for (_, _, waits, _) in plan.steps:
        for (s, _) in waits:
            names.add(s)
    sems = {n: nc.alloc_semaphore(n) for n in sorted(names)}
    with nc.Block() as blk:
        for nm, attr in ENGMAP:
            def mk(nm=nm):
                def body(eng):
                    seen = {}
                    auto = Plan.AUTO.get(nm)
                    nauto = 0
                    for (e, emit, waits, incs) in plan.steps:
                        if e != nm:
                            continue
                        if auto is not None and nauto > 0:
                            waits = waits + [(auto, nauto)]
                        for (s, v) in waits:
                            if v > 0 and seen.get(s, -1) < v:
                                eng.wait_ge(sems[s], v)
                                seen[s] = v
                        inst = emit(eng)
                        for (s, d) in incs:
                            inst = inst.then_inc(sems[s], d)
                        if auto is not None:
                            nauto += 1
                return body
            getattr(blk, attr)(mk())
    return sems


class Arena:

    def __init__(self, nc, name, dtype, words):
        self.t = nc.alloc_sbuf_tensor(name, [128, words], dtype)
        self.off = 0
        self.marks = []

    def alloc(self, *shape):
        n = int(np.prod(shape))
        v = self.t[:, self.off:self.off + n]
        self.off += n
        if len(shape) > 1:
            pat = " ".join(f"a{i}" for i in range(len(shape)))
            v = v.rearrange(f"p ({pat}) -> p {pat}",
                            **{f"a{i}": shape[i] for i in range(len(shape))})
        return v

    def push(self):
        self.marks.append(self.off)

    def pop(self):
        self.off = self.marks.pop()


def ln_phase(p, T, n_tiles, src_tm, dst_tm, eps=1e-5):
    xc, sq, st = T["xc"], T["sq"], T["st"]
    for t in range(n_tiles):
        x_t = src_tm[:, t, :]
        p.add("dve", lambda e, x_t=x_t, st=st: e.reduce_sum(
            st[:, 0:1], x_t, axis=AX.X), waits=p.now("ACTa"))
        p.add("dve", lambda e, st=st: e.tensor_scalar(
            st[:, 1:2], st[:, 0:1], 1.0 / 1024.0, None, ALU.mult))
        p.add("dve", lambda e, x_t=x_t, xc=xc, st=st: e.tensor_scalar(
            xc, x_t, st[:, 1:2], None, ALU.subtract))
        p.add("dve", lambda e, xc=xc, sq=sq, st=st: e.scalar_tensor_tensor(
            sq, xc, 0.0, xc, ALU.add, ALU.mult, accum_out=st[:, 2:3]))
        p.add("act", lambda e, st=st, eps_ap=T["eps"]: e.activation(
            st[:, 3:4], st[:, 2:3], AF.Ln, bias=eps_ap, scale=1.0 / 1024.0),
            waits=p.now("DVEa"))
        p.add("act", lambda e, st=st: e.activation(
            st[:, 4:5], st[:, 3:4], AF.Exp, bias=0.0, scale=-0.5))
        p.add("dve", lambda e, xc=xc, st=st, d=dst_tm[:, t, :]: e.tensor_scalar(
            d, xc, st[:, 4:5], None, ALU.mult), waits=p.now("ACTa"))


def ln_t_phase(p, T, n_tiles, src_tm, dst_fm, pt_bf, ident, sem, eps=1e-5,
               first_waits=()):
    xc, sq, st, xnt = T["xc"], T["sq"], T["st"], T["xnt"]
    te_tok = 0
    for t in range(n_tiles):
        x_t = src_tm[:, t, :]
        p.add("dve", lambda e, x_t=x_t, st=st: e.reduce_sum(
            st[:, 0:1], x_t, axis=AX.X),
            waits=p.now("ACTa") + (list(first_waits) if t == 0 else []))
        p.add("dve", lambda e, st=st: e.tensor_scalar(
            st[:, 1:2], st[:, 0:1], 1.0 / 1024.0, None, ALU.mult))
        p.add("dve", lambda e, x_t=x_t, xc=xc, st=st: e.tensor_scalar(
            xc, x_t, st[:, 1:2], None, ALU.subtract))
        p.add("dve", lambda e, xc=xc, sq=sq, st=st: e.scalar_tensor_tensor(
            sq, xc, 0.0, xc, ALU.add, ALU.mult, accum_out=st[:, 2:3]))
        p.add("act", lambda e, st=st, eps_ap=T["eps"]: e.activation(
            st[:, 3:4], st[:, 2:3], AF.Ln, bias=eps_ap, scale=1.0 / 1024.0),
            waits=p.now("DVEa"))
        p.add("act", lambda e, st=st: e.activation(
            st[:, 4:5], st[:, 3:4], AF.Exp, bias=0.0, scale=-0.5))
        w = p.now("ACTa")
        if t > 0:
            w += [(sem, te_tok)]
        p.add("dve", lambda e, xc=xc, st=st, xnt=xnt: e.tensor_scalar(
            xnt, xc, st[:, 4:5], None, ALU.mult), waits=w)
        mult_tok = p.now("DVEa")
        for dt in range(ND):
            w = list(mult_tok) if dt == 0 else []
            if dt == 0:
                w += p.now("DVEa")
            p.add("te", lambda e, dt=dt, xnt=xnt: e.transpose(
                pt_bf[:, bass.ts(dt, 128)], xnt[:, bass.ts(dt, 128)], ident),
                waits=w, incs=([(sem, 1)] if dt == ND - 1 else None))
        te_tok = p.c(sem)
        p.add("dve", lambda e, t=t: e.tensor_copy(
            dst_fm[:, :, bass.ts(t, 128)],
            pt_bf[:].rearrange("p (a b) -> p a b", a=ND)),
            waits=[(sem, te_tok)])


def transpose_tm_to_fm(p, T, src_tm, dst_fm, n_tiles, pt_bf, ident, sem="TT",
                       src_ready=()):
    copy_tok = None
    for t in range(n_tiles):
        for dt in range(ND):
            last = dt == ND - 1
            w = list(src_ready) if (t == 0 and dt == 0) else []
            if dt == 0 and copy_tok is not None:
                w += list(copy_tok.items())
            p.add("te", lambda e, s=src_tm[:, t, bass.ts(dt, 128)],
                  d=pt_bf[:, bass.ts(dt, 128)], ident=ident:
                  e.transpose(d, s, ident),
                  waits=w,
                  incs=([(sem, 1)] if last else None))
        copy_tok = p.add(
            "dve", lambda e, t=t, pt_bf=pt_bf:
            e.tensor_copy(dst_fm[:, :, bass.ts(t, 128)],
                          pt_bf[:].rearrange("p (a b) -> p a b", a=ND)),
            waits=p.now(sem))


def host_prep(inputs):
    H = {}
    x = np.asarray(inputs["L0_attn_x"])
    H["x_shards"] = [np.ascontiguousarray(
        x[:, c * POS:(c + 1) * POS, :]).reshape(TOK, D) for c in range(NCORE)]
    H["x_full"] = np.ascontiguousarray(x.reshape(B * S, D))
    for nm in ("q", "k", "v", "o"):
        H[f"basis_{nm}T"] = np.ascontiguousarray(
            np.asarray(inputs[f"L0_attn_basis_{nm}"]).T)
        H[f"phase_{nm}T"] = np.ascontiguousarray(
            np.asarray(inputs[f"L0_attn_phase_{nm}"]).T)
        H[f"amp_{nm}T"] = np.ascontiguousarray(
            np.asarray(inputs[f"L0_attn_amp_{nm}"]).T)
    for nm in ("wf", "uf", "wi", "ui", "wc", "uc", "wo", "uo"):
        H[nm + "T"] = np.ascontiguousarray(np.asarray(inputs["L0_ffn_" + nm]).T)
    for nm in ("bf", "bi", "bc", "bo", "in_c"):
        H[nm + "_r"] = np.ascontiguousarray(
            np.asarray(inputs["L0_ffn_" + nm]).reshape(ND, 128).T)
    H["bo_row"] = np.ascontiguousarray(
        np.asarray(inputs["L0_ffn_bo"]).reshape(1, D))
    for nm in ("Wq", "Wk", "Wv", "Wo"):
        H[nm + "T"] = np.ascontiguousarray(np.asarray(inputs["L1_attn_" + nm]).T)
    H["rmT"] = np.ascontiguousarray(
        np.asarray(inputs["L1_ffn_rail_maker"]).transpose(0, 1, 3, 2))
    H["rupT"] = np.ascontiguousarray(
        np.asarray(inputs["L1_ffn_rail_up_proj"]).transpose(0, 1, 3, 2))
    wcT = np.asarray(inputs["L1_ffn_w_cproj"]).T
    wcT = wcT.reshape(4, 8, 8, 64, 1024).transpose(1, 2, 0, 3, 4)
    H["wcprojT"] = np.ascontiguousarray(wcT.reshape(16384, 1024))
    H["bkvT"] = np.ascontiguousarray(np.concatenate(
        [np.asarray(inputs["L0_attn_basis_k"]).T,
         np.asarray(inputs["L0_attn_basis_v"]).T], axis=1))
    H["rmJ"] = np.ascontiguousarray(
        np.asarray(inputs["L1_ffn_rail_maker"]).transpose(3, 0, 1, 2))
    H["rupJ"] = np.ascontiguousarray(
        np.asarray(inputs["L1_ffn_rail_up_proj"]).transpose(3, 0, 1, 2))
    H["ident_bf"] = np.eye(128, dtype=BF)
    H["ident_f32"] = np.eye(128, dtype=np.float32)
    H["triu"] = np.triu(np.ones((128, 128), np.float32)).astype(BF)
    return H


def build_l1():
    nc = bass.Bass("TRN2", target_bir_lowering=False)
    x_d = nc.dram_tensor("x", [TOK, D], F32, kind="ExternalInput")
    ident_d = nc.dram_tensor("ident_bf", [128, 128], BF16, kind="ExternalInput")
    basis_d = {nm: nc.dram_tensor(f"basis_{nm}T", [D, HARM], F32,
                                  kind="ExternalInput") for nm in ("q", "k", "v")}
    r_out = {nm: nc.dram_tensor(f"r{nm}", [HARM, TOK], BF16,
                                kind="ExternalOutput") for nm in ("q", "k", "v")}

    x_tm = nc.alloc_sbuf_tensor("x_tm", [128, NT, D], F32)
    xn_tm = nc.alloc_sbuf_tensor("xn_tm", [128, NT, D], BF16)
    xn_fm = nc.alloc_sbuf_tensor("xn_fm", [128, ND, TOK], BF16)
    ident = nc.alloc_sbuf_tensor("ident", [128, 128], BF16)
    bsb = {nm: nc.alloc_sbuf_tensor(f"b_{nm}", [128, ND, HARM], BF16)
           for nm in ("q", "k", "v")}
    rsb = {nm: nc.alloc_sbuf_tensor(f"r_{nm}", [HARM, TOK], BF16)
           for nm in ("q", "k", "v")}
    xc = nc.alloc_sbuf_tensor("xc", [128, D], F32)
    sq = nc.alloc_sbuf_tensor("sq", [128, D], F32)
    st = nc.alloc_sbuf_tensor("st", [128, 16], F32)
    pt_bf = nc.alloc_psum_tensor("pt_bf", [128, 1024], BF16)
    pj = nc.alloc_psum_tensor("pj", [128, 1024], F32)

    T = {"xc": xc[:], "sq": sq[:], "st": st, "eps": st[:, 7:8]}
    p = Plan()
    p.add("dve", lambda e: e.memset(st[:, 7:8], 1e-5))
    p.dma("sy", lambda e: e.dma_start(out=x_tm[:],
                                      in_=x_d[:].rearrange("(t p) d -> p t d", p=128)))
    p.dma("sy", lambda e: e.dma_start(out=ident[:], in_=ident_d[:]))
    for i, nm in enumerate(("q", "k", "v")):
        p.dma("gp", lambda e, nm=nm: e.dma_start(
            out=bsb[nm][:],
            in_=basis_d[nm][:].rearrange("(a p) b -> p a b", p=128)))
    p.add("dve", lambda e: e.memset(st[:, 6:7], 0.0),
          waits=p.dma_done("sy"))
    ln_phase(p, T, NT, x_tm, xn_tm)
    transpose_tm_to_fm(p, T, xn_tm, xn_fm, NT, pt_bf, ident[:],
                       src_ready=p.now("DVEa") + p.dma_done("sy"))
    p.add("te", lambda e: e.nop(), waits=p.dma_done("gp"))
    for i, nm in enumerate(("q", "k", "v")):
        for n in range(4):
            buf = pj[:, bass.ts(n % 2, 512)]
            for dt in range(ND):
                p.add("te", lambda e, nm=nm, n=n, dt=dt, buf=buf: e.matmul(
                    buf[0:HARM, :], bsb[nm][:, dt, :],
                    xn_fm[:, dt, bass.ts(n, 512)],
                    start=(dt == 0), stop=(dt == ND - 1)),
                    waits=(p.now("DVEa", "ACTa") if dt == 0 else None),
                    incs=([("TEP", 1)] if dt == ND - 1 else None))
            p.add("act", lambda e, nm=nm, n=n, buf=buf: e.copy(
                rsb[nm][:, bass.ts(n, 512)], buf[0:HARM, :]),
                waits=p.now("TEP"))
    for nm in ("q", "k", "v"):
        p.dma("sy", lambda e, nm=nm: e.dma_start(out=r_out[nm][:], in_=rsb[nm][:]),
              waits=p.now("ACTa"))
    p.add("sy", lambda e: e.nop(), waits=p.dma_done("sy"))
    realize(nc, p)
    return nc


def build_l2():
    nc = bass.Bass("TRN2", target_bir_lowering=False)
    x_d = nc.dram_tensor("x", [TOK, D], F32, kind="ExternalInput")
    rkvf_d = nc.dram_tensor("rkvf", [64, B * S], BF16, kind="ExternalInput")
    rqin_d = nc.dram_tensor("rqin", [HARM, TOK], BF16, kind="ExternalInput")
    ident_d = nc.dram_tensor("ident_bf", [128, 128], BF16, kind="ExternalInput")
    identf_d = nc.dram_tensor("ident_f32", [128, 128], F32, kind="ExternalInput")
    pa_d = {}
    for nm in ("q", "k", "v", "o"):
        pa_d[nm] = (nc.dram_tensor(f"phase_{nm}T", [HARM, D], F32, kind="ExternalInput"),
                    nc.dram_tensor(f"amp_{nm}T", [HARM, D], F32, kind="ExternalInput"))
    basiso_d = nc.dram_tensor("basis_oT", [D, HARM], F32, kind="ExternalInput")
    w_d = {nm: nc.dram_tensor(nm + "T", [D, D], F32, kind="ExternalInput")
           for nm in ("wf", "uf", "wi", "ui", "wc", "uc", "wo", "uo")}
    b_d = {nm: nc.dram_tensor(nm + "_r", [128, ND], F32, kind="ExternalInput")
           for nm in ("bf", "bi", "bc", "bo", "in_c")}
    borow_d = nc.dram_tensor("bo_row", [1, D], F32, kind="ExternalInput")
    wl1_d = {nm: nc.dram_tensor(nm + "T", [D, 64], F32, kind="ExternalInput")
             for nm in ("Wq", "Wk", "Wv")}
    scr_d = nc.dram_tensor("w_scr", [8, ND, 128, ND, 128], BF16)
    x1_d = nc.dram_tensor("x1", [TOK, D], F32, kind="ExternalOutput")
    xn1_d = nc.dram_tensor("xn1", [D, TOK], BF16, kind="ExternalOutput")
    q1_d = nc.dram_tensor("Q1", [64, TOK], BF16, kind="ExternalOutput")
    k1_d = nc.dram_tensor("K1", [64, TOK], BF16, kind="ExternalOutput")
    v1_d = nc.dram_tensor("V1", [64, TOK], BF16, kind="ExternalOutput")
    st_d = nc.dram_tensor("states", [B, 64, 65], F32, kind="ExternalOutput")

    x_tm = nc.alloc_sbuf_tensor("x_tm", [128, NT, D], F32)
    xn_fm = nc.alloc_sbuf_tensor("xn_fm", [128, ND, TOK], BF16)
    UD = nc.alloc_sbuf_tensor("UD", [128, 8192], F32)
    UE = nc.alloc_sbuf_tensor("UE", [128, 10240], F32)
    UF = nc.alloc_sbuf_tensor("UF", [128, 8192], F32)
    ident = nc.alloc_sbuf_tensor("ident", [128, 128], BF16)
    identf = nc.alloc_sbuf_tensor("identf", [128, 128], F32)
    wmatA = nc.alloc_sbuf_tensor("wmatA", [128, D], BF16)
    wmatB = nc.alloc_sbuf_tensor("wmatB", [32, D], BF16)
    basiso = nc.alloc_sbuf_tensor("basiso", [128, ND, HARM], BF16)
    st = nc.alloc_sbuf_tensor("st", [128, 16], F32)
    rec = nc.alloc_sbuf_tensor("rec", [128, 8], F32)
    bia = nc.alloc_sbuf_tensor("bia", [128, 5 * ND], F32)

    def carve(arena, w0, w1, dtype=None, shape=None):
        v = arena[:, w0:w1]
        if dtype is not None:
            v = v.bitcast(dtype)
        if shape is not None:
            pat = " ".join(f"a{i}" for i in range(len(shape)))
            v = v.rearrange(f"p ({pat}) -> p {pat}",
                            **{f"a{i}": shape[i] for i in range(len(shape))})
        return v

    rkv = carve(UD, 0, 8192, BF16)
    c_fm = carve(UD, 0, 4096, None, (ND, POS))
    c_bf = carve(UD, 4096, 6144, BF16, (ND, POS))
    gf = carve(UD, 6144, 6656)
    gi = carve(UD, 6656, 7168)
    gg = carve(UD, 7168, 7680)
    htmp = carve(UD, 7680, 8192)
    htmp2 = carve(UD, 0, 2048, None, (2, D))
    st64 = carve(UD, 0, 260)
    k2 = carve(UE, 0, 2048, BF16)
    v2p = carve(UE, 2048, 4352, BF16, (32, 2, 72))
    q2 = carve(UE, 4352, 4608, BF16)
    e4 = carve(UE, 4608, 5632, BF16, (4, POS))
    o_tm = carve(UE, 5632, 7680, BF16, (4, D))
    o_fm = carve(UE, 7680, 9728, BF16, (ND, POS))
    ro_sb = carve(UE, 9728, 9984, BF16)
    q1sb = carve(UE, 0, 1024, BF16)
    kfsb = carve(UE, 1024, 2048, BF16)
    v1sb = carve(UE, 2048, 3072, BF16)
    kf_tmv = carve(UE, 3072, 3584, BF16, (NT, 64))
    vp_tmv = carve(UE, 3584, 4160, BF16, (NT, 72))
    el_t = carve(UE, 4160, 4672)
    el_e = carve(UE, 4672, 5184)
    wuo = carve(UE, 0, 8192, BF16, (2, ND, D))
    borow = carve(UE, 8192, 8704, BF16)
    ones1 = carve(UE, 8704, 8768, BF16)
    xf_tm = carve(UE, 0, 4096, None, (4, D))
    xnf_t = carve(UE, 4096, 5120, BF16, (2, ND, 128))
    bq_sb = carve(UE, 5120, 5248, BF16, (ND, HARM))
    bkv_sb = carve(UE, 5248, 5504, BF16, (ND, 64))
    stA = carve(UE, 0, 4096, BF16, (ND, D))
    stB = carve(UE, 5632, 9728, BF16, (ND, ND, 128))
    xc = carve(UF, 0, 1024)
    sq = carve(UF, 1024, 2048)
    pst = carve(UF, 2048, 3072)
    wtmp = carve(UF, 3072, 4096)
    xnt = carve(UF, 4096, 4608, BF16)
    os_bf = carve(UF, 4608, 4864, BF16)
    xnt2 = carve(UF, 4864, 5376, BF16)
    wstr = carve(UF, 0, 6144, BF16, (2, 6, ND, 128))
    wl1 = carve(UF, 6144, 6912, BF16, (3, ND, 64))

    PZ = nc.alloc_psum_tensor("PZ", [128, 2048], F32)
    PY = nc.alloc_psum_tensor("PY", [128, 1536], F32)
    PTB = nc.alloc_psum_tensor("PTB", [128, 1024], BF16)
    LOG = [PZ[:, 0:512], PZ[:, 512:1024]]
    PO = PZ[:, 1024:1536]
    PG = PZ[:, 1536:2048]
    PW = PY[:, 0:1024]
    PW0 = PY[:, 0:512]
    PW1 = PY[:, 512:1024]
    PT32 = PY[:, 1024:1536]

    T = {"xc": xc, "sq": sq, "st": st, "eps": st[:, 7:8], "xnt": xnt}
    p = Plan()

    p.dma("sy", lambda e: e.dma_start(out=x_tm[:],
          in_=x_d[:].rearrange("(t p) d -> p t d", p=128)))
    p.dma("sy", lambda e: e.dma_start(out=ident[:], in_=ident_d[:]))
    p.dma("sy", lambda e: e.dma_start(out=identf[:], in_=identf_d[:]))
    ds_loads = p.dma_done("sy")
    p.dma("gp", lambda e: e.dma_start(out=basiso[:],
          in_=basiso_d[:].rearrange("(a p) b -> p a b", p=128)))
    for i, nm in enumerate(("bf", "bi", "bc", "bo", "in_c")):
        p.dma("gp", lambda e, nm=nm, i=i: e.dma_start(
            out=bia[:, bass.ts(i, ND)], in_=b_d[nm][:]))
    for i, nm in enumerate(("Wq", "Wk", "Wv")):
        p.dma("gp", lambda e, nm=nm, i=i: e.dma_start(
            out=wl1[:, i, :, :], in_=wl1_d[nm][:].rearrange(
                "(a p) b -> p a b", p=128)))
    dl_loads = p.dma_done("gp")

    wdst = {"q": wmatA[64:96, :], "k": wmatA[0:32, :],
            "v": wmatA[32:64, :], "o": wmatB[0:32, :]}
    pa_tok = None
    for nm in ("q", "k", "v", "o"):
        w = list(pa_tok.items()) if pa_tok else []
        p.dma("gp", lambda e, nm=nm: e.dma_start(out=pst[0:32, :], in_=pa_d[nm][0][:]),
              waits=w)
        p.dma("gp", lambda e, nm=nm: e.dma_start(out=wtmp[0:32, :], in_=pa_d[nm][1][:]))
        dl_now = p.dma_done("gp")
        ph, am = pst[0:32, :], wtmp[0:32, :]
        t1, t2 = xc[0:32, :], sq[0:32, :]
        p.add("dve", lambda e, ph=ph, t1=t1: e.tensor_scalar(
            t1, ph, 1.0 / TWO_PI, MAGIC, ALU.mult, ALU.add),
            waits=dl_now)
        p.add("dve", lambda e, t1=t1: e.tensor_scalar(
            t1, t1, MAGIC, None, ALU.subtract))
        p.add("dve", lambda e, ph=ph, t1=t1: e.scalar_tensor_tensor(
            t1, t1, -TWO_PI, ph, ALU.mult, ALU.add))
        p.add("act", lambda e, t1=t1, t2=t2: e.activation(
            t2, t1, AF.Sin, bias=0.0, scale=0.5), waits=p.now("DVEa"))
        p.add("dve", lambda e, t1=t1, t2=t2: e.scalar_tensor_tensor(
            t1, t2, -2.0, t2, ALU.mult, ALU.mult),
            waits=p.now("ACTa"))
        p.add("dve", lambda e, t1=t1: e.tensor_scalar(
            t1, t1, 1.0, None, ALU.add))
        pa_tok = p.add("dve", lambda e, am=am, t1=t1, dd=wdst[nm]: e.tensor_tensor(
            dd, am, t1, ALU.mult))

    WMATS = ("wf", "uf", "wi", "ui", "wc", "uc", "uo", "wo")
    stg_cp = None
    stg_out = None
    for mi, nm in enumerate(WMATS):
        w = []
        if stg_out is not None:
            w = list(stg_out)
        p.dma("gp", lambda e, nm=nm: e.dma_start(
            out=stA[:], in_=w_d[nm][:].rearrange("(a p) b -> p a b", p=128)),
            waits=(list(stg_cp.items()) if stg_cp else []))
        stg_dma = p.dma_done("gp")
        stg_cp = p.add("dve", lambda e: e.tensor_copy(
            stB[:], stA[:].rearrange("p a (o f) -> p o a f", o=ND)),
            waits=stg_dma + (stg_out or []))
        p.dma("sy", lambda e, mi=mi: e.dma_start(
            out=scr_d[mi, :, :, :, :].rearrange("o q a f -> q o a f"),
            in_=stB[:]),
            waits=list(stg_cp.items()))
        stg_out = p.dma_done("sy")
    scr_ready = stg_out

    p.add("dve", lambda e: e.memset(st[:, 7:8], 1e-5))
    rkv_ld = p.dma("sy", lambda e: e.dma_start(out=rkv[0:64, :], in_=rkvf_d[:]))
    rq_ld = p.dma("sy", lambda e: e.dma_start(
        out=rkv[64:96, 0:TOK], in_=rqin_d[:]))
    rkv_kv_done = list(rkv_ld.items()) + list(rq_ld.items())

    p.add("dve", lambda e: e.memset(st[:, 6:7], 0.0), waits=ds_loads)
    ln_t_phase(p, T, NT, x_tm, xn_fm, PTB, ident[:], sem="TT",
               first_waits=p.now("TTf"))

    def tok_w(tok):
        return list(tok.items()) if tok else []

    rq_done = []

    LOG4 = [PZ[:, bass.ts(i, 512)] for i in range(4)]
    POa = PY[:, 0:512]
    PEX = [PY[:, 512:1024], PY[:, 1024:1536]]
    PEXw = PY[:, 512:1536]
    p.add("dve", lambda e: e.memset(
        v2p[:].rearrange("p a b c -> p (a b) c")[:, :, 64:65], 1.0))

    exp_super = [None] * 4
    eslot = [None] * 4
    kq_free = None
    v_free = None
    pw_tok = None
    poA_tok = None
    poB_tok = None
    otm_tok = None
    wm_tok = pa_tok
    kexp_cp = None
    PEX1 = PY[:, 512:1024]
    POa = PY[:, 0:512]
    POb = PY[:, 1024:1536]
    pair_i = 0
    for b in range(B):
        for j in range(NH // 2):
            hA, hB = 2 * j, 2 * j + 1
            EXR = [PEX1, POa, POb]
            exg = [tok_w(pw_tok), tok_w(poA_tok), tok_w(poB_tok)]
            ecp = [None, None, None]
            ei = 0
            for n in range(ND):
                w = list(exg[ei % 3]) + tok_w(ecp[ei % 3])
                if pair_i == 0 and n == 0:
                    w += (tok_w(wm_tok) + ds_loads + rq_done + rkv_kv_done
                          + scr_ready)
                if n == 0:
                    w += tok_w(kq_free)
                p.add("te", lambda e, hA=hA, b=b, n=n, ei=ei: e.matmul(
                    EXR[ei % 3][:, :], wmatA[0:32, hA * HD:hA * HD + 128],
                    rkv[0:32, b * 4096 + n * 512: b * 4096 + (n + 1) * 512],
                    start=True, stop=True), waits=w, incs=[("TEk", 1)])
                tek = p.c("TEk")
                ecp[ei % 3] = p.add("dve", lambda e, n=n, ei=ei: e.tensor_copy(
                    k2[:, bass.ts(n, 512)], EXR[ei % 3][:, :]),
                    waits=[("TEk", tek)])
                ei += 1
            w = list(exg[ei % 3]) + tok_w(ecp[ei % 3])
            p.add("te", lambda e, hA=hA, b=b, ei=ei: e.matmul(
                EXR[ei % 3][:, :], wmatA[64:96, hA * HD:hA * HD + 128],
                rkv[64:96, b * 512:(b + 1) * 512], start=True, stop=True),
                waits=w, incs=[("TEk", 1)])
            tek = p.c("TEk")
            q_cp = p.add("dve", lambda e, ei=ei: e.tensor_copy(
                q2[:, :], EXR[ei % 3][:, :]), waits=[("TEk", tek)])
            ecp[ei % 3] = q_cp
            ei += 1
            kexp_cp = q_cp
            for g in range(8):
                w0 = list(exg[ei % 3]) + tok_w(ecp[ei % 3])
                for j4 in range(4):
                    kt = g * 4 + j4
                    w = w0 if j4 == 0 else []
                    if g == 0 and j4 == 0:
                        w = w + tok_w(v_free)
                    p.add("te", lambda e, b=b, hA=hA, kt=kt, j4=j4, ei=ei:
                          e.matmul(
                              EXR[ei % 3][:, bass.ts(j4, 128)],
                              rkv[32:64, b * 4096 + kt * 128:b * 4096 + (kt + 1) * 128],
                              wmatA[32:64, hA * HD:hA * HD + 128],
                              start=True, stop=True), waits=w, incs=[("TEk", 1)])
                tek = p.c("TEk")
                ecp[ei % 3] = p.add("dve", lambda e, g=g, ei=ei: e.tensor_copy(
                    v2p[:, bass.ts(g, 4), :, 0:64],
                    EXR[ei % 3][:].rearrange("p (a c d) -> p a c d", a=4, c=2)),
                    waits=[("TEk", tek)])
                ei += 1
            v_cp = ecp[(ei - 1) % 3]
            stg_guard = tok_w(ecp[0]) + tok_w(ecp[1]) + tok_w(ecp[2])
            pw_tok = v_cp
            def emit_av(kt):
                for st_h, base, po in (("A", 0, POa), ("B", 2, POb)):
                    hp = 0 if st_h == "A" else 1
                    sl = base + kt % 2
                    w = [("ACTa", exp_super[sl]["ACTa"])]
                    if kt == 0 and st_h == "A":
                        w += stg_guard
                    p.add("te", lambda e, hp=hp, kt=kt, po=po, sl=sl:
                          e.matmul(po[0:65, :], v2p[:, kt, hp, 0:65],
                                   e4[:, sl, :],
                                   start=(kt == 0), stop=(kt == 31)),
                          waits=w, incs=[("TEo", 1)])
                    eslot[sl] = {"TEo": p.c("TEo")}
            for kt in range(32):
                qk_tok = {}
                for st_h, base in (("A", 0), ("B", 2)):
                    hp = 0 if st_h == "A" else 1
                    sl = base + kt % 2
                    w = []
                    if exp_super[sl]:
                        w += [("ACTa", exp_super[sl]["ACTa"])]
                    if kt == 0 and st_h == "A":
                        w += tok_w(kexp_cp) + tok_w(q_cp)
                    qk_tok[st_h] = p.add(
                        "te", lambda e, hp=hp, kt=kt, sl=sl:
                        e.matmul(LOG4[sl][:, :],
                                 k2[bass.ts(hp, 64), bass.ts(kt, 128)],
                                 q2[bass.ts(hp, 64), :],
                                 start=True, stop=True),
                        waits=w, incs=[("TEl", 1)])
                for st_h, base in (("A", 0), ("B", 2)):
                    sl = base + kt % 2
                    w = [("TEl", qk_tok[st_h]["TEl"])]
                    if eslot[sl]:
                        w += [("TEo", eslot[sl]["TEo"])]
                    exp_super[sl] = p.add(
                        "act", lambda e, sl=sl: e.activation(
                            e4[:, sl, :], LOG4[sl][:, :],
                            AF.Exp, bias=0.0, scale=0.125), waits=w)
                if kt > 0:
                    emit_av(kt - 1)
            emit_av(31)
            v_free = {"TEo": p.c("TEo")}
            kq_free = {"TEl": p.c("TEl")}
            pw_tok = v_cp
            for st_h, po in (("A", POa), ("B", POb)):
                h = hA if st_h == "A" else hB
                pt = p.add("dve", lambda e, po=po: e.tensor_copy(
                    os_bf[0:65, 0:512], po[0:65, :]),
                    waits=[("TEo", p.c("TEo"))] + tok_w(otm_tok) + p.now("TEt"))
                if st_h == "A":
                    poA_tok = pt
                else:
                    poB_tok = pt
                for i in range(4):
                    w = tok_w(pt) + p.now("DVEa") if i == 0 else []
                    p.add("te", lambda e, i=i: e.transpose(
                        PTB[:, i * 256:i * 256 + 65],
                        os_bf[0:65, bass.ts(i, 128)],
                        ident[0:65, 0:65]), waits=w, incs=[("TEt", 1)])
                tet = p.c("TEt")
                for i in range(4):
                    p.add("dve", lambda e, i=i: e.reciprocal(
                        rec[:, i:i + 1], PTB[:, i * 256 + 64:i * 256 + 65]),
                        waits=([("TEt", tet)] if i == 0 else []))
                    p.add("dve", lambda e, i=i, h=h: e.tensor_scalar(
                        o_tm[:, i, bass.ts(h, HD)], PTB[:, i * 256:i * 256 + 64],
                        rec[:, i:i + 1], None, ALU.mult))
            pair_i += 1
        pg_tok = v_cp
        transpose_tm_to_fm(p, T, o_tm, o_fm, 4, PTB, ident[:], sem="TTo",
                           src_ready=p.now("DVEa"))
        otm_tok = {"TTo": p.c("TTo")}
        ofm_cp = p.now("DVEa")
        for dt in range(ND):
            w = list(ofm_cp) if dt == 0 else []
            if dt == 0:
                w += tok_w(pg_tok) + tok_w(pw_tok)
            p.add("te", lambda e, dt=dt: e.matmul(
                PEX[0][0:32, :], basiso[:, dt, :], o_fm[:, dt, :],
                start=(dt == 0), stop=(dt == ND - 1)), waits=w,
                incs=([("TEr", 1)] if dt == ND - 1 else None))
        ro_cp = p.add("act", lambda e: e.copy(ro_sb[0:32, :], PEX[0][0:32, :]),
                      waits=[("TEr", p.c("TEr"))])
        pg_tok = ro_cp
        pw_tok = ro_cp
        for qt in range(4):
            w = (tok_w(ro_cp) + tok_w(poA_tok) + tok_w(poB_tok)) if qt == 0 else []
            w += tok_w(pw_tok)
            for half in range(2):
                p.add("te", lambda e, qt=qt, half=half: e.matmul(
                    PEX[half][:, :], ro_sb[0:32, bass.ts(qt, 128)],
                    wmatB[0:32, bass.ts(half, 512)],
                    start=True, stop=True),
                    waits=(w if half == 0 else []), incs=[("TEk", 1)])
            tek = p.c("TEk")
            pw_tok = p.add("dve", lambda e, b=b, qt=qt: e.tensor_add(
                x_tm[:, b * 4 + qt, :], x_tm[:, b * 4 + qt, :], PEXw[:, :]),
                waits=[("TEk", tek)])

    exp_bank = [exp_super[3], exp_super[3]]
    pt32_tok = {"DVEa": p.c("DVEa"), "ACTa": p.c("ACTa")}
    po_tok = poB_tok

    for dt in range(ND):
        p.add("dve", lambda e, dt=dt: e.memset(c_fm[:, dt, :], 1.0),
              waits=(dl_loads + p.now("TEk", "TEl", "TEo")
                     if dt == 0 else []))
        p.add("dve", lambda e, dt=dt: e.tensor_scalar(
            c_fm[:, dt, :], c_fm[:, dt, :], bia[:, 4 * ND + dt:4 * ND + dt + 1],
            None, ALU.mult))
    cbf_tok = p.add("dve", lambda e: e.tensor_copy(c_bf[:], c_fm[:]))
    ws_tok = [None, None]
    gact = [None, None, None]
    gsb_tok = None
    GACT_F = [AF.Sigmoid, AF.Sigmoid, AF.Tanh]
    gate_w = ("wf", "wi", "wc")
    for t in range(4):
        for ot in range(ND):
            s = (t * ND + ot) % 2
            for g in range(3):
                w = []
                if g == 0:
                    w = p.now("TEf")
                    if t == 0 and ot == 0:
                        w += p.now("TEt", "ACTa", "DVEa") + scr_ready
                p.dma("gp", lambda e, g=g, ot=ot, s=s: e.dma_start(
                    out=wstr[:, s, g, :, :],
                    in_=scr_d[2 * g, ot, :, :, :]),
                    waits=w)
            for g in range(3):
                p.dma("gp", lambda e, g=g, ot=ot, s=s: e.dma_start(
                    out=wstr[:, s, 3 + g, :, :],
                    in_=scr_d[2 * g + 1, ot, :, :, :]))
            dl_now = p.dma_done("gp")
            for g in range(3):
                pgm = PZ[:, bass.ts(g, 512)]
                for dt in range(ND):
                    w = []
                    if dt == 0:
                        w = list(dl_now) + tok_w(gact[g])
                        if g == 0:
                            w += tok_w(cbf_tok)
                        if t == 0 and ot == 0:
                            w += tok_w(exp_bank[0]) + tok_w(exp_bank[1])
                    p.add("te", lambda e, g=g, dt=dt, ot=ot, t=t, s=s, pgm=pgm: e.matmul(
                        pgm, wstr[:, s, g, dt, :], xn_fm[:, dt, bass.ts(t, 512)],
                        start=(dt == 0), stop=False), waits=w)
                for dt in range(ND):
                    p.add("te", lambda e, g=g, dt=dt, ot=ot, s=s, pgm=pgm: e.matmul(
                        pgm, wstr[:, s, 3 + g, dt, :], c_bf[:, dt, :],
                        start=False, stop=(dt == ND - 1)),
                        incs=([("TEf", 1)] if dt == ND - 1 else None))
                tef = p.c("TEf")
                gact[g] = p.add("act", lambda e, g=g, ot=ot, pgm=pgm,
                                d=(gf, gi, gg)[g]: e.activation(
                    d[:], pgm, GACT_F[g], bias=bia[:, g * ND + ot:g * ND + ot + 1],
                    scale=1.0), waits=[("TEf", tef)] + (tok_w(gsb_tok) if g == 0 else []))
            ws_tok[s] = {"TEf": p.c("TEf")}
            w = tok_w(gact[0]) + tok_w(gact[1]) + tok_w(gact[2])
            p.add("dve", lambda e, ot=ot: e.tensor_mul(
                gf[:], gf[:], c_fm[:, ot, :]), waits=w)
            p.add("dve", lambda e: e.tensor_mul(gi[:], gi[:], gg[:]))
            gsb_tok = p.add("dve", lambda e, ot=ot: e.tensor_add(
                c_fm[:, ot, :], gf[:], gi[:]))
        cbf_tok = p.add("dve", lambda e: e.tensor_copy(c_bf[:], c_fm[:]))
    p.add("dve", lambda e: e.memset(ones1[0:1, 0:128], 1.0))
    p.dma("gp", lambda e: e.dma_start(out=borow[0:1, 0:1024], in_=borow_d[:]),
          waits=p.now("TEo"))
    for wi_, nm in ((0, "wo"), (1, "uo")):
        for dt in range(ND):
            p.dma("gp", lambda e, wi_=wi_, nm=nm, dt=dt: e.dma_start(
                out=wuo[:, wi_, dt, :], in_=w_d[nm][bass.ts(dt, 128), :]))
    wuo_done = p.dma_done("gp")
    hfree = [None, None]
    htanh = [None, None]
    for ti in range(NT):
        t, pt = ti // 4, ti % 4
        hb = ti % 2
        HP = PZ[:, hb * 1024:(hb + 1) * 1024]
        tsl = slice(t * 512 + pt * 128, t * 512 + (pt + 1) * 128)
        w0 = tok_w(cbf_tok) + wuo_done + tok_w(hfree[hb]) \
            + (p.now("DVEa", "ACTa") if ti < 2 else [])
        first = True
        for half in range(2):
            osl = slice(half * 512, (half + 1) * 512)
            for dt in range(ND):
                w = list(w0) if first else []
                first = False
                p.add("te", lambda e, dt=dt, tsl=tsl, osl=osl, HP=HP, half=half:
                      e.matmul(HP[:, osl], xn_fm[:, dt, tsl],
                               wuo[:, 0, dt, osl],
                               start=(dt == 0), stop=False),
                      waits=w)
            psl = slice(pt * 128, (pt + 1) * 128)
            for dt in range(ND):
                p.add("te", lambda e, dt=dt, psl=psl, osl=osl, HP=HP, half=half:
                      e.matmul(HP[:, osl], c_bf[:, dt, psl],
                               wuo[:, 1, dt, osl],
                               start=False, stop=False))
        p.add("te", lambda e, HP=HP: e.matmul(
            HP[:, 0:512], ones1[0:1, 0:128], borow[0:1, 0:512],
            start=False, stop=True))
        p.add("te", lambda e, HP=HP: e.matmul(
            HP[:, 512:1024], ones1[0:1, 0:128], borow[0:1, 512:1024],
            start=False, stop=True), incs=[("TEf", 1)])
        tef = p.c("TEf")
        ht = p.add("act", lambda e, HP=HP, hb=hb: e.activation(
            htmp2[:, hb, :], HP[:, :], AF.Tanh, bias=0.0, scale=1.0),
            waits=[("TEf", tef)] + tok_w(htanh[hb]))
        htanh[hb] = ht
        hfree[hb] = p.add("dve", lambda e, ti=ti, hb=hb: e.tensor_add(
            x_tm[:, ti, :], x_tm[:, ti, :], htmp2[:, hb, :]),
            waits=list(ht.items()))
    ln_t_phase(p, T, NT, x_tm, xn_fm, PTB, ident[:], sem="TT1",
               first_waits=p.now("TEf"))
    xn1_done = p.now("DVEa")
    p.dma("sy", lambda e: e.dma_start(
        out=x1_d[:].rearrange("(t p) d -> p t d", p=128), in_=x_tm[:]),
        waits=p.now("DVEa"))
    p.dma("sy", lambda e: e.dma_start(
        out=xn1_d[:].rearrange("(a p) n -> p a n", p=128), in_=xn_fm[:]),
        waits=xn1_done)

    l1dst = [q1sb[0:64, 0:TOK], kfsb[0:64, 0:TOK], v1sb[0:64, 0:TOK]]
    prev = None
    for i in range(3):
        for n in range(4):
            for dt in range(ND):
                w = []
                if dt == 0:
                    w = list(xn1_done) + dl_loads
                    w += tok_w(prev)
                    if i == 0 and n == 0:
                        w += tok_w(pg_tok) + tok_w(exp_bank[1])
                p.add("te", lambda e, i=i, n=n, dt=dt: e.matmul(
                    PG[0:64, :], wl1[:, i, dt, :], xn_fm[:, dt, bass.ts(n, 512)],
                    start=(dt == 0), stop=(dt == ND - 1)),
                    waits=w, incs=([("TEr", 1)] if dt == ND - 1 else None))
            ter = p.c("TEr")
            if i < 2:
                p.add("dve", lambda e: e.tensor_scalar(
                    el_t[0:64, :], PG[0:64, :], 0.0, None, ALU.min),
                    waits=[("TEr", ter)] + (p.now("TEt", "TEf", "TEo", "TEl",
                                                  "ACTa")
                                            if (i == 0 and n == 0) else []))
                p.add("act", lambda e: e.activation(
                    el_e[0:64, :], el_t[0:64, :], AF.Exp, bias=0.0, scale=1.0),
                    waits=p.now("DVEa"))
                prev = p.add("dve", lambda e, i=i, n=n: e.scalar_tensor_tensor(
                    l1dst[i][:, bass.ts(n, 512)], PG[0:64, :], 0.0, el_e[0:64, :],
                    ALU.max, ALU.add), waits=p.now("ACTa"))
            else:
                prev = p.add("act", lambda e, i=i, n=n: e.copy(
                    l1dst[i][:, bass.ts(n, 512)], PG[0:64, :]),
                    waits=[("TEr", ter)])
    qkv_done = p.now("DVEa", "ACTa")
    for nm, dd in (("Q1", q1sb[0:64, 0:TOK]), ("K1", kfsb[0:64, 0:TOK]),
                   ("V1", v1sb[0:64, 0:TOK])):
        p.dma("sy", lambda e, nm=nm, dd=dd: e.dma_start(out={
            "Q1": q1_d, "K1": k1_d, "V1": v1_d}[nm][:], in_=dd),
            waits=qkv_done)
    p.add("dve", lambda e: e.memset(vp_tmv[:, :, 64:65], 1.0))
    for src, dst, tag in ((kfsb[0:64, 0:TOK], kf_tmv, "K"),
                          (v1sb[0:64, 0:TOK], vp_tmv, "V")):
        for rnd in range(2):
            for j in range(8):
                kt = rnd * 8 + j
                w = list(qkv_done) if (rnd == 0 and j == 0) else []
                if j == 0:
                    w += p.now("DVEa")
                p.add("te", lambda e, src=src, kt=kt, j=j: e.transpose(
                    PTB[:, j * 64:(j + 1) * 64],
                    src[:, bass.ts(kt, 128)], ident[0:64, 0:64]),
                    waits=w, incs=([("TEt", 1)] if j == 7 else None))
            tet = p.c("TEt")
            p.add("dve", lambda e, dst=dst, rnd=rnd: e.tensor_copy(
                dst[:, bass.ts(rnd, 8), 0:64],
                PTB[:, 0:512].rearrange("p (a b) -> p a b", a=8)),
                waits=[("TEt", tet)])
    tr_done = p.now("DVEa")
    for b in range(B):
        for ct in range(4):
            w = tr_done + tok_w(prev) if ct == 0 else []
            p.add("te", lambda e, b=b, ct=ct: e.matmul(
                PG[0:64, 0:65], kf_tmv[:, b * 4 + ct, 0:64],
                vp_tmv[:, b * 4 + ct, 0:65],
                start=(ct == 0), stop=(ct == 3)),
                waits=w, incs=([("TEr", 1)] if ct == 3 else None))
        prev = p.add("dve", lambda e, b=b: e.tensor_copy(
            st64[0:64, b * 65:(b + 1) * 65], PG[0:64, 0:65]),
            waits=[("TEr", p.c("TEr"))])
    p.dma("sy", lambda e: e.dma_start(
        out=st_d[:].rearrange("b k c -> k b c"),
        in_=st64[0:64, :].rearrange("p (b c) -> p b c", b=B)),
        waits=p.now("DVEa"))
    p.add("sy", lambda e: e.nop(), waits=p.dma_done("sy"))
    realize(nc, p)
    return nc


def build_fused():
    nc = bass.Bass("TRN2", target_bir_lowering=False)
    x_d = nc.dram_tensor("x", [TOK, D], F32, kind="ExternalInput")
    basis_d = {nm: nc.dram_tensor(f"basis_{nm}T", [D, HARM], F32,
                                  kind="ExternalInput") for nm in ("q", "k", "v")}
    ident_d = nc.dram_tensor("ident_bf", [128, 128], BF16, kind="ExternalInput")
    identf_d = nc.dram_tensor("ident_f32", [128, 128], F32, kind="ExternalInput")
    pa_d = {}
    for nm in ("q", "k", "v", "o"):
        pa_d[nm] = (nc.dram_tensor(f"phase_{nm}T", [HARM, D], F32, kind="ExternalInput"),
                    nc.dram_tensor(f"amp_{nm}T", [HARM, D], F32, kind="ExternalInput"))
    basiso_d = nc.dram_tensor("basis_oT", [D, HARM], F32, kind="ExternalInput")
    pm_d = nc.dram_tensor("prefix_mask", [64, NCORE], F32, kind="ExternalInput")
    w_d = {nm: nc.dram_tensor(nm + "T", [D, D], F32, kind="ExternalInput")
           for nm in ("wf", "uf", "wi", "ui", "wc", "uc", "wo", "uo")}
    b_d = {nm: nc.dram_tensor(nm + "_r", [128, ND], F32, kind="ExternalInput")
           for nm in ("bf", "bi", "bc", "bo", "in_c")}
    borow_d = nc.dram_tensor("bo_row", [1, D], F32, kind="ExternalInput")
    wl1_d = {nm: nc.dram_tensor(nm + "T", [D, 64], F32, kind="ExternalInput")
             for nm in ("Wq", "Wk", "Wv")}
    wo1_d = nc.dram_tensor("Wo1T", [64, D], F32, kind="ExternalInput")
    rm_d = nc.dram_tensor("rmJ", [16, 4, 64, 16], F32, kind="ExternalInput")
    rup_d = nc.dram_tensor("rupJ", [16, 4, 64, 64], F32, kind="ExternalInput")
    wc_d = nc.dram_tensor("wcprojT", [16384, D], F32, kind="ExternalInput")
    triu_d = nc.dram_tensor("triu", [128, 128], BF16, kind="ExternalInput")
    out_d = nc.dram_tensor("out", [TOK, D], F32, kind="ExternalOutput")
    c_scr = nc.dram_tensor("c_scr", [16, 2, 2, 8, 8, 64], BF16)
    rkv_loc = nc.dram_tensor("rkv_loc", [64, TOK], BF16)
    rkv_gath = nc.dram_tensor("rkv_gath", [NCORE * 64, TOK], BF16)
    stt_loc = nc.dram_tensor("stt_loc", [64, B * 65], F32)
    stt_gath = nc.dram_tensor("stt_gath", [NCORE * 64, B * 65], F32)

    x_tm = nc.alloc_sbuf_tensor("x_tm", [128, NT, D], F32)
    xn_fm = nc.alloc_sbuf_tensor("xn_fm", [128, ND, TOK], BF16)
    UD = nc.alloc_sbuf_tensor("UD", [128, 8192], F32)
    UE = nc.alloc_sbuf_tensor("UE", [128, 10240], F32)
    UF = nc.alloc_sbuf_tensor("UF", [128, 8192], F32)
    ident = nc.alloc_sbuf_tensor("ident", [128, 128], BF16)
    identf = nc.alloc_sbuf_tensor("identf", [128, 128], F32)
    wmatA = nc.alloc_sbuf_tensor("wmatA", [128, D], BF16)
    wmatB = nc.alloc_sbuf_tensor("wmatB", [32, D], BF16)
    basiso = nc.alloc_sbuf_tensor("basiso", [128, ND, HARM], BF16)
    st = nc.alloc_sbuf_tensor("st", [128, 16], F32)
    rec = nc.alloc_sbuf_tensor("rec", [128, 8], F32)
    bia = nc.alloc_sbuf_tensor("bia", [128, 5 * ND], F32)

    def carve(arena, w0, w1, dtype=None, shape=None):
        v = arena[:, w0:w1]
        if dtype is not None:
            v = v.bitcast(dtype)
        if shape is not None:
            pat = " ".join(f"a{i}" for i in range(len(shape)))
            v = v.rearrange(f"p ({pat}) -> p {pat}",
                            **{f"a{i}": shape[i] for i in range(len(shape))})
        return v

    rkv = carve(UD, 0, 8192, BF16)
    c_fm = carve(UD, 0, 4096, None, (ND, POS))
    c_bf = carve(UD, 4096, 6144, BF16, (ND, POS))
    gf = carve(UD, 6144, 6656)
    gi = carve(UD, 6656, 7168)
    gg = carve(UD, 7168, 7680)
    htmp = carve(UD, 7680, 8192)
    htmp2 = carve(UD, 0, 2048, None, (2, D))
    st64 = carve(UD, 0, 260)
    k2 = carve(UE, 0, 2048, BF16)
    v2p = carve(UE, 2048, 4352, BF16, (32, 2, 72))
    q2 = carve(UE, 4352, 4608, BF16)
    e4 = carve(UE, 4608, 5632, BF16, (4, POS))
    o_tm = carve(UE, 5632, 7680, BF16, (4, D))
    o_fm = carve(UE, 7680, 9728, BF16, (ND, POS))
    ro_sb = carve(UE, 9728, 9984, BF16)
    q1sb = carve(UE, 0, 1024, BF16)
    kfsb = carve(UE, 1024, 2048, BF16)
    v1sb = carve(UE, 2048, 3072, BF16)
    kf_tmv = carve(UE, 3072, 3584, BF16, (NT, 64))
    vp_tmv = carve(UE, 3584, 4160, BF16, (NT, 72))
    el_t = carve(UE, 4160, 4672)
    el_e = carve(UE, 4672, 5184)
    wuo = carve(UE, 0, 8192, BF16, (2, ND, D))
    borow = carve(UE, 8192, 8704, BF16)
    ones1 = carve(UE, 8704, 8768, BF16)
    xf_tm = carve(UE, 0, 4096, None, (4, D))
    xnf_t = carve(UE, 4096, 5120, BF16, (2, ND, 128))
    bq_sb = carve(UE, 5120, 5248, BF16, (ND, HARM))
    bkv_sb = carve(UE, 5248, 5504, BF16, (ND, 64))
    stA = carve(UE, 0, 4096, BF16, (ND, D))
    stB = carve(UE, 5632, 9728, BF16, (ND, ND, 128))
    xc = carve(UF, 0, 1024)
    sq = carve(UF, 1024, 2048)
    pst = carve(UF, 2048, 3072)
    wtmp = carve(UF, 3072, 4096)
    xnt = carve(UF, 4096, 4608, BF16)
    os_bf = carve(UF, 4608, 4864, BF16)
    xnt2 = carve(UF, 4864, 5376, BF16)
    wstr = carve(UF, 0, 6144, BF16, (2, 6, ND, 128))
    wl1 = carve(UF, 6144, 6912, BF16, (3, ND, 64))
    rloc = carve(UE, 0, 1024, BF16)
    bsb = {nm: carve(UE, 1024 + 128 * i, 1152 + 128 * i, BF16, (ND, HARM))
           for i, nm in enumerate(("q", "k", "v"))}
    rm_sb = carve(UE, 7680, 9728, BF16)
    rup_sb = carve(UF, 0, 8192, BF16)
    cflat = carve(UD, 0, 8192, BF16)
    wcs = carve(UF, 0, 8192, BF16, (2, ND, D))
    hhg = carve(UD, 0, 8192, BF16, (ND, TOK))
    wzg = carve(UE, 5632, 7680, BF16, (2, 16, 128))
    sa_sb = carve(UE, 4160, 6240)
    sa4 = sa_sb.rearrange("p (r b c) -> p r b c", r=NCORE, b=B)
    wo1 = carve(UF, 0, 512, BF16)
    triu_sb = carve(UF, 512, 576, BF16)
    et_sb = carve(UF, 576, 640, BF16)
    obf = carve(UF, 640, 672, BF16)
    ofs = carve(UF, 672, 736, BF16)
    ssb = carve(UF, 736, 769, BF16)
    sacc = carve(UF, 769, 834)
    s0_sb = carve(UF, 834, 1094, None, (B, 65))
    pm_sb = carve(UF, 1094, 1102)

    PZ = nc.alloc_psum_tensor("PZ", [128, 2048], F32)
    PY = nc.alloc_psum_tensor("PY", [128, 1536], F32)
    PTB = nc.alloc_psum_tensor("PTB", [128, 1024], BF16)
    LOG = [PZ[:, 0:512], PZ[:, 512:1024]]
    PO = PZ[:, 1024:1536]
    PG = PZ[:, 1536:2048]
    PW = PY[:, 0:1024]
    PW0 = PY[:, 0:512]
    PW1 = PY[:, 512:1024]
    PT32 = PY[:, 1024:1536]

    T = {"xc": xc, "sq": sq, "st": st, "eps": st[:, 7:8], "xnt": xnt}

    def tok_wf(tok):
        return list(tok.items()) if tok else []

    p = Plan()

    p.dma("sy", lambda e: e.dma_start(out=x_tm[:],
          in_=x_d[:].rearrange("(t p) d -> p t d", p=128)))
    p.dma("sy", lambda e: e.dma_start(out=ident[:], in_=ident_d[:]))
    p.dma("sy", lambda e: e.dma_start(out=identf[:], in_=identf_d[:]))
    ds_loads = p.dma_done("sy")
    p.dma("gp", lambda e: e.dma_start(out=basiso[:],
          in_=basiso_d[:].rearrange("(a p) b -> p a b", p=128)))
    for i, nm in enumerate(("bf", "bi", "bc", "bo", "in_c")):
        p.dma("gp", lambda e, nm=nm, i=i: e.dma_start(
            out=bia[:, bass.ts(i, ND)], in_=b_d[nm][:]))
    for i, nm in enumerate(("Wq", "Wk", "Wv")):
        p.dma("gp", lambda e, nm=nm, i=i: e.dma_start(
            out=wl1[:, i, :, :], in_=wl1_d[nm][:].rearrange(
                "(a p) b -> p a b", p=128)))
    for nm in ("q", "k", "v"):
        p.dma("gp", lambda e, nm=nm: e.dma_start(
            out=bsb[nm][:],
            in_=basis_d[nm][:].rearrange("(a p) b -> p a b", p=128)))
    dl_loads = p.dma_done("gp")

    wdst = {"q": wmatA[64:96, :], "k": wmatA[0:32, :],
            "v": wmatA[32:64, :], "o": wmatB[0:32, :]}
    pa_tok = None
    for nm in ("q", "k", "v", "o"):
        w = list(pa_tok.items()) if pa_tok else []
        p.dma("gp", lambda e, nm=nm: e.dma_start(out=pst[0:32, :], in_=pa_d[nm][0][:]),
              waits=w)
        p.dma("gp", lambda e, nm=nm: e.dma_start(out=wtmp[0:32, :], in_=pa_d[nm][1][:]))
        dl_now = p.dma_done("gp")
        ph, am = pst[0:32, :], wtmp[0:32, :]
        t1, t2 = xc[0:32, :], sq[0:32, :]
        p.add("dve", lambda e, ph=ph, t1=t1: e.tensor_scalar(
            t1, ph, 1.0 / TWO_PI, MAGIC, ALU.mult, ALU.add),
            waits=dl_now)
        p.add("dve", lambda e, t1=t1: e.tensor_scalar(
            t1, t1, MAGIC, None, ALU.subtract))
        p.add("dve", lambda e, ph=ph, t1=t1: e.scalar_tensor_tensor(
            t1, t1, -TWO_PI, ph, ALU.mult, ALU.add))
        p.add("act", lambda e, t1=t1, t2=t2: e.activation(
            t2, t1, AF.Sin, bias=0.0, scale=0.5), waits=p.now("DVEa"))
        p.add("dve", lambda e, t1=t1, t2=t2: e.scalar_tensor_tensor(
            t1, t2, -2.0, t2, ALU.mult, ALU.mult),
            waits=p.now("ACTa"))
        p.add("dve", lambda e, t1=t1: e.tensor_scalar(
            t1, t1, 1.0, None, ALU.add))
        pa_tok = p.add("dve", lambda e, am=am, t1=t1, dd=wdst[nm]: e.tensor_tensor(
            dd, am, t1, ALU.mult))

    WMATS = ("wf", "uf", "wi", "ui", "wc", "uc", "uo", "wo")
    stg_cp = None
    stg_out = None
    for mi, nm in enumerate(WMATS):
        w = []
        if stg_out is not None:
            w = list(stg_out)
        p.dma("gp", lambda e, nm=nm: e.dma_start(
            out=stA[:], in_=w_d[nm][:].rearrange("(a p) b -> p a b", p=128)),
            waits=(list(stg_cp.items()) if stg_cp else []))
        stg_dma = p.dma_done("gp")
        stg_cp = p.add("dve", lambda e: e.tensor_copy(
            stB[:], stA[:].rearrange("p a (o f) -> p o a f", o=ND)),
            waits=stg_dma + (stg_out or []))
        p.dma("sy", lambda e, mi=mi: e.dma_start(
            out=scr_d[mi, :, :, :, :].rearrange("o q a f -> q o a f"),
            in_=stB[:]),
            waits=list(stg_cp.items()))
        stg_out = p.dma_done("sy")
    scr_ready = stg_out

    p.add("dve", lambda e: e.memset(st[:, 7:8], 1e-5))
    evac_tok = [None, None]
    mm_tok = [None, None]
    xn_cp = None
    act_tok = [None, None]
    for ft in range(B * S // 128):
        rb = ft % 4
        pw2 = ft % 2
        sb_ = 8 * pw2
        xnt_p = (xnt, xnt2)[pw2]
        sqr_p = (xc, sq)[pw2]
        p.dma("sy", lambda e, ft=ft, rb=rb: e.dma_start(
            out=xf_tm[:, rb, :],
            in_=xfull_d[:].rearrange("(t p) d -> p t d", p=128)[:, ft, :]),
            waits=((mm_tok[rb % 2] or []) if ft >= 4 else
                   (p.now("DVEa") if ft == 0 else [])))
        xf_dma = p.dma_done("sy")
        x_t = xf_tm[:, rb, :]
        p.add("dve", lambda e, x_t=x_t, sb_=sb_: e.reduce_sum(
            st[:, sb_ + 0:sb_ + 1], x_t, axis=AX.X),
            waits=xf_dma + (act_tok[pw2] or [])
            + (scr_ready if ft == 0 else []))
        p.add("dve", lambda e, sb_=sb_: e.tensor_scalar(
            st[:, sb_ + 1:sb_ + 2], st[:, sb_ + 0:sb_ + 1],
            1.0 / 1024.0, None, ALU.mult))
        p.add("dve", lambda e, sb_=sb_: e.tensor_scalar(
            st[:, sb_ + 5:sb_ + 6], st[:, sb_ + 0:sb_ + 1],
            -1.0 / 1024.0, None, ALU.mult))
        p.add("act", lambda e, x_t=x_t, sb_=sb_, sqr_p=sqr_p: e.activation(
            sqr_p, x_t, AF.Square, bias=st[:, sb_ + 5:sb_ + 6], scale=1.0,
            accum_out=st[:, sb_ + 2:sb_ + 3]), waits=p.now("DVEa"))
        p.add("act", lambda e, sb_=sb_: e.activation(
            st[:, sb_ + 3:sb_ + 4], st[:, sb_ + 2:sb_ + 3], AF.Ln,
            bias=T["eps"], scale=1.0 / 1024.0))
        act_tok[pw2] = list(p.add("act", lambda e, sb_=sb_: e.activation(
            st[:, sb_ + 4:sb_ + 5], st[:, sb_ + 3:sb_ + 4], AF.Exp,
            bias=0.0, scale=-0.5)).items())
        w = [("ACTa", p.c("ACTa"))]
        if ft > 1:
            w += [("TTf", p.c("TTf") - 1)]
        p.add("dve", lambda e, x_t=x_t, sb_=sb_, xnt_p=xnt_p: e.tensor_scalar(
            xnt_p, x_t, st[:, sb_ + 1:sb_ + 2], st[:, sb_ + 4:sb_ + 5],
            ALU.subtract, ALU.mult), waits=w)
        mult_tok = p.now("DVEa")
        for dt in range(ND):
            w2 = list(mult_tok) if dt == 0 else []
            if dt == 0:
                w2 += p.now("DVEa")
            p.add("te", lambda e, dt=dt, xnt_p=xnt_p: e.transpose(
                PTB[:, bass.ts(dt, 128)], xnt_p[:, bass.ts(dt, 128)], ident[:]),
                waits=w2, incs=([("TTf", 1)] if dt == ND - 1 else None))
        xb = pw2
        xn_cp = p.add("act", lambda e, xb=xb: e.copy(
            xnf_t[:, xb, :, :], PTB[:].rearrange("p (a b) -> p a b", a=ND)),
            waits=[("TTf", p.c("TTf"))] + (mm_tok[xb] or []))
        for dt in range(ND):
            w2 = []
            if dt == 0:
                w2 = list(xn_cp.items()) + (evac_tok[xb] or [])
                if ft == 0:
                    w2 += dl_loads
            p.add("te", lambda e, dt=dt, xb=xb: e.matmul(
                PW[0:64, bass.ts(xb, 512)][:, 0:128], bkv_sb[:, dt, :],
                xnf_t[:, xb, dt, :], start=(dt == 0), stop=(dt == ND - 1)),
                waits=w2, incs=([("TEx", 1)] if dt == ND - 1 else None))
        mm_tok[xb] = [("TEx", p.c("TEx"))]
        evac_tok[xb] = list(p.add("act", lambda e, ft=ft, xb=xb: e.copy(
            rkv[0:64, ft * 128:(ft + 1) * 128],
            PW[0:64, bass.ts(xb, 512)][:, 0:128]),
            waits=[("TEx", p.c("TEx"))]).items())
        rkv_kv_done = p.now("ACTa")

    p.add("dve", lambda e: e.memset(st[:, 6:7], 0.0), waits=ds_loads)
    ln_t_phase(p, T, NT, x_tm, xn_fm, PTB, ident[:], sem="TT",
               first_waits=p.now("TTf"))

    prj_prev = None
    prj_dst = {"q": rkv[64:96, 0:TOK], "k": rloc[0:32, 0:TOK],
               "v": rloc[32:64, 0:TOK]}
    for pi, pn in enumerate(("q", "k", "v")):
        for n in range(4):
            pjb = PG if (n % 2 == 0) else PO
            for dt in range(ND):
                w = []
                if dt == 0:
                    w = p.now("DVEa") + dl_loads + tok_wf(prj_prev)
                p.add("te", lambda e, pn=pn, n=n, dt=dt, pjb=pjb: e.matmul(
                    pjb[0:HARM, :], bsb[pn][:, dt, :],
                    xn_fm[:, dt, bass.ts(n, 512)],
                    start=(dt == 0), stop=(dt == ND - 1)),
                    waits=w, incs=([("TEp", 1)] if dt == ND - 1 else None))
            prj_prev = p.add("act", lambda e, pn=pn, n=n, pjb=pjb: e.copy(
                prj_dst[pn][:, bass.ts(n, 512)], pjb[0:HARM, :]),
                waits=[("TEp", p.c("TEp"))])
    p.dma("sy", lambda e: e.dma_start(out=rkv_loc[:], in_=rloc[0:64, 0:TOK]),
          waits=p.now("ACTa"))
    cc_in_done = p.dma_done("sy")
    p.add("gp", lambda e: e.collective_compute(
        "AllGather", ALU.bypass, replica_groups=[list(range(NCORE))],
        ins=[rkv_loc[:]], outs=[rkv_gath[:]]),
        waits=cc_in_done, incs=[("CC", 16)])
    p.dma("sy", lambda e: e.dma_start(
        out=rkv[0:32, :],
        in_=rkv_gath[:].rearrange("(r hh) (b pp) -> hh b r pp",
                                  hh=64, b=B)[0:32, :, :, :]),
        waits=[("CC", 16)])
    p.dma("sy", lambda e: e.dma_start(
        out=rkv[32:64, :],
        in_=rkv_gath[:].rearrange("(r hh) (b pp) -> hh b r pp",
                                  hh=64, b=B)[32:64, :, :, :]),
        waits=[("CC", 16)])
    rkv_ready = p.dma_done("sy")

    def tok_w(tok):
        return list(tok.items()) if tok else []

    rq_prev = None
    for n in range(4):
        for dt in range(ND):
            w = []
            if dt == 0:
                w = p.now("DVEa") + tok_w(rq_prev)
            p.add("te", lambda e, n=n, dt=dt: e.matmul(
                PG[0:HARM, :], bq_sb[:, dt, :], xn_fm[:, dt, bass.ts(n, 512)],
                start=(dt == 0), stop=(dt == ND - 1)),
                waits=w, incs=([("TEx", 1)] if dt == ND - 1 else None))
        rq_prev = p.add("act", lambda e, n=n: e.copy(
            rkv[64:96, bass.ts(n, 512)], PG[0:HARM, :]),
            waits=[("TEx", p.c("TEx"))])
    rq_done = list(rq_prev.items())

    p.add("dve", lambda e: e.memset(
        v2p[:].rearrange("p a b c -> p (a b) c")[:, :, 64:65], 1.0))

    exp_bank = [None, None]
    eslot = [None] * 4
    k_free = [None, None]
    q_free = [None, None]
    v_free = [None, None]
    pw_tok = None
    pg_tok = None
    po_tok = None
    pt32_tok = None
    otm_tok = None
    ofm_cp = None
    wm_tok = pa_tok
    ln_done = p.now("DVEa")
    hi = 0
    kexp_cp = None
    for b in range(B):
        for h in range(NH):
            kb = hi % 2
            for n in range(ND):
                w = []
                if hi == 0 and n == 0:
                    w += tok_w(wm_tok) + rkv_ready
                if n == 0:
                    w += tok_w(k_free[kb])
                if n >= 2 and n % 2 == 0:
                    w += tok_w(kexp_cp)
                if n == 0 and pw_tok:
                    w += tok_w(pw_tok)
                p.add("te", lambda e, h=h, b=b, n=n: e.matmul(
                    PW[0:64, bass.ts(n % 2, 512)], wmatA[0:32, bass.ts(h, HD)],
                    rkv[0:32, b * 4096 + n * 512: b * 4096 + (n + 1) * 512],
                    start=True, stop=True), waits=w, incs=[("TEk", 1)])
                if n % 2 == 1:
                    tek = p.c("TEk")
                    kexp_cp = p.add(
                        "dve", lambda e, kb=kb, n=n:
                        e.tensor_copy(k2[bass.ts(kb, 64),
                                         (n - 1) * 512:(n + 1) * 512],
                                      PW[0:64, :]),
                        waits=[("TEk", tek)])
            pw_tok = kexp_cp
            p.add("te", lambda e, h=h, b=b: e.matmul(
                PW[0:64, 0:512], wmatA[64:96, bass.ts(h, HD)],
                rkv[64:96, b * 512:(b + 1) * 512], start=True, stop=True),
                waits=tok_w(pw_tok) + tok_w(q_free[kb]), incs=[("TEk", 1)])
            tek = p.c("TEk")
            q_cp = p.add("dve", lambda e, kb=kb: e.tensor_copy(
                q2[bass.ts(kb, 64), :], PW[0:64, 0:512]), waits=[("TEk", tek)])
            pw_tok = q_cp
            for g in range(4):
                for j in range(8):
                    kt = g * 8 + j
                    w = []
                    if g == 0 and j == 0:
                        w += (tok_w(pg_tok) if pg_tok else rq_done) + tok_w(v_free[kb])
                    if j == 0 and g > 0:
                        w += tok_w(v_cp)
                    p.add("te", lambda e, b=b, h=h, kt=kt, j=j: e.matmul(
                        PG[:, bass.ts(j, 64)],
                        rkv[32:64, b * 4096 + kt * 128:b * 4096 + (kt + 1) * 128],
                        wmatA[32:64, bass.ts(h, HD)],
                        start=True, stop=True), waits=w, incs=[("TEk", 1)])
                tek = p.c("TEk")
                v_cp = p.add("dve", lambda e, kb=kb, g=g: e.tensor_copy(
                    v2[:, kb, bass.ts(g, 8), 0:64],
                    PG[:].rearrange("p (a b) -> p a b", a=8)),
                    waits=[("TEk", tek)])
            pg_tok = v_cp
            for kt in range(32):
                w = [("ACTa", exp_bank[kt % 2]["ACTa"])] if exp_bank[kt % 2] else []
                if kt == 0:
                    w += tok_w(kexp_cp) + tok_w(q_cp)
                qk = p.add("te", lambda e, kb=kb, kt=kt: e.matmul(
                    LOG[kt % 2][:, :], k2[bass.ts(kb, 64), bass.ts(kt, 128)],
                    q2[bass.ts(kb, 64), :], start=True, stop=True),
                    waits=w, incs=[("TEl", 1)])
                k_free[kb] = {"TEl": p.c("TEl")}
                q_free[kb] = {"TEl": p.c("TEl")}
                w = [("TEl", p.c("TEl"))]
                if eslot[kt % 4]:
                    w += [("TEo", eslot[kt % 4]["TEo"])]
                ex = p.add("act", lambda e, kt=kt: e.activation(
                    e4[:, kt % 4, :], LOG[kt % 2][:, :], AF.Exp,
                    bias=0.0, scale=0.125), waits=w)
                exp_bank[kt % 2] = ex
                w = [("ACTa", ex["ACTa"])]
                if kt == 0:
                    w += tok_w(po_tok) + tok_w(v_cp)
                p.add("te", lambda e, kb=kb, kt=kt: e.matmul(
                    PO[0:65, :], v2[:, kb, kt, 0:65], e4[:, kt % 4, :],
                    start=(kt == 0), stop=(kt == 31)),
                    waits=w, incs=[("TEo", 1)])
                eslot[kt % 4] = {"TEo": p.c("TEo")}
                v_free[kb] = {"TEo": p.c("TEo")}
            po_tok = p.add("dve", lambda e: e.tensor_copy(os_sb[0:65, 0:512], PO[0:65, :]),
                           waits=[("TEo", p.c("TEo"))] + tok_w(otm_tok)
                           + p.now("TEt"))
            trs = []
            for i in range(4):
                w = tok_w(po_tok) if i == 0 else []
                if i == 0 and pt32_tok:
                    w += tok_w(pt32_tok)
                p.add("te", lambda e, i=i: e.transpose(
                    PT32[:, i * 128:i * 128 + 65], os_sb[0:65, bass.ts(i, 128)],
                    identf[0:65, 0:65]), waits=w, incs=[("TEt", 1)])
            tet = p.c("TEt")
            for i in range(4):
                p.add("dve", lambda e, i=i: e.reciprocal(
                    rec[:, i:i + 1], PT32[:, i * 128 + 64:i * 128 + 65]),
                    waits=([("TEt", tet)] if i == 0 else []))
                pt32_tok = p.add("dve", lambda e, i=i, h=h: e.tensor_scalar(
                    o_tm[:, i, bass.ts(h, HD)], PT32[:, i * 128:i * 128 + 64],
                    rec[:, i:i + 1], None, ALU.mult))
            hi += 1
        transpose_tm_to_fm(p, T, o_tm, o_fm, 4, PTB, ident[:], sem="TTo",
                           src_ready=p.now("DVEa"))
        otm_tok = p.now("TTo") and {"TTo": p.c("TTo")}
        ofm_cp = p.now("DVEa")
        for dt in range(ND):
            w = list(ofm_cp) if dt == 0 else []
            if dt == 0:
                w += tok_w(pg_tok)
            p.add("te", lambda e, dt=dt: e.matmul(
                PG[0:32, :], basiso[:, dt, :], o_fm[:, dt, :],
                start=(dt == 0), stop=(dt == ND - 1)), waits=w,
                incs=([("TEr", 1)] if dt == ND - 1 else None))
        ro_cp = p.add("act", lambda e: e.copy(ro_sb[0:32, :], PG[0:32, :]),
                      waits=[("TEr", p.c("TEr"))])
        pg_tok = ro_cp
        for qt in range(4):
            w = (tok_w(ro_cp) + tok_w(poA_tok) + tok_w(poB_tok)) if qt == 0 else []
            w += tok_w(pw_tok)
            for half in range(2):
                p.add("te", lambda e, qt=qt, half=half: e.matmul(
                    PW[:, bass.ts(half, 512)], ro_sb[0:32, bass.ts(qt, 128)],
                    wmatB[0:32, bass.ts(half, 512)],
                    start=True, stop=True),
                    waits=(w if half == 0 else []), incs=[("TEk", 1)])
            tek = p.c("TEk")
            pw_tok = p.add("dve", lambda e, b=b, qt=qt: e.tensor_add(
                x_tm[:, b * 4 + qt, :], x_tm[:, b * 4 + qt, :], PW[:, :]),
                waits=[("TEk", tek)])

    for dt in range(ND):
        p.add("dve", lambda e, dt=dt: e.memset(c_fm[:, dt, :], 1.0),
              waits=(dl_loads + p.now("TEk", "TEl", "TEo")
                     if dt == 0 else []))
        p.add("dve", lambda e, dt=dt: e.tensor_scalar(
            c_fm[:, dt, :], c_fm[:, dt, :], bia[:, 4 * ND + dt:4 * ND + dt + 1],
            None, ALU.mult))
    cbf_tok = p.add("dve", lambda e: e.tensor_copy(c_bf[:], c_fm[:]))
    ws_tok = [None, None]
    gact = [None, None, None]
    gsb_tok = None
    GACT_F = [AF.Sigmoid, AF.Sigmoid, AF.Tanh]
    gate_w = ("wf", "wi", "wc")
    for t in range(4):
        for ot in range(ND):
            s = (t * ND + ot) % 2
            for g in range(3):
                w = []
                if g == 0:
                    w = p.now("TEf")
                    if t == 0 and ot == 0:
                        w += p.now("TEt", "ACTa", "DVEa") + scr_ready
                p.dma("gp", lambda e, g=g, ot=ot, s=s: e.dma_start(
                    out=wstr[:, s, g, :, :],
                    in_=scr_d[2 * g, ot, :, :, :]),
                    waits=w)
            for g in range(3):
                p.dma("gp", lambda e, g=g, ot=ot, s=s: e.dma_start(
                    out=wstr[:, s, 3 + g, :, :],
                    in_=scr_d[2 * g + 1, ot, :, :, :]))
            dl_now = p.dma_done("gp")
            for g in range(3):
                pgm = PZ[:, bass.ts(g, 512)]
                for dt in range(ND):
                    w = []
                    if dt == 0:
                        w = list(dl_now) + tok_w(gact[g])
                        if g == 0:
                            w += tok_w(cbf_tok)
                        if t == 0 and ot == 0:
                            w += tok_w(exp_bank[0]) + tok_w(exp_bank[1])
                    p.add("te", lambda e, g=g, dt=dt, ot=ot, t=t, s=s, pgm=pgm: e.matmul(
                        pgm, wstr[:, s, g, dt, :], xn_fm[:, dt, bass.ts(t, 512)],
                        start=(dt == 0), stop=False), waits=w)
                for dt in range(ND):
                    p.add("te", lambda e, g=g, dt=dt, ot=ot, s=s, pgm=pgm: e.matmul(
                        pgm, wstr[:, s, 3 + g, dt, :], c_bf[:, dt, :],
                        start=False, stop=(dt == ND - 1)),
                        incs=([("TEf", 1)] if dt == ND - 1 else None))
                tef = p.c("TEf")
                gact[g] = p.add("act", lambda e, g=g, ot=ot, pgm=pgm,
                                d=(gf, gi, gg)[g]: e.activation(
                    d[:], pgm, GACT_F[g], bias=bia[:, g * ND + ot:g * ND + ot + 1],
                    scale=1.0), waits=[("TEf", tef)] + (tok_w(gsb_tok) if g == 0 else []))
            ws_tok[s] = {"TEf": p.c("TEf")}
            w = tok_w(gact[0]) + tok_w(gact[1]) + tok_w(gact[2])
            p.add("dve", lambda e, ot=ot: e.tensor_mul(
                gf[:], gf[:], c_fm[:, ot, :]), waits=w)
            p.add("dve", lambda e: e.tensor_mul(gi[:], gi[:], gg[:]))
            gsb_tok = p.add("dve", lambda e, ot=ot: e.tensor_add(
                c_fm[:, ot, :], gf[:], gi[:]))
        cbf_tok = p.add("dve", lambda e: e.tensor_copy(c_bf[:], c_fm[:]))
    cuo_tok = [None] * ND
    for ot in range(ND):
        s = ot % 2
        p.dma("gp", lambda e, ot=ot, s=s: e.dma_start(
            out=wstr[:, s, 0, :, :], in_=scr_d[6, ot, :, :, :]),
            waits=p.now("TEf"))
        dl_now = p.dma_done("gp")
        for dt in range(ND):
            w = list(dl_now) if dt == 0 else []
            if dt == 0:
                w += tok_w(cbf_tok)
            if dt == 0 and ot >= 2:
                w += tok_w(cuo_tok[ot - 2])
            if dt == 0 and ot < 2:
                w += tok_w(gact[0]) + tok_w(gact[1]) + tok_w(gact[2])
            p.add("te", lambda e, ot=ot, dt=dt, s=s: e.matmul(
                PZ[:, bass.ts(ot % 2, 512)], wstr[:, s, 0, dt, :], c_bf[:, dt, :],
                start=(dt == 0), stop=(dt == ND - 1)),
                waits=w, incs=([("TEf", 1)] if dt == ND - 1 else None))
        ws_tok[s] = {"TEf": p.c("TEf")}
        cuo_tok[ot] = p.add("dve", lambda e, ot=ot: e.tensor_copy(
            cuo[:, ot, :], PZ[:, bass.ts(ot % 2, 512)]),
            waits=[("TEf", p.c("TEf"))])
    ht_tok = [None] * ND
    haddB = [tok_w(cuo_tok[ND - 2]), tok_w(cuo_tok[ND - 1])]
    tet_after = None
    for t in range(4):
        for ot in range(ND):
            s = ot % 2
            p.dma("gp", lambda e, ot=ot, s=s: e.dma_start(
                out=wstr[:, s, 1, :, :], in_=scr_d[7, ot, :, :, :]),
                waits=p.now("TEf"))
            dl_now = p.dma_done("gp")
            for dt in range(ND):
                w = list(dl_now) if dt == 0 else []
                if dt == 0:
                    w += list(haddB[ot % 2])
                p.add("te", lambda e, t=t, ot=ot, dt=dt, s=s: e.matmul(
                    PZ[:, bass.ts(ot % 2, 512)], wstr[:, s, 1, dt, :],
                    xn_fm[:, dt, bass.ts(t, 512)],
                    start=(dt == 0), stop=(dt == ND - 1)),
                    waits=w, incs=([("TEf", 1)] if dt == ND - 1 else None))
            ws_tok[s] = {"TEf": p.c("TEf")}
            hadd = p.add("dve", lambda e, ot=ot: e.tensor_add(
                htmp[:], cuo[:, ot, :], PZ[:, bass.ts(ot % 2, 512)]),
                waits=[("TEf", p.c("TEf"))] + tok_w(cuo_tok[ot]) + p.now("ACTa"))
            haddB[ot % 2] = tok_w(hadd)
            aw = p.now("DVEa")
            if tet_after is not None:
                aw = aw + [("TEt", tet_after)]
            ht_tok[ot] = p.add("act", lambda e, ot=ot: e.activation(
                h_fm[:, ot, :], htmp[:], AF.Tanh,
                bias=bia[:, 3 * ND + ot:3 * ND + ot + 1], scale=1.0),
                waits=aw)
        hw = []
        for g in range(ND):
            hw += tok_w(ht_tok[g])
        for pt in range(4):
            for rnd in range(2):
                for j in range(4):
                    dt = rnd * 4 + j
                    w = []
                    if pt == 0 and rnd == 0 and j == 0:
                        w += hw
                    if j == 0 and pt32_tok:
                        w += tok_w(pt32_tok)
                    p.add("te", lambda e, t=t, pt=pt, dt=dt, j=j: e.transpose(
                        PT32[:, bass.ts(j, 128)], h_fm[:, dt, bass.ts(pt, 128)],
                        identf[:]), waits=w,
                        incs=([("TEt", 1)] if j == 3 else None))
                tet = p.c("TEt")
                pt32_tok = p.add("dve", lambda e, t=t, pt=pt, rnd=rnd: e.tensor_add(
                    x_tm[:, t * 4 + pt, bass.ts(rnd, 512)],
                    x_tm[:, t * 4 + pt, bass.ts(rnd, 512)], PT32[:, 0:512]),
                    waits=[("TEt", tet)])
        tet_after = p.c("TEt")

    ln_t_phase(p, T, NT, x_tm, xn_fm, PTB, ident[:], sem="TT1",
               first_waits=p.now("TEf"))
    xn1_done = p.now("DVEa")

    l1dst = [q1sb[0:64, 0:TOK], kfsb[0:64, 0:TOK], v1sb[0:64, 0:TOK]]
    prev = None
    for i in range(3):
        for n in range(4):
            for dt in range(ND):
                w = []
                if dt == 0:
                    w = list(xn1_done) + dl_loads
                    w += tok_w(prev)
                    if i == 0 and n == 0:
                        w += tok_w(pg_tok) + tok_w(exp_bank[1])
                p.add("te", lambda e, i=i, n=n, dt=dt: e.matmul(
                    PG[0:64, :], wl1[:, i, dt, :], xn_fm[:, dt, bass.ts(n, 512)],
                    start=(dt == 0), stop=(dt == ND - 1)),
                    waits=w, incs=([("TEr", 1)] if dt == ND - 1 else None))
            ter = p.c("TEr")
            if i < 2:
                p.add("dve", lambda e: e.tensor_scalar(
                    el_t[0:64, :], PG[0:64, :], 0.0, None, ALU.min),
                    waits=[("TEr", ter)] + (p.now("TEt", "TEf", "TEo", "TEl",
                                                  "ACTa")
                                            if (i == 0 and n == 0) else []))
                p.add("act", lambda e: e.activation(
                    el_e[0:64, :], el_t[0:64, :], AF.Exp, bias=0.0, scale=1.0),
                    waits=p.now("DVEa"))
                prev = p.add("dve", lambda e, i=i, n=n: e.scalar_tensor_tensor(
                    l1dst[i][:, bass.ts(n, 512)], PG[0:64, :], 0.0, el_e[0:64, :],
                    ALU.max, ALU.add), waits=p.now("ACTa"))
            else:
                prev = p.add("act", lambda e, i=i, n=n: e.copy(
                    l1dst[i][:, bass.ts(n, 512)], PG[0:64, :]),
                    waits=[("TEr", ter)])
    qkv_done = p.now("DVEa", "ACTa")
    p.add("dve", lambda e: e.memset(vp_tmv[:, :, 64:65], 1.0))
    for src, dst, tag in ((kfsb[0:64, 0:TOK], kf_tmv, "K"),
                          (v1sb[0:64, 0:TOK], vp_tmv, "V")):
        for rnd in range(2):
            for j in range(8):
                kt = rnd * 8 + j
                w = list(qkv_done) if (rnd == 0 and j == 0) else []
                if j == 0:
                    w += p.now("DVEa")
                p.add("te", lambda e, src=src, kt=kt, j=j: e.transpose(
                    PTB[:, j * 64:(j + 1) * 64],
                    src[:, bass.ts(kt, 128)], ident[0:64, 0:64]),
                    waits=w, incs=([("TEt", 1)] if j == 7 else None))
            tet = p.c("TEt")
            p.add("dve", lambda e, dst=dst, rnd=rnd: e.tensor_copy(
                dst[:, bass.ts(rnd, 8), 0:64],
                PTB[:, 0:512].rearrange("p (a b) -> p a b", a=8)),
                waits=[("TEt", tet)])
    tr_done = p.now("DVEa")
    for b in range(B):
        for ct in range(4):
            w = tr_done + tok_w(prev) if ct == 0 else []
            p.add("te", lambda e, b=b, ct=ct: e.matmul(
                PG[0:64, 0:65], kf_tmv[:, b * 4 + ct, 0:64],
                vp_tmv[:, b * 4 + ct, 0:65],
                start=(ct == 0), stop=(ct == 3)),
                waits=w, incs=([("TEr", 1)] if ct == 3 else None))
        prev = p.add("dve", lambda e, b=b: e.tensor_copy(
            st64[0:64, b * 65:(b + 1) * 65], PG[0:64, 0:65]),
            waits=[("TEr", p.c("TEr"))])
    p.dma("sy", lambda e: e.dma_start(out=stt_loc[:], in_=st64[0:64, :]),
          waits=p.now("DVEa"))
    st_in_done = p.dma_done("sy")
    p.add("gp", lambda e: e.collective_compute(
        "AllGather", ALU.bypass, replica_groups=[list(range(NCORE))],
        ins=[stt_loc[:]], outs=[stt_gath[:]]),
        waits=st_in_done, incs=[("CC", 16)])
    p.dma("sy", lambda e: e.dma_start(
        out=sa_sb[0:64, :],
        in_=stt_gath[:].rearrange("(r kk) bc -> kk r bc", kk=64)),
        waits=[("CC", 32)] + p.now("DVEa"))
    sa_ready = p.dma_done("sy")

    PGc = PZ[:, 1536:2048]
    PECc = PZ[:, 0:512]
    POc = PZ[:, 512:1024]
    PSc = PZ[:, 1024:1536]
    p.dma("gp", lambda e: e.dma_start(
        out=rm_sb[0:16, 0:4096], in_=rm_d[:].rearrange("j h r i -> j (h r i)")),
        waits=p.now("DVEa", "ACTa", "TEr", "TEf", "TEt"))
    p.dma("gp", lambda e: e.dma_start(
        out=rup_sb[0:16, 0:16384], in_=rup_d[:].rearrange("j h r k -> j (h r k)")))
    rails_done = p.dma_done("gp")
    ccp = None
    for blk in range(32):
        for j in range(8):
            hr = blk * 8 + j
            w = []
            if blk == 0 and j == 0:
                w = list(rails_done) + p.now('DVEa', 'TEf')
            if j == 0 and ccp is not None:
                w = w + tok_w(ccp)
            p.add("te", lambda e, hr=hr, j=j: e.matmul(
                PGc[0:16, bass.ts(j, 64)], rm_sb[0:16, hr * 16:(hr + 1) * 16],
                rup_sb[0:16, hr * 64:(hr + 1) * 64], start=True, stop=True),
                waits=w, incs=([("TEc", 1)] if j == 7 else None))
        ccp = p.add("dve", lambda e, blk=blk: e.tensor_copy(
            cflat[0:16, bass.ts(blk, 512)], PGc[0:16, :]),
            waits=[("TEc", p.c("TEc"))]
            + (p.now("TEf") + st_in_done if blk == 0 else []))
    p.dma("sy", lambda e: e.dma_start(
        out=c_scr[:].rearrange("i a b g rl k -> i (a b g rl k)"),
        in_=cflat[0:16, :]),
        waits=p.now("DVEa"))
    cscr_done = p.dma_done("sy")
    p.dma("gp", lambda e: e.dma_start(out=wo1[0:64, 0:1024], in_=wo1_d[:]),
          waits=p.now("TEc"))
    p.dma("sy", lambda e: e.dma_start(out=triu_sb[0:128, 0:128], in_=triu_d[:]),
          waits=p.now("TEc"))
    p.dma("sy", lambda e: e.dma_start(out=pm_sb[0:64, 0:NCORE], in_=pm_d[:]))
    late_done = p.dma_done("sy") + p.dma_done("gp")

    for b in range(B):
        p.add("dve", lambda e, b=b: e.tensor_scalar(
            s0_sb[0:64, b, :], sa4[0:64, :, :, :][:, 0, b, :], pm_sb[0:64, 0:1],
            None, ALU.mult),
            waits=(sa_ready + late_done + p.now("TEc") if b == 0 else []))
        for r in range(1, NCORE):
            p.add("dve", lambda e, b=b, r=r: e.scalar_tensor_tensor(
                s0_sb[0:64, b, :], sa4[0:64, :, :, :][:, r, b, :],
                pm_sb[0:64, r:r + 1], s0_sb[0:64, b, :], ALU.mult, ALU.add))
    l1pfx_done = p.now("DVEa")
    sup_tok = None
    for b in range(B):
        for ct in range(4):
            tile = b * 4 + ct
            if ct == 0:
                sw = list(l1pfx_done) + list(tr_done) if b == 0 else p.now("TEe")
                scp = p.add("dve", lambda e, b=b: e.tensor_copy(
                    ssb[0:64, 0:65], s0_sb[0:64, b, :]), waits=sw)
            else:
                scp = p.add("dve", lambda e, b=b: e.tensor_add(
                    ssb[0:64, 0:65], s0_sb[0:64, b, :], sacc[0:64, 0:65]),
                    waits=p.now("TEe"))
            w = p.now("DVEa")
            p.add("te", lambda e, tile=tile: e.matmul(
                PECc[0:128, 0:128], kfsb[0:64, 0:TOK][:, bass.ts(tile, 128)],
                q1sb[0:64, 0:TOK][:, bass.ts(tile, 128)], start=True, stop=True),
                waits=w, incs=[("TEe", 1)])
            mcp = p.add("dve", lambda e: e.tensor_tensor(
                et_sb[0:128, 0:128], PECc[0:128, 0:128], triu_sb[0:128, 0:128],
                ALU.mult),
                waits=[("TEe", p.c("TEe"))])
            p.add("te", lambda e, tile=tile: e.matmul(
                POc[0:128, 0:65], et_sb[0:128, 0:128], vp_tmv[:, tile, 0:65],
                start=True, stop=False), waits=tok_w(mcp) + tok_w(sup_tok))
            p.add("te", lambda e, tile=tile: e.matmul(
                POc[0:128, 0:65], q1sb[0:64, 0:TOK][:, bass.ts(tile, 128)],
                ssb[0:64, 0:65],
                start=False, stop=True), incs=[("TEe", 1)])
            p.add("dve", lambda e: e.tensor_scalar(
                rec[:, 0:1], POc[:, 64:65], 1e-6, None, ALU.add),
                waits=[("TEe", p.c("TEe"))])
            p.add("dve", lambda e: e.reciprocal(rec[:, 1:2], rec[:, 0:1]))
            ob = p.add("dve", lambda e: e.tensor_scalar(
                obf[0:128, 0:64], POc[:, 0:64], rec[:, 1:2], None, ALU.mult))
            p.add("te", lambda e: e.transpose(
                PTB[0:64, 0:128], obf[0:128, 0:64], ident[:]),
                waits=tok_w(ob) + p.now("DVEa"), incs=[("TEt", 1)])
            ocp = p.add("act", lambda e: e.copy(ofs[0:64, 0:128], PTB[0:64, 0:128]),
                waits=[("TEt", p.c("TEt"))])
            for half in range(2):
                w = tok_w(ocp) + tok_w(pw_tok) if half == 0 else []
                p.add("te", lambda e, half=half: e.matmul(
                    PW[:, bass.ts(half, 512)], ofs[0:64, 0:128],
                    wo1[0:64, bass.ts(half, 512)], start=True, stop=True),
                    waits=w + (late_done if (b == 0 and ct == 0 and half == 0) else []),
                    incs=([("TEe", 1)] if half == 1 else None))
            pw_tok = p.add("dve", lambda e, tile=tile: e.tensor_add(
                x_tm[:, tile, :], x_tm[:, tile, :], PW[:, :]),
                waits=[("TEe", p.c("TEe"))])
            if ct < 3:
                p.add("te", lambda e, tile=tile: e.matmul(
                    PSc[0:64, 0:65], kf_tmv[:, tile, 0:64], vp_tmv[:, tile, 0:65],
                    start=True, stop=True),
                    waits=p.now("DVEa"), incs=[("TEs", 1)])
                if ct == 0:
                    p.add("dve", lambda e: e.tensor_copy(
                        sacc[0:64, 0:65], PSc[0:64, 0:65]),
                        waits=[("TEs", p.c("TEs"))])
                else:
                    p.add("dve", lambda e: e.tensor_add(
                        sacc[0:64, 0:65], sacc[0:64, 0:65], PSc[0:64, 0:65]),
                        waits=[("TEs", p.c("TEs"))])
            sup_tok = {"TEs": p.c("TEs")}

    wz_tok = [None, None]
    wc_tok = [None, None]
    gelu_toks = [None] * ND
    zfree = None
    hfree = None
    for g in range(ND):
        sg = g % 2
        mz = p.add("dve", lambda e, sg=sg: e.memset(wzg[:, sg, :, :], 0.0),
                   waits=tok_w(wz_tok[sg]) + (cscr_done if g == 0 else []))
        for rl in range(8):
            p.dma("gp", lambda e, g=g, sg=sg, rl=rl: e.dma_start(
                out=wzg[rl * 16:(rl + 1) * 16, sg, 2 * rl:2 * rl + 2, :]
                .rearrange("p a f -> p a f") if False else
                wzg[rl * 16:(rl + 1) * 16, sg, :, :].rearrange(
                    "p m f -> p (m f)")[:, (2 * rl) * 128:(2 * rl + 2) * 128]
                .rearrange("p (a b k) -> p a b k", a=2, b=2),
                in_=c_scr[:, :, :, g, rl, :]),
                waits=tok_w(mz) if rl == 0 else [])
        wzdma = p.dma_done("gp")
        for half in range(2):
            hg = g * 2 + half
            sc = hg % 2
            p.dma("gp", lambda e, g=g, half=half, sc=sc: e.dma_start(
                out=wcs[:, sc, :, :],
                in_=wc_d[:].rearrange("(gg m p) d -> p gg m d", m=8, p=128)[
                    :, 2 * g + half, :, :]),
                waits=tok_w(wc_tok[sc]) + (cscr_done + p.now("TEe", "DVEa")
                                           if (g == 0 and half == 0) else []))
            wcdma = p.dma_done("gp")
            for m in range(8):
                mi = half * 8 + m
                for n in range(4):
                    w = []
                    if n == 0:
                        w = tok_w(zfree)
                    if m == 0 and n == 0:
                        w = w + wzdma + tok_w(hfree)
                        if hg == 0:
                            w += p.now("ACTa", "DVEa") + st_in_done
                    p.add("te", lambda e, g=g, mi=mi, n=n: e.matmul(
                        PZ[:, bass.ts(n, 512)], wzg[:, g % 2, mi, :],
                        xn_fm[:, g, bass.ts(n, 512)], start=True, stop=True),
                        waits=w, incs=([("TEz", 1)] if n == 3 else None))
                zfree = p.add("act", lambda e, m=m: e.activation(
                    hhg[:, m, :], PZ[:, 0:2048], AF.Gelu, bias=0.0, scale=1.0),
                    waits=[("TEz", p.c("TEz"))])
            for tt in range(NT):
                for m in range(8):
                    w = []
                    if tt == 0 and m == 0:
                        w = wcdma + p.now("ACTa")
                    if m == 0:
                        w = w + tok_w(pw_tok)
                    for half2 in range(2):
                        p.add("te", lambda e, tt=tt, m=m, sc=sc, half2=half2:
                              e.matmul(PW[:, bass.ts(half2, 512)],
                                       hhg[:, m, bass.ts(tt, 128)],
                                       wcs[:, sc, m, bass.ts(half2, 512)],
                                       start=(m == 0), stop=(m == 7)),
                              waits=(w if half2 == 0 else []),
                              incs=([("TEe", 1)] if (m == 7 and half2 == 1)
                                    else None))
                pw_tok = p.add("dve", lambda e, tt=tt: e.tensor_add(
                    x_tm[:, tt, :], x_tm[:, tt, :], PW[:, :]),
                    waits=[("TEe", p.c("TEe"))])
            hfree = {"TEe": p.c("TEe")}
            wc_tok[sc] = {"TEe": p.c("TEe")}
        wz_tok[sg] = {"TEe": p.c("TEe")}
    p.dma("sy", lambda e: e.dma_start(
        out=out_d[:].rearrange("(t p) d -> p t d", p=128), in_=x_tm[:]),
        waits=p.now("DVEa"))
    p.add("sy", lambda e: e.nop(), waits=p.dma_done("sy"))

    realize(nc, p)
    return nc




def assemble_rkv(r1):
    def asm(nm):
        return np.stack([np.asarray(r1[c][nm]).reshape(HARM, B, POS)
                         for c in range(NCORE)], axis=2).reshape(HARM, B * S)
    return np.ascontiguousarray(np.concatenate([asm("rk"), asm("rv")], axis=0))


def run_l2(H, r1):
    nc = build_l2()
    rkvf = assemble_rkv(r1)
    in_maps = []
    for c in range(NCORE):
        m = {"x": H["x_shards"][c], "rkvf": rkvf,
             "rqin": np.asarray(r1[c]["rq"]),
             "ident_bf": H["ident_bf"], "ident_f32": H["ident_f32"],
             "basis_oT": H["basis_oT"]}
        for nm in ("q", "k", "v", "o"):
            m[f"phase_{nm}T"] = H[f"phase_{nm}T"]
            m[f"amp_{nm}T"] = H[f"amp_{nm}T"]
        for nm in ("wf", "uf", "wi", "ui", "wc", "uc", "wo", "uo"):
            m[nm + "T"] = H[nm + "T"]
        for nm in ("bf", "bi", "bc", "bo", "in_c"):
            m[nm + "_r"] = H[nm + "_r"]
        m["bo_row"] = H["bo_row"]
        for nm in ("Wq", "Wk", "Wv"):
            m[nm + "T"] = H[nm + "T"]
        in_maps.append(m)
    res = run_bass_kernel_spmd(nc, in_maps, core_ids=list(range(NCORE)))
    return res.results


def build_l3():
    nc = bass.Bass("TRN2", target_bir_lowering=False)
    x1_d = nc.dram_tensor("x1", [TOK, D], F32, kind="ExternalInput")
    xn1_d = nc.dram_tensor("xn1", [D, TOK], BF16, kind="ExternalInput")
    q1_d = nc.dram_tensor("Q1", [64, TOK], BF16, kind="ExternalInput")
    k1_d = nc.dram_tensor("K1", [64, TOK], BF16, kind="ExternalInput")
    v1_d = nc.dram_tensor("V1", [64, TOK], BF16, kind="ExternalInput")
    sa_d = nc.dram_tensor("states_all", [NCORE, B, 64, 65], F32, kind="ExternalInput")
    pm_d = nc.dram_tensor("prefix_mask", [64, NCORE], F32, kind="ExternalInput")
    wo1_d = nc.dram_tensor("Wo1T", [64, D], F32, kind="ExternalInput")
    rm_d = nc.dram_tensor("rmJ", [16, 4, 64, 16], F32, kind="ExternalInput")
    rup_d = nc.dram_tensor("rupJ", [16, 4, 64, 64], F32, kind="ExternalInput")
    wc_d = nc.dram_tensor("wcprojT", [16384, D], F32, kind="ExternalInput")
    triu_d = nc.dram_tensor("triu", [128, 128], BF16, kind="ExternalInput")
    ident_d = nc.dram_tensor("ident_bf", [128, 128], BF16, kind="ExternalInput")
    out_d = nc.dram_tensor("out", [TOK, D], F32, kind="ExternalOutput")
    c_scr = nc.dram_tensor("c_scr", [16, 2, 2, 8, 8, 64], BF16)

    x_tm = nc.alloc_sbuf_tensor("x_tm", [128, NT, D], F32)
    xn_fm = nc.alloc_sbuf_tensor("xn_fm", [128, ND, TOK], BF16)
    UC = nc.alloc_sbuf_tensor("UC", [128, 18432], F32)

    def carve(arena, w0, w1, dtype=None, shape=None):
        v = arena[:, w0:w1]
        if dtype is not None:
            v = v.bitcast(dtype)
        if shape is not None:
            pat = " ".join(f"a{i}" for i in range(len(shape)))
            v = v.rearrange(f"p ({pat}) -> p {pat}",
                            **{f"a{i}": shape[i] for i in range(len(shape))})
        return v

    wcs = carve(UC, 0, 8192, BF16, (2, ND, D))
    hhg = carve(UC, 8192, 16384, BF16, (ND, TOK))
    wzg = carve(UC, 16384, 18432, BF16, (2, 16, 128))
    qf = nc.alloc_sbuf_tensor("qf", [64, TOK], BF16)
    kff = nc.alloc_sbuf_tensor("kff", [64, TOK], BF16)
    vf = nc.alloc_sbuf_tensor("vf", [64, TOK], BF16)
    kf_tmv = nc.alloc_sbuf_tensor("kf_tmv", [128, NT, 64], BF16)
    vp_tmv = nc.alloc_sbuf_tensor("vp_tmv", [128, NT, 72], BF16)
    sa_sb = nc.alloc_sbuf_tensor("sa_sb", [64, NCORE, B, 65], F32)
    s0_sb = nc.alloc_sbuf_tensor("s0_sb", [64, B, 65], F32)
    pm_sb = nc.alloc_sbuf_tensor("pm_sb", [64, NCORE], F32)
    ssb = nc.alloc_sbuf_tensor("ssb", [64, 65], BF16)
    sacc = nc.alloc_sbuf_tensor("sacc", [64, 65], F32)
    et_sb = nc.alloc_sbuf_tensor("et_sb", [128, 128], BF16)
    obf = nc.alloc_sbuf_tensor("obf", [128, 64], BF16)
    ofs = nc.alloc_sbuf_tensor("ofs", [64, 128], BF16)
    wo1 = nc.alloc_sbuf_tensor("wo1", [64, D], BF16)
    rm_sb = carve(UC, 0, 2048, BF16)
    rup_sb = carve(UC, 2048, 10240, BF16)
    cflat = carve(UC, 10240, 18432, BF16)
    triu_sb = nc.alloc_sbuf_tensor("triu_sb", [128, 128], BF16)
    ident = nc.alloc_sbuf_tensor("ident", [128, 128], BF16)
    rec = nc.alloc_sbuf_tensor("rec", [128, 8], F32)

    PZ = nc.alloc_psum_tensor("PZ", [128, 2048], F32)
    PY = nc.alloc_psum_tensor("PY", [128, 1536], F32)
    PTB = nc.alloc_psum_tensor("PTB", [128, 1024], BF16)
    PW = PY[:, 0:1024]
    PEC = PZ[:, 0:512]
    PO = PZ[:, 512:1024]
    PS = PZ[:, 1024:1536]
    PC = PZ[:, 1536:2048]

    def tok_w(tok):
        return list(tok.items()) if tok else []

    p = Plan()
    p.dma("sy", lambda e: e.dma_start(out=x_tm[:],
          in_=x1_d[:].rearrange("(t p) d -> p t d", p=128)))
    p.dma("sy", lambda e: e.dma_start(out=xn_fm[:],
          in_=xn1_d[:].rearrange("(a p) n -> p a n", p=128)))
    p.dma("sy", lambda e: e.dma_start(out=qf[:], in_=q1_d[:]))
    p.dma("sy", lambda e: e.dma_start(out=kff[:], in_=k1_d[:]))
    p.dma("sy", lambda e: e.dma_start(out=vf[:], in_=v1_d[:]))
    p.dma("sy", lambda e: e.dma_start(out=triu_sb[:], in_=triu_d[:]))
    p.dma("sy", lambda e: e.dma_start(out=ident[:], in_=ident_d[:]))
    p.dma("sy", lambda e: e.dma_start(out=sa_sb[:],
          in_=sa_d[:].rearrange("r b k c -> k r b c")))
    p.dma("sy", lambda e: e.dma_start(out=pm_sb[:], in_=pm_d[:]))
    ds0 = p.dma_done("sy")
    p.dma("gp", lambda e: e.dma_start(out=wo1[:], in_=wo1_d[:]))
    p.dma("gp", lambda e: e.dma_start(
        out=rm_sb[0:16, 0:4096], in_=rm_d[:].rearrange("j h r i -> j (h r i)")))
    p.dma("gp", lambda e: e.dma_start(
        out=rup_sb[0:16, :], in_=rup_d[:].rearrange("j h r k -> j (h r k)")))
    dl0 = p.dma_done("gp")

    ccp = None
    for blk in range(32):
        for j in range(8):
            hr = blk * 8 + j
            w = []
            if blk == 0 and j == 0:
                w = dl0
            if j == 0 and ccp is not None:
                w = w + tok_w(ccp)
            p.add("te", lambda e, hr=hr, j=j: e.matmul(
                PC[0:16, bass.ts(j, 64)], rm_sb[0:16, hr * 16:(hr + 1) * 16],
                rup_sb[0:16, hr * 64:(hr + 1) * 64], start=True, stop=True),
                waits=w, incs=([("TEc", 1)] if j == 7 else None))
        ccp = p.add("dve", lambda e, blk=blk: e.tensor_copy(
            cflat[0:16, bass.ts(blk, 512)], PC[0:16, :]),
            waits=[("TEc", p.c("TEc"))])
    p.dma("sy", lambda e: e.dma_start(
        out=c_scr[:].rearrange("i a b g rl k -> i (a b g rl k)"),
        in_=cflat[0:16, :]),
        waits=p.now("DVEa"))
    cscr_done = p.dma_done("sy")

    p.add("dve", lambda e: e.memset(vp_tmv[:, :, 64:65], 1.0), waits=ds0)
    for srcv, dst in ((kff, kf_tmv), (vf, vp_tmv)):
        for rnd in range(2):
            for j in range(8):
                kt = rnd * 8 + j
                w = p.now("DVEa") if j == 0 else []
                p.add("te", lambda e, srcv=srcv, kt=kt, j=j: e.transpose(
                    PTB[:, j * 64:(j + 1) * 64], srcv[:, bass.ts(kt, 128)],
                    ident[0:64, 0:64]),
                    waits=w, incs=([("TEt", 1)] if j == 7 else None))
            p.add("dve", lambda e, dst=dst, rnd=rnd: e.tensor_copy(
                dst[:, bass.ts(rnd, 8), 0:64],
                PTB[:, 0:512].rearrange("p (a b) -> p a b", a=8)),
                waits=[("TEt", p.c("TEt"))])
    for b in range(B):
        p.add("dve", lambda e, b=b: e.tensor_scalar(
            s0_sb[:, b, :], sa_sb[:, 0, b, :], pm_sb[:, 0:1], None, ALU.mult))
        for r in range(1, NCORE):
            p.add("dve", lambda e, b=b, r=r: e.scalar_tensor_tensor(
                s0_sb[:, b, :], sa_sb[:, r, b, :], pm_sb[:, r:r + 1],
                s0_sb[:, b, :], ALU.mult, ALU.add))
    tr_done = p.now("DVEa")

    pw_tok = None
    sup_tok = None
    for b in range(B):
        for ct in range(4):
            tile = b * 4 + ct
            if ct == 0:
                sw = list(tr_done) if b == 0 else p.now("TEe")
                scp = p.add("dve", lambda e, b=b: e.tensor_copy(
                    ssb[0:64, 0:65], s0_sb[0:64, b, :]), waits=sw)
            else:
                scp = p.add("dve", lambda e, b=b: e.tensor_add(
                    ssb[0:64, 0:65], s0_sb[0:64, b, :], sacc[0:64, 0:65]),
                    waits=p.now("TEe"))
            w = p.now("DVEa")
            p.add("te", lambda e, tile=tile: e.matmul(
                PEC[0:128, 0:128], kff[:, bass.ts(tile, 128)],
                qf[:, bass.ts(tile, 128)], start=True, stop=True),
                waits=w, incs=[("TEe", 1)])
            mcp = p.add("dve", lambda e: e.tensor_tensor(
                et_sb[:], PEC[0:128, 0:128], triu_sb[:], ALU.mult),
                waits=[("TEe", p.c("TEe"))])
            p.add("te", lambda e, tile=tile: e.matmul(
                PO[0:128, 0:65], et_sb[:], vp_tmv[:, tile, 0:65],
                start=True, stop=False), waits=tok_w(mcp) + tok_w(sup_tok))
            p.add("te", lambda e, tile=tile: e.matmul(
                PO[0:128, 0:65], qf[:, bass.ts(tile, 128)], ssb[0:64, 0:65],
                start=False, stop=True), incs=[("TEe", 1)])
            p.add("dve", lambda e: e.tensor_scalar(
                rec[:, 0:1], PO[:, 64:65], 1e-6, None, ALU.add),
                waits=[("TEe", p.c("TEe"))])
            p.add("dve", lambda e: e.reciprocal(rec[:, 1:2], rec[:, 0:1]))
            ob = p.add("dve", lambda e: e.tensor_scalar(
                obf[:], PO[:, 0:64], rec[:, 1:2], None, ALU.mult))
            p.add("te", lambda e: e.transpose(
                PTB[0:64, 0:128], obf[0:128, 0:64], ident[:]),
                waits=tok_w(ob) + p.now("DVEa"), incs=[("TEt", 1)])
            ocp = p.add("act", lambda e: e.copy(ofs[0:64, 0:128], PTB[0:64, 0:128]),
                waits=[("TEt", p.c("TEt"))])
            for half in range(2):
                w = tok_w(ocp) + tok_w(pw_tok) if half == 0 else []
                p.add("te", lambda e, half=half: e.matmul(
                    PW[:, bass.ts(half, 512)], ofs[0:64, 0:128],
                    wo1[0:64, bass.ts(half, 512)], start=True, stop=True),
                    waits=w + (dl0 if (b == 0 and ct == 0 and half == 0) else []),
                    incs=([("TEe", 1)] if half == 1 else None))
            pw_tok = p.add("dve", lambda e, tile=tile: e.tensor_add(
                x_tm[:, tile, :], x_tm[:, tile, :], PW[:, :]),
                waits=[("TEe", p.c("TEe"))])
            if ct < 3:
                p.add("te", lambda e, tile=tile: e.matmul(
                    PS[0:64, 0:65], kf_tmv[:, tile, 0:64], vp_tmv[:, tile, 0:65],
                    start=True, stop=True),
                    waits=p.now("DVEa"), incs=[("TEs", 1)])
                if ct == 0:
                    p.add("dve", lambda e: e.tensor_copy(sacc[:], PS[0:64, 0:65]),
                          waits=[("TEs", p.c("TEs"))])
                else:
                    p.add("dve", lambda e: e.tensor_add(
                        sacc[:], sacc[:], PS[0:64, 0:65]),
                        waits=[("TEs", p.c("TEs"))])
            sup_tok = {"TEs": p.c("TEs")}

    PWp = [PY[:, 0:1024], PZ[:, 1024:2048]]
    wz_tok = [None, None]
    wc_tok = [None, None]
    zfree = None
    hfree = None
    pw_evac = [None, None]
    for g in range(ND):
        sg = g % 2
        mz = p.add("dve", lambda e, sg=sg: e.memset(wzg[:, sg, :, :], 0.0),
                   waits=tok_w(wz_tok[sg]) + (cscr_done if g == 0 else []))
        for rl in range(8):
            p.dma("gp", lambda e, g=g, sg=sg, rl=rl: e.dma_start(
                out=wzg[rl * 16:(rl + 1) * 16, sg, :, :].rearrange(
                    "p m f -> p (m f)")[:, (2 * rl) * 128:(2 * rl + 2) * 128]
                .rearrange("p (a b k) -> p a b k", a=2, b=2),
                in_=c_scr[:, :, :, g, rl, :]),
                waits=tok_w(mz) if rl == 0 else [])
        wzdma = p.dma_done("gp")
        for half in range(2):
            hg = g * 2 + half
            sc = hg % 2
            p.dma("gp", lambda e, g=g, half=half, sc=sc: e.dma_start(
                out=wcs[:, sc, :, :],
                in_=wc_d[:].rearrange("(gg m p) d -> p gg m d", m=8, p=128)[
                    :, 2 * g + half, :, :]),
                waits=tok_w(wc_tok[sc]) + (cscr_done + p.now("TEe", "DVEa")
                                           if (g == 0 and half == 0) else []))
            wcdma = p.dma_done("gp")
            for m in range(8):
                mi = half * 8 + m
                for n in range(4):
                    w = []
                    if n == 0:
                        w = tok_w(zfree)
                    if n == 2:
                        w = tok_w(zhalf)
                    if m == 0 and n == 0:
                        w = w + wzdma + tok_w(hfree)
                        if hg == 0:
                            w += p.now("ACTa", "DVEa")
                    p.add("te", lambda e, g=g, mi=mi, n=n: e.matmul(
                        PZ[:, bass.ts(n % 2, 512)], wzg[:, g % 2, mi, :],
                        xn_fm[:, g, bass.ts(n, 512)], start=True, stop=True),
                        waits=w, incs=[("TEz", 1)] if n in (1, 3) else None)
                    if n == 1:
                        zhalf = p.add("act", lambda e, m=m: e.activation(
                            hhg[:, m, 0:1024], PZ[:, 0:1024], AF.Gelu,
                            bias=0.0, scale=1.0), waits=[("TEz", p.c("TEz"))])
                zfree = p.add("act", lambda e, m=m: e.activation(
                    hhg[:, m, 1024:2048], PZ[:, 0:1024], AF.Gelu,
                    bias=0.0, scale=1.0), waits=[("TEz", p.c("TEz"))])
            for tt in range(NT):
                par = tt % 2
                w = []
                if tt >= 2 or g > 0 or half > 0:
                    w = w + tok_w(pw_evac[par])
                if tt == 1 and g == 0 and half == 0:
                    w = w + p.now("TEs", "TEc")
                if tt == 0:
                    w = w + wcdma + p.now("ACTa")
                    if g == 0 and half == 0:
                        w = w + p.now("DVEa")
                for m in range(8):
                    for half2 in range(2):
                        p.add("te", lambda e, tt=tt, m=m, sc=sc, par=par,
                              half2=half2:
                              e.matmul(PWp[par][:, bass.ts(half2, 512)],
                                       hhg[:, m, bass.ts(tt, 128)],
                                       wcs[:, sc, m, bass.ts(half2, 512)],
                                       start=(m == 0), stop=(m == 7)),
                              waits=(w if (m == 0 and half2 == 0) else []),
                              incs=([("TEe", 1)] if (m == 7 and half2 == 1)
                                    else None))
                pw_evac[par] = p.add("dve", lambda e, tt=tt, par=par:
                                     e.tensor_add(
                                         x_tm[:, tt, :], x_tm[:, tt, :],
                                         PWp[par][:, :]),
                                     waits=[("TEe", p.c("TEe"))])
            hfree = {"TEe": p.c("TEe")}
            wc_tok[sc] = {"TEe": p.c("TEe")}
        wz_tok[sg] = {"TEe": p.c("TEe")}
    p.dma("sy", lambda e: e.dma_start(
        out=out_d[:].rearrange("(t p) d -> p t d", p=128), in_=x_tm[:]),
        waits=p.now("DVEa"))
    p.add("sy", lambda e: e.nop(), waits=p.dma_done("sy"))
    realize(nc, p)
    return nc


def run_l3(H, r2):
    nc = build_l3()
    states_all = np.stack([np.asarray(r["states"]) for r in r2])
    in_maps = []
    for c in range(NCORE):
        pm = np.tile((np.arange(NCORE) < c).astype(np.float32), (64, 1))
        in_maps.append({
            "x1": np.asarray(r2[c]["x1"]), "xn1": np.asarray(r2[c]["xn1"]),
            "Q1": np.asarray(r2[c]["Q1"]), "K1": np.asarray(r2[c]["K1"]),
            "V1": np.asarray(r2[c]["V1"]),
            "states_all": states_all, "prefix_mask": np.ascontiguousarray(pm),
            "Wo1T": H["WoT"], "rmJ": H["rmJ"], "rupJ": H["rupJ"],
            "wcprojT": H["wcprojT"], "triu": H["triu"],
            "ident_bf": H["ident_bf"],
        })
    res = run_bass_kernel_spmd(nc, in_maps, core_ids=list(range(NCORE)))
    return res.results


def run_l1(H):
    nc = build_l1()
    in_maps = []
    for c in range(NCORE):
        in_maps.append({
            "x": H["x_shards"][c],
            "ident_bf": H["ident_bf"],
            "basis_qT": H["basis_qT"], "basis_kT": H["basis_kT"],
            "basis_vT": H["basis_vT"],
        })
    res = run_bass_kernel_spmd(nc, in_maps, core_ids=list(range(NCORE)))
    return res.results


def fused_in_map(H, c):
    m = {"x": H["x_shards"][c],
         "ident_bf": H["ident_bf"], "ident_f32": H["ident_f32"],
         "basis_oT": H["basis_oT"],
         "basis_qT": H["basis_qT"], "basis_kT": H["basis_kT"],
         "basis_vT": H["basis_vT"],
         "Wo1T": H["WoT"], "rmJ": H["rmJ"], "rupJ": H["rupJ"],
         "wcprojT": H["wcprojT"], "triu": H["triu"],
         "prefix_mask": np.ascontiguousarray(
             np.tile((np.arange(NCORE) < c).astype(np.float32), (64, 1)))}
    for nm in ("q", "k", "v", "o"):
        m[f"phase_{nm}T"] = H[f"phase_{nm}T"]
        m[f"amp_{nm}T"] = H[f"amp_{nm}T"]
    for nm in ("wf", "uf", "wi", "ui", "wc", "uc", "wo", "uo"):
        m[nm + "T"] = H[nm + "T"]
    for nm in ("bf", "bi", "bc", "bo", "in_c"):
        m[nm + "_r"] = H[nm + "_r"]
    for nm in ("Wq", "Wk", "Wv"):
        m[nm + "T"] = H[nm + "T"]
    return m


def run_fused(H):
    nc = build_fused()
    in_maps = [fused_in_map(H, c) for c in range(NCORE)]
    res = run_bass_kernel_spmd(nc, in_maps, core_ids=list(range(NCORE)))
    return res.results


def kernel(**inputs):
    H = host_prep(inputs)
    r1 = run_l1(H)
    r2 = run_l2(H, r1)
    r3 = run_l3(H, r2)
    out = np.empty((B, S, D), np.float32)
    for c in range(NCORE):
        out[:, c * POS:(c + 1) * POS, :] = np.asarray(
            r3[c]["out"]).reshape(B, POS, D)
    return out


if __name__ == "__main__":
    pass


# version 15
# speedup vs baseline: 1.2509x; 1.0067x over previous
import numpy as np
import ml_dtypes
import concourse.bass as bass
import concourse.mybir as mybir
from concourse.bass_utils import run_bass_kernel_spmd

F32 = mybir.dt.float32
BF16 = mybir.dt.bfloat16
AF = mybir.ActivationFunctionType
ALU = mybir.AluOpType
AX = mybir.AxisListType
BF = ml_dtypes.bfloat16

NCORE = 8
B, S, D = 4, 4096, 1024
POS = S // NCORE
TOK = B * POS
NT = TOK // 128
ND = D // 128
NH, HD, HARM = 16, 64, 32
TWO_PI = float(2.0 * np.pi)
MAGIC = float(1.5 * 2 ** 23)


class Plan:
    AUTO = {"dve": "DVEa", "act": "ACTa"}

    def __init__(self):
        self.steps = []
        self.cnt = {}

    def c(self, s):
        return self.cnt.get(s, 0)

    def add(self, eng, emit, waits=None, incs=None):
        incs = list(incs or [])
        auto = self.AUTO.get(eng)
        if auto is not None:
            incs.append((auto, 1))
        for s, d in incs:
            self.cnt[s] = self.c(s) + d
        self.steps.append((eng, emit, list(waits or []), incs))
        return {s: self.c(s) for s, _ in incs}

    def now(self, *sems):
        return [(s, self.c(s)) for s in sems]

    NQ = 8

    def dma(self, eng, emit, waits=None):
        i = self.cnt.get("_dmaidx_" + eng, 0)
        self.cnt["_dmaidx_" + eng] = i + 1
        sem = f"D{eng}{i % self.NQ}"
        w = list(waits or []) + [(sem, self.c(sem))]
        return self.add(eng, emit, waits=w, incs=[(sem, 16)])

    def dma_done(self, eng):
        return [(f"D{eng}{j}", self.c(f"D{eng}{j}")) for j in range(self.NQ)
                if self.c(f"D{eng}{j}") > 0]


ENGMAP = [("sy", "sync"), ("te", "tensor"), ("dve", "vector"),
          ("act", "scalar"), ("gp", "gpsimd")]


def realize(nc, plan):
    names = set(plan.cnt.keys())
    for (_, _, waits, _) in plan.steps:
        for (s, _) in waits:
            names.add(s)
    sems = {n: nc.alloc_semaphore(n) for n in sorted(names)}
    with nc.Block() as blk:
        for nm, attr in ENGMAP:
            def mk(nm=nm):
                def body(eng):
                    seen = {}
                    auto = Plan.AUTO.get(nm)
                    nauto = 0
                    for (e, emit, waits, incs) in plan.steps:
                        if e != nm:
                            continue
                        if auto is not None and nauto > 0:
                            waits = waits + [(auto, nauto)]
                        for (s, v) in waits:
                            if v > 0 and seen.get(s, -1) < v:
                                eng.wait_ge(sems[s], v)
                                seen[s] = v
                        inst = emit(eng)
                        for (s, d) in incs:
                            inst = inst.then_inc(sems[s], d)
                        if auto is not None:
                            nauto += 1
                return body
            getattr(blk, attr)(mk())
    return sems


class Arena:

    def __init__(self, nc, name, dtype, words):
        self.t = nc.alloc_sbuf_tensor(name, [128, words], dtype)
        self.off = 0
        self.marks = []

    def alloc(self, *shape):
        n = int(np.prod(shape))
        v = self.t[:, self.off:self.off + n]
        self.off += n
        if len(shape) > 1:
            pat = " ".join(f"a{i}" for i in range(len(shape)))
            v = v.rearrange(f"p ({pat}) -> p {pat}",
                            **{f"a{i}": shape[i] for i in range(len(shape))})
        return v

    def push(self):
        self.marks.append(self.off)

    def pop(self):
        self.off = self.marks.pop()


def ln_phase(p, T, n_tiles, src_tm, dst_tm, eps=1e-5):
    xc, sq, st = T["xc"], T["sq"], T["st"]
    for t in range(n_tiles):
        x_t = src_tm[:, t, :]
        p.add("dve", lambda e, x_t=x_t, st=st: e.reduce_sum(
            st[:, 0:1], x_t, axis=AX.X), waits=p.now("ACTa"))
        p.add("dve", lambda e, st=st: e.tensor_scalar(
            st[:, 1:2], st[:, 0:1], 1.0 / 1024.0, None, ALU.mult))
        p.add("dve", lambda e, x_t=x_t, xc=xc, st=st: e.tensor_scalar(
            xc, x_t, st[:, 1:2], None, ALU.subtract))
        p.add("dve", lambda e, xc=xc, sq=sq, st=st: e.scalar_tensor_tensor(
            sq, xc, 0.0, xc, ALU.add, ALU.mult, accum_out=st[:, 2:3]))
        p.add("act", lambda e, st=st, eps_ap=T["eps"]: e.activation(
            st[:, 3:4], st[:, 2:3], AF.Ln, bias=eps_ap, scale=1.0 / 1024.0),
            waits=p.now("DVEa"))
        p.add("act", lambda e, st=st: e.activation(
            st[:, 4:5], st[:, 3:4], AF.Exp, bias=0.0, scale=-0.5))
        p.add("dve", lambda e, xc=xc, st=st, d=dst_tm[:, t, :]: e.tensor_scalar(
            d, xc, st[:, 4:5], None, ALU.mult), waits=p.now("ACTa"))


def ln_t_phase(p, T, n_tiles, src_tm, dst_fm, pt_bf, ident, sem, eps=1e-5,
               first_waits=()):
    xc, sq, st, xnt = T["xc"], T["sq"], T["st"], T["xnt"]
    te_tok = 0
    for t in range(n_tiles):
        x_t = src_tm[:, t, :]
        p.add("dve", lambda e, x_t=x_t, st=st: e.reduce_sum(
            st[:, 0:1], x_t, axis=AX.X),
            waits=p.now("ACTa") + (list(first_waits) if t == 0 else []))
        p.add("dve", lambda e, st=st: e.tensor_scalar(
            st[:, 1:2], st[:, 0:1], 1.0 / 1024.0, None, ALU.mult))
        p.add("dve", lambda e, x_t=x_t, xc=xc, st=st: e.tensor_scalar(
            xc, x_t, st[:, 1:2], None, ALU.subtract))
        p.add("dve", lambda e, xc=xc, sq=sq, st=st: e.scalar_tensor_tensor(
            sq, xc, 0.0, xc, ALU.add, ALU.mult, accum_out=st[:, 2:3]))
        p.add("act", lambda e, st=st, eps_ap=T["eps"]: e.activation(
            st[:, 3:4], st[:, 2:3], AF.Ln, bias=eps_ap, scale=1.0 / 1024.0),
            waits=p.now("DVEa"))
        p.add("act", lambda e, st=st: e.activation(
            st[:, 4:5], st[:, 3:4], AF.Exp, bias=0.0, scale=-0.5))
        w = p.now("ACTa")
        if t > 0:
            w += [(sem, te_tok)]
        p.add("dve", lambda e, xc=xc, st=st, xnt=xnt: e.tensor_scalar(
            xnt, xc, st[:, 4:5], None, ALU.mult), waits=w)
        mult_tok = p.now("DVEa")
        for dt in range(ND):
            w = list(mult_tok) if dt == 0 else []
            if dt == 0:
                w += p.now("DVEa")
            p.add("te", lambda e, dt=dt, xnt=xnt: e.transpose(
                pt_bf[:, bass.ts(dt, 128)], xnt[:, bass.ts(dt, 128)], ident),
                waits=w, incs=([(sem, 1)] if dt == ND - 1 else None))
        te_tok = p.c(sem)
        p.add("dve", lambda e, t=t: e.tensor_copy(
            dst_fm[:, :, bass.ts(t, 128)],
            pt_bf[:].rearrange("p (a b) -> p a b", a=ND)),
            waits=[(sem, te_tok)])


def transpose_tm_to_fm(p, T, src_tm, dst_fm, n_tiles, pt_bf, ident, sem="TT",
                       src_ready=()):
    copy_tok = None
    for t in range(n_tiles):
        for dt in range(ND):
            last = dt == ND - 1
            w = list(src_ready) if (t == 0 and dt == 0) else []
            if dt == 0 and copy_tok is not None:
                w += list(copy_tok.items())
            p.add("te", lambda e, s=src_tm[:, t, bass.ts(dt, 128)],
                  d=pt_bf[:, bass.ts(dt, 128)], ident=ident:
                  e.transpose(d, s, ident),
                  waits=w,
                  incs=([(sem, 1)] if last else None))
        copy_tok = p.add(
            "dve", lambda e, t=t, pt_bf=pt_bf:
            e.tensor_copy(dst_fm[:, :, bass.ts(t, 128)],
                          pt_bf[:].rearrange("p (a b) -> p a b", a=ND)),
            waits=p.now(sem))


def host_prep(inputs):
    H = {}
    x = np.asarray(inputs["L0_attn_x"])
    H["x_shards"] = [np.ascontiguousarray(
        x[:, c * POS:(c + 1) * POS, :]).reshape(TOK, D) for c in range(NCORE)]
    H["x_full"] = np.ascontiguousarray(x.reshape(B * S, D))
    for nm in ("q", "k", "v", "o"):
        H[f"basis_{nm}T"] = np.ascontiguousarray(
            np.asarray(inputs[f"L0_attn_basis_{nm}"]).T)
        H[f"phase_{nm}T"] = np.ascontiguousarray(
            np.asarray(inputs[f"L0_attn_phase_{nm}"]).T)
        H[f"amp_{nm}T"] = np.ascontiguousarray(
            np.asarray(inputs[f"L0_attn_amp_{nm}"]).T)
    for nm in ("wf", "uf", "wi", "ui", "wc", "uc", "wo", "uo"):
        H[nm + "T"] = np.ascontiguousarray(np.asarray(inputs["L0_ffn_" + nm]).T)
    for nm in ("bf", "bi", "bc", "bo", "in_c"):
        H[nm + "_r"] = np.ascontiguousarray(
            np.asarray(inputs["L0_ffn_" + nm]).reshape(ND, 128).T)
    H["bo_row"] = np.ascontiguousarray(
        np.asarray(inputs["L0_ffn_bo"]).reshape(1, D))
    for nm in ("Wq", "Wk", "Wv", "Wo"):
        H[nm + "T"] = np.ascontiguousarray(np.asarray(inputs["L1_attn_" + nm]).T)
    H["rmT"] = np.ascontiguousarray(
        np.asarray(inputs["L1_ffn_rail_maker"]).transpose(0, 1, 3, 2))
    H["rupT"] = np.ascontiguousarray(
        np.asarray(inputs["L1_ffn_rail_up_proj"]).transpose(0, 1, 3, 2))
    wcT = np.asarray(inputs["L1_ffn_w_cproj"]).T
    wcT = wcT.reshape(4, 8, 8, 64, 1024).transpose(1, 2, 0, 3, 4)
    H["wcprojT"] = np.ascontiguousarray(wcT.reshape(16384, 1024))
    H["bkvT"] = np.ascontiguousarray(np.concatenate(
        [np.asarray(inputs["L0_attn_basis_k"]).T,
         np.asarray(inputs["L0_attn_basis_v"]).T], axis=1))
    H["rmJ"] = np.ascontiguousarray(
        np.asarray(inputs["L1_ffn_rail_maker"]).transpose(3, 0, 1, 2))
    H["rupJ"] = np.ascontiguousarray(
        np.asarray(inputs["L1_ffn_rail_up_proj"]).transpose(3, 0, 1, 2))
    H["ident_bf"] = np.eye(128, dtype=BF)
    H["ident_f32"] = np.eye(128, dtype=np.float32)
    H["triu"] = np.triu(np.ones((128, 128), np.float32)).astype(BF)
    return H


def build_l1():
    nc = bass.Bass("TRN2", target_bir_lowering=False)
    x_d = nc.dram_tensor("x", [TOK, D], F32, kind="ExternalInput")
    ident_d = nc.dram_tensor("ident_bf", [128, 128], BF16, kind="ExternalInput")
    basis_d = {nm: nc.dram_tensor(f"basis_{nm}T", [D, HARM], F32,
                                  kind="ExternalInput") for nm in ("q", "k", "v")}
    r_out = {nm: nc.dram_tensor(f"r{nm}", [HARM, TOK], BF16,
                                kind="ExternalOutput") for nm in ("q", "k", "v")}

    x_tm = nc.alloc_sbuf_tensor("x_tm", [128, NT, D], F32)
    xn_tm = nc.alloc_sbuf_tensor("xn_tm", [128, NT, D], BF16)
    xn_fm = nc.alloc_sbuf_tensor("xn_fm", [128, ND, TOK], BF16)
    ident = nc.alloc_sbuf_tensor("ident", [128, 128], BF16)
    bsb = {nm: nc.alloc_sbuf_tensor(f"b_{nm}", [128, ND, HARM], BF16)
           for nm in ("q", "k", "v")}
    rsb = {nm: nc.alloc_sbuf_tensor(f"r_{nm}", [HARM, TOK], BF16)
           for nm in ("q", "k", "v")}
    xc = nc.alloc_sbuf_tensor("xc", [128, D], F32)
    sq = nc.alloc_sbuf_tensor("sq", [128, D], F32)
    st = nc.alloc_sbuf_tensor("st", [128, 16], F32)
    pt_bf = nc.alloc_psum_tensor("pt_bf", [128, 1024], BF16)
    pj = nc.alloc_psum_tensor("pj", [128, 1024], F32)

    T = {"xc": xc[:], "sq": sq[:], "st": st, "eps": st[:, 7:8]}
    p = Plan()
    p.add("dve", lambda e: e.memset(st[:, 7:8], 1e-5))
    p.dma("sy", lambda e: e.dma_start(out=x_tm[:],
                                      in_=x_d[:].rearrange("(t p) d -> p t d", p=128)))
    p.dma("sy", lambda e: e.dma_start(out=ident[:], in_=ident_d[:]))
    for i, nm in enumerate(("q", "k", "v")):
        p.dma("gp", lambda e, nm=nm: e.dma_start(
            out=bsb[nm][:],
            in_=basis_d[nm][:].rearrange("(a p) b -> p a b", p=128)))
    p.add("dve", lambda e: e.memset(st[:, 6:7], 0.0),
          waits=p.dma_done("sy"))
    ln_phase(p, T, NT, x_tm, xn_tm)
    transpose_tm_to_fm(p, T, xn_tm, xn_fm, NT, pt_bf, ident[:],
                       src_ready=p.now("DVEa") + p.dma_done("sy"))
    p.add("te", lambda e: e.nop(), waits=p.dma_done("gp"))
    for i, nm in enumerate(("q", "k", "v")):
        for n in range(4):
            buf = pj[:, bass.ts(n % 2, 512)]
            for dt in range(ND):
                p.add("te", lambda e, nm=nm, n=n, dt=dt, buf=buf: e.matmul(
                    buf[0:HARM, :], bsb[nm][:, dt, :],
                    xn_fm[:, dt, bass.ts(n, 512)],
                    start=(dt == 0), stop=(dt == ND - 1)),
                    waits=(p.now("DVEa", "ACTa") if dt == 0 else None),
                    incs=([("TEP", 1)] if dt == ND - 1 else None))
            p.add("act", lambda e, nm=nm, n=n, buf=buf: e.copy(
                rsb[nm][:, bass.ts(n, 512)], buf[0:HARM, :]),
                waits=p.now("TEP"))
    for nm in ("q", "k", "v"):
        p.dma("sy", lambda e, nm=nm: e.dma_start(out=r_out[nm][:], in_=rsb[nm][:]),
              waits=p.now("ACTa"))
    p.add("sy", lambda e: e.nop(), waits=p.dma_done("sy"))
    realize(nc, p)
    return nc


def build_l2():
    nc = bass.Bass("TRN2", target_bir_lowering=False)
    x_d = nc.dram_tensor("x", [TOK, D], F32, kind="ExternalInput")
    rkvf_d = nc.dram_tensor("rkvf", [64, B * S], BF16, kind="ExternalInput")
    rqin_d = nc.dram_tensor("rqin", [HARM, TOK], BF16, kind="ExternalInput")
    ident_d = nc.dram_tensor("ident_bf", [128, 128], BF16, kind="ExternalInput")
    identf_d = nc.dram_tensor("ident_f32", [128, 128], F32, kind="ExternalInput")
    pa_d = {}
    for nm in ("q", "k", "v", "o"):
        pa_d[nm] = (nc.dram_tensor(f"phase_{nm}T", [HARM, D], F32, kind="ExternalInput"),
                    nc.dram_tensor(f"amp_{nm}T", [HARM, D], F32, kind="ExternalInput"))
    basiso_d = nc.dram_tensor("basis_oT", [D, HARM], F32, kind="ExternalInput")
    w_d = {nm: nc.dram_tensor(nm + "T", [D, D], F32, kind="ExternalInput")
           for nm in ("wf", "uf", "wi", "ui", "wc", "uc", "wo", "uo")}
    b_d = {nm: nc.dram_tensor(nm + "_r", [128, ND], F32, kind="ExternalInput")
           for nm in ("bf", "bi", "bc", "bo", "in_c")}
    borow_d = nc.dram_tensor("bo_row", [1, D], F32, kind="ExternalInput")
    wl1_d = {nm: nc.dram_tensor(nm + "T", [D, 64], F32, kind="ExternalInput")
             for nm in ("Wq", "Wk", "Wv")}
    scr_d = nc.dram_tensor("w_scr", [8, ND, 128, ND, 128], BF16)
    x1_d = nc.dram_tensor("x1", [TOK, D], F32, kind="ExternalOutput")
    xn1_d = nc.dram_tensor("xn1", [D, TOK], BF16, kind="ExternalOutput")
    q1_d = nc.dram_tensor("Q1", [64, TOK], BF16, kind="ExternalOutput")
    k1_d = nc.dram_tensor("K1", [64, TOK], BF16, kind="ExternalOutput")
    v1_d = nc.dram_tensor("V1", [64, TOK], BF16, kind="ExternalOutput")
    st_d = nc.dram_tensor("states", [B, 64, 65], F32, kind="ExternalOutput")

    x_tm = nc.alloc_sbuf_tensor("x_tm", [128, NT, D], F32)
    xn_fm = nc.alloc_sbuf_tensor("xn_fm", [128, ND, TOK], BF16)
    UD = nc.alloc_sbuf_tensor("UD", [128, 8192], F32)
    UE = nc.alloc_sbuf_tensor("UE", [128, 10240], F32)
    UF = nc.alloc_sbuf_tensor("UF", [128, 8192], F32)
    ident = nc.alloc_sbuf_tensor("ident", [128, 128], BF16)
    identf = nc.alloc_sbuf_tensor("identf", [128, 128], F32)
    wmatA = nc.alloc_sbuf_tensor("wmatA", [128, D], BF16)
    wmatB = nc.alloc_sbuf_tensor("wmatB", [32, D], BF16)
    basiso = nc.alloc_sbuf_tensor("basiso", [128, ND, HARM], BF16)
    st = nc.alloc_sbuf_tensor("st", [128, 16], F32)
    rec = nc.alloc_sbuf_tensor("rec", [128, 8], F32)
    bia = nc.alloc_sbuf_tensor("bia", [128, 5 * ND], F32)

    def carve(arena, w0, w1, dtype=None, shape=None):
        v = arena[:, w0:w1]
        if dtype is not None:
            v = v.bitcast(dtype)
        if shape is not None:
            pat = " ".join(f"a{i}" for i in range(len(shape)))
            v = v.rearrange(f"p ({pat}) -> p {pat}",
                            **{f"a{i}": shape[i] for i in range(len(shape))})
        return v

    rkv = carve(UD, 0, 8192, BF16)
    c_fm = carve(UD, 0, 4096, None, (ND, POS))
    c_bf = carve(UD, 4096, 6144, BF16, (ND, POS))
    gf = carve(UD, 6144, 6656)
    gi = carve(UD, 6656, 7168)
    gg = carve(UD, 7168, 7680)
    htmp = carve(UD, 7680, 8192)
    htmp2 = carve(UD, 0, 2048, None, (2, D))
    st64 = carve(UD, 0, 260)
    k2 = carve(UE, 0, 2048, BF16)
    v2p = carve(UE, 2048, 4352, BF16, (32, 2, 72))
    q2 = carve(UE, 4352, 4608, BF16)
    e4 = carve(UE, 4608, 5632, BF16, (4, POS))
    o_tm = carve(UE, 5632, 7680, BF16, (4, D))
    o_fm = carve(UE, 7680, 9728, BF16, (ND, POS))
    ro_sb = carve(UE, 9728, 9984, BF16)
    q1sb = carve(UE, 0, 1024, BF16)
    kfsb = carve(UE, 1024, 2048, BF16)
    v1sb = carve(UE, 2048, 3072, BF16)
    kf_tmv = carve(UE, 3072, 3584, BF16, (NT, 64))
    vp_tmv = carve(UE, 3584, 4160, BF16, (NT, 72))
    el_t = carve(UE, 4160, 4672)
    el_e = carve(UE, 4672, 5184)
    wuo = carve(UE, 0, 8192, BF16, (2, ND, D))
    borow = carve(UE, 8192, 8704, BF16)
    ones1 = carve(UE, 8704, 8768, BF16)
    xf_tm = carve(UE, 0, 4096, None, (4, D))
    xnf_t = carve(UE, 4096, 5120, BF16, (2, ND, 128))
    bq_sb = carve(UE, 5120, 5248, BF16, (ND, HARM))
    bkv_sb = carve(UE, 5248, 5504, BF16, (ND, 64))
    stA = carve(UE, 0, 4096, BF16, (ND, D))
    stB = carve(UE, 5632, 9728, BF16, (ND, ND, 128))
    xc = carve(UF, 0, 1024)
    sq = carve(UF, 1024, 2048)
    pst = carve(UF, 2048, 3072)
    wtmp = carve(UF, 3072, 4096)
    xnt = carve(UF, 4096, 4608, BF16)
    os_bf = carve(UF, 4608, 4864, BF16)
    xnt2 = carve(UF, 4864, 5376, BF16)
    wstr = carve(UF, 0, 6144, BF16, (2, 6, ND, 128))
    wl1 = carve(UF, 6144, 6912, BF16, (3, ND, 64))

    PZ = nc.alloc_psum_tensor("PZ", [128, 2048], F32)
    PY = nc.alloc_psum_tensor("PY", [128, 1536], F32)
    PTB = nc.alloc_psum_tensor("PTB", [128, 1024], BF16)
    LOG = [PZ[:, 0:512], PZ[:, 512:1024]]
    PO = PZ[:, 1024:1536]
    PG = PZ[:, 1536:2048]
    PW = PY[:, 0:1024]
    PW0 = PY[:, 0:512]
    PW1 = PY[:, 512:1024]
    PT32 = PY[:, 1024:1536]

    T = {"xc": xc, "sq": sq, "st": st, "eps": st[:, 7:8], "xnt": xnt}
    p = Plan()

    p.dma("sy", lambda e: e.dma_start(out=x_tm[:],
          in_=x_d[:].rearrange("(t p) d -> p t d", p=128)))
    p.dma("sy", lambda e: e.dma_start(out=ident[:], in_=ident_d[:]))
    p.dma("sy", lambda e: e.dma_start(out=identf[:], in_=identf_d[:]))
    ds_loads = p.dma_done("sy")
    p.dma("gp", lambda e: e.dma_start(out=basiso[:],
          in_=basiso_d[:].rearrange("(a p) b -> p a b", p=128)))
    for i, nm in enumerate(("bf", "bi", "bc", "bo", "in_c")):
        p.dma("gp", lambda e, nm=nm, i=i: e.dma_start(
            out=bia[:, bass.ts(i, ND)], in_=b_d[nm][:]))
    for i, nm in enumerate(("Wq", "Wk", "Wv")):
        p.dma("gp", lambda e, nm=nm, i=i: e.dma_start(
            out=wl1[:, i, :, :], in_=wl1_d[nm][:].rearrange(
                "(a p) b -> p a b", p=128)))
    dl_loads = p.dma_done("gp")

    wdst = {"q": wmatA[64:96, :], "k": wmatA[0:32, :],
            "v": wmatA[32:64, :], "o": wmatB[0:32, :]}
    pa_tok = None
    for nm in ("q", "k", "v", "o"):
        w = list(pa_tok.items()) if pa_tok else []
        p.dma("gp", lambda e, nm=nm: e.dma_start(out=pst[0:32, :], in_=pa_d[nm][0][:]),
              waits=w)
        p.dma("gp", lambda e, nm=nm: e.dma_start(out=wtmp[0:32, :], in_=pa_d[nm][1][:]))
        dl_now = p.dma_done("gp")
        ph, am = pst[0:32, :], wtmp[0:32, :]
        t1, t2 = xc[0:32, :], sq[0:32, :]
        p.add("dve", lambda e, ph=ph, t1=t1: e.tensor_scalar(
            t1, ph, 1.0 / TWO_PI, MAGIC, ALU.mult, ALU.add),
            waits=dl_now)
        p.add("dve", lambda e, t1=t1: e.tensor_scalar(
            t1, t1, MAGIC, None, ALU.subtract))
        p.add("dve", lambda e, ph=ph, t1=t1: e.scalar_tensor_tensor(
            t1, t1, -TWO_PI, ph, ALU.mult, ALU.add))
        p.add("act", lambda e, t1=t1, t2=t2: e.activation(
            t2, t1, AF.Sin, bias=0.0, scale=0.5), waits=p.now("DVEa"))
        p.add("dve", lambda e, t1=t1, t2=t2: e.scalar_tensor_tensor(
            t1, t2, -2.0, t2, ALU.mult, ALU.mult),
            waits=p.now("ACTa"))
        p.add("dve", lambda e, t1=t1: e.tensor_scalar(
            t1, t1, 1.0, None, ALU.add))
        pa_tok = p.add("dve", lambda e, am=am, t1=t1, dd=wdst[nm]: e.tensor_tensor(
            dd, am, t1, ALU.mult))

    WMATS = ("wf", "uf", "wi", "ui", "wc", "uc", "uo", "wo")
    stg_cp = None
    stg_out = None
    for mi, nm in enumerate(WMATS):
        w = []
        if stg_out is not None:
            w = list(stg_out)
        p.dma("gp", lambda e, nm=nm: e.dma_start(
            out=stA[:], in_=w_d[nm][:].rearrange("(a p) b -> p a b", p=128)),
            waits=(list(stg_cp.items()) if stg_cp else []))
        stg_dma = p.dma_done("gp")
        stg_cp = p.add("dve", lambda e: e.tensor_copy(
            stB[:], stA[:].rearrange("p a (o f) -> p o a f", o=ND)),
            waits=stg_dma + (stg_out or []))
        p.dma("sy", lambda e, mi=mi: e.dma_start(
            out=scr_d[mi, :, :, :, :].rearrange("o q a f -> q o a f"),
            in_=stB[:]),
            waits=list(stg_cp.items()))
        stg_out = p.dma_done("sy")
    scr_ready = stg_out

    p.add("dve", lambda e: e.memset(st[:, 7:8], 1e-5))
    rkv_ld = p.dma("sy", lambda e: e.dma_start(out=rkv[0:64, :], in_=rkvf_d[:]))
    rq_ld = p.dma("sy", lambda e: e.dma_start(
        out=rkv[64:96, 0:TOK], in_=rqin_d[:]))
    rkv_kv_done = list(rkv_ld.items()) + list(rq_ld.items())

    p.add("dve", lambda e: e.memset(st[:, 6:7], 0.0), waits=ds_loads)
    ln_t_phase(p, T, NT, x_tm, xn_fm, PTB, ident[:], sem="TT",
               first_waits=p.now("TTf"))

    def tok_w(tok):
        return list(tok.items()) if tok else []

    rq_done = []

    LOG4 = [PZ[:, bass.ts(i, 512)] for i in range(4)]
    POa = PY[:, 0:512]
    PEX = [PY[:, 512:1024], PY[:, 1024:1536]]
    PEXw = PY[:, 512:1536]
    p.add("dve", lambda e: e.memset(
        v2p[:].rearrange("p a b c -> p (a b) c")[:, :, 64:65], 1.0))

    exp_super = [None] * 4
    eslot = [None] * 4
    kq_free = None
    v_free = None
    pw_tok = None
    poA_tok = None
    poB_tok = None
    otm_tok = None
    wm_tok = pa_tok
    kexp_cp = None
    PEX1 = PY[:, 512:1024]
    POa = PY[:, 0:512]
    POb = PY[:, 1024:1536]
    pair_i = 0
    for b in range(B):
        for j in range(NH // 2):
            hA, hB = 2 * j, 2 * j + 1
            EXR = [PEX1, POa, POb]
            exg = [tok_w(pw_tok), tok_w(poA_tok), tok_w(poB_tok)]
            ecp = [None, None, None]
            ei = 0
            for n in range(ND):
                w = list(exg[ei % 3]) + tok_w(ecp[ei % 3])
                if pair_i == 0 and n == 0:
                    w += (tok_w(wm_tok) + ds_loads + rq_done + rkv_kv_done
                          + scr_ready)
                if n == 0:
                    w += tok_w(kq_free)
                p.add("te", lambda e, hA=hA, b=b, n=n, ei=ei: e.matmul(
                    EXR[ei % 3][:, :], wmatA[0:32, hA * HD:hA * HD + 128],
                    rkv[0:32, b * 4096 + n * 512: b * 4096 + (n + 1) * 512],
                    start=True, stop=True), waits=w, incs=[("TEk", 1)])
                tek = p.c("TEk")
                ecp[ei % 3] = p.add("dve", lambda e, n=n, ei=ei: e.tensor_copy(
                    k2[:, bass.ts(n, 512)], EXR[ei % 3][:, :]),
                    waits=[("TEk", tek)])
                ei += 1
            w = list(exg[ei % 3]) + tok_w(ecp[ei % 3])
            p.add("te", lambda e, hA=hA, b=b, ei=ei: e.matmul(
                EXR[ei % 3][:, :], wmatA[64:96, hA * HD:hA * HD + 128],
                rkv[64:96, b * 512:(b + 1) * 512], start=True, stop=True),
                waits=w, incs=[("TEk", 1)])
            tek = p.c("TEk")
            q_cp = p.add("dve", lambda e, ei=ei: e.tensor_copy(
                q2[:, :], EXR[ei % 3][:, :]), waits=[("TEk", tek)])
            ecp[ei % 3] = q_cp
            ei += 1
            kexp_cp = q_cp
            for g in range(8):
                w0 = list(exg[ei % 3]) + tok_w(ecp[ei % 3])
                for j4 in range(4):
                    kt = g * 4 + j4
                    w = w0 if j4 == 0 else []
                    if g == 0 and j4 == 0:
                        w = w + tok_w(v_free)
                    p.add("te", lambda e, b=b, hA=hA, kt=kt, j4=j4, ei=ei:
                          e.matmul(
                              EXR[ei % 3][:, bass.ts(j4, 128)],
                              rkv[32:64, b * 4096 + kt * 128:b * 4096 + (kt + 1) * 128],
                              wmatA[32:64, hA * HD:hA * HD + 128],
                              start=True, stop=True), waits=w, incs=[("TEk", 1)])
                tek = p.c("TEk")
                ecp[ei % 3] = p.add("dve", lambda e, g=g, ei=ei: e.tensor_copy(
                    v2p[:, bass.ts(g, 4), :, 0:64],
                    EXR[ei % 3][:].rearrange("p (a c d) -> p a c d", a=4, c=2)),
                    waits=[("TEk", tek)])
                ei += 1
            v_cp = ecp[(ei - 1) % 3]
            stg_guard = tok_w(ecp[0]) + tok_w(ecp[1]) + tok_w(ecp[2])
            pw_tok = v_cp
            def emit_av(kt):
                for st_h, base, po in (("A", 0, POa), ("B", 2, POb)):
                    hp = 0 if st_h == "A" else 1
                    sl = base + kt % 2
                    w = [("ACTa", exp_super[sl]["ACTa"])]
                    if kt == 0 and st_h == "A":
                        w += stg_guard
                    p.add("te", lambda e, hp=hp, kt=kt, po=po, sl=sl:
                          e.matmul(po[0:65, :], v2p[:, kt, hp, 0:65],
                                   e4[:, sl, :],
                                   start=(kt == 0), stop=(kt == 31)),
                          waits=w, incs=[("TEo", 1)])
                    eslot[sl] = {"TEo": p.c("TEo")}
            for kt in range(32):
                qk_tok = {}
                for st_h, base in (("A", 0), ("B", 2)):
                    hp = 0 if st_h == "A" else 1
                    sl = base + kt % 2
                    w = []
                    if exp_super[sl]:
                        w += [("ACTa", exp_super[sl]["ACTa"])]
                    if kt == 0 and st_h == "A":
                        w += tok_w(kexp_cp) + tok_w(q_cp)
                    qk_tok[st_h] = p.add(
                        "te", lambda e, hp=hp, kt=kt, sl=sl:
                        e.matmul(LOG4[sl][:, :],
                                 k2[bass.ts(hp, 64), bass.ts(kt, 128)],
                                 q2[bass.ts(hp, 64), :],
                                 start=True, stop=True),
                        waits=w, incs=[("TEl", 1)])
                for st_h, base in (("A", 0), ("B", 2)):
                    sl = base + kt % 2
                    w = [("TEl", qk_tok[st_h]["TEl"])]
                    if eslot[sl]:
                        w += [("TEo", eslot[sl]["TEo"])]
                    exp_super[sl] = p.add(
                        "act", lambda e, sl=sl: e.activation(
                            e4[:, sl, :], LOG4[sl][:, :],
                            AF.Exp, bias=0.0, scale=0.125), waits=w)
                if kt > 0:
                    emit_av(kt - 1)
            emit_av(31)
            v_free = {"TEo": p.c("TEo")}
            kq_free = {"TEl": p.c("TEl")}
            pw_tok = v_cp
            for st_h, po in (("A", POa), ("B", POb)):
                h = hA if st_h == "A" else hB
                pt = p.add("dve", lambda e, po=po: e.tensor_copy(
                    os_bf[0:65, 0:512], po[0:65, :]),
                    waits=[("TEo", p.c("TEo"))] + tok_w(otm_tok) + p.now("TEt"))
                if st_h == "A":
                    poA_tok = pt
                else:
                    poB_tok = pt
                for i in range(4):
                    w = tok_w(pt) + p.now("DVEa") if i == 0 else []
                    p.add("te", lambda e, i=i: e.transpose(
                        PTB[:, i * 256:i * 256 + 65],
                        os_bf[0:65, bass.ts(i, 128)],
                        ident[0:65, 0:65]), waits=w, incs=[("TEt", 1)])
                tet = p.c("TEt")
                for i in range(4):
                    p.add("dve", lambda e, i=i: e.reciprocal(
                        rec[:, i:i + 1], PTB[:, i * 256 + 64:i * 256 + 65]),
                        waits=([("TEt", tet)] if i == 0 else []))
                    p.add("dve", lambda e, i=i, h=h: e.tensor_scalar(
                        o_tm[:, i, bass.ts(h, HD)], PTB[:, i * 256:i * 256 + 64],
                        rec[:, i:i + 1], None, ALU.mult))
            pair_i += 1
        pg_tok = v_cp
        transpose_tm_to_fm(p, T, o_tm, o_fm, 4, PTB, ident[:], sem="TTo",
                           src_ready=p.now("DVEa"))
        otm_tok = {"TTo": p.c("TTo")}
        ofm_cp = p.now("DVEa")
        for dt in range(ND):
            w = list(ofm_cp) if dt == 0 else []
            if dt == 0:
                w += tok_w(pg_tok) + tok_w(pw_tok)
            p.add("te", lambda e, dt=dt: e.matmul(
                PEX[0][0:32, :], basiso[:, dt, :], o_fm[:, dt, :],
                start=(dt == 0), stop=(dt == ND - 1)), waits=w,
                incs=([("TEr", 1)] if dt == ND - 1 else None))
        ro_cp = p.add("act", lambda e: e.copy(ro_sb[0:32, :], PEX[0][0:32, :]),
                      waits=[("TEr", p.c("TEr"))])
        pg_tok = ro_cp
        pw_tok = ro_cp
        for qt in range(4):
            w = (tok_w(ro_cp) + tok_w(poA_tok) + tok_w(poB_tok)) if qt == 0 else []
            w += tok_w(pw_tok)
            for half in range(2):
                p.add("te", lambda e, qt=qt, half=half: e.matmul(
                    PEX[half][:, :], ro_sb[0:32, bass.ts(qt, 128)],
                    wmatB[0:32, bass.ts(half, 512)],
                    start=True, stop=True),
                    waits=(w if half == 0 else []), incs=[("TEk", 1)])
            tek = p.c("TEk")
            pw_tok = p.add("dve", lambda e, b=b, qt=qt: e.tensor_add(
                x_tm[:, b * 4 + qt, :], x_tm[:, b * 4 + qt, :], PEXw[:, :]),
                waits=[("TEk", tek)])

    exp_bank = [exp_super[3], exp_super[3]]
    pt32_tok = {"DVEa": p.c("DVEa"), "ACTa": p.c("ACTa")}
    po_tok = poB_tok

    for dt in range(ND):
        p.add("dve", lambda e, dt=dt: e.memset(c_fm[:, dt, :], 1.0),
              waits=(dl_loads + p.now("TEk", "TEl", "TEo")
                     if dt == 0 else []))
        p.add("dve", lambda e, dt=dt: e.tensor_scalar(
            c_fm[:, dt, :], c_fm[:, dt, :], bia[:, 4 * ND + dt:4 * ND + dt + 1],
            None, ALU.mult))
    cbf_tok = p.add("dve", lambda e: e.tensor_copy(c_bf[:], c_fm[:]))
    ws_tok = [None, None]
    gact = [None, None, None]
    gsb_tok = None
    GACT_F = [AF.Sigmoid, AF.Sigmoid, AF.Tanh]
    gate_w = ("wf", "wi", "wc")
    for t in range(4):
        for ot in range(ND):
            s = (t * ND + ot) % 2
            for g in range(3):
                w = []
                if g == 0:
                    w = p.now("TEf")
                    if t == 0 and ot == 0:
                        w += p.now("TEt", "ACTa", "DVEa") + scr_ready
                p.dma("gp", lambda e, g=g, ot=ot, s=s: e.dma_start(
                    out=wstr[:, s, g, :, :],
                    in_=scr_d[2 * g, ot, :, :, :]),
                    waits=w)
            for g in range(3):
                w = []
                if g == 0:
                    w = p.now("TEf")
                    if t == 0 and ot == 0:
                        w += p.now("TEt", "ACTa", "DVEa") + scr_ready
                p.dma("sy", lambda e, g=g, ot=ot, s=s: e.dma_start(
                    out=wstr[:, s, 3 + g, :, :],
                    in_=scr_d[2 * g + 1, ot, :, :, :]),
                    waits=w)
            dl_now = p.dma_done("gp") + p.dma_done("sy")
            for g in range(3):
                pgm = PZ[:, bass.ts(g, 512)]
                for dt in range(ND):
                    w = []
                    if dt == 0:
                        w = list(dl_now) + tok_w(gact[g])
                        if g == 0:
                            w += tok_w(cbf_tok)
                        if t == 0 and ot == 0:
                            w += tok_w(exp_bank[0]) + tok_w(exp_bank[1])
                    p.add("te", lambda e, g=g, dt=dt, ot=ot, t=t, s=s, pgm=pgm: e.matmul(
                        pgm, wstr[:, s, g, dt, :], xn_fm[:, dt, bass.ts(t, 512)],
                        start=(dt == 0), stop=False), waits=w)
                for dt in range(ND):
                    p.add("te", lambda e, g=g, dt=dt, ot=ot, s=s, pgm=pgm: e.matmul(
                        pgm, wstr[:, s, 3 + g, dt, :], c_bf[:, dt, :],
                        start=False, stop=(dt == ND - 1)),
                        incs=([("TEf", 1)] if dt == ND - 1 else None))
                tef = p.c("TEf")
                gact[g] = p.add("act", lambda e, g=g, ot=ot, pgm=pgm,
                                d=(gf, gi, gg)[g]: e.activation(
                    d[:], pgm, GACT_F[g], bias=bia[:, g * ND + ot:g * ND + ot + 1],
                    scale=1.0), waits=[("TEf", tef)] + (tok_w(gsb_tok) if g == 0 else []))
            ws_tok[s] = {"TEf": p.c("TEf")}
            w = tok_w(gact[0]) + tok_w(gact[1]) + tok_w(gact[2])
            p.add("dve", lambda e, ot=ot: e.tensor_mul(
                gf[:], gf[:], c_fm[:, ot, :]), waits=w)
            p.add("dve", lambda e: e.tensor_mul(gi[:], gi[:], gg[:]))
            gsb_tok = p.add("dve", lambda e, ot=ot: e.tensor_add(
                c_fm[:, ot, :], gf[:], gi[:]))
        cbf_tok = p.add("dve", lambda e: e.tensor_copy(c_bf[:], c_fm[:]))
    p.add("dve", lambda e: e.memset(ones1[0:1, 0:128], 1.0))
    p.dma("gp", lambda e: e.dma_start(out=borow[0:1, 0:1024], in_=borow_d[:]),
          waits=p.now("TEo"))
    for wi_, nm in ((0, "wo"), (1, "uo")):
        for dt in range(ND):
            p.dma("gp", lambda e, wi_=wi_, nm=nm, dt=dt: e.dma_start(
                out=wuo[:, wi_, dt, :], in_=w_d[nm][bass.ts(dt, 128), :]))
    wuo_done = p.dma_done("gp")
    hfree = [None, None]
    htanh = [None, None]
    for ti in range(NT):
        t, pt = ti // 4, ti % 4
        hb = ti % 2
        HP = PZ[:, hb * 1024:(hb + 1) * 1024]
        tsl = slice(t * 512 + pt * 128, t * 512 + (pt + 1) * 128)
        w0 = tok_w(cbf_tok) + wuo_done + tok_w(hfree[hb]) \
            + (p.now("DVEa", "ACTa") if ti < 2 else [])
        first = True
        for half in range(2):
            osl = slice(half * 512, (half + 1) * 512)
            for dt in range(ND):
                w = list(w0) if first else []
                first = False
                p.add("te", lambda e, dt=dt, tsl=tsl, osl=osl, HP=HP, half=half:
                      e.matmul(HP[:, osl], xn_fm[:, dt, tsl],
                               wuo[:, 0, dt, osl],
                               start=(dt == 0), stop=False),
                      waits=w)
            psl = slice(pt * 128, (pt + 1) * 128)
            for dt in range(ND):
                p.add("te", lambda e, dt=dt, psl=psl, osl=osl, HP=HP, half=half:
                      e.matmul(HP[:, osl], c_bf[:, dt, psl],
                               wuo[:, 1, dt, osl],
                               start=False, stop=False))
        p.add("te", lambda e, HP=HP: e.matmul(
            HP[:, 0:512], ones1[0:1, 0:128], borow[0:1, 0:512],
            start=False, stop=True))
        p.add("te", lambda e, HP=HP: e.matmul(
            HP[:, 512:1024], ones1[0:1, 0:128], borow[0:1, 512:1024],
            start=False, stop=True), incs=[("TEf", 1)])
        tef = p.c("TEf")
        ht = p.add("act", lambda e, HP=HP, hb=hb: e.activation(
            htmp2[:, hb, :], HP[:, :], AF.Tanh, bias=0.0, scale=1.0),
            waits=[("TEf", tef)] + tok_w(htanh[hb]))
        htanh[hb] = ht
        hfree[hb] = p.add("dve", lambda e, ti=ti, hb=hb: e.tensor_add(
            x_tm[:, ti, :], x_tm[:, ti, :], htmp2[:, hb, :]),
            waits=list(ht.items()))
    ln_t_phase(p, T, NT, x_tm, xn_fm, PTB, ident[:], sem="TT1",
               first_waits=p.now("TEf"))
    xn1_done = p.now("DVEa")
    p.dma("sy", lambda e: e.dma_start(
        out=x1_d[:].rearrange("(t p) d -> p t d", p=128), in_=x_tm[:]),
        waits=p.now("DVEa"))
    p.dma("sy", lambda e: e.dma_start(
        out=xn1_d[:].rearrange("(a p) n -> p a n", p=128), in_=xn_fm[:]),
        waits=xn1_done)

    l1dst = [q1sb[0:64, 0:TOK], kfsb[0:64, 0:TOK], v1sb[0:64, 0:TOK]]
    prev = None
    for i in range(3):
        for n in range(4):
            for dt in range(ND):
                w = []
                if dt == 0:
                    w = list(xn1_done) + dl_loads
                    w += tok_w(prev)
                    if i == 0 and n == 0:
                        w += tok_w(pg_tok) + tok_w(exp_bank[1])
                p.add("te", lambda e, i=i, n=n, dt=dt: e.matmul(
                    PG[0:64, :], wl1[:, i, dt, :], xn_fm[:, dt, bass.ts(n, 512)],
                    start=(dt == 0), stop=(dt == ND - 1)),
                    waits=w, incs=([("TEr", 1)] if dt == ND - 1 else None))
            ter = p.c("TEr")
            if i < 2:
                p.add("dve", lambda e: e.tensor_scalar(
                    el_t[0:64, :], PG[0:64, :], 0.0, None, ALU.min),
                    waits=[("TEr", ter)] + (p.now("TEt", "TEf", "TEo", "TEl",
                                                  "ACTa")
                                            if (i == 0 and n == 0) else []))
                p.add("act", lambda e: e.activation(
                    el_e[0:64, :], el_t[0:64, :], AF.Exp, bias=0.0, scale=1.0),
                    waits=p.now("DVEa"))
                prev = p.add("dve", lambda e, i=i, n=n: e.scalar_tensor_tensor(
                    l1dst[i][:, bass.ts(n, 512)], PG[0:64, :], 0.0, el_e[0:64, :],
                    ALU.max, ALU.add), waits=p.now("ACTa"))
            else:
                prev = p.add("act", lambda e, i=i, n=n: e.copy(
                    l1dst[i][:, bass.ts(n, 512)], PG[0:64, :]),
                    waits=[("TEr", ter)])
    qkv_done = p.now("DVEa", "ACTa")
    for nm, dd in (("Q1", q1sb[0:64, 0:TOK]), ("K1", kfsb[0:64, 0:TOK]),
                   ("V1", v1sb[0:64, 0:TOK])):
        p.dma("sy", lambda e, nm=nm, dd=dd: e.dma_start(out={
            "Q1": q1_d, "K1": k1_d, "V1": v1_d}[nm][:], in_=dd),
            waits=qkv_done)
    p.add("dve", lambda e: e.memset(vp_tmv[:, :, 64:65], 1.0))
    for src, dst, tag in ((kfsb[0:64, 0:TOK], kf_tmv, "K"),
                          (v1sb[0:64, 0:TOK], vp_tmv, "V")):
        for rnd in range(2):
            for j in range(8):
                kt = rnd * 8 + j
                w = list(qkv_done) if (rnd == 0 and j == 0) else []
                if j == 0:
                    w += p.now("DVEa")
                p.add("te", lambda e, src=src, kt=kt, j=j: e.transpose(
                    PTB[:, j * 64:(j + 1) * 64],
                    src[:, bass.ts(kt, 128)], ident[0:64, 0:64]),
                    waits=w, incs=([("TEt", 1)] if j == 7 else None))
            tet = p.c("TEt")
            p.add("dve", lambda e, dst=dst, rnd=rnd: e.tensor_copy(
                dst[:, bass.ts(rnd, 8), 0:64],
                PTB[:, 0:512].rearrange("p (a b) -> p a b", a=8)),
                waits=[("TEt", tet)])
    tr_done = p.now("DVEa")
    for b in range(B):
        for ct in range(4):
            w = tr_done + tok_w(prev) if ct == 0 else []
            p.add("te", lambda e, b=b, ct=ct: e.matmul(
                PG[0:64, 0:65], kf_tmv[:, b * 4 + ct, 0:64],
                vp_tmv[:, b * 4 + ct, 0:65],
                start=(ct == 0), stop=(ct == 3)),
                waits=w, incs=([("TEr", 1)] if ct == 3 else None))
        prev = p.add("dve", lambda e, b=b: e.tensor_copy(
            st64[0:64, b * 65:(b + 1) * 65], PG[0:64, 0:65]),
            waits=[("TEr", p.c("TEr"))])
    p.dma("sy", lambda e: e.dma_start(
        out=st_d[:].rearrange("b k c -> k b c"),
        in_=st64[0:64, :].rearrange("p (b c) -> p b c", b=B)),
        waits=p.now("DVEa"))
    p.add("sy", lambda e: e.nop(), waits=p.dma_done("sy"))
    realize(nc, p)
    return nc


def build_fused():
    nc = bass.Bass("TRN2", target_bir_lowering=False)
    x_d = nc.dram_tensor("x", [TOK, D], F32, kind="ExternalInput")
    basis_d = {nm: nc.dram_tensor(f"basis_{nm}T", [D, HARM], F32,
                                  kind="ExternalInput") for nm in ("q", "k", "v")}
    ident_d = nc.dram_tensor("ident_bf", [128, 128], BF16, kind="ExternalInput")
    identf_d = nc.dram_tensor("ident_f32", [128, 128], F32, kind="ExternalInput")
    pa_d = {}
    for nm in ("q", "k", "v", "o"):
        pa_d[nm] = (nc.dram_tensor(f"phase_{nm}T", [HARM, D], F32, kind="ExternalInput"),
                    nc.dram_tensor(f"amp_{nm}T", [HARM, D], F32, kind="ExternalInput"))
    basiso_d = nc.dram_tensor("basis_oT", [D, HARM], F32, kind="ExternalInput")
    pm_d = nc.dram_tensor("prefix_mask", [64, NCORE], F32, kind="ExternalInput")
    w_d = {nm: nc.dram_tensor(nm + "T", [D, D], F32, kind="ExternalInput")
           for nm in ("wf", "uf", "wi", "ui", "wc", "uc", "wo", "uo")}
    b_d = {nm: nc.dram_tensor(nm + "_r", [128, ND], F32, kind="ExternalInput")
           for nm in ("bf", "bi", "bc", "bo", "in_c")}
    borow_d = nc.dram_tensor("bo_row", [1, D], F32, kind="ExternalInput")
    wl1_d = {nm: nc.dram_tensor(nm + "T", [D, 64], F32, kind="ExternalInput")
             for nm in ("Wq", "Wk", "Wv")}
    wo1_d = nc.dram_tensor("Wo1T", [64, D], F32, kind="ExternalInput")
    rm_d = nc.dram_tensor("rmJ", [16, 4, 64, 16], F32, kind="ExternalInput")
    rup_d = nc.dram_tensor("rupJ", [16, 4, 64, 64], F32, kind="ExternalInput")
    wc_d = nc.dram_tensor("wcprojT", [16384, D], F32, kind="ExternalInput")
    triu_d = nc.dram_tensor("triu", [128, 128], BF16, kind="ExternalInput")
    out_d = nc.dram_tensor("out", [TOK, D], F32, kind="ExternalOutput")
    c_scr = nc.dram_tensor("c_scr", [16, 2, 2, 8, 8, 64], BF16)
    rkv_loc = nc.dram_tensor("rkv_loc", [64, TOK], BF16)
    rkv_gath = nc.dram_tensor("rkv_gath", [NCORE * 64, TOK], BF16)
    stt_loc = nc.dram_tensor("stt_loc", [64, B * 65], F32)
    stt_gath = nc.dram_tensor("stt_gath", [NCORE * 64, B * 65], F32)

    x_tm = nc.alloc_sbuf_tensor("x_tm", [128, NT, D], F32)
    xn_fm = nc.alloc_sbuf_tensor("xn_fm", [128, ND, TOK], BF16)
    UD = nc.alloc_sbuf_tensor("UD", [128, 8192], F32)
    UE = nc.alloc_sbuf_tensor("UE", [128, 10240], F32)
    UF = nc.alloc_sbuf_tensor("UF", [128, 8192], F32)
    ident = nc.alloc_sbuf_tensor("ident", [128, 128], BF16)
    identf = nc.alloc_sbuf_tensor("identf", [128, 128], F32)
    wmatA = nc.alloc_sbuf_tensor("wmatA", [128, D], BF16)
    wmatB = nc.alloc_sbuf_tensor("wmatB", [32, D], BF16)
    basiso = nc.alloc_sbuf_tensor("basiso", [128, ND, HARM], BF16)
    st = nc.alloc_sbuf_tensor("st", [128, 16], F32)
    rec = nc.alloc_sbuf_tensor("rec", [128, 8], F32)
    bia = nc.alloc_sbuf_tensor("bia", [128, 5 * ND], F32)

    def carve(arena, w0, w1, dtype=None, shape=None):
        v = arena[:, w0:w1]
        if dtype is not None:
            v = v.bitcast(dtype)
        if shape is not None:
            pat = " ".join(f"a{i}" for i in range(len(shape)))
            v = v.rearrange(f"p ({pat}) -> p {pat}",
                            **{f"a{i}": shape[i] for i in range(len(shape))})
        return v

    rkv = carve(UD, 0, 8192, BF16)
    c_fm = carve(UD, 0, 4096, None, (ND, POS))
    c_bf = carve(UD, 4096, 6144, BF16, (ND, POS))
    gf = carve(UD, 6144, 6656)
    gi = carve(UD, 6656, 7168)
    gg = carve(UD, 7168, 7680)
    htmp = carve(UD, 7680, 8192)
    htmp2 = carve(UD, 0, 2048, None, (2, D))
    st64 = carve(UD, 0, 260)
    k2 = carve(UE, 0, 2048, BF16)
    v2p = carve(UE, 2048, 4352, BF16, (32, 2, 72))
    q2 = carve(UE, 4352, 4608, BF16)
    e4 = carve(UE, 4608, 5632, BF16, (4, POS))
    o_tm = carve(UE, 5632, 7680, BF16, (4, D))
    o_fm = carve(UE, 7680, 9728, BF16, (ND, POS))
    ro_sb = carve(UE, 9728, 9984, BF16)
    q1sb = carve(UE, 0, 1024, BF16)
    kfsb = carve(UE, 1024, 2048, BF16)
    v1sb = carve(UE, 2048, 3072, BF16)
    kf_tmv = carve(UE, 3072, 3584, BF16, (NT, 64))
    vp_tmv = carve(UE, 3584, 4160, BF16, (NT, 72))
    el_t = carve(UE, 4160, 4672)
    el_e = carve(UE, 4672, 5184)
    wuo = carve(UE, 0, 8192, BF16, (2, ND, D))
    borow = carve(UE, 8192, 8704, BF16)
    ones1 = carve(UE, 8704, 8768, BF16)
    xf_tm = carve(UE, 0, 4096, None, (4, D))
    xnf_t = carve(UE, 4096, 5120, BF16, (2, ND, 128))
    bq_sb = carve(UE, 5120, 5248, BF16, (ND, HARM))
    bkv_sb = carve(UE, 5248, 5504, BF16, (ND, 64))
    stA = carve(UE, 0, 4096, BF16, (ND, D))
    stB = carve(UE, 5632, 9728, BF16, (ND, ND, 128))
    xc = carve(UF, 0, 1024)
    sq = carve(UF, 1024, 2048)
    pst = carve(UF, 2048, 3072)
    wtmp = carve(UF, 3072, 4096)
    xnt = carve(UF, 4096, 4608, BF16)
    os_bf = carve(UF, 4608, 4864, BF16)
    xnt2 = carve(UF, 4864, 5376, BF16)
    wstr = carve(UF, 0, 6144, BF16, (2, 6, ND, 128))
    wl1 = carve(UF, 6144, 6912, BF16, (3, ND, 64))
    rloc = carve(UE, 0, 1024, BF16)
    bsb = {nm: carve(UE, 1024 + 128 * i, 1152 + 128 * i, BF16, (ND, HARM))
           for i, nm in enumerate(("q", "k", "v"))}
    rm_sb = carve(UE, 7680, 9728, BF16)
    rup_sb = carve(UF, 0, 8192, BF16)
    cflat = carve(UD, 0, 8192, BF16)
    wcs = carve(UF, 0, 8192, BF16, (2, ND, D))
    hhg = carve(UD, 0, 8192, BF16, (ND, TOK))
    wzg = carve(UE, 5632, 7680, BF16, (2, 16, 128))
    sa_sb = carve(UE, 4160, 6240)
    sa4 = sa_sb.rearrange("p (r b c) -> p r b c", r=NCORE, b=B)
    wo1 = carve(UF, 0, 512, BF16)
    triu_sb = carve(UF, 512, 576, BF16)
    et_sb = carve(UF, 576, 640, BF16)
    obf = carve(UF, 640, 672, BF16)
    ofs = carve(UF, 672, 736, BF16)
    ssb = carve(UF, 736, 769, BF16)
    sacc = carve(UF, 769, 834)
    s0_sb = carve(UF, 834, 1094, None, (B, 65))
    pm_sb = carve(UF, 1094, 1102)

    PZ = nc.alloc_psum_tensor("PZ", [128, 2048], F32)
    PY = nc.alloc_psum_tensor("PY", [128, 1536], F32)
    PTB = nc.alloc_psum_tensor("PTB", [128, 1024], BF16)
    LOG = [PZ[:, 0:512], PZ[:, 512:1024]]
    PO = PZ[:, 1024:1536]
    PG = PZ[:, 1536:2048]
    PW = PY[:, 0:1024]
    PW0 = PY[:, 0:512]
    PW1 = PY[:, 512:1024]
    PT32 = PY[:, 1024:1536]

    T = {"xc": xc, "sq": sq, "st": st, "eps": st[:, 7:8], "xnt": xnt}

    def tok_wf(tok):
        return list(tok.items()) if tok else []

    p = Plan()

    p.dma("sy", lambda e: e.dma_start(out=x_tm[:],
          in_=x_d[:].rearrange("(t p) d -> p t d", p=128)))
    p.dma("sy", lambda e: e.dma_start(out=ident[:], in_=ident_d[:]))
    p.dma("sy", lambda e: e.dma_start(out=identf[:], in_=identf_d[:]))
    ds_loads = p.dma_done("sy")
    p.dma("gp", lambda e: e.dma_start(out=basiso[:],
          in_=basiso_d[:].rearrange("(a p) b -> p a b", p=128)))
    for i, nm in enumerate(("bf", "bi", "bc", "bo", "in_c")):
        p.dma("gp", lambda e, nm=nm, i=i: e.dma_start(
            out=bia[:, bass.ts(i, ND)], in_=b_d[nm][:]))
    for i, nm in enumerate(("Wq", "Wk", "Wv")):
        p.dma("gp", lambda e, nm=nm, i=i: e.dma_start(
            out=wl1[:, i, :, :], in_=wl1_d[nm][:].rearrange(
                "(a p) b -> p a b", p=128)))
    for nm in ("q", "k", "v"):
        p.dma("gp", lambda e, nm=nm: e.dma_start(
            out=bsb[nm][:],
            in_=basis_d[nm][:].rearrange("(a p) b -> p a b", p=128)))
    dl_loads = p.dma_done("gp")

    wdst = {"q": wmatA[64:96, :], "k": wmatA[0:32, :],
            "v": wmatA[32:64, :], "o": wmatB[0:32, :]}
    pa_tok = None
    for nm in ("q", "k", "v", "o"):
        w = list(pa_tok.items()) if pa_tok else []
        p.dma("gp", lambda e, nm=nm: e.dma_start(out=pst[0:32, :], in_=pa_d[nm][0][:]),
              waits=w)
        p.dma("gp", lambda e, nm=nm: e.dma_start(out=wtmp[0:32, :], in_=pa_d[nm][1][:]))
        dl_now = p.dma_done("gp")
        ph, am = pst[0:32, :], wtmp[0:32, :]
        t1, t2 = xc[0:32, :], sq[0:32, :]
        p.add("dve", lambda e, ph=ph, t1=t1: e.tensor_scalar(
            t1, ph, 1.0 / TWO_PI, MAGIC, ALU.mult, ALU.add),
            waits=dl_now)
        p.add("dve", lambda e, t1=t1: e.tensor_scalar(
            t1, t1, MAGIC, None, ALU.subtract))
        p.add("dve", lambda e, ph=ph, t1=t1: e.scalar_tensor_tensor(
            t1, t1, -TWO_PI, ph, ALU.mult, ALU.add))
        p.add("act", lambda e, t1=t1, t2=t2: e.activation(
            t2, t1, AF.Sin, bias=0.0, scale=0.5), waits=p.now("DVEa"))
        p.add("dve", lambda e, t1=t1, t2=t2: e.scalar_tensor_tensor(
            t1, t2, -2.0, t2, ALU.mult, ALU.mult),
            waits=p.now("ACTa"))
        p.add("dve", lambda e, t1=t1: e.tensor_scalar(
            t1, t1, 1.0, None, ALU.add))
        pa_tok = p.add("dve", lambda e, am=am, t1=t1, dd=wdst[nm]: e.tensor_tensor(
            dd, am, t1, ALU.mult))

    WMATS = ("wf", "uf", "wi", "ui", "wc", "uc", "uo", "wo")
    stg_cp = None
    stg_out = None
    for mi, nm in enumerate(WMATS):
        w = []
        if stg_out is not None:
            w = list(stg_out)
        p.dma("gp", lambda e, nm=nm: e.dma_start(
            out=stA[:], in_=w_d[nm][:].rearrange("(a p) b -> p a b", p=128)),
            waits=(list(stg_cp.items()) if stg_cp else []))
        stg_dma = p.dma_done("gp")
        stg_cp = p.add("dve", lambda e: e.tensor_copy(
            stB[:], stA[:].rearrange("p a (o f) -> p o a f", o=ND)),
            waits=stg_dma + (stg_out or []))
        p.dma("sy", lambda e, mi=mi: e.dma_start(
            out=scr_d[mi, :, :, :, :].rearrange("o q a f -> q o a f"),
            in_=stB[:]),
            waits=list(stg_cp.items()))
        stg_out = p.dma_done("sy")
    scr_ready = stg_out

    p.add("dve", lambda e: e.memset(st[:, 7:8], 1e-5))
    evac_tok = [None, None]
    mm_tok = [None, None]
    xn_cp = None
    act_tok = [None, None]
    for ft in range(B * S // 128):
        rb = ft % 4
        pw2 = ft % 2
        sb_ = 8 * pw2
        xnt_p = (xnt, xnt2)[pw2]
        sqr_p = (xc, sq)[pw2]
        p.dma("sy", lambda e, ft=ft, rb=rb: e.dma_start(
            out=xf_tm[:, rb, :],
            in_=xfull_d[:].rearrange("(t p) d -> p t d", p=128)[:, ft, :]),
            waits=((mm_tok[rb % 2] or []) if ft >= 4 else
                   (p.now("DVEa") if ft == 0 else [])))
        xf_dma = p.dma_done("sy")
        x_t = xf_tm[:, rb, :]
        p.add("dve", lambda e, x_t=x_t, sb_=sb_: e.reduce_sum(
            st[:, sb_ + 0:sb_ + 1], x_t, axis=AX.X),
            waits=xf_dma + (act_tok[pw2] or [])
            + (scr_ready if ft == 0 else []))
        p.add("dve", lambda e, sb_=sb_: e.tensor_scalar(
            st[:, sb_ + 1:sb_ + 2], st[:, sb_ + 0:sb_ + 1],
            1.0 / 1024.0, None, ALU.mult))
        p.add("dve", lambda e, sb_=sb_: e.tensor_scalar(
            st[:, sb_ + 5:sb_ + 6], st[:, sb_ + 0:sb_ + 1],
            -1.0 / 1024.0, None, ALU.mult))
        p.add("act", lambda e, x_t=x_t, sb_=sb_, sqr_p=sqr_p: e.activation(
            sqr_p, x_t, AF.Square, bias=st[:, sb_ + 5:sb_ + 6], scale=1.0,
            accum_out=st[:, sb_ + 2:sb_ + 3]), waits=p.now("DVEa"))
        p.add("act", lambda e, sb_=sb_: e.activation(
            st[:, sb_ + 3:sb_ + 4], st[:, sb_ + 2:sb_ + 3], AF.Ln,
            bias=T["eps"], scale=1.0 / 1024.0))
        act_tok[pw2] = list(p.add("act", lambda e, sb_=sb_: e.activation(
            st[:, sb_ + 4:sb_ + 5], st[:, sb_ + 3:sb_ + 4], AF.Exp,
            bias=0.0, scale=-0.5)).items())
        w = [("ACTa", p.c("ACTa"))]
        if ft > 1:
            w += [("TTf", p.c("TTf") - 1)]
        p.add("dve", lambda e, x_t=x_t, sb_=sb_, xnt_p=xnt_p: e.tensor_scalar(
            xnt_p, x_t, st[:, sb_ + 1:sb_ + 2], st[:, sb_ + 4:sb_ + 5],
            ALU.subtract, ALU.mult), waits=w)
        mult_tok = p.now("DVEa")
        for dt in range(ND):
            w2 = list(mult_tok) if dt == 0 else []
            if dt == 0:
                w2 += p.now("DVEa")
            p.add("te", lambda e, dt=dt, xnt_p=xnt_p: e.transpose(
                PTB[:, bass.ts(dt, 128)], xnt_p[:, bass.ts(dt, 128)], ident[:]),
                waits=w2, incs=([("TTf", 1)] if dt == ND - 1 else None))
        xb = pw2
        xn_cp = p.add("act", lambda e, xb=xb: e.copy(
            xnf_t[:, xb, :, :], PTB[:].rearrange("p (a b) -> p a b", a=ND)),
            waits=[("TTf", p.c("TTf"))] + (mm_tok[xb] or []))
        for dt in range(ND):
            w2 = []
            if dt == 0:
                w2 = list(xn_cp.items()) + (evac_tok[xb] or [])
                if ft == 0:
                    w2 += dl_loads
            p.add("te", lambda e, dt=dt, xb=xb: e.matmul(
                PW[0:64, bass.ts(xb, 512)][:, 0:128], bkv_sb[:, dt, :],
                xnf_t[:, xb, dt, :], start=(dt == 0), stop=(dt == ND - 1)),
                waits=w2, incs=([("TEx", 1)] if dt == ND - 1 else None))
        mm_tok[xb] = [("TEx", p.c("TEx"))]
        evac_tok[xb] = list(p.add("act", lambda e, ft=ft, xb=xb: e.copy(
            rkv[0:64, ft * 128:(ft + 1) * 128],
            PW[0:64, bass.ts(xb, 512)][:, 0:128]),
            waits=[("TEx", p.c("TEx"))]).items())
        rkv_kv_done = p.now("ACTa")

    p.add("dve", lambda e: e.memset(st[:, 6:7], 0.0), waits=ds_loads)
    ln_t_phase(p, T, NT, x_tm, xn_fm, PTB, ident[:], sem="TT",
               first_waits=p.now("TTf"))

    prj_prev = None
    prj_dst = {"q": rkv[64:96, 0:TOK], "k": rloc[0:32, 0:TOK],
               "v": rloc[32:64, 0:TOK]}
    for pi, pn in enumerate(("q", "k", "v")):
        for n in range(4):
            pjb = PG if (n % 2 == 0) else PO
            for dt in range(ND):
                w = []
                if dt == 0:
                    w = p.now("DVEa") + dl_loads + tok_wf(prj_prev)
                p.add("te", lambda e, pn=pn, n=n, dt=dt, pjb=pjb: e.matmul(
                    pjb[0:HARM, :], bsb[pn][:, dt, :],
                    xn_fm[:, dt, bass.ts(n, 512)],
                    start=(dt == 0), stop=(dt == ND - 1)),
                    waits=w, incs=([("TEp", 1)] if dt == ND - 1 else None))
            prj_prev = p.add("act", lambda e, pn=pn, n=n, pjb=pjb: e.copy(
                prj_dst[pn][:, bass.ts(n, 512)], pjb[0:HARM, :]),
                waits=[("TEp", p.c("TEp"))])
    p.dma("sy", lambda e: e.dma_start(out=rkv_loc[:], in_=rloc[0:64, 0:TOK]),
          waits=p.now("ACTa"))
    cc_in_done = p.dma_done("sy")
    p.add("gp", lambda e: e.collective_compute(
        "AllGather", ALU.bypass, replica_groups=[list(range(NCORE))],
        ins=[rkv_loc[:]], outs=[rkv_gath[:]]),
        waits=cc_in_done, incs=[("CC", 16)])
    p.dma("sy", lambda e: e.dma_start(
        out=rkv[0:32, :],
        in_=rkv_gath[:].rearrange("(r hh) (b pp) -> hh b r pp",
                                  hh=64, b=B)[0:32, :, :, :]),
        waits=[("CC", 16)])
    p.dma("sy", lambda e: e.dma_start(
        out=rkv[32:64, :],
        in_=rkv_gath[:].rearrange("(r hh) (b pp) -> hh b r pp",
                                  hh=64, b=B)[32:64, :, :, :]),
        waits=[("CC", 16)])
    rkv_ready = p.dma_done("sy")

    def tok_w(tok):
        return list(tok.items()) if tok else []

    rq_prev = None
    for n in range(4):
        for dt in range(ND):
            w = []
            if dt == 0:
                w = p.now("DVEa") + tok_w(rq_prev)
            p.add("te", lambda e, n=n, dt=dt: e.matmul(
                PG[0:HARM, :], bq_sb[:, dt, :], xn_fm[:, dt, bass.ts(n, 512)],
                start=(dt == 0), stop=(dt == ND - 1)),
                waits=w, incs=([("TEx", 1)] if dt == ND - 1 else None))
        rq_prev = p.add("act", lambda e, n=n: e.copy(
            rkv[64:96, bass.ts(n, 512)], PG[0:HARM, :]),
            waits=[("TEx", p.c("TEx"))])
    rq_done = list(rq_prev.items())

    p.add("dve", lambda e: e.memset(
        v2p[:].rearrange("p a b c -> p (a b) c")[:, :, 64:65], 1.0))

    exp_bank = [None, None]
    eslot = [None] * 4
    k_free = [None, None]
    q_free = [None, None]
    v_free = [None, None]
    pw_tok = None
    pg_tok = None
    po_tok = None
    pt32_tok = None
    otm_tok = None
    ofm_cp = None
    wm_tok = pa_tok
    ln_done = p.now("DVEa")
    hi = 0
    kexp_cp = None
    for b in range(B):
        for h in range(NH):
            kb = hi % 2
            for n in range(ND):
                w = []
                if hi == 0 and n == 0:
                    w += tok_w(wm_tok) + rkv_ready
                if n == 0:
                    w += tok_w(k_free[kb])
                if n >= 2 and n % 2 == 0:
                    w += tok_w(kexp_cp)
                if n == 0 and pw_tok:
                    w += tok_w(pw_tok)
                p.add("te", lambda e, h=h, b=b, n=n: e.matmul(
                    PW[0:64, bass.ts(n % 2, 512)], wmatA[0:32, bass.ts(h, HD)],
                    rkv[0:32, b * 4096 + n * 512: b * 4096 + (n + 1) * 512],
                    start=True, stop=True), waits=w, incs=[("TEk", 1)])
                if n % 2 == 1:
                    tek = p.c("TEk")
                    kexp_cp = p.add(
                        "dve", lambda e, kb=kb, n=n:
                        e.tensor_copy(k2[bass.ts(kb, 64),
                                         (n - 1) * 512:(n + 1) * 512],
                                      PW[0:64, :]),
                        waits=[("TEk", tek)])
            pw_tok = kexp_cp
            p.add("te", lambda e, h=h, b=b: e.matmul(
                PW[0:64, 0:512], wmatA[64:96, bass.ts(h, HD)],
                rkv[64:96, b * 512:(b + 1) * 512], start=True, stop=True),
                waits=tok_w(pw_tok) + tok_w(q_free[kb]), incs=[("TEk", 1)])
            tek = p.c("TEk")
            q_cp = p.add("dve", lambda e, kb=kb: e.tensor_copy(
                q2[bass.ts(kb, 64), :], PW[0:64, 0:512]), waits=[("TEk", tek)])
            pw_tok = q_cp
            for g in range(4):
                for j in range(8):
                    kt = g * 8 + j
                    w = []
                    if g == 0 and j == 0:
                        w += (tok_w(pg_tok) if pg_tok else rq_done) + tok_w(v_free[kb])
                    if j == 0 and g > 0:
                        w += tok_w(v_cp)
                    p.add("te", lambda e, b=b, h=h, kt=kt, j=j: e.matmul(
                        PG[:, bass.ts(j, 64)],
                        rkv[32:64, b * 4096 + kt * 128:b * 4096 + (kt + 1) * 128],
                        wmatA[32:64, bass.ts(h, HD)],
                        start=True, stop=True), waits=w, incs=[("TEk", 1)])
                tek = p.c("TEk")
                v_cp = p.add("dve", lambda e, kb=kb, g=g: e.tensor_copy(
                    v2[:, kb, bass.ts(g, 8), 0:64],
                    PG[:].rearrange("p (a b) -> p a b", a=8)),
                    waits=[("TEk", tek)])
            pg_tok = v_cp
            for kt in range(32):
                w = [("ACTa", exp_bank[kt % 2]["ACTa"])] if exp_bank[kt % 2] else []
                if kt == 0:
                    w += tok_w(kexp_cp) + tok_w(q_cp)
                qk = p.add("te", lambda e, kb=kb, kt=kt: e.matmul(
                    LOG[kt % 2][:, :], k2[bass.ts(kb, 64), bass.ts(kt, 128)],
                    q2[bass.ts(kb, 64), :], start=True, stop=True),
                    waits=w, incs=[("TEl", 1)])
                k_free[kb] = {"TEl": p.c("TEl")}
                q_free[kb] = {"TEl": p.c("TEl")}
                w = [("TEl", p.c("TEl"))]
                if eslot[kt % 4]:
                    w += [("TEo", eslot[kt % 4]["TEo"])]
                ex = p.add("act", lambda e, kt=kt: e.activation(
                    e4[:, kt % 4, :], LOG[kt % 2][:, :], AF.Exp,
                    bias=0.0, scale=0.125), waits=w)
                exp_bank[kt % 2] = ex
                w = [("ACTa", ex["ACTa"])]
                if kt == 0:
                    w += tok_w(po_tok) + tok_w(v_cp)
                p.add("te", lambda e, kb=kb, kt=kt: e.matmul(
                    PO[0:65, :], v2[:, kb, kt, 0:65], e4[:, kt % 4, :],
                    start=(kt == 0), stop=(kt == 31)),
                    waits=w, incs=[("TEo", 1)])
                eslot[kt % 4] = {"TEo": p.c("TEo")}
                v_free[kb] = {"TEo": p.c("TEo")}
            po_tok = p.add("dve", lambda e: e.tensor_copy(os_sb[0:65, 0:512], PO[0:65, :]),
                           waits=[("TEo", p.c("TEo"))] + tok_w(otm_tok)
                           + p.now("TEt"))
            trs = []
            for i in range(4):
                w = tok_w(po_tok) if i == 0 else []
                if i == 0 and pt32_tok:
                    w += tok_w(pt32_tok)
                p.add("te", lambda e, i=i: e.transpose(
                    PT32[:, i * 128:i * 128 + 65], os_sb[0:65, bass.ts(i, 128)],
                    identf[0:65, 0:65]), waits=w, incs=[("TEt", 1)])
            tet = p.c("TEt")
            for i in range(4):
                p.add("dve", lambda e, i=i: e.reciprocal(
                    rec[:, i:i + 1], PT32[:, i * 128 + 64:i * 128 + 65]),
                    waits=([("TEt", tet)] if i == 0 else []))
                pt32_tok = p.add("dve", lambda e, i=i, h=h: e.tensor_scalar(
                    o_tm[:, i, bass.ts(h, HD)], PT32[:, i * 128:i * 128 + 64],
                    rec[:, i:i + 1], None, ALU.mult))
            hi += 1
        transpose_tm_to_fm(p, T, o_tm, o_fm, 4, PTB, ident[:], sem="TTo",
                           src_ready=p.now("DVEa"))
        otm_tok = p.now("TTo") and {"TTo": p.c("TTo")}
        ofm_cp = p.now("DVEa")
        for dt in range(ND):
            w = list(ofm_cp) if dt == 0 else []
            if dt == 0:
                w += tok_w(pg_tok)
            p.add("te", lambda e, dt=dt: e.matmul(
                PG[0:32, :], basiso[:, dt, :], o_fm[:, dt, :],
                start=(dt == 0), stop=(dt == ND - 1)), waits=w,
                incs=([("TEr", 1)] if dt == ND - 1 else None))
        ro_cp = p.add("act", lambda e: e.copy(ro_sb[0:32, :], PG[0:32, :]),
                      waits=[("TEr", p.c("TEr"))])
        pg_tok = ro_cp
        for qt in range(4):
            w = (tok_w(ro_cp) + tok_w(poA_tok) + tok_w(poB_tok)) if qt == 0 else []
            w += tok_w(pw_tok)
            for half in range(2):
                p.add("te", lambda e, qt=qt, half=half: e.matmul(
                    PW[:, bass.ts(half, 512)], ro_sb[0:32, bass.ts(qt, 128)],
                    wmatB[0:32, bass.ts(half, 512)],
                    start=True, stop=True),
                    waits=(w if half == 0 else []), incs=[("TEk", 1)])
            tek = p.c("TEk")
            pw_tok = p.add("dve", lambda e, b=b, qt=qt: e.tensor_add(
                x_tm[:, b * 4 + qt, :], x_tm[:, b * 4 + qt, :], PW[:, :]),
                waits=[("TEk", tek)])

    for dt in range(ND):
        p.add("dve", lambda e, dt=dt: e.memset(c_fm[:, dt, :], 1.0),
              waits=(dl_loads + p.now("TEk", "TEl", "TEo")
                     if dt == 0 else []))
        p.add("dve", lambda e, dt=dt: e.tensor_scalar(
            c_fm[:, dt, :], c_fm[:, dt, :], bia[:, 4 * ND + dt:4 * ND + dt + 1],
            None, ALU.mult))
    cbf_tok = p.add("dve", lambda e: e.tensor_copy(c_bf[:], c_fm[:]))
    ws_tok = [None, None]
    gact = [None, None, None]
    gsb_tok = None
    GACT_F = [AF.Sigmoid, AF.Sigmoid, AF.Tanh]
    gate_w = ("wf", "wi", "wc")
    for t in range(4):
        for ot in range(ND):
            s = (t * ND + ot) % 2
            for g in range(3):
                w = []
                if g == 0:
                    w = p.now("TEf")
                    if t == 0 and ot == 0:
                        w += p.now("TEt", "ACTa", "DVEa") + scr_ready
                p.dma("gp", lambda e, g=g, ot=ot, s=s: e.dma_start(
                    out=wstr[:, s, g, :, :],
                    in_=scr_d[2 * g, ot, :, :, :]),
                    waits=w)
            for g in range(3):
                p.dma("gp", lambda e, g=g, ot=ot, s=s: e.dma_start(
                    out=wstr[:, s, 3 + g, :, :],
                    in_=scr_d[2 * g + 1, ot, :, :, :]))
            dl_now = p.dma_done("gp")
            for g in range(3):
                pgm = PZ[:, bass.ts(g, 512)]
                for dt in range(ND):
                    w = []
                    if dt == 0:
                        w = list(dl_now) + tok_w(gact[g])
                        if g == 0:
                            w += tok_w(cbf_tok)
                        if t == 0 and ot == 0:
                            w += tok_w(exp_bank[0]) + tok_w(exp_bank[1])
                    p.add("te", lambda e, g=g, dt=dt, ot=ot, t=t, s=s, pgm=pgm: e.matmul(
                        pgm, wstr[:, s, g, dt, :], xn_fm[:, dt, bass.ts(t, 512)],
                        start=(dt == 0), stop=False), waits=w)
                for dt in range(ND):
                    p.add("te", lambda e, g=g, dt=dt, ot=ot, s=s, pgm=pgm: e.matmul(
                        pgm, wstr[:, s, 3 + g, dt, :], c_bf[:, dt, :],
                        start=False, stop=(dt == ND - 1)),
                        incs=([("TEf", 1)] if dt == ND - 1 else None))
                tef = p.c("TEf")
                gact[g] = p.add("act", lambda e, g=g, ot=ot, pgm=pgm,
                                d=(gf, gi, gg)[g]: e.activation(
                    d[:], pgm, GACT_F[g], bias=bia[:, g * ND + ot:g * ND + ot + 1],
                    scale=1.0), waits=[("TEf", tef)] + (tok_w(gsb_tok) if g == 0 else []))
            ws_tok[s] = {"TEf": p.c("TEf")}
            w = tok_w(gact[0]) + tok_w(gact[1]) + tok_w(gact[2])
            p.add("dve", lambda e, ot=ot: e.tensor_mul(
                gf[:], gf[:], c_fm[:, ot, :]), waits=w)
            p.add("dve", lambda e: e.tensor_mul(gi[:], gi[:], gg[:]))
            gsb_tok = p.add("dve", lambda e, ot=ot: e.tensor_add(
                c_fm[:, ot, :], gf[:], gi[:]))
        cbf_tok = p.add("dve", lambda e: e.tensor_copy(c_bf[:], c_fm[:]))
    cuo_tok = [None] * ND
    for ot in range(ND):
        s = ot % 2
        p.dma("gp", lambda e, ot=ot, s=s: e.dma_start(
            out=wstr[:, s, 0, :, :], in_=scr_d[6, ot, :, :, :]),
            waits=p.now("TEf"))
        dl_now = p.dma_done("gp")
        for dt in range(ND):
            w = list(dl_now) if dt == 0 else []
            if dt == 0:
                w += tok_w(cbf_tok)
            if dt == 0 and ot >= 2:
                w += tok_w(cuo_tok[ot - 2])
            if dt == 0 and ot < 2:
                w += tok_w(gact[0]) + tok_w(gact[1]) + tok_w(gact[2])
            p.add("te", lambda e, ot=ot, dt=dt, s=s: e.matmul(
                PZ[:, bass.ts(ot % 2, 512)], wstr[:, s, 0, dt, :], c_bf[:, dt, :],
                start=(dt == 0), stop=(dt == ND - 1)),
                waits=w, incs=([("TEf", 1)] if dt == ND - 1 else None))
        ws_tok[s] = {"TEf": p.c("TEf")}
        cuo_tok[ot] = p.add("dve", lambda e, ot=ot: e.tensor_copy(
            cuo[:, ot, :], PZ[:, bass.ts(ot % 2, 512)]),
            waits=[("TEf", p.c("TEf"))])
    ht_tok = [None] * ND
    haddB = [tok_w(cuo_tok[ND - 2]), tok_w(cuo_tok[ND - 1])]
    tet_after = None
    for t in range(4):
        for ot in range(ND):
            s = ot % 2
            p.dma("gp", lambda e, ot=ot, s=s: e.dma_start(
                out=wstr[:, s, 1, :, :], in_=scr_d[7, ot, :, :, :]),
                waits=p.now("TEf"))
            dl_now = p.dma_done("gp")
            for dt in range(ND):
                w = list(dl_now) if dt == 0 else []
                if dt == 0:
                    w += list(haddB[ot % 2])
                p.add("te", lambda e, t=t, ot=ot, dt=dt, s=s: e.matmul(
                    PZ[:, bass.ts(ot % 2, 512)], wstr[:, s, 1, dt, :],
                    xn_fm[:, dt, bass.ts(t, 512)],
                    start=(dt == 0), stop=(dt == ND - 1)),
                    waits=w, incs=([("TEf", 1)] if dt == ND - 1 else None))
            ws_tok[s] = {"TEf": p.c("TEf")}
            hadd = p.add("dve", lambda e, ot=ot: e.tensor_add(
                htmp[:], cuo[:, ot, :], PZ[:, bass.ts(ot % 2, 512)]),
                waits=[("TEf", p.c("TEf"))] + tok_w(cuo_tok[ot]) + p.now("ACTa"))
            haddB[ot % 2] = tok_w(hadd)
            aw = p.now("DVEa")
            if tet_after is not None:
                aw = aw + [("TEt", tet_after)]
            ht_tok[ot] = p.add("act", lambda e, ot=ot: e.activation(
                h_fm[:, ot, :], htmp[:], AF.Tanh,
                bias=bia[:, 3 * ND + ot:3 * ND + ot + 1], scale=1.0),
                waits=aw)
        hw = []
        for g in range(ND):
            hw += tok_w(ht_tok[g])
        for pt in range(4):
            for rnd in range(2):
                for j in range(4):
                    dt = rnd * 4 + j
                    w = []
                    if pt == 0 and rnd == 0 and j == 0:
                        w += hw
                    if j == 0 and pt32_tok:
                        w += tok_w(pt32_tok)
                    p.add("te", lambda e, t=t, pt=pt, dt=dt, j=j: e.transpose(
                        PT32[:, bass.ts(j, 128)], h_fm[:, dt, bass.ts(pt, 128)],
                        identf[:]), waits=w,
                        incs=([("TEt", 1)] if j == 3 else None))
                tet = p.c("TEt")
                pt32_tok = p.add("dve", lambda e, t=t, pt=pt, rnd=rnd: e.tensor_add(
                    x_tm[:, t * 4 + pt, bass.ts(rnd, 512)],
                    x_tm[:, t * 4 + pt, bass.ts(rnd, 512)], PT32[:, 0:512]),
                    waits=[("TEt", tet)])
        tet_after = p.c("TEt")

    ln_t_phase(p, T, NT, x_tm, xn_fm, PTB, ident[:], sem="TT1",
               first_waits=p.now("TEf"))
    xn1_done = p.now("DVEa")

    l1dst = [q1sb[0:64, 0:TOK], kfsb[0:64, 0:TOK], v1sb[0:64, 0:TOK]]
    prev = None
    for i in range(3):
        for n in range(4):
            for dt in range(ND):
                w = []
                if dt == 0:
                    w = list(xn1_done) + dl_loads
                    w += tok_w(prev)
                    if i == 0 and n == 0:
                        w += tok_w(pg_tok) + tok_w(exp_bank[1])
                p.add("te", lambda e, i=i, n=n, dt=dt: e.matmul(
                    PG[0:64, :], wl1[:, i, dt, :], xn_fm[:, dt, bass.ts(n, 512)],
                    start=(dt == 0), stop=(dt == ND - 1)),
                    waits=w, incs=([("TEr", 1)] if dt == ND - 1 else None))
            ter = p.c("TEr")
            if i < 2:
                p.add("dve", lambda e: e.tensor_scalar(
                    el_t[0:64, :], PG[0:64, :], 0.0, None, ALU.min),
                    waits=[("TEr", ter)] + (p.now("TEt", "TEf", "TEo", "TEl",
                                                  "ACTa")
                                            if (i == 0 and n == 0) else []))
                p.add("act", lambda e: e.activation(
                    el_e[0:64, :], el_t[0:64, :], AF.Exp, bias=0.0, scale=1.0),
                    waits=p.now("DVEa"))
                prev = p.add("dve", lambda e, i=i, n=n: e.scalar_tensor_tensor(
                    l1dst[i][:, bass.ts(n, 512)], PG[0:64, :], 0.0, el_e[0:64, :],
                    ALU.max, ALU.add), waits=p.now("ACTa"))
            else:
                prev = p.add("act", lambda e, i=i, n=n: e.copy(
                    l1dst[i][:, bass.ts(n, 512)], PG[0:64, :]),
                    waits=[("TEr", ter)])
    qkv_done = p.now("DVEa", "ACTa")
    p.add("dve", lambda e: e.memset(vp_tmv[:, :, 64:65], 1.0))
    for src, dst, tag in ((kfsb[0:64, 0:TOK], kf_tmv, "K"),
                          (v1sb[0:64, 0:TOK], vp_tmv, "V")):
        for rnd in range(2):
            for j in range(8):
                kt = rnd * 8 + j
                w = list(qkv_done) if (rnd == 0 and j == 0) else []
                if j == 0:
                    w += p.now("DVEa")
                p.add("te", lambda e, src=src, kt=kt, j=j: e.transpose(
                    PTB[:, j * 64:(j + 1) * 64],
                    src[:, bass.ts(kt, 128)], ident[0:64, 0:64]),
                    waits=w, incs=([("TEt", 1)] if j == 7 else None))
            tet = p.c("TEt")
            p.add("dve", lambda e, dst=dst, rnd=rnd: e.tensor_copy(
                dst[:, bass.ts(rnd, 8), 0:64],
                PTB[:, 0:512].rearrange("p (a b) -> p a b", a=8)),
                waits=[("TEt", tet)])
    tr_done = p.now("DVEa")
    for b in range(B):
        for ct in range(4):
            w = tr_done + tok_w(prev) if ct == 0 else []
            p.add("te", lambda e, b=b, ct=ct: e.matmul(
                PG[0:64, 0:65], kf_tmv[:, b * 4 + ct, 0:64],
                vp_tmv[:, b * 4 + ct, 0:65],
                start=(ct == 0), stop=(ct == 3)),
                waits=w, incs=([("TEr", 1)] if ct == 3 else None))
        prev = p.add("dve", lambda e, b=b: e.tensor_copy(
            st64[0:64, b * 65:(b + 1) * 65], PG[0:64, 0:65]),
            waits=[("TEr", p.c("TEr"))])
    p.dma("sy", lambda e: e.dma_start(out=stt_loc[:], in_=st64[0:64, :]),
          waits=p.now("DVEa"))
    st_in_done = p.dma_done("sy")
    p.add("gp", lambda e: e.collective_compute(
        "AllGather", ALU.bypass, replica_groups=[list(range(NCORE))],
        ins=[stt_loc[:]], outs=[stt_gath[:]]),
        waits=st_in_done, incs=[("CC", 16)])
    p.dma("sy", lambda e: e.dma_start(
        out=sa_sb[0:64, :],
        in_=stt_gath[:].rearrange("(r kk) bc -> kk r bc", kk=64)),
        waits=[("CC", 32)] + p.now("DVEa"))
    sa_ready = p.dma_done("sy")

    PGc = PZ[:, 1536:2048]
    PECc = PZ[:, 0:512]
    POc = PZ[:, 512:1024]
    PSc = PZ[:, 1024:1536]
    p.dma("gp", lambda e: e.dma_start(
        out=rm_sb[0:16, 0:4096], in_=rm_d[:].rearrange("j h r i -> j (h r i)")),
        waits=p.now("DVEa", "ACTa", "TEr", "TEf", "TEt"))
    p.dma("gp", lambda e: e.dma_start(
        out=rup_sb[0:16, 0:16384], in_=rup_d[:].rearrange("j h r k -> j (h r k)")))
    rails_done = p.dma_done("gp")
    ccp = None
    for blk in range(32):
        for j in range(8):
            hr = blk * 8 + j
            w = []
            if blk == 0 and j == 0:
                w = list(rails_done) + p.now('DVEa', 'TEf')
            if j == 0 and ccp is not None:
                w = w + tok_w(ccp)
            p.add("te", lambda e, hr=hr, j=j: e.matmul(
                PGc[0:16, bass.ts(j, 64)], rm_sb[0:16, hr * 16:(hr + 1) * 16],
                rup_sb[0:16, hr * 64:(hr + 1) * 64], start=True, stop=True),
                waits=w, incs=([("TEc", 1)] if j == 7 else None))
        ccp = p.add("dve", lambda e, blk=blk: e.tensor_copy(
            cflat[0:16, bass.ts(blk, 512)], PGc[0:16, :]),
            waits=[("TEc", p.c("TEc"))]
            + (p.now("TEf") + st_in_done if blk == 0 else []))
    p.dma("sy", lambda e: e.dma_start(
        out=c_scr[:].rearrange("i a b g rl k -> i (a b g rl k)"),
        in_=cflat[0:16, :]),
        waits=p.now("DVEa"))
    cscr_done = p.dma_done("sy")
    p.dma("gp", lambda e: e.dma_start(out=wo1[0:64, 0:1024], in_=wo1_d[:]),
          waits=p.now("TEc"))
    p.dma("sy", lambda e: e.dma_start(out=triu_sb[0:128, 0:128], in_=triu_d[:]),
          waits=p.now("TEc"))
    p.dma("sy", lambda e: e.dma_start(out=pm_sb[0:64, 0:NCORE], in_=pm_d[:]))
    late_done = p.dma_done("sy") + p.dma_done("gp")

    for b in range(B):
        p.add("dve", lambda e, b=b: e.tensor_scalar(
            s0_sb[0:64, b, :], sa4[0:64, :, :, :][:, 0, b, :], pm_sb[0:64, 0:1],
            None, ALU.mult),
            waits=(sa_ready + late_done + p.now("TEc") if b == 0 else []))
        for r in range(1, NCORE):
            p.add("dve", lambda e, b=b, r=r: e.scalar_tensor_tensor(
                s0_sb[0:64, b, :], sa4[0:64, :, :, :][:, r, b, :],
                pm_sb[0:64, r:r + 1], s0_sb[0:64, b, :], ALU.mult, ALU.add))
    l1pfx_done = p.now("DVEa")
    sup_tok = None
    for b in range(B):
        for ct in range(4):
            tile = b * 4 + ct
            if ct == 0:
                sw = list(l1pfx_done) + list(tr_done) if b == 0 else p.now("TEe")
                scp = p.add("dve", lambda e, b=b: e.tensor_copy(
                    ssb[0:64, 0:65], s0_sb[0:64, b, :]), waits=sw)
            else:
                scp = p.add("dve", lambda e, b=b: e.tensor_add(
                    ssb[0:64, 0:65], s0_sb[0:64, b, :], sacc[0:64, 0:65]),
                    waits=p.now("TEe"))
            w = p.now("DVEa")
            p.add("te", lambda e, tile=tile: e.matmul(
                PECc[0:128, 0:128], kfsb[0:64, 0:TOK][:, bass.ts(tile, 128)],
                q1sb[0:64, 0:TOK][:, bass.ts(tile, 128)], start=True, stop=True),
                waits=w, incs=[("TEe", 1)])
            mcp = p.add("dve", lambda e: e.tensor_tensor(
                et_sb[0:128, 0:128], PECc[0:128, 0:128], triu_sb[0:128, 0:128],
                ALU.mult),
                waits=[("TEe", p.c("TEe"))])
            p.add("te", lambda e, tile=tile: e.matmul(
                POc[0:128, 0:65], et_sb[0:128, 0:128], vp_tmv[:, tile, 0:65],
                start=True, stop=False), waits=tok_w(mcp) + tok_w(sup_tok))
            p.add("te", lambda e, tile=tile: e.matmul(
                POc[0:128, 0:65], q1sb[0:64, 0:TOK][:, bass.ts(tile, 128)],
                ssb[0:64, 0:65],
                start=False, stop=True), incs=[("TEe", 1)])
            p.add("dve", lambda e: e.tensor_scalar(
                rec[:, 0:1], POc[:, 64:65], 1e-6, None, ALU.add),
                waits=[("TEe", p.c("TEe"))])
            p.add("dve", lambda e: e.reciprocal(rec[:, 1:2], rec[:, 0:1]))
            ob = p.add("dve", lambda e: e.tensor_scalar(
                obf[0:128, 0:64], POc[:, 0:64], rec[:, 1:2], None, ALU.mult))
            p.add("te", lambda e: e.transpose(
                PTB[0:64, 0:128], obf[0:128, 0:64], ident[:]),
                waits=tok_w(ob) + p.now("DVEa"), incs=[("TEt", 1)])
            ocp = p.add("act", lambda e: e.copy(ofs[0:64, 0:128], PTB[0:64, 0:128]),
                waits=[("TEt", p.c("TEt"))])
            for half in range(2):
                w = tok_w(ocp) + tok_w(pw_tok) if half == 0 else []
                p.add("te", lambda e, half=half: e.matmul(
                    PW[:, bass.ts(half, 512)], ofs[0:64, 0:128],
                    wo1[0:64, bass.ts(half, 512)], start=True, stop=True),
                    waits=w + (late_done if (b == 0 and ct == 0 and half == 0) else []),
                    incs=([("TEe", 1)] if half == 1 else None))
            pw_tok = p.add("dve", lambda e, tile=tile: e.tensor_add(
                x_tm[:, tile, :], x_tm[:, tile, :], PW[:, :]),
                waits=[("TEe", p.c("TEe"))])
            if ct < 3:
                p.add("te", lambda e, tile=tile: e.matmul(
                    PSc[0:64, 0:65], kf_tmv[:, tile, 0:64], vp_tmv[:, tile, 0:65],
                    start=True, stop=True),
                    waits=p.now("DVEa"), incs=[("TEs", 1)])
                if ct == 0:
                    p.add("dve", lambda e: e.tensor_copy(
                        sacc[0:64, 0:65], PSc[0:64, 0:65]),
                        waits=[("TEs", p.c("TEs"))])
                else:
                    p.add("dve", lambda e: e.tensor_add(
                        sacc[0:64, 0:65], sacc[0:64, 0:65], PSc[0:64, 0:65]),
                        waits=[("TEs", p.c("TEs"))])
            sup_tok = {"TEs": p.c("TEs")}

    wz_tok = [None, None]
    wc_tok = [None, None]
    gelu_toks = [None] * ND
    zfree = None
    hfree = None
    for g in range(ND):
        sg = g % 2
        mz = p.add("dve", lambda e, sg=sg: e.memset(wzg[:, sg, :, :], 0.0),
                   waits=tok_w(wz_tok[sg]) + (cscr_done if g == 0 else []))
        for rl in range(8):
            p.dma("gp", lambda e, g=g, sg=sg, rl=rl: e.dma_start(
                out=wzg[rl * 16:(rl + 1) * 16, sg, 2 * rl:2 * rl + 2, :]
                .rearrange("p a f -> p a f") if False else
                wzg[rl * 16:(rl + 1) * 16, sg, :, :].rearrange(
                    "p m f -> p (m f)")[:, (2 * rl) * 128:(2 * rl + 2) * 128]
                .rearrange("p (a b k) -> p a b k", a=2, b=2),
                in_=c_scr[:, :, :, g, rl, :]),
                waits=tok_w(mz) if rl == 0 else [])
        wzdma = p.dma_done("gp")
        for half in range(2):
            hg = g * 2 + half
            sc = hg % 2
            p.dma("gp", lambda e, g=g, half=half, sc=sc: e.dma_start(
                out=wcs[:, sc, :, :],
                in_=wc_d[:].rearrange("(gg m p) d -> p gg m d", m=8, p=128)[
                    :, 2 * g + half, :, :]),
                waits=tok_w(wc_tok[sc]) + (cscr_done + p.now("TEe", "DVEa")
                                           if (g == 0 and half == 0) else []))
            wcdma = p.dma_done("gp")
            for m in range(8):
                mi = half * 8 + m
                for n in range(4):
                    w = []
                    if n == 0:
                        w = tok_w(zfree)
                    if m == 0 and n == 0:
                        w = w + wzdma + tok_w(hfree)
                        if hg == 0:
                            w += p.now("ACTa", "DVEa") + st_in_done
                    p.add("te", lambda e, g=g, mi=mi, n=n: e.matmul(
                        PZ[:, bass.ts(n, 512)], wzg[:, g % 2, mi, :],
                        xn_fm[:, g, bass.ts(n, 512)], start=True, stop=True),
                        waits=w, incs=([("TEz", 1)] if n == 3 else None))
                zfree = p.add("act", lambda e, m=m: e.activation(
                    hhg[:, m, :], PZ[:, 0:2048], AF.Gelu, bias=0.0, scale=1.0),
                    waits=[("TEz", p.c("TEz"))])
            for tt in range(NT):
                for m in range(8):
                    w = []
                    if tt == 0 and m == 0:
                        w = wcdma + p.now("ACTa")
                    if m == 0:
                        w = w + tok_w(pw_tok)
                    for half2 in range(2):
                        p.add("te", lambda e, tt=tt, m=m, sc=sc, half2=half2:
                              e.matmul(PW[:, bass.ts(half2, 512)],
                                       hhg[:, m, bass.ts(tt, 128)],
                                       wcs[:, sc, m, bass.ts(half2, 512)],
                                       start=(m == 0), stop=(m == 7)),
                              waits=(w if half2 == 0 else []),
                              incs=([("TEe", 1)] if (m == 7 and half2 == 1)
                                    else None))
                pw_tok = p.add("dve", lambda e, tt=tt: e.tensor_add(
                    x_tm[:, tt, :], x_tm[:, tt, :], PW[:, :]),
                    waits=[("TEe", p.c("TEe"))])
            hfree = {"TEe": p.c("TEe")}
            wc_tok[sc] = {"TEe": p.c("TEe")}
        wz_tok[sg] = {"TEe": p.c("TEe")}
    p.dma("sy", lambda e: e.dma_start(
        out=out_d[:].rearrange("(t p) d -> p t d", p=128), in_=x_tm[:]),
        waits=p.now("DVEa"))
    p.add("sy", lambda e: e.nop(), waits=p.dma_done("sy"))

    realize(nc, p)
    return nc




def assemble_rkv(r1):
    def asm(nm):
        return np.stack([np.asarray(r1[c][nm]).reshape(HARM, B, POS)
                         for c in range(NCORE)], axis=2).reshape(HARM, B * S)
    return np.ascontiguousarray(np.concatenate([asm("rk"), asm("rv")], axis=0))


def run_l2(H, r1):
    nc = build_l2()
    rkvf = assemble_rkv(r1)
    in_maps = []
    for c in range(NCORE):
        m = {"x": H["x_shards"][c], "rkvf": rkvf,
             "rqin": np.asarray(r1[c]["rq"]),
             "ident_bf": H["ident_bf"], "ident_f32": H["ident_f32"],
             "basis_oT": H["basis_oT"]}
        for nm in ("q", "k", "v", "o"):
            m[f"phase_{nm}T"] = H[f"phase_{nm}T"]
            m[f"amp_{nm}T"] = H[f"amp_{nm}T"]
        for nm in ("wf", "uf", "wi", "ui", "wc", "uc", "wo", "uo"):
            m[nm + "T"] = H[nm + "T"]
        for nm in ("bf", "bi", "bc", "bo", "in_c"):
            m[nm + "_r"] = H[nm + "_r"]
        m["bo_row"] = H["bo_row"]
        for nm in ("Wq", "Wk", "Wv"):
            m[nm + "T"] = H[nm + "T"]
        in_maps.append(m)
    res = run_bass_kernel_spmd(nc, in_maps, core_ids=list(range(NCORE)))
    return res.results


def build_l3():
    nc = bass.Bass("TRN2", target_bir_lowering=False)
    x1_d = nc.dram_tensor("x1", [TOK, D], F32, kind="ExternalInput")
    xn1_d = nc.dram_tensor("xn1", [D, TOK], BF16, kind="ExternalInput")
    q1_d = nc.dram_tensor("Q1", [64, TOK], BF16, kind="ExternalInput")
    k1_d = nc.dram_tensor("K1", [64, TOK], BF16, kind="ExternalInput")
    v1_d = nc.dram_tensor("V1", [64, TOK], BF16, kind="ExternalInput")
    sa_d = nc.dram_tensor("states_all", [NCORE, B, 64, 65], F32, kind="ExternalInput")
    pm_d = nc.dram_tensor("prefix_mask", [64, NCORE], F32, kind="ExternalInput")
    wo1_d = nc.dram_tensor("Wo1T", [64, D], F32, kind="ExternalInput")
    rm_d = nc.dram_tensor("rmJ", [16, 4, 64, 16], F32, kind="ExternalInput")
    rup_d = nc.dram_tensor("rupJ", [16, 4, 64, 64], F32, kind="ExternalInput")
    wc_d = nc.dram_tensor("wcprojT", [16384, D], F32, kind="ExternalInput")
    triu_d = nc.dram_tensor("triu", [128, 128], BF16, kind="ExternalInput")
    ident_d = nc.dram_tensor("ident_bf", [128, 128], BF16, kind="ExternalInput")
    out_d = nc.dram_tensor("out", [TOK, D], F32, kind="ExternalOutput")
    c_scr = nc.dram_tensor("c_scr", [16, 2, 2, 8, 8, 64], BF16)

    x_tm = nc.alloc_sbuf_tensor("x_tm", [128, NT, D], F32)
    xn_fm = nc.alloc_sbuf_tensor("xn_fm", [128, ND, TOK], BF16)
    UC = nc.alloc_sbuf_tensor("UC", [128, 18432], F32)

    def carve(arena, w0, w1, dtype=None, shape=None):
        v = arena[:, w0:w1]
        if dtype is not None:
            v = v.bitcast(dtype)
        if shape is not None:
            pat = " ".join(f"a{i}" for i in range(len(shape)))
            v = v.rearrange(f"p ({pat}) -> p {pat}",
                            **{f"a{i}": shape[i] for i in range(len(shape))})
        return v

    wcs = carve(UC, 0, 8192, BF16, (2, ND, D))
    hhg = carve(UC, 8192, 16384, BF16, (ND, TOK))
    wzg = carve(UC, 16384, 18432, BF16, (2, 16, 128))
    qf = nc.alloc_sbuf_tensor("qf", [64, TOK], BF16)
    kff = nc.alloc_sbuf_tensor("kff", [64, TOK], BF16)
    vf = nc.alloc_sbuf_tensor("vf", [64, TOK], BF16)
    kf_tmv = nc.alloc_sbuf_tensor("kf_tmv", [128, NT, 64], BF16)
    vp_tmv = nc.alloc_sbuf_tensor("vp_tmv", [128, NT, 72], BF16)
    sa_sb = nc.alloc_sbuf_tensor("sa_sb", [64, NCORE, B, 65], F32)
    s0_sb = nc.alloc_sbuf_tensor("s0_sb", [64, B, 65], F32)
    pm_sb = nc.alloc_sbuf_tensor("pm_sb", [64, NCORE], F32)
    ssb = nc.alloc_sbuf_tensor("ssb", [64, 65], BF16)
    sacc = nc.alloc_sbuf_tensor("sacc", [64, 65], F32)
    et_sb = nc.alloc_sbuf_tensor("et_sb", [128, 128], BF16)
    obf = nc.alloc_sbuf_tensor("obf", [128, 64], BF16)
    ofs = nc.alloc_sbuf_tensor("ofs", [64, 128], BF16)
    wo1 = nc.alloc_sbuf_tensor("wo1", [64, D], BF16)
    rm_sb = carve(UC, 0, 2048, BF16)
    rup_sb = carve(UC, 2048, 10240, BF16)
    cflat = carve(UC, 10240, 18432, BF16)
    triu_sb = nc.alloc_sbuf_tensor("triu_sb", [128, 128], BF16)
    ident = nc.alloc_sbuf_tensor("ident", [128, 128], BF16)
    rec = nc.alloc_sbuf_tensor("rec", [128, 8], F32)

    PZ = nc.alloc_psum_tensor("PZ", [128, 2048], F32)
    PY = nc.alloc_psum_tensor("PY", [128, 1536], F32)
    PTB = nc.alloc_psum_tensor("PTB", [128, 1024], BF16)
    PW = PY[:, 0:1024]
    PEC = PZ[:, 0:512]
    PO = PZ[:, 512:1024]
    PS = PZ[:, 1024:1536]
    PC = PZ[:, 1536:2048]

    def tok_w(tok):
        return list(tok.items()) if tok else []

    p = Plan()
    p.dma("sy", lambda e: e.dma_start(out=x_tm[:],
          in_=x1_d[:].rearrange("(t p) d -> p t d", p=128)))
    p.dma("sy", lambda e: e.dma_start(out=xn_fm[:],
          in_=xn1_d[:].rearrange("(a p) n -> p a n", p=128)))
    p.dma("sy", lambda e: e.dma_start(out=qf[:], in_=q1_d[:]))
    p.dma("sy", lambda e: e.dma_start(out=kff[:], in_=k1_d[:]))
    p.dma("sy", lambda e: e.dma_start(out=vf[:], in_=v1_d[:]))
    p.dma("sy", lambda e: e.dma_start(out=triu_sb[:], in_=triu_d[:]))
    p.dma("sy", lambda e: e.dma_start(out=ident[:], in_=ident_d[:]))
    p.dma("sy", lambda e: e.dma_start(out=sa_sb[:],
          in_=sa_d[:].rearrange("r b k c -> k r b c")))
    p.dma("sy", lambda e: e.dma_start(out=pm_sb[:], in_=pm_d[:]))
    ds0 = p.dma_done("sy")
    p.dma("gp", lambda e: e.dma_start(out=wo1[:], in_=wo1_d[:]))
    p.dma("gp", lambda e: e.dma_start(
        out=rm_sb[0:16, 0:4096], in_=rm_d[:].rearrange("j h r i -> j (h r i)")))
    p.dma("gp", lambda e: e.dma_start(
        out=rup_sb[0:16, :], in_=rup_d[:].rearrange("j h r k -> j (h r k)")))
    dl0 = p.dma_done("gp")

    ccp = None
    for blk in range(32):
        for j in range(8):
            hr = blk * 8 + j
            w = []
            if blk == 0 and j == 0:
                w = dl0
            if j == 0 and ccp is not None:
                w = w + tok_w(ccp)
            p.add("te", lambda e, hr=hr, j=j: e.matmul(
                PC[0:16, bass.ts(j, 64)], rm_sb[0:16, hr * 16:(hr + 1) * 16],
                rup_sb[0:16, hr * 64:(hr + 1) * 64], start=True, stop=True),
                waits=w, incs=([("TEc", 1)] if j == 7 else None))
        ccp = p.add("dve", lambda e, blk=blk: e.tensor_copy(
            cflat[0:16, bass.ts(blk, 512)], PC[0:16, :]),
            waits=[("TEc", p.c("TEc"))])
    p.dma("sy", lambda e: e.dma_start(
        out=c_scr[:].rearrange("i a b g rl k -> i (a b g rl k)"),
        in_=cflat[0:16, :]),
        waits=p.now("DVEa"))
    cscr_done = p.dma_done("sy")

    p.add("dve", lambda e: e.memset(vp_tmv[:, :, 64:65], 1.0), waits=ds0)
    for srcv, dst in ((kff, kf_tmv), (vf, vp_tmv)):
        for rnd in range(2):
            for j in range(8):
                kt = rnd * 8 + j
                w = p.now("DVEa") if j == 0 else []
                p.add("te", lambda e, srcv=srcv, kt=kt, j=j: e.transpose(
                    PTB[:, j * 64:(j + 1) * 64], srcv[:, bass.ts(kt, 128)],
                    ident[0:64, 0:64]),
                    waits=w, incs=([("TEt", 1)] if j == 7 else None))
            p.add("dve", lambda e, dst=dst, rnd=rnd: e.tensor_copy(
                dst[:, bass.ts(rnd, 8), 0:64],
                PTB[:, 0:512].rearrange("p (a b) -> p a b", a=8)),
                waits=[("TEt", p.c("TEt"))])
    for b in range(B):
        p.add("dve", lambda e, b=b: e.tensor_scalar(
            s0_sb[:, b, :], sa_sb[:, 0, b, :], pm_sb[:, 0:1], None, ALU.mult))
        for r in range(1, NCORE):
            p.add("dve", lambda e, b=b, r=r: e.scalar_tensor_tensor(
                s0_sb[:, b, :], sa_sb[:, r, b, :], pm_sb[:, r:r + 1],
                s0_sb[:, b, :], ALU.mult, ALU.add))
    tr_done = p.now("DVEa")

    pw_tok = None
    sup_tok = None
    for b in range(B):
        for ct in range(4):
            tile = b * 4 + ct
            if ct == 0:
                sw = list(tr_done) if b == 0 else p.now("TEe")
                scp = p.add("dve", lambda e, b=b: e.tensor_copy(
                    ssb[0:64, 0:65], s0_sb[0:64, b, :]), waits=sw)
            else:
                scp = p.add("dve", lambda e, b=b: e.tensor_add(
                    ssb[0:64, 0:65], s0_sb[0:64, b, :], sacc[0:64, 0:65]),
                    waits=p.now("TEe"))
            w = p.now("DVEa")
            p.add("te", lambda e, tile=tile: e.matmul(
                PEC[0:128, 0:128], kff[:, bass.ts(tile, 128)],
                qf[:, bass.ts(tile, 128)], start=True, stop=True),
                waits=w, incs=[("TEe", 1)])
            mcp = p.add("dve", lambda e: e.tensor_tensor(
                et_sb[:], PEC[0:128, 0:128], triu_sb[:], ALU.mult),
                waits=[("TEe", p.c("TEe"))])
            p.add("te", lambda e, tile=tile: e.matmul(
                PO[0:128, 0:65], et_sb[:], vp_tmv[:, tile, 0:65],
                start=True, stop=False), waits=tok_w(mcp) + tok_w(sup_tok))
            p.add("te", lambda e, tile=tile: e.matmul(
                PO[0:128, 0:65], qf[:, bass.ts(tile, 128)], ssb[0:64, 0:65],
                start=False, stop=True), incs=[("TEe", 1)])
            p.add("dve", lambda e: e.tensor_scalar(
                rec[:, 0:1], PO[:, 64:65], 1e-6, None, ALU.add),
                waits=[("TEe", p.c("TEe"))])
            p.add("dve", lambda e: e.reciprocal(rec[:, 1:2], rec[:, 0:1]))
            ob = p.add("dve", lambda e: e.tensor_scalar(
                obf[:], PO[:, 0:64], rec[:, 1:2], None, ALU.mult))
            p.add("te", lambda e: e.transpose(
                PTB[0:64, 0:128], obf[0:128, 0:64], ident[:]),
                waits=tok_w(ob) + p.now("DVEa"), incs=[("TEt", 1)])
            ocp = p.add("act", lambda e: e.copy(ofs[0:64, 0:128], PTB[0:64, 0:128]),
                waits=[("TEt", p.c("TEt"))])
            for half in range(2):
                w = tok_w(ocp) + tok_w(pw_tok) if half == 0 else []
                p.add("te", lambda e, half=half: e.matmul(
                    PW[:, bass.ts(half, 512)], ofs[0:64, 0:128],
                    wo1[0:64, bass.ts(half, 512)], start=True, stop=True),
                    waits=w + (dl0 if (b == 0 and ct == 0 and half == 0) else []),
                    incs=([("TEe", 1)] if half == 1 else None))
            pw_tok = p.add("dve", lambda e, tile=tile: e.tensor_add(
                x_tm[:, tile, :], x_tm[:, tile, :], PW[:, :]),
                waits=[("TEe", p.c("TEe"))])
            if ct < 3:
                p.add("te", lambda e, tile=tile: e.matmul(
                    PS[0:64, 0:65], kf_tmv[:, tile, 0:64], vp_tmv[:, tile, 0:65],
                    start=True, stop=True),
                    waits=p.now("DVEa"), incs=[("TEs", 1)])
                if ct == 0:
                    p.add("dve", lambda e: e.tensor_copy(sacc[:], PS[0:64, 0:65]),
                          waits=[("TEs", p.c("TEs"))])
                else:
                    p.add("dve", lambda e: e.tensor_add(
                        sacc[:], sacc[:], PS[0:64, 0:65]),
                        waits=[("TEs", p.c("TEs"))])
            sup_tok = {"TEs": p.c("TEs")}

    PWp = [PY[:, 0:1024], PZ[:, 1024:2048]]
    wz_tok = [None, None]
    wc_tok = [None, None]
    zfree = None
    hfree = None
    pw_evac = [None, None]
    for g in range(ND):
        sg = g % 2
        mz = p.add("dve", lambda e, sg=sg: e.memset(wzg[:, sg, :, :], 0.0),
                   waits=tok_w(wz_tok[sg]) + (cscr_done if g == 0 else []))
        for rl in range(8):
            p.dma("gp", lambda e, g=g, sg=sg, rl=rl: e.dma_start(
                out=wzg[rl * 16:(rl + 1) * 16, sg, :, :].rearrange(
                    "p m f -> p (m f)")[:, (2 * rl) * 128:(2 * rl + 2) * 128]
                .rearrange("p (a b k) -> p a b k", a=2, b=2),
                in_=c_scr[:, :, :, g, rl, :]),
                waits=tok_w(mz) if rl == 0 else [])
        wzdma = p.dma_done("gp")
        for half in range(2):
            hg = g * 2 + half
            sc = hg % 2
            p.dma("gp", lambda e, g=g, half=half, sc=sc: e.dma_start(
                out=wcs[:, sc, :, :],
                in_=wc_d[:].rearrange("(gg m p) d -> p gg m d", m=8, p=128)[
                    :, 2 * g + half, :, :]),
                waits=tok_w(wc_tok[sc]) + (cscr_done + p.now("TEe", "DVEa")
                                           if (g == 0 and half == 0) else []))
            wcdma = p.dma_done("gp")
            for m in range(8):
                mi = half * 8 + m
                for n in range(4):
                    w = []
                    if n == 0:
                        w = tok_w(zfree)
                    if n == 2:
                        w = tok_w(zhalf)
                    if m == 0 and n == 0:
                        w = w + wzdma + tok_w(hfree)
                        if hg == 0:
                            w += p.now("ACTa", "DVEa")
                    p.add("te", lambda e, g=g, mi=mi, n=n: e.matmul(
                        PZ[:, bass.ts(n % 2, 512)], wzg[:, g % 2, mi, :],
                        xn_fm[:, g, bass.ts(n, 512)], start=True, stop=True),
                        waits=w, incs=[("TEz", 1)] if n in (1, 3) else None)
                    if n == 1:
                        zhalf = p.add("act", lambda e, m=m: e.activation(
                            hhg[:, m, 0:1024], PZ[:, 0:1024], AF.Gelu,
                            bias=0.0, scale=1.0), waits=[("TEz", p.c("TEz"))])
                zfree = p.add("act", lambda e, m=m: e.activation(
                    hhg[:, m, 1024:2048], PZ[:, 0:1024], AF.Gelu,
                    bias=0.0, scale=1.0), waits=[("TEz", p.c("TEz"))])
            for tt in range(NT):
                par = tt % 2
                w = []
                if tt >= 2 or g > 0 or half > 0:
                    w = w + tok_w(pw_evac[par])
                if tt == 1 and g == 0 and half == 0:
                    w = w + p.now("TEs", "TEc")
                if tt == 0:
                    w = w + wcdma + p.now("ACTa")
                    if g == 0 and half == 0:
                        w = w + p.now("DVEa")
                for m in range(8):
                    for half2 in range(2):
                        p.add("te", lambda e, tt=tt, m=m, sc=sc, par=par,
                              half2=half2:
                              e.matmul(PWp[par][:, bass.ts(half2, 512)],
                                       hhg[:, m, bass.ts(tt, 128)],
                                       wcs[:, sc, m, bass.ts(half2, 512)],
                                       start=(m == 0), stop=(m == 7)),
                              waits=(w if (m == 0 and half2 == 0) else []),
                              incs=([("TEe", 1)] if (m == 7 and half2 == 1)
                                    else None))
                pw_evac[par] = p.add("dve", lambda e, tt=tt, par=par:
                                     e.tensor_add(
                                         x_tm[:, tt, :], x_tm[:, tt, :],
                                         PWp[par][:, :]),
                                     waits=[("TEe", p.c("TEe"))])
            hfree = {"TEe": p.c("TEe")}
            wc_tok[sc] = {"TEe": p.c("TEe")}
        wz_tok[sg] = {"TEe": p.c("TEe")}
    p.dma("sy", lambda e: e.dma_start(
        out=out_d[:].rearrange("(t p) d -> p t d", p=128), in_=x_tm[:]),
        waits=p.now("DVEa"))
    p.add("sy", lambda e: e.nop(), waits=p.dma_done("sy"))
    realize(nc, p)
    return nc


def run_l3(H, r2):
    nc = build_l3()
    states_all = np.stack([np.asarray(r["states"]) for r in r2])
    in_maps = []
    for c in range(NCORE):
        pm = np.tile((np.arange(NCORE) < c).astype(np.float32), (64, 1))
        in_maps.append({
            "x1": np.asarray(r2[c]["x1"]), "xn1": np.asarray(r2[c]["xn1"]),
            "Q1": np.asarray(r2[c]["Q1"]), "K1": np.asarray(r2[c]["K1"]),
            "V1": np.asarray(r2[c]["V1"]),
            "states_all": states_all, "prefix_mask": np.ascontiguousarray(pm),
            "Wo1T": H["WoT"], "rmJ": H["rmJ"], "rupJ": H["rupJ"],
            "wcprojT": H["wcprojT"], "triu": H["triu"],
            "ident_bf": H["ident_bf"],
        })
    res = run_bass_kernel_spmd(nc, in_maps, core_ids=list(range(NCORE)))
    return res.results


def run_l1(H):
    nc = build_l1()
    in_maps = []
    for c in range(NCORE):
        in_maps.append({
            "x": H["x_shards"][c],
            "ident_bf": H["ident_bf"],
            "basis_qT": H["basis_qT"], "basis_kT": H["basis_kT"],
            "basis_vT": H["basis_vT"],
        })
    res = run_bass_kernel_spmd(nc, in_maps, core_ids=list(range(NCORE)))
    return res.results


def fused_in_map(H, c):
    m = {"x": H["x_shards"][c],
         "ident_bf": H["ident_bf"], "ident_f32": H["ident_f32"],
         "basis_oT": H["basis_oT"],
         "basis_qT": H["basis_qT"], "basis_kT": H["basis_kT"],
         "basis_vT": H["basis_vT"],
         "Wo1T": H["WoT"], "rmJ": H["rmJ"], "rupJ": H["rupJ"],
         "wcprojT": H["wcprojT"], "triu": H["triu"],
         "prefix_mask": np.ascontiguousarray(
             np.tile((np.arange(NCORE) < c).astype(np.float32), (64, 1)))}
    for nm in ("q", "k", "v", "o"):
        m[f"phase_{nm}T"] = H[f"phase_{nm}T"]
        m[f"amp_{nm}T"] = H[f"amp_{nm}T"]
    for nm in ("wf", "uf", "wi", "ui", "wc", "uc", "wo", "uo"):
        m[nm + "T"] = H[nm + "T"]
    for nm in ("bf", "bi", "bc", "bo", "in_c"):
        m[nm + "_r"] = H[nm + "_r"]
    for nm in ("Wq", "Wk", "Wv"):
        m[nm + "T"] = H[nm + "T"]
    return m


def run_fused(H):
    nc = build_fused()
    in_maps = [fused_in_map(H, c) for c in range(NCORE)]
    res = run_bass_kernel_spmd(nc, in_maps, core_ids=list(range(NCORE)))
    return res.results


def kernel(**inputs):
    H = host_prep(inputs)
    r1 = run_l1(H)
    r2 = run_l2(H, r1)
    r3 = run_l3(H, r2)
    out = np.empty((B, S, D), np.float32)
    for c in range(NCORE):
        out[:, c * POS:(c + 1) * POS, :] = np.asarray(
            r3[c]["out"]).reshape(B, POS, D)
    return out


if __name__ == "__main__":
    pass
